# version 01
# baseline (speedup 1.0000x reference)
import contextlib
import numpy as np
import concourse.bass as bass
import concourse.mybir as mybir
from concourse.bass_utils import run_bass_kernel_spmd

F32 = mybir.dt.float32
BF16 = mybir.dt.bfloat16
I32 = mybir.dt.int32
AF = mybir.ActivationFunctionType
ALU = mybir.AluOpType
AX = mybir.AxisListType

NCORE = 8
B, T, D = 4, 4096, 1024
NL = 2048
NCH = D // 128
NT = NL // 128
H, HD = 16, 64


class Sem:
    __slots__ = ("h", "n")
    def __init__(self, h):
        self.h = h
        self.n = 0
    def inc(self, k=1):
        self.n += k
        return self.n

class Pool:
    def __init__(self, slots, ev=None):
        self.slots = slots
        self.free_at = [None] * len(slots)
        self.rr = 0
    def acquire(self):
        i = self.rr % len(self.slots)
        self.rr += 1
        return i, self.slots[i], self.free_at[i]
    def release(self, i, sem, val=None):
        if val is None:
            raise TypeError("release(i, sem, val)")
        self.free_at[i] = (sem, val)


def pwait(e, free):
    if free is not None:
        e.wait_ge(free[0].h, free[1])


class VSelf:
    def __init__(self, sem):
        self.sem = sem
        self.n = 0
    def edge(self, e, ins):
        self.n += 1
        ins.then_inc(self.sem.h, 1)
        e.wait_ge(self.sem.h, self.n)

class Ctx:
    def __init__(self, nc, es):
        self.nc = nc
        self.es = es
        self.ops = {e: [] for e in ("g", "t", "v", "s", "y")}
        self._semi = 0
    def sbuf(self, name, shape, dtype):
        return self.es.enter_context(self.nc.sbuf_tensor(name, shape, dtype))
    def psum(self, name, shape, dtype=F32):
        return self.es.enter_context(self.nc.psum_tensor(name, shape, dtype))
    def sem(self, name=None):
        self._semi += 1
        return Sem(self.es.enter_context(self.nc.semaphore(name or f"s{self._semi}")))
    def g(self, f): self.ops["g"].append(f)
    def t(self, f): self.ops["t"].append(f)
    def v(self, f): self.ops["v"].append(f)
    def s(self, f): self.ops["s"].append(f)
    def y(self, f): self.ops["y"].append(f)
    def finish(self):
        blk = self.es.enter_context(self.nc.Block())
        ops = self.ops
        @blk.gpsimd
        def _(e):
            for f in ops["g"]: f(e)
        @blk.tensor
        def _(e):
            for f in ops["t"]: f(e)
        @blk.vector
        def _(e):
            for f in ops["v"]: f(e)
        @blk.scalar
        def _(e):
            for f in ops["s"]: f(e)
        @blk.sync
        def _(e):
            for f in ops["y"]: f(e)


def dma_in(ctx, dst_ap, src_ap, ld, queue="g"):
    def f(e):
        e.dma_start(out=dst_ap, in_=src_ap).then_inc(ld.h, 16)
    getattr(ctx, queue)(f)
    return ld.inc(16)


class EluEvac:
    def __init__(self, ctx, width, mm, va, sa, ev, tag):
        self.ctx = ctx
        self.w = width
        self.u = [ctx.sbuf(f"elu_u{tag}_{i}", [128, width], F32) for i in range(2)]
        self.e = [ctx.sbuf(f"elu_e{tag}_{i}", [128, width], F32) for i in range(2)]
        self.mm, self.va, self.sa, self.ev = mm, va, sa, ev
        self.i = 0
        self.sa_of = {}
        self.ev_of = {}
    def plan(self, ps_ap, out_ap, mmv, out_war=None):
        ctx, idx, p = self.ctx, self.i, self.i % 2
        self.i += 1
        u, eb = self.u[p], self.e[p]
        va, sa, ev, mm = self.va, self.sa, self.ev, self.mm
        war_sa = self.sa_of.get(idx - 2)
        def ev1(e):
            e.wait_ge(mm.h, mmv)
            if war_sa:
                e.wait_ge(sa.h, war_sa)
            e.tensor_scalar_min(u[:, :], ps_ap, 0.0).then_inc(va.h, 1)
        ctx.v(ev1)
        vav = va.inc()
        war_ev = self.ev_of.get(idx - 2)
        def ev2(e):
            e.wait_ge(va.h, vav)
            if war_ev:
                e.wait_ge(ev.h, war_ev)
            e.activation(eb[:, :], u[:, :], AF.Exp).then_inc(sa.h, 1)
        ctx.s(ev2)
        sav = sa.inc()
        self.sa_of[idx] = sav
        def ev3(e):
            e.wait_ge(sa.h, sav)
            if out_war is not None:
                e.wait_ge(out_war[0].h, out_war[1])
            e.scalar_tensor_tensor(out_ap, ps_ap, 0.0, eb[:, :], ALU.max, ALU.add).then_inc(ev.h, 1)
        ctx.v(ev3)
        evv = ev.inc()
        self.ev_of[idx] = evv
        return evv



def build_A1():
    nc = bass.Bass("TRN2", target_bir_lowering=False)
    x = nc.dram_tensor("x", [D, NL], F32, kind="ExternalInput").ap()
    wqkvT = nc.dram_tensor("wqkvT", [D, 3 * D], F32, kind="ExternalInput").ap()
    projm = nc.dram_tensor("projm", [D, 256], F32, kind="ExternalInput").ap()
    qf_o = nc.dram_tensor("qf", [128, 32 * NL], BF16, kind="ExternalOutput").ap()
    kvx_o = nc.dram_tensor("kvx", [128, H * 130], F32, kind="ExternalOutput").ap()

    es = contextlib.ExitStack()
    with es:
        ctx = Ctx(nc, es)
        x_bf = ctx.sbuf("x_bf", [128, NCH * NL], BF16)
        wqk = ctx.sbuf("wqk", [128, NCH * 2048], BF16)
        pj = ctx.sbuf("pj", [128, NCH * 256], BF16)
        qk_bf = ctx.sbuf("qk_bf", [128, 16 * NL], BF16)
        vaug = ctx.sbuf("vaug", [128, NT * H * 65], BF16)
        kf_h = ctx.sbuf("kf_h", [128, NT * 256], BF16)
        qf_t = ctx.sbuf("qf_t", [128, 4 * 512], BF16)
        kvx_sb = ctx.sbuf("kvx_sb", [128, H * 130], F32)
        wv = wqk

        ld = ctx.sem("ld")
        mm = ctx.sem("mm")
        ev = ctx.sem("ev")
        va = ctx.sem("va")
        sa = ctx.sem("sa")
        st = ctx.sem("st")
        qst = [ctx.sem(f"qst{i}") for i in range(4)]

        ps512 = Pool([ctx.psum(f"p5_{i}", [128, 512]) for i in range(4)], ev)
        pkf = Pool([ctx.psum(f"pf_{i}", [128, 512]) for i in range(2)], ev)
        pkv = Pool([ctx.psum(f"pv_{i}", [128, 512]) for i in range(2)], ev)

        elu_q = EluEvac(ctx, 512, mm, va, sa, ev, "q")
        elu_k = EluEvac(ctx, 256, mm, va, sa, ev, "k")

        ld_x = dma_in(ctx, x_bf[:, :], x.rearrange("(c p) n -> p c n", p=128), ld)
        ld_wqk = dma_in(ctx, wqk[:, :], wqkvT.rearrange("(k p) m -> p k m", p=128)[:, :, 0:2048], ld)
        ld_pj = dma_in(ctx, pj[:, :], projm.rearrange("(k p) m -> p k m", p=128), ld)
        ld_all = ld.n
        ctx.v(lambda e: e.memset(
            vaug[:, :].rearrange("p (a h c) -> p a h c", h=H, c=65)[:, :, :, 64:65], 1.0))

        qk_ev = {}
        for m in range(16):
            for nt in range(4):
                i, ps, wfree = ps512.acquire()
                def mmf(e, m=m, nt=nt, ps=ps, wfree=wfree):
                    e.wait_ge(ld.h, ld_all)
                    pwait(e, wfree)
                    for k in range(NCH):
                        ins = e.matmul(ps[:, :], wqk[:, k * 2048 + m * 128:k * 2048 + (m + 1) * 128],
                                       x_bf[:, k * NL + nt * 512:k * NL + (nt + 1) * 512],
                                       start=(k == 0), stop=(k == NCH - 1))
                    ins.then_inc(mm.h, 1)
                ctx.t(mmf)
                mmv = mm.inc()
                def evf(e, ps=ps, m=m, nt=nt, mmv=mmv):
                    e.wait_ge(mm.h, mmv)
                    e.tensor_copy(qk_bf[:, m * NL + nt * 512:m * NL + (nt + 1) * 512],
                                  ps[:, :]).then_inc(ev.h, 1)
                ctx.v(evf)
                qk_ev[(m, nt)] = ev.inc()
                ps512.release(i, ev, qk_ev[(m, nt)])

        qk_mm_done = mm.n
        def ld_wv_f(e):
            e.wait_ge(mm.h, qk_mm_done)
            e.dma_start(out=wv[:, 0:NCH * 1024],
                        in_=wqkvT.rearrange("(k p) m -> p k m", p=128)[:, :, 2048:3072]).then_inc(ld.h, 16)
        ctx.g(ld_wv_f)
        ld_wv = ld.inc(16)

        for mc in range(NT):
            for nt in range(2):
                i, ps, wfree = ps512.acquire()
                def mmf(e, mc=mc, nt=nt, ps=ps, wfree=wfree):
                    e.wait_ge(ld.h, ld_wv)
                    pwait(e, wfree)
                    for k in range(NCH):
                        ins = e.matmul(ps[:, :], x_bf[:, k * NL + mc * 128:k * NL + (mc + 1) * 128],
                                       wv[:, k * 1024 + nt * 512:k * 1024 + (nt + 1) * 512],
                                       start=(k == 0), stop=(k == NCH - 1))
                    ins.then_inc(mm.h, 1)
                ctx.t(mmf)
                mmv = mm.inc()
                dst = vaug[:, :].rearrange("p (a h c) -> p a h c", h=H, c=65)[:, mc, nt * 8:(nt + 1) * 8, 0:64]
                def evf(e, ps=ps, dst=dst, mmv=mmv):
                    e.wait_ge(mm.h, mmv)
                    e.tensor_copy(dst, ps[:, :]).then_inc(ev.h, 1)
                ctx.v(evf)
                ps512.release(i, ev, ev.inc())
        qkv_done = ev.n

        qf_i = 0
        qf_war = {}
        for h in range(H):
            rb = (h % 2) * 64
            qc = h // 2
            for fc in range(2):
                for nt in range(4):
                    i, ps, wfree = ps512.acquire()
                    def mmf(e, rb=rb, qc=qc, fc=fc, nt=nt, ps=ps, wfree=wfree):
                        e.wait_ge(ld.h, ld_all)
                        e.wait_ge(ev.h, qkv_done)
                        pwait(e, wfree)
                        e.matmul(ps[:, :], pj[rb:rb + 64, qc * 256 + fc * 128:qc * 256 + (fc + 1) * 128],
                                 qk_bf[rb:rb + 64, qc * NL + nt * 512:qc * NL + (nt + 1) * 512],
                                 start=True, stop=True).then_inc(mm.h, 1)
                    ctx.t(mmf)
                    mmv = mm.inc()
                    col = (h * 2 + fc) * NL + nt * 512
                    qslot = qf_i % 4
                    war = qf_war.get(qslot)
                    evv = elu_q.plan(ps[:, :], qf_t[:, qslot * 512:(qslot + 1) * 512], mmv,
                                     out_war=(qst[qslot], war) if war else None)
                    ps512.release(i, ev, evv)
                    def stf(e, qslot=qslot, col=col, evv=evv):
                        e.wait_ge(ev.h, evv)
                        e.dma_start(out=qf_o[:, col:col + 512],
                                    in_=qf_t[:, qslot * 512:(qslot + 1) * 512]).then_inc(qst[qslot].h, 16)
                    ctx.y(stf)
                    qf_war[qslot] = qst[qslot].inc(16)
                    qf_i += 1

        prev_kv_mm = 0
        for h in range(H):
            rb = (h % 2) * 64
            kc = 8 + h // 2
            for mc in range(NT):
                i, ps, wfree = pkf.acquire()
                def mmf(e, rb=rb, kc=kc, h=h, mc=mc, ps=ps, wfree=wfree, pkm=prev_kv_mm):
                    e.wait_ge(ev.h, qkv_done)
                    pwait(e, wfree)
                    if pkm:
                        e.wait_ge(mm.h, pkm)
                    e.matmul(ps[:, 0:256],
                             qk_bf[rb:rb + 64, kc * NL + mc * 128:kc * NL + (mc + 1) * 128],
                             pj[rb:rb + 64, (h // 2) * 256:(h // 2) * 256 + 256],
                             start=True, stop=True).then_inc(mm.h, 1)
                ctx.t(mmf)
                mmv = mm.inc()
                evv = elu_k.plan(ps[:, 0:256], kf_h[:, mc * 256:(mc + 1) * 256], mmv)
                pkf.release(i, ev, evv)
            kf_done = ev.n

            ik, psv, wfree = pkv.acquire()
            def kvmm(e, h=h, psv=psv, wfree=wfree, kf_done=kf_done):
                e.wait_ge(ev.h, kf_done)
                pwait(e, wfree)
                for fc in range(2):
                    for mc in range(NT):
                        ins = e.matmul(psv[:, fc * 65:(fc + 1) * 65],
                                       kf_h[:, mc * 256 + fc * 128:mc * 256 + fc * 128 + 128],
                                       vaug[:, (mc * H + h) * 65:(mc * H + h) * 65 + 65],
                                       start=(mc == 0), stop=(mc == NT - 1))
                ins.then_inc(mm.h, 1)
            ctx.t(kvmm)
            prev_kv_mm = mm.inc()
            def kvev(e, h=h, psv=psv, mmv=prev_kv_mm):
                e.wait_ge(mm.h, mmv)
                e.tensor_copy(kvx_sb[:, h * 130:(h + 1) * 130], psv[:, 0:130]).then_inc(ev.h, 1)
            ctx.v(kvev)
            pkv.release(ik, ev, ev.inc())

        all_ev = ev.n
        def stores(e):
            e.wait_ge(ev.h, all_ev)
            e.dma_start(out=kvx_o, in_=kvx_sb[:, :]).then_inc(st.h, 16)
        ctx.y(stores)
        fin = st.inc(16)
        qfin = [q.n for q in qst]
        def finw(e):
            e.wait_ge(st.h, fin)
            for q, v_ in zip(qst, qfin):
                e.wait_ge(q.h, v_)
        ctx.y(finw)

        ctx.finish()
    return nc




class S:
    def __init__(self, ctx, n512=4):
        self.ctx = ctx
        self.ld = ctx.sem("ld")
        self.mm = ctx.sem("mm")
        self.ev = ctx.sem("ev")
        self.va = ctx.sem("va")
        self.sa = ctx.sem("sa")
        self.st = ctx.sem("st")
        self.ps = Pool([ctx.psum(f"ps_{i}", [128, 512]) for i in range(n512)], self.ev)
        self.ones128 = ctx.sbuf("ones128", [128, 1], F32)
        self.onesrow = ctx.sbuf("onesrow", [1, 128], F32)
        self.onesrow_bf = ctx.sbuf("s_onesrow_bf", [1, 128], BF16)
        self.sq = [ctx.sbuf(f"sq_{i}", [128, 512], F32) for i in range(2)]
        self.rowbuf = ctx.sbuf("rowbuf", [1, 4 * 512], F32)
        self.lnt = [ctx.sbuf(f"lnt_{i}", [128, 512], F32) for i in range(2)]
        self._sq_war = {}
        self._sq_i = 0
        self.eps_ln = ctx.sbuf("eps_ln", [1, 1], F32)
        self.vself = VSelf(ctx.sem("vself"))
        def _init(e):
            e.memset(self.ones128[:, :], 1.0)
            e.memset(self.onesrow[:, :], 1.0)
            e.memset(self.onesrow_bf[:, :], 1.0)
            e.memset(self.eps_ln[:, :], 1e-5).then_inc(self.ev.h, 1)
        ctx.v(_init)
        self.init_ev = self.ev.inc()


def plan_ln_tile(ctx, s, src_chunk, dst_chunk, *, gamma=None, beta=None,
                 bf_chunk=None, eps=1e-5, nch=8, dim=None, wait_ev=0, wait_mm=0):
    dim = dim or nch * 128
    mm, ev, va, sa = s.mm, s.ev, s.va, s.sa
    i_sum, ps_sum, w_sum = s.ps.acquire()
    i_sq, ps_sq, w_sq = s.ps.acquire()
    def sumf(e, w_sum=w_sum):
        e.wait_ge(ev.h, max(wait_ev, s.init_ev))
        if wait_mm:
            e.wait_ge(mm.h, wait_mm)
        pwait(e, w_sum)
        for mc in range(nch):
            ins = e.matmul(ps_sum[0:1, :], s.ones128[:, :], src_chunk(mc),
                           start=(mc == 0), stop=(mc == nch - 1))
        ins.then_inc(mm.h, 1)
    ctx.t(sumf)
    mm.inc()
    for mc in range(nch):
        p = s._sq_i % 2
        s._sq_i += 1
        war = s._sq_war.get(s._sq_i - 3)
        def sqf(e, mc=mc, p=p, war=war):
            if war:
                e.wait_ge(mm.h, war)
            if wait_ev:
                e.wait_ge(ev.h, wait_ev)
            e.activation(s.sq[p][:, :], src_chunk(mc), AF.Square).then_inc(sa.h, 1)
        ctx.s(sqf)
        sav = sa.inc()
        def sqmm(e, mc=mc, p=p, sav=sav, w_sq=w_sq):
            e.wait_ge(sa.h, sav)
            if mc == 0:
                pwait(e, w_sq)
            e.matmul(ps_sq[0:1, :], s.ones128[:, :], s.sq[p][:, :],
                     start=(mc == 0), stop=(mc == nch - 1)).then_inc(mm.h, 1)
        ctx.t(sqmm)
        s._sq_war[s._sq_i - 1] = mm.inc()
    mm_sq = mm.n
    R = s.rowbuf
    mu, var, rstd, bsh = R[0:1, 0:512], R[0:1, 512:1024], R[0:1, 1024:1536], R[0:1, 1536:2048]
    def rowv(e):
        e.wait_ge(mm.h, mm_sq)
        s.vself.edge(e, e.tensor_scalar_mul(mu, ps_sum[0:1, :], 1.0 / dim))
        s.vself.edge(e, e.tensor_mul(var, mu, mu))
        e.scalar_tensor_tensor(var, ps_sq[0:1, :], 1.0 / dim, var,
                               ALU.mult, ALU.subtract).then_inc(ev.h, 1)
    ctx.v(rowv)
    stats_free = ev.inc()
    s.ps.release(i_sum, ev, stats_free)
    s.ps.release(i_sq, ev, stats_free)
    def rows_(e):
        e.wait_ge(ev.h, stats_free)
        e.activation(rstd, var, AF.Sqrt, bias=s.eps_ln[0:1, 0:1]).then_inc(sa.h, 1)
    ctx.s(rows_)
    sav = sa.inc()
    def rowv3(e, sav=sav):
        e.wait_ge(sa.h, sav)
        s.vself.edge(e, e.reciprocal(rstd, rstd))
        e.scalar_tensor_tensor(bsh, mu, -1.0, rstd, ALU.mult, ALU.mult).then_inc(ev.h, 1)
    ctx.v(rowv3)
    rows_done = ev.inc()
    i_a, ps_a, w_a = s.ps.acquire()
    i_b, ps_b, w_b = s.ps.acquire()
    def bcast(e, w_a=w_a, w_b=w_b):
        e.wait_ge(ev.h, rows_done)
        pwait(e, w_a)
        pwait(e, w_b)
        e.matmul(ps_a[:, :], s.onesrow[:, :], rstd, start=True, stop=True)
        e.matmul(ps_b[:, :], s.onesrow[:, :], bsh, start=True, stop=True).then_inc(mm.h, 1)
    ctx.t(bcast)
    mm_ab = mm.inc()
    last_ev = None
    for mc in range(nch):
        p = mc % 2
        t1 = s.lnt[p]
        def app(e, mc=mc, t1=t1):
            e.wait_ge(mm.h, mm_ab)
            s.vself.edge(e, e.tensor_mul(t1[:, :], src_chunk(mc), ps_a[:, :]))
            s.vself.edge(e, e.tensor_add(t1[:, :], t1[:, :], ps_b[:, :]))
            dst = dst_chunk(mc) if dst_chunk is not None else bf_chunk(mc)
            if gamma is not None:
                ins = e.tensor_scalar(dst, t1[:, :], gamma[:, mc:mc + 1],
                                      beta[:, mc:mc + 1], ALU.mult, ALU.add)
            else:
                ins = e.tensor_copy(dst, t1[:, :])
            if bf_chunk is not None and dst_chunk is not None:
                s.vself.edge(e, ins)
                ins = e.tensor_copy(bf_chunk(mc), dst_chunk(mc))
            ins.then_inc(ev.h, 1)
        ctx.v(app)
        last_ev = ev.inc()
    s.ps.release(i_a, ev, last_ev)
    s.ps.release(i_b, ev, last_ev)
    return last_ev




def build_A2():
    nc = bass.Bass("TRN2", target_bir_lowering=False)
    x = nc.dram_tensor("x", [D, NL], F32, kind="ExternalInput").ap()
    qf = nc.dram_tensor("qf", [128, 32 * NL], BF16, kind="ExternalInput").ap()
    kvx_a = nc.dram_tensor("kvx_a", [128, H * 130], F32, kind="ExternalInput").ap()
    kvx_b = nc.dram_tensor("kvx_b", [128, H * 130], F32, kind="ExternalInput").ap()
    w_outT = nc.dram_tensor("w_outT", [D, D], F32, kind="ExternalInput").ap()
    b_out = nc.dram_tensor("b_out", [128, 8], F32, kind="ExternalInput").ap()
    gamma = nc.dram_tensor("gamma", [128, 8], F32, kind="ExternalInput").ap()
    beta = nc.dram_tensor("beta", [128, 8], F32, kind="ExternalInput").ap()
    w1T = nc.dram_tensor("w1T", [D, 4 * D], F32, kind="ExternalInput").ap()
    b1 = nc.dram_tensor("b1", [128, 32], F32, kind="ExternalInput").ap()
    w2T = nc.dram_tensor("w2T", [4 * D, D], F32, kind="ExternalInput").ap()
    b2 = nc.dram_tensor("b2", [128, 8], F32, kind="ExternalInput").ap()
    xa_o = nc.dram_tensor("xa", [D, NL], F32, kind="ExternalOutput").ap()

    es = contextlib.ExitStack()
    with es:
        ctx = Ctx(nc, es)
        s = S(ctx, n512=8)
        mm, ev, va, sa, ld, st = s.mm, s.ev, s.va, s.sa, s.ld, s.st

        x0 = [ctx.sbuf(f"x0_{i}", [128, 8 * 512], F32) for i in range(2)]
        rs = [ctx.sbuf(f"rs_{i}", [128, 8 * 512], F32) for i in range(2)]
        attn_bf = ctx.sbuf("attn_bf", [128, 8 * 512], BF16)
        x1_bf = ctx.sbuf("x1_bf", [128, 8 * 512], BF16)
        h_bf = ctx.sbuf("h_bf", [128, 32 * 512], BF16)
        wout = ctx.sbuf("wout", [128, 8 * 1024], BF16)
        w1b = [ctx.sbuf(f"w1b_{i}", [128, 8 * 128], BF16) for i in range(2)]
        w2b = [ctx.sbuf(f"w2b_{i}", [128, 32 * 128], BF16) for i in range(2)]
        qft = [ctx.sbuf(f"qft_{i}", [128, 512], BF16) for i in range(8)]
        kvt_a = ctx.sbuf("kvt_a", [128, H * 130], F32)
        kvt_b = ctx.sbuf("kvt_b", [128, H * 130], F32)
        kv_bf = ctx.sbuf("kv_bf", [128, H * 130], BF16)
        bo_fm = ctx.sbuf("bo_fm", [128, 8], F32)
        g_fm = ctx.sbuf("g_fm", [128, 8], F32)
        be_fm = ctx.sbuf("be_fm", [128, 8], F32)
        b2_fm = ctx.sbuf("b2_fm", [128, 8], F32)
        b1_fm = ctx.sbuf("b1_fm", [128, 32], F32)
        onesrow_bf = ctx.sbuf("onesrow_bf", [1, 128], BF16)
        zr_bf = ctx.sbuf("zr_bf", [1, 2 * 512], BF16)
        zb_sb = ctx.sbuf("zb_sb", [128, 512], F32)

        x0ld = [ctx.sem(f"x0ld{i}") for i in range(2)]
        qld = [ctx.sem(f"qld{i}") for i in range(8)]
        w1ld = [ctx.sem(f"w1ld{i}") for i in range(2)]
        w2ld = [ctx.sem(f"w2ld{i}") for i in range(2)]
        xst = [ctx.sem(f"xst{i}") for i in range(2)]

        ctx.v(lambda e: e.memset(onesrow_bf[:, :], 1.0))

        dma_in(ctx, wout[:, :], w_outT.rearrange("(k p) m -> p k m", p=128), ld)
        dma_in(ctx, kvt_a[:, :], kvx_a, ld)
        dma_in(ctx, kvt_b[:, :], kvx_b, ld)
        dma_in(ctx, bo_fm[:, :], b_out, ld)
        dma_in(ctx, g_fm[:, :], gamma, ld)
        dma_in(ctx, be_fm[:, :], beta, ld)
        dma_in(ctx, b2_fm[:, :], b2, ld)
        dma_in(ctx, b1_fm[:, :], b1, ld)
        ld_all = ld.n
        def kvadd(e):
            e.wait_ge(ld.h, ld_all)
            e.tensor_add(kv_bf[:, :], kvt_a[:, :], kvt_b[:, :]).then_inc(ev.h, 1)
        ctx.v(kvadd)
        kv_ready = ev.inc()

        qf_i = 0
        qslot_war = [0] * 8
        qld_cnt = [0] * 8
        x0_war = [0, 0]
        store_war = [0, 0]
        w1_war = [0, 0]
        w2_war = [0, 0]
        w1_cnt = [0, 0]
        w2_cnt = [0, 0]

        for nt in range(4):
            ntp = nt % 2
            def x0f(e, nt=nt, ntp=ntp):
                e.dma_start(out=x0[ntp][:, :],
                            in_=x.rearrange("(c p) n -> p c n", p=128)[:, :, nt * 512:(nt + 1) * 512]
                            ).then_inc(x0ld[ntp].h, 16)
            if nt >= 2:
                war_val = x0_war[ntp]
                def x0g(e, war_val=war_val, f=x0f):
                    e.wait_ge(ev.h, war_val)
                    f(e)
                ctx.g(x0g)
            else:
                ctx.g(x0f)
            x0_rdy = x0ld[ntp].inc(16)

            for hp in range(8):
                h0, h1 = 2 * hp, 2 * hp + 1
                tiles = {}
                for (hh, fc) in [(h0, 0), (h0, 1), (h1, 0), (h1, 1)]:
                    sl = qf_i % 8
                    qf_i += 1
                    war = qslot_war[sl]
                    def qlf(e, hh=hh, fc=fc, sl=sl, war=war, nt=nt):
                        if war:
                            e.wait_ge(mm.h, war)
                        e.dma_start(out=qft[sl][:, :],
                                    in_=qf[:, (hh * 2 + fc) * NL + nt * 512:(hh * 2 + fc) * NL + (nt + 1) * 512]
                                    ).then_inc(qld[sl].h, 16)
                    ctx.g(qlf)
                    qld_cnt[sl] += 16
                    tiles[(hh, fc)] = (sl, qld_cnt[sl])
                zrow_done = []
                for j, hh in enumerate((h0, h1)):
                    iz, psz, wfree = s.ps.acquire()
                    def zmm(e, hh=hh, psz=psz, wfree=wfree, tiles=tiles):
                        e.wait_ge(ev.h, kv_ready)
                        pwait(e, wfree)
                        for fc in range(2):
                            sl, cnt = tiles[(hh, fc)]
                            e.wait_ge(qld[sl].h, cnt)
                            ins = e.matmul(psz[0:1, :], kv_bf[:, hh * 130 + fc * 65 + 64:hh * 130 + fc * 65 + 65],
                                           qft[sl][:, :], start=(fc == 0), stop=(fc == 1))
                        ins.then_inc(mm.h, 1)
                    ctx.t(zmm)
                    mmv = mm.inc()
                    def zrv(e, j=j, psz=psz, mmv=mmv):
                        e.wait_ge(mm.h, mmv)
                        s.vself.edge(e, e.tensor_scalar_add(s.rowbuf[0:1, 0:512], psz[0:1, :], 1e-8))
                        with nc.allow_low_precision(reason="z normalizer is a bf16 matmul operand"):
                            e.reciprocal(zr_bf[0:1, j * 512:(j + 1) * 512], s.rowbuf[0:1, 0:512]).then_inc(ev.h, 1)
                    ctx.v(zrv)
                    s.ps.release(iz, ev, ev.inc())
                zr_done = ev.n
                izb, pzb, wfree = s.ps.acquire()
                def zbm(e, pzb=pzb, wfree=wfree, zr_done=zr_done):
                    e.wait_ge(ev.h, zr_done)
                    pwait(e, wfree)
                    e.matmul(pzb[0:64, :], onesrow_bf[0:1, 0:64], zr_bf[0:1, 0:512], start=True, stop=True)
                    e.matmul(pzb[64:128, :], onesrow_bf[0:1, 0:64], zr_bf[0:1, 512:1024],
                             start=True, stop=True).then_inc(mm.h, 1)
                ctx.t(zbm)
                mm_zb = mm.inc()
                def zbe(e, pzb=pzb, mm_zb=mm_zb):
                    e.wait_ge(mm.h, mm_zb)
                    e.tensor_copy(zb_sb[:, :], pzb[:, :]).then_inc(ev.h, 1)
                ctx.v(zbe)
                s.ps.release(izb, ev, ev.inc())
                zb_done = ev.n
                ia, psa, wfree = s.ps.acquire()
                def amm(e, h0=h0, h1=h1, psa=psa, wfree=wfree, tiles=tiles):
                    e.wait_ge(ev.h, kv_ready)
                    pwait(e, wfree)
                    for j, hh in enumerate((h0, h1)):
                        for fc in range(2):
                            sl, cnt = tiles[(hh, fc)]
                            e.wait_ge(qld[sl].h, cnt)
                            ins = e.matmul(psa[j * 64:(j + 1) * 64, :],
                                           kv_bf[:, hh * 130 + fc * 65:hh * 130 + fc * 65 + 64],
                                           qft[sl][:, :], start=(fc == 0), stop=(fc == 1))
                    ins.then_inc(mm.h, 1)
                ctx.t(amm)
                mm_a = mm.inc()
                for (hh, fc) in tiles:
                    qslot_war[tiles[(hh, fc)][0]] = mm_a
                def aev(e, hp=hp, psa=psa, mm_a=mm_a, zb_done=zb_done):
                    e.wait_ge(mm.h, mm_a)
                    e.wait_ge(ev.h, zb_done)
                    e.tensor_mul(attn_bf[:, hp * 512:(hp + 1) * 512], psa[:, :], zb_sb[:, :]).then_inc(ev.h, 1)
                ctx.v(aev)
                s.ps.release(ia, ev, ev.inc())
            attn_done = ev.n

            for mc in range(8):
                io, pso, wfree = s.ps.acquire()
                def omm(e, mc=mc, pso=pso, wfree=wfree, attn_done=attn_done):
                    e.wait_ge(ld.h, ld_all)
                    e.wait_ge(ev.h, attn_done)
                    pwait(e, wfree)
                    for k in range(8):
                        ins = e.matmul(pso[:, :], wout[:, k * 1024 + mc * 128:k * 1024 + (mc + 1) * 128],
                                       attn_bf[:, k * 512:(k + 1) * 512],
                                       start=(k == 0), stop=(k == 7))
                    ins.then_inc(mm.h, 1)
                ctx.t(omm)
                mmv = mm.inc()
                def oev(e, mc=mc, pso=pso, mmv=mmv, ntp=ntp, x0_rdy=x0_rdy, swar=store_war[ntp]):
                    e.wait_ge(mm.h, mmv)
                    e.wait_ge(x0ld[ntp].h, x0_rdy)
                    if swar:
                        e.wait_ge(xst[ntp].h, swar)
                    e.scalar_tensor_tensor(rs[ntp][:, mc * 512:(mc + 1) * 512], pso[:, :],
                                           bo_fm[:, mc:mc + 1], x0[ntp][:, mc * 512:(mc + 1) * 512],
                                           ALU.add, ALU.add).then_inc(ev.h, 1)
                ctx.v(oev)
                s.ps.release(io, ev, ev.inc())
            r1_done = ev.n

            ln1_ev = plan_ln_tile(ctx, s,
                                  (lambda mc, ntp=ntp: rs[ntp][:, mc * 512:(mc + 1) * 512]),
                                  (lambda mc, ntp=ntp: rs[ntp][:, mc * 512:(mc + 1) * 512]),
                                  gamma=g_fm, beta=be_fm,
                                  bf_chunk=(lambda mc: x1_bf[:, mc * 512:(mc + 1) * 512]),
                                  wait_ev=r1_done)

            for m in range(32):
                p = m % 2
                gi = nt * 32 + m
                war = w1_war[p]
                def w1l(e, m=m, p=p, war=war):
                    if war:
                        e.wait_ge(mm.h, war)
                    e.dma_start(out=w1b[p][:, :],
                                in_=w1T.rearrange("(k p) m -> p k m", p=128)[:, :, m * 128:(m + 1) * 128]
                                ).then_inc(w1ld[p].h, 16)
                ctx.g(w1l)
                w1_cnt[p] += 16
                i1, ps1, wfree = s.ps.acquire()
                def w1m(e, m=m, p=p, ps1=ps1, wfree=wfree, cnt=w1_cnt[p], ln1_ev=ln1_ev):
                    e.wait_ge(w1ld[p].h, cnt)
                    e.wait_ge(ev.h, ln1_ev)
                    pwait(e, wfree)
                    for k in range(8):
                        ins = e.matmul(ps1[:, :], w1b[p][:, k * 128:(k + 1) * 128],
                                       x1_bf[:, k * 512:(k + 1) * 512],
                                       start=(k == 0), stop=(k == 7))
                    ins.then_inc(mm.h, 1)
                ctx.t(w1m)
                w1_war[p] = mm.inc()
                mmv = w1_war[p]
                def gel(e, m=m, ps1=ps1, mmv=mmv):
                    e.wait_ge(mm.h, mmv)
                    e.activation(h_bf[:, m * 512:(m + 1) * 512], ps1[:, :], AF.Gelu,
                                 bias=b1_fm[:, m:m + 1]).then_inc(sa.h, 1)
                ctx.s(gel)
                s.ps.release(i1, sa, sa.inc())
            gelu_done = sa.n

            for m2 in range(8):
                p = m2 % 2
                war = w2_war[p]
                def w2l(e, m2=m2, p=p, war=war):
                    if war:
                        e.wait_ge(mm.h, war)
                    e.dma_start(out=w2b[p][:, :],
                                in_=w2T.rearrange("(k p) m -> p k m", p=128)[:, :, m2 * 128:(m2 + 1) * 128]
                                ).then_inc(w2ld[p].h, 16)
                ctx.g(w2l)
                w2_cnt[p] += 16
                i2, ps2, wfree = s.ps.acquire()
                def w2m(e, m2=m2, p=p, ps2=ps2, wfree=wfree, cnt=w2_cnt[p], gelu_done=gelu_done):
                    e.wait_ge(w2ld[p].h, cnt)
                    e.wait_ge(sa.h, gelu_done)
                    pwait(e, wfree)
                    for k in range(32):
                        ins = e.matmul(ps2[:, :], w2b[p][:, k * 128:(k + 1) * 128],
                                       h_bf[:, k * 512:(k + 1) * 512],
                                       start=(k == 0), stop=(k == 31))
                    ins.then_inc(mm.h, 1)
                ctx.t(w2m)
                w2_war[p] = mm.inc()
                mmv = w2_war[p]
                def x2e(e, m2=m2, ps2=ps2, mmv=mmv, ntp=ntp):
                    e.wait_ge(mm.h, mmv)
                    e.scalar_tensor_tensor(rs[ntp][:, m2 * 512:(m2 + 1) * 512], ps2[:, :],
                                           b2_fm[:, m2:m2 + 1], rs[ntp][:, m2 * 512:(m2 + 1) * 512],
                                           ALU.add, ALU.add).then_inc(ev.h, 1)
                ctx.v(x2e)
                s.ps.release(i2, ev, ev.inc())
            x2_done = ev.n

            ln2_ev = plan_ln_tile(ctx, s,
                                  (lambda mc, ntp=ntp: rs[ntp][:, mc * 512:(mc + 1) * 512]),
                                  (lambda mc, ntp=ntp: rs[ntp][:, mc * 512:(mc + 1) * 512]),
                                  gamma=g_fm, beta=be_fm, wait_ev=x2_done)

            for mc in range(8):
                def padd(e, mc=mc, ntp=ntp, ln2_ev=ln2_ev):
                    e.wait_ge(ev.h, ln2_ev)
                    e.tensor_add(rs[ntp][:, mc * 512:(mc + 1) * 512],
                                 rs[ntp][:, mc * 512:(mc + 1) * 512],
                                 x0[ntp][:, mc * 512:(mc + 1) * 512]).then_inc(ev.h, 1)
                ctx.v(padd)
                ev.inc()
            x0_war[ntp] = ev.n
            pre_done = ev.n
            out_ev = plan_ln_tile(ctx, s,
                                  (lambda mc, ntp=ntp: rs[ntp][:, mc * 512:(mc + 1) * 512]),
                                  (lambda mc, ntp=ntp: rs[ntp][:, mc * 512:(mc + 1) * 512]),
                                  wait_ev=pre_done)
            def stf(e, nt=nt, ntp=ntp, out_ev=out_ev):
                e.wait_ge(ev.h, out_ev)
                e.dma_start(out=xa_o.rearrange("(c p) n -> p c n", p=128)[:, :, nt * 512:(nt + 1) * 512],
                            in_=rs[ntp][:, :]).then_inc(xst[ntp].h, 16)
            ctx.y(stf)
            store_war[ntp] = xst[ntp].inc(16)

        f0, f1 = xst[0].n, xst[1].n
        def finw(e):
            e.wait_ge(xst[0].h, f0)
            e.wait_ge(xst[1].h, f1)
        ctx.y(finw)
        ctx.finish()
    return nc




def build_B1():
    nc = bass.Bass("TRN2", target_bir_lowering=False)
    xa = nc.dram_tensor("xa", [D, NL], F32, kind="ExternalInput").ap()
    ln1w = nc.dram_tensor("ln1w", [128, 8], F32, kind="ExternalInput").ap()
    wdefs = {
        "kvd": (nc.dram_tensor("kvdT", [D, 256], F32, kind="ExternalInput").ap(), 8, 256),
        "qd": (nc.dram_tensor("qdT", [D, 256], F32, kind="ExternalInput").ap(), 8, 256),
        "kun": (nc.dram_tensor("kunT", [256, 512], F32, kind="ExternalInput").ap(), 2, 512),
        "qun": (nc.dram_tensor("qunT", [256, 512], F32, kind="ExternalInput").ap(), 2, 512),
        "vu": (nc.dram_tensor("vuT", [256, D], F32, kind="ExternalInput").ap(), 2, 1024),
        "rqa": (nc.dram_tensor("rqaT", [256, 256], F32, kind="ExternalInput").ap(), 2, 256),
        "rqb": (nc.dram_tensor("rqbT", [256, 256], F32, kind="ExternalInput").ap(), 2, 256),
        "rka": (nc.dram_tensor("rkaT", [D, 256], F32, kind="ExternalInput").ap(), 8, 256),
        "rkb": (nc.dram_tensor("rkbT", [D, 256], F32, kind="ExternalInput").ap(), 8, 256),
    }
    cosf = nc.dram_tensor("cosf", [128, NL], F32, kind="ExternalInput").ap()
    sinf = nc.dram_tensor("sinf", [128, NL], F32, kind="ExternalInput").ap()
    K_o = nc.dram_tensor("K", [D, NL], BF16, kind="ExternalOutput").ap()
    Q_o = nc.dram_tensor("Q", [D, NL], BF16, kind="ExternalOutput").ap()
    V_o = nc.dram_tensor("V", [NL, D], BF16, kind="ExternalOutput").ap()

    es = contextlib.ExitStack()
    with es:
        ctx = Ctx(nc, es)
        s = S(ctx, n512=8)
        mm, ev, va, sa, ld, st = s.mm, s.ev, s.va, s.sa, s.ld, s.st

        xat = [ctx.sbuf(f"xat{i}", [128, 8 * 512], F32) for i in range(2)]
        xl_bf = ctx.sbuf("xl_bf", [128, NCH * NL], BF16)
        lat_bf = ctx.sbuf("lat_bf", [128, 4 * NL], BF16)
        nopet = [ctx.sbuf(f"nopet{i}", [128, NL], BF16) for i in range(2)]
        vtm = [ctx.sbuf(f"vtm{i}", [128, 1024], BF16) for i in range(2)]
        ra = ctx.sbuf("ra", [128, 2 * 1024], F32)
        rb = ctx.sbuf("rb", [128, 2 * 1024], F32)
        lo = ctx.sbuf("lo", [128, 2 * 1024], BF16)
        hi = ctx.sbuf("hi", [128, 2 * 1024], BF16)
        tA = ctx.sbuf("tA", [128, 1024], F32)
        tB = ctx.sbuf("tB", [128, 1024], F32)
        tC = ctx.sbuf("tC", [128, 1024], F32)
        tD = ctx.sbuf("tD", [128, 1024], F32)
        cs = ctx.sbuf("cs", [128, NL], F32)
        sn = ctx.sbuf("sn", [128, NL], F32)
        ln1_fm = ctx.sbuf("ln1_fm", [128, 8], F32)
        zeros8 = ctx.sbuf("zeros8", [128, 8], F32)
        x0ld = [ctx.sem(f"x0ld{i}") for i in range(2)]
        nst = [ctx.sem(f"nst{i}") for i in range(2)]
        vsm = [ctx.sem(f"vsm{i}") for i in range(2)]
        rst = ctx.sem("rst")

        ctx.v(lambda e: e.memset(zeros8[:, :], 0.0))

        off = {}
        cur = 0
        for nm, (wT, kch, M) in wdefs.items():
            off[nm] = (cur, M)
            cur += kch * M
        wbuf = ctx.sbuf("wbuf", [128, cur], BF16)
        for nm, (wT, kch, M) in wdefs.items():
            dma_in(ctx, wbuf[:, off[nm][0]:off[nm][0] + kch * M],
                   wT.rearrange("(k p) m -> p k m", p=128), ld)
        dma_in(ctx, cs[:, :], cosf, ld)
        dma_in(ctx, sn[:, :], sinf, ld)
        dma_in(ctx, ln1_fm[:, :], ln1w, ld)
        ld_all = ld.n

        def wt(nm, k, m):
            base, M = off[nm]
            c = base + k * M + m * 128
            return wbuf[:, c:c + 128]

        ln_done = [0, 0]
        for nt in range(4):
            ntp = nt % 2
            war = ln_done[ntp]
            def xaf(e, nt=nt, ntp=ntp, war=war):
                if war:
                    e.wait_ge(ev.h, war)
                e.dma_start(out=xat[ntp][:, :],
                            in_=xa.rearrange("(c p) n -> p c n", p=128)[:, :, nt * 512:(nt + 1) * 512]
                            ).then_inc(x0ld[ntp].h, 16)
            ctx.g(xaf)
            x0_rdy = x0ld[ntp].inc(16)
            def gate(e, x0_rdy=x0_rdy, ntp=ntp):
                e.wait_ge(x0ld[ntp].h, x0_rdy)
                e.wait_ge(ld.h, ld_all)
            ctx.t(gate)
            ctx.s(gate)
            lev = plan_ln_tile(ctx, s,
                               (lambda mc, ntp=ntp: xat[ntp][:, mc * 512:(mc + 1) * 512]),
                               None,
                               gamma=ln1_fm, beta=zeros8,
                               bf_chunk=(lambda mc, nt=nt: xl_bf[:, mc * NL + nt * 512:mc * NL + nt * 512 + 512]))
            ln_done[ntp] = lev
        xl_done = max(ln_done)

        for ci, nm in enumerate(["kvd", "kvd", "qd", "qd"]):
            mch = ci % 2
            for nt in range(4):
                i, ps, wfree = s.ps.acquire()
                def mmf(e, nm=nm, mch=mch, nt=nt, ps=ps, wfree=wfree):
                    e.wait_ge(ev.h, xl_done)
                    pwait(e, wfree)
                    for k in range(8):
                        ins = e.matmul(ps[:, :], wt(nm, k, mch),
                                       xl_bf[:, k * NL + nt * 512:k * NL + (nt + 1) * 512],
                                       start=(k == 0), stop=(k == 7))
                    ins.then_inc(mm.h, 1)
                ctx.t(mmf)
                mmv = mm.inc()
                def evf(e, ci=ci, nt=nt, ps=ps, mmv=mmv):
                    e.wait_ge(mm.h, mmv)
                    e.tensor_copy(lat_bf[:, ci * NL + nt * 512:ci * NL + (nt + 1) * 512],
                                  ps[:, :]).then_inc(ev.h, 1)
                ctx.v(evf)
                s.ps.release(i, ev, ev.inc())
        lat_done = ev.n

        nst_war = [0, 0]
        for half, (nm, lat0, O) in enumerate([("kun", 0, K_o), ("qun", 2, Q_o)]):
            for mch in range(4):
                cp = (half * 4 + mch) % 2
                war = nst_war[cp]
                for nt in range(4):
                    i, ps, wfree = s.ps.acquire()
                    def mmf(e, nm=nm, lat0=lat0, mch=mch, nt=nt, ps=ps, wfree=wfree):
                        e.wait_ge(ev.h, lat_done)
                        pwait(e, wfree)
                        for k in range(2):
                            ins = e.matmul(ps[:, :], wt(nm, k, mch),
                                           lat_bf[:, (lat0 + k) * NL + nt * 512:(lat0 + k) * NL + (nt + 1) * 512],
                                           start=(k == 0), stop=(k == 1))
                        ins.then_inc(mm.h, 1)
                    ctx.t(mmf)
                    mmv = mm.inc()
                    def evf(e, cp=cp, nt=nt, ps=ps, mmv=mmv, war=war):
                        e.wait_ge(mm.h, mmv)
                        if war:
                            e.wait_ge(nst[cp].h, war)
                        e.tensor_copy(nopet[cp][:, nt * 512:(nt + 1) * 512],
                                      ps[:, :]).then_inc(ev.h, 1)
                    ctx.v(evf)
                    s.ps.release(i, ev, ev.inc())
                ch_done = ev.n
                def nope_st(e, mch=mch, cp=cp, ch_done=ch_done, O=O):
                    e.wait_ge(ev.h, ch_done)
                    for j in range(4):
                        hh = mch * 4 + j
                        e.dma_start(out=O[hh * 64 + 32:hh * 64 + 64, :],
                                    in_=nopet[cp][j * 32:(j + 1) * 32, :]).then_inc(nst[cp].h, 16)
                ctx.y(nope_st)
                nst_war[cp] = nst[cp].inc(16 * 4)

        vst_war = [0, 0]
        for ncid in range(NT):
            p = ncid % 2
            i, ps, wfree = s.ps.acquire()
            i2, ps2, wfree2 = s.ps.acquire()
            def mmf(e, ncid=ncid, ps=ps, ps2=ps2, wfree=wfree, wfree2=wfree2):
                e.wait_ge(ev.h, lat_done)
                pwait(e, wfree)
                pwait(e, wfree2)
                for half2, pp in ((0, ps), (1, ps2)):
                    for k in range(2):
                        ins = e.matmul(pp[:, :],
                                       lat_bf[:, k * NL + ncid * 128:k * NL + (ncid + 1) * 128],
                                       wbuf[:, off["vu"][0] + k * 1024 + half2 * 512:
                                            off["vu"][0] + k * 1024 + (half2 + 1) * 512],
                                       start=(k == 0), stop=(k == 1))
                ins.then_inc(mm.h, 1)
            ctx.t(mmf)
            mmv = mm.inc()
            war = vst_war[p]
            def evf(e, p=p, ps=ps, ps2=ps2, mmv=mmv, war=war):
                e.wait_ge(mm.h, mmv)
                if war:
                    e.wait_ge(vsm[p].h, war)
                e.tensor_copy(vtm[p][:, 0:512], ps[:, :])
                e.tensor_copy(vtm[p][:, 512:1024], ps2[:, :]).then_inc(ev.h, 1)
            ctx.v(evf)
            evv = ev.inc()
            s.ps.release(i, ev, evv)
            s.ps.release(i2, ev, evv)
            def vst(e, ncid=ncid, p=p, evv=evv):
                e.wait_ge(ev.h, evv)
                e.dma_start(out=V_o[ncid * 128:(ncid + 1) * 128, :], in_=vtm[p][:, :]).then_inc(vsm[p].h, 16)
            ctx.y(vst)
            vst_war[p] = vsm[p].inc(16)

        rope_st_war = 0
        for which, (nma, nmb, kch, lat0, O) in enumerate([
                ("rqa", "rqb", 2, 2, Q_o), ("rka", "rkb", 8, None, K_o)]):
            for half in range(2):
                for nm, dstbuf in ((nma, ra), (nmb, rb)):
                    for mch in range(2):
                        for sub in range(2):
                            i, ps, wfree = s.ps.acquire()
                            def mmf(e, nm=nm, mch=mch, half=half, sub=sub, ps=ps,
                                    wfree=wfree, kch=kch, lat0=lat0):
                                e.wait_ge(ev.h, xl_done if lat0 is None else lat_done)
                                pwait(e, wfree)
                                for k in range(kch):
                                    col = (lat0 + k if lat0 is not None else k) * NL + half * 1024 + sub * 512
                                    rhs = (lat_bf if lat0 is not None else xl_bf)[:, col:col + 512]
                                    ins = e.matmul(ps[:, :], wt(nm, k, mch), rhs,
                                                   start=(k == 0), stop=(k == kch - 1))
                                ins.then_inc(mm.h, 1)
                            ctx.t(mmf)
                            mmv = mm.inc()
                            def evf(e, dstbuf=dstbuf, mch=mch, sub=sub, ps=ps, mmv=mmv,
                                    war=rope_st_war):
                                e.wait_ge(mm.h, mmv)
                                if war:
                                    e.wait_ge(rst.h, war)
                                e.tensor_copy(dstbuf[:, mch * 1024 + sub * 512:mch * 1024 + (sub + 1) * 512],
                                              ps[:, :]).then_inc(ev.h, 1)
                            ctx.v(evf)
                            s.ps.release(i, ev, ev.inc())
                ab_done = ev.n
                for c in range(2):
                    def rop(e, c=c, half=half, ab_done=ab_done):
                        e.wait_ge(ev.h, ab_done)
                        csl = cs[:, half * 1024:(half + 1) * 1024]
                        snl = sn[:, half * 1024:(half + 1) * 1024]
                        rau = ra[:, c * 1024:(c + 1) * 1024]
                        rbu = rb[:, c * 1024:(c + 1) * 1024]
                        e.tensor_mul(tA[:, :], rau, csl)
                        e.tensor_mul(tB[:, :], rbu, snl)
                        e.tensor_mul(tC[:, :], rbu, csl)
                        s.vself.edge(e, e.tensor_mul(tD[:, :], rau, snl))
                        e.tensor_sub(lo[:, c * 1024:(c + 1) * 1024], tA[:, :], tB[:, :])
                        e.tensor_add(hi[:, c * 1024:(c + 1) * 1024], tC[:, :],
                                     tD[:, :]).then_inc(ev.h, 1)
                    ctx.v(rop)
                    ev.inc()
                rope_done = ev.n
                def rstf(e, half=half, rope_done=rope_done, O=O):
                    e.wait_ge(ev.h, rope_done)
                    for c in range(2):
                        for j in range(8):
                            hh = c * 8 + j
                            e.dma_start(out=O[hh * 64 + 0:hh * 64 + 16, half * 1024:(half + 1) * 1024],
                                        in_=lo[j * 16:(j + 1) * 16, c * 1024:(c + 1) * 1024]).then_inc(rst.h, 16)
                            e.dma_start(out=O[hh * 64 + 16:hh * 64 + 32, half * 1024:(half + 1) * 1024],
                                        in_=hi[j * 16:(j + 1) * 16, c * 1024:(c + 1) * 1024]).then_inc(rst.h, 16)
                ctx.y(rstf)
                rope_st_war = rst.inc(16 * 32)

        fins = [(nst[0], nst[0].n), (nst[1], nst[1].n), (vsm[0], vsm[0].n),
                (vsm[1], vsm[1].n), (rst, rst.n)]
        def finw(e):
            for sm, v_ in fins:
                e.wait_ge(sm.h, v_)
        ctx.y(finw)
        ctx.finish()
    return nc



def build_B2(odd, debug=False):
    KT = 32 if odd else 16
    off_q = 2048 if odd else 0
    nc = bass.Bass("TRN2", target_bir_lowering=False)
    xa = nc.dram_tensor("xa", [D, NL], F32, kind="ExternalInput").ap()
    Kd = nc.dram_tensor("K", [D, KT * 128], BF16, kind="ExternalInput").ap()
    Qd = nc.dram_tensor("Q", [D, NL], BF16, kind="ExternalInput").ap()
    Vd = nc.dram_tensor("V", [KT * 128, D], BF16, kind="ExternalInput").ap()
    stair = nc.dram_tensor("stair", [128, 2176], BF16, kind="ExternalInput").ap()
    owT = nc.dram_tensor("owT", [D, D], F32, kind="ExternalInput").ap()
    t1_o = nc.dram_tensor("t1", [D, NL], F32, kind="ExternalOutput").ap()
    if debug:
        yfm_o = nc.dram_tensor("Yfm_dbg", [128, NCH * NL], BF16, kind="ExternalOutput").ap()

    es = contextlib.ExitStack()
    with es:
        ctx = Ctx(nc, es)
        s = S(ctx, n512=4)
        mm, ev, va, sa, ld, st = s.mm, s.ev, s.va, s.sa, s.ld, s.st

        Kb = [ctx.sbuf(f"Kb{i}", [128, KT * 128], BF16) for i in range(2)]
        Qb = [ctx.sbuf(f"Qb{i}", [128, NL], BF16) for i in range(2)]
        aug = [ctx.sbuf(f"aug{i}", [128, KT * 130], BF16) for i in range(2)]
        Pt = [[ctx.sbuf(f"Pt{h}_{i}", [128, 1024], BF16) for i in range(2)] for h in range(2)]
        stb = ctx.sbuf("stb", [128, 2176], BF16)
        zrow = ctx.sbuf("zrow", [1, 4 * 512], F32)
        zb_sb = ctx.sbuf("zb_sb", [128, 1024], F32)
        Yfm = ctx.sbuf("Yfm", [128, NCH * NL], BF16)
        owb = ctx.sbuf("owb", [128, NCH * D], BF16)
        xat = [ctx.sbuf(f"xat{i}", [128, 8 * 512], F32) for i in range(2)]
        rs = [ctx.sbuf(f"rs{i}", [128, 8 * 512], F32) for i in range(2)]
        psA = [ctx.psum(f"psA{h}", [128, 1024], F32) for h in range(2)]

        kld = [ctx.sem(f"kld{i}") for i in range(2)]
        qld = [ctx.sem(f"qld{i}") for i in range(2)]
        vld = [ctx.sem(f"vld{i}") for i in range(2)]
        x0ld = [ctx.sem(f"x0ld{i}") for i in range(2)]
        yst2 = [ctx.sem(f"yst2_{i}") for i in range(2)]
        tst2 = [ctx.sem(f"tst2_{i}") for i in range(2)]

        dma_in(ctx, stb[:, :], stair, ld)
        dma_in(ctx, owb[:, :], owT.rearrange("(k p) m -> p k m", p=128), ld)
        ld_all = ld.n
        def ones_init(e):
            e.memset(aug[0][:, :].rearrange("q (a j c) -> q a j c", j=2, c=65)[:, :, :, 64:65], 1.0)
            e.memset(aug[1][:, :].rearrange("q (a j c) -> q a j c", j=2, c=65)[:, :, :, 64:65],
                     1.0).then_inc(ev.h, 1)
        ctx.v(ones_init)
        ones_done = ev.inc()

        kt_hi = {0: (KT - 8), 1: KT}
        def need_mask(strip, kt):
            d = kt * 128 - strip * 1024 - off_q
            return -127 < d <= 1023
        def stair_col(strip, kt):
            d = kt * 128 - strip * 1024 - off_q
            return 1023 - d

        hp_war = [(0, 0)] * 2
        last_exp = {0: 0, 1: 0}
        last_pv = {}
        zb_done = {0: 0, 1: 0}
        for hp in range(8):
            p = hp % 2
            war_mm, war_ev = hp_war[p]
            def kqv_load(e, hp=hp, p=p, war_mm=war_mm, war_ev=war_ev):
                if war_mm:
                    e.wait_ge(mm.h, war_mm)
                if war_ev:
                    e.wait_ge(ev.h, war_ev)
                e.dma_start(out=Kb[p][:, :], in_=Kd[hp * 128:(hp + 1) * 128, :]).then_inc(kld[p].h, 16)
                e.dma_start(out=Qb[p][:, :], in_=Qd[hp * 128:(hp + 1) * 128, :]).then_inc(qld[p].h, 16)
                for j in range(2):
                    e.dma_start(out=aug[p][:, :].rearrange("q (a j c) -> q a j c", j=2, c=65)[:, :, j, 0:64],
                                in_=Vd[:, hp * 128 + j * 64:hp * 128 + (j + 1) * 64]
                                .rearrange("(a q) c -> q a c", q=128)
                                ).then_inc(vld[p].h, 16)
            ctx.g(kqv_load)
            k_rdy = kld[p].inc(16)
            q_rdy = qld[p].inc(16)
            v_rdy = vld[p].inc(32)
            aug_rdy = ones_done

            for strip in range(2):
                nk = kt_hi[strip]
                pvs = {}
                for h in range(2):
                    for half in range(2):
                        pvs[(h, half)] = s.ps.acquire()
                mask_ev = {}
                for kt in range(nk):
                    first, last = (kt == 0), (kt == nk - 1)
                    for h in range(2):
                        ktp = kt % 2
                        def smm(e, p=p, h=h, kt=kt, strip=strip, k_rdy=k_rdy, q_rdy=q_rdy,
                                first=first, esa=last_exp[h], zbd=zb_done[h],
                                frees=[pvs[(h, hf)][2] for hf in range(2)] if first else None):
                            e.wait_ge(kld[p].h, k_rdy)
                            e.wait_ge(qld[p].h, q_rdy)
                            if esa:
                                e.wait_ge(sa.h, esa)
                            if zbd:
                                e.wait_ge(ev.h, zbd)
                            if first:
                                for w in frees:
                                    pwait(e, w)
                            for half in range(2):
                                ins = e.matmul(psA[h][:, half * 512:(half + 1) * 512],
                                               Kb[p][h * 64:(h + 1) * 64, kt * 128:(kt + 1) * 128],
                                               Qb[p][h * 64:(h + 1) * 64,
                                                     strip * 1024 + half * 512:strip * 1024 + (half + 1) * 512],
                                               start=True, stop=True)
                            ins.then_inc(mm.h, 1)
                        ctx.t(smm)
                        smm_v = mm.inc()
                        def expf(e, h=h, ktp=ktp, smm_v=smm_v, pvw=last_pv.get((h, ktp))):
                            e.wait_ge(mm.h, smm_v)
                            if pvw:
                                e.wait_ge(mm.h, pvw)
                            e.activation(Pt[h][ktp][:, :], psA[h][:, :], AF.Exp,
                                         scale=0.125).then_inc(sa.h, 1)
                        ctx.s(expf)
                        sav = sa.inc()
                        last_exp[h] = sav
                        if need_mask(strip, kt):
                            c0 = stair_col(strip, kt)
                            def mskf(e, h=h, ktp=ktp, c0=c0, sav=sav):
                                e.wait_ge(sa.h, sav)
                                e.wait_ge(ld.h, ld_all)
                                e.tensor_mul(Pt[h][ktp][:, :], Pt[h][ktp][:, :],
                                             stb[:, c0:c0 + 1024]).then_inc(ev.h, 1)
                            ctx.v(mskf)
                            mask_ev[(kt, h)] = ev.inc()
                        def pvf(e, p=p, h=h, kt=kt, ktp=ktp, first=first, last=last,
                                sav=sav, mev=mask_ev.get((kt, h)), aug_rdy=aug_rdy,
                                v_rdy=v_rdy,
                                s0=pvs[(h, 0)][1], s1=pvs[(h, 1)][1]):
                            if mev:
                                e.wait_ge(ev.h, mev)
                            else:
                                e.wait_ge(sa.h, sav)
                            e.wait_ge(ev.h, aug_rdy)
                            e.wait_ge(vld[p].h, v_rdy)
                            for half, sl in ((0, s0), (1, s1)):
                                ins = e.matmul(sl[0:65, :],
                                               aug[p][:, (kt * 2 + h) * 65:(kt * 2 + h) * 65 + 65],
                                               Pt[h][ktp][:, half * 512:(half + 1) * 512],
                                               start=first, stop=last)
                            ins.then_inc(mm.h, 1)
                        ctx.t(pvf)
                        last_pv[(h, ktp)] = mm.inc()
                pv_all = mm.n
                def zrec(e, pv_all=pv_all, sls=[pvs[(h, hf)][1] for h in range(2) for hf in range(2)]):
                    e.wait_ge(mm.h, pv_all)
                    for i4, sl in enumerate(sls):
                        ins = e.reciprocal(zrow[0:1, i4 * 512:(i4 + 1) * 512], sl[64:65, :])
                    ins.then_inc(ev.h, 1)
                ctx.v(zrec)
                zr_done = ev.inc()
                def zbm(e, zr_done=zr_done, e0=last_exp[0], e1=last_exp[1]):
                    e.wait_ge(ev.h, zr_done)
                    e.wait_ge(sa.h, max(e0, e1))
                    for h in range(2):
                        for half in range(2):
                            i4 = h * 2 + half
                            ins = e.matmul(psA[0][h * 64:(h + 1) * 64, half * 512:(half + 1) * 512],
                                           s.onesrow[0:1, 0:64],
                                           zrow[0:1, i4 * 512:(i4 + 1) * 512],
                                           start=True, stop=True)
                    ins.then_inc(mm.h, 1)
                ctx.t(zbm)
                zb_mm = mm.inc()
                def zbc(e, zb_mm=zb_mm):
                    e.wait_ge(mm.h, zb_mm)
                    e.tensor_copy(zb_sb[:, :], psA[0][:, :]).then_inc(ev.h, 1)
                ctx.v(zbc)
                zb_done[0] = ev.inc()
                zb_done[1] = zb_done[0]
                evs = []
                for h in range(2):
                    for half in range(2):
                        def yev(e, h=h, half=half, hp=hp, strip=strip,
                                sl=pvs[(h, half)][1], zbd=zb_done[0]):
                            e.wait_ge(ev.h, zbd)
                            col = hp * NL + strip * 1024 + half * 512
                            e.tensor_mul(Yfm[h * 64:(h + 1) * 64, col:col + 512],
                                         sl[0:64, :],
                                         zb_sb[h * 64:(h + 1) * 64, half * 512:(half + 1) * 512]
                                         ).then_inc(ev.h, 1)
                        ctx.v(yev)
                        evs.append(ev.inc())
                for h in range(2):
                    for half in range(2):
                        s.ps.release(pvs[(h, half)][0], ev, evs[-1])
            hp_war[p] = (mm.n, 0)
        attn_done_ev = ev.n

        st_war = [0, 0]
        for nt in range(4):
            ntp = nt % 2
            war = st_war[ntp]
            def xaf(e, nt=nt, ntp=ntp, war=war):
                if war:
                    e.wait_ge(tst2[ntp].h, war)
                e.dma_start(out=xat[ntp][:, :],
                            in_=xa.rearrange("(c p) n -> p c n", p=128)[:, :, nt * 512:(nt + 1) * 512]
                            ).then_inc(x0ld[ntp].h, 16)
            ctx.g(xaf)
            x_rdy = x0ld[ntp].inc(16)
            for mc in range(8):
                io, pso, wfree = s.ps.acquire()
                def omm(e, mc=mc, nt=nt, pso=pso, wfree=wfree):
                    e.wait_ge(ld.h, ld_all)
                    e.wait_ge(ev.h, attn_done_ev)
                    pwait(e, wfree)
                    for k in range(8):
                        ins = e.matmul(pso[:, :], owb[:, k * D + mc * 128:k * D + (mc + 1) * 128],
                                       Yfm[:, k * NL + nt * 512:k * NL + (nt + 1) * 512],
                                       start=(k == 0), stop=(k == 7))
                    ins.then_inc(mm.h, 1)
                ctx.t(omm)
                mmv = mm.inc()
                def oev(e, mc=mc, ntp=ntp, pso=pso, mmv=mmv, x_rdy=x_rdy, war=war):
                    e.wait_ge(mm.h, mmv)
                    e.wait_ge(x0ld[ntp].h, x_rdy)
                    if war:
                        e.wait_ge(tst2[ntp].h, war)
                    e.tensor_add(rs[ntp][:, mc * 512:(mc + 1) * 512], pso[:, :],
                                 xat[ntp][:, mc * 512:(mc + 1) * 512]).then_inc(ev.h, 1)
                ctx.v(oev)
                s.ps.release(io, ev, ev.inc())
            t1_done = ev.n
            def tst(e, nt=nt, ntp=ntp, t1_done=t1_done):
                e.wait_ge(ev.h, t1_done)
                e.dma_start(out=t1_o.rearrange("(c p) n -> p c n", p=128)[:, :, nt * 512:(nt + 1) * 512],
                            in_=rs[ntp][:, :]).then_inc(tst2[ntp].h, 16)
            ctx.y(tst)
            st_war[ntp] = tst2[ntp].inc(16)

        if debug:
            def ydbg(e):
                e.wait_ge(ev.h, attn_done_ev)
                e.dma_start(out=yfm_o, in_=Yfm[:, :]).then_inc(st.h, 16)
            ctx.y(ydbg)
            st.inc(16)
        fin = st.n
        tf0, tf1 = tst2[0].n, tst2[1].n
        def finw(e):
            e.wait_ge(st.h, fin)
            e.wait_ge(tst2[0].h, tf0)
            e.wait_ge(tst2[1].h, tf1)
        ctx.y(finw)
        ctx.finish()
    return nc



def build_B3():
    nc = bass.Bass("TRN2", target_bir_lowering=False)
    t1 = nc.dram_tensor("t1", [D, NL], F32, kind="ExternalInput").ap()
    xa = nc.dram_tensor("xa", [D, NL], F32, kind="ExternalInput").ap()
    ln2w = nc.dram_tensor("ln2w", [128, 8], F32, kind="ExternalInput").ap()
    rwT = nc.dram_tensor("rwT", [D, 7], F32, kind="ExternalInput").ap()
    rbias = nc.dram_tensor("rbias", [1, 7], F32, kind="ExternalInput").ap()
    gT = nc.dram_tensor("gT", [8, D, 2048], F32, kind="ExternalInput").ap()
    uT = nc.dram_tensor("uT", [8, D, 2048], F32, kind="ExternalInput").ap()
    dT = nc.dram_tensor("dT", [8, 2048, D], F32, kind="ExternalInput").ap()
    xb_o = nc.dram_tensor("xb", [D, NL], F32, kind="ExternalOutput").ap()

    es = contextlib.ExitStack()
    with es:
        ctx = Ctx(nc, es)
        s = S(ctx, n512=8)
        mm, ev, va, sa, ld, st = s.mm, s.ev, s.va, s.sa, s.ld, s.st

        tt = [ctx.sbuf(f"tt{i}", [128, 8 * 512], F32) for i in range(2)]
        xat = [ctx.sbuf(f"xat{i}", [128, 8 * 512], F32) for i in range(2)]
        xm = ctx.sbuf("xm", [128, 8 * 512], F32)
        xm_bf = ctx.sbuf("xm_bf", [128, 8 * 512], BF16)
        hu_bf = ctx.sbuf("hu_bf", [128, 16 * 512], BF16)
        macc = ctx.sbuf("macc", [128, 8 * 512], F32)
        wb_sb = ctx.sbuf("wb_sb", [128, 7 * 512], F32)
        probs = ctx.sbuf("probs", [128, 7], F32)
        scr7 = ctx.sbuf("scr7", [128, 7], F32)
        m1 = ctx.sbuf("m1", [128, 1], F32)
        m2 = ctx.sbuf("m2", [128, 1], F32)
        wgt_f = ctx.sbuf("wgt_f", [128, 7], F32)
        wrow = ctx.sbuf("wrow", [1, 7 * 512], BF16)
        sg_sb = [ctx.sbuf(f"sg_sb{i}", [128, 512], F32) for i in range(2)]
        rw_sb = ctx.sbuf("rw_sb", [128, 8 * 7], F32)
        rb_sb = ctx.sbuf("rb_sb", [1, 7], F32)
        ln2_fm = ctx.sbuf("ln2_fm", [128, 8], F32)
        zeros8 = ctx.sbuf("zeros8", [128, 8], F32)
        identb = ctx.sbuf("identb", [128, 128], BF16)
        identf = ctx.sbuf("identf", [128, 128], F32)
        coliota = ctx.sbuf("coliota", [128, 128], F32)
        rowv = ctx.sbuf("rowv", [128, 1], F32)
        onesrow_bf = ctx.sbuf("onesrow_bf", [1, 128], BF16)
        gld = [ctx.sem(f"gld{i}") for i in range(2)]
        uld = [ctx.sem(f"uld{i}") for i in range(2)]
        dld = [ctx.sem(f"dld{i}") for i in range(2)]
        t1ld = [ctx.sem(f"t1ld{i}") for i in range(2)]
        xald = [ctx.sem(f"xald{i}") for i in range(2)]
        xst = [ctx.sem(f"xst{i}") for i in range(2)]
        gb = [ctx.sbuf(f"gb{i}", [128, 8 * 128], BF16) for i in range(2)]
        ub = [ctx.sbuf(f"ub{i}", [128, 8 * 128], BF16) for i in range(2)]
        db = [ctx.sbuf(f"db{i}", [128, 16 * 128], BF16) for i in range(2)]

        ctx.v(lambda e: e.memset(zeros8[:, :], 0.0))
        ctx.v(lambda e: e.memset(onesrow_bf[:, :], 1.0))
        ctx.g(lambda e: e.iota(coliota[:, :], [[1, 128]], channel_multiplier=0,
                               allow_small_or_imprecise_dtypes=True))
        iosem = ctx.sem("iosem")
        def iot2(e):
            e.iota(rowv[:, :], [[0, 1]], channel_multiplier=1,
                   allow_small_or_imprecise_dtypes=True).then_inc(iosem.h, 1)
        ctx.g(iot2)
        ld_iota = iosem.inc(1)
        def mkident(e):
            e.wait_ge(iosem.h, ld_iota)
            e.tensor_single_scalar(identf[:, :], coliota[:, :], rowv[:, :],
                                   ALU.is_equal).then_inc(va.h, 1)
        ctx.v(mkident)
        idva = va.inc()
        def mkident2(e, idva=idva):
            e.wait_ge(va.h, idva)
            e.tensor_copy(identb[:, :], identf[:, :]).then_inc(ev.h, 1)
        ctx.v(mkident2)
        ident_done = ev.inc()

        dma_in(ctx, rw_sb[:, :], rwT.rearrange("(k p) m -> p k m", p=128), ld)
        dma_in(ctx, rb_sb[:, :], rbias, ld)
        dma_in(ctx, ln2_fm[:, :], ln2w, ld)
        ld_all = ld.n

        g_war = [0, 0]
        u_war = [0, 0]
        d_war = [0, 0]
        hum_ev = [0, 0]
        g_cnt = [0, 0]
        u_cnt = [0, 0]
        d_cnt = [0, 0]
        t1_war = [0, 0]
        xa_war = [0, 0]
        st_war = [0, 0]

        for nt in range(4):
            ntp = nt % 2
            def t1f(e, nt=nt, ntp=ntp, war=t1_war[ntp], sw=st_war[ntp]):
                if war:
                    e.wait_ge(ev.h, war)
                if sw:
                    e.wait_ge(xst[ntp].h, sw)
                e.dma_start(out=tt[ntp][:, :],
                            in_=t1.rearrange("(c p) n -> p c n", p=128)[:, :, nt * 512:(nt + 1) * 512]
                            ).then_inc(t1ld[ntp].h, 16)
            ctx.g(t1f)
            t1_rdy = t1ld[ntp].inc(16)
            def xaf(e, nt=nt, ntp=ntp, war=xa_war[ntp]):
                if war:
                    e.wait_ge(ev.h, war)
                e.dma_start(out=xat[ntp][:, :],
                            in_=xa.rearrange("(c p) n -> p c n", p=128)[:, :, nt * 512:(nt + 1) * 512]
                            ).then_inc(xald[ntp].h, 16)
            ctx.g(xaf)
            xa_rdy = xald[ntp].inc(16)

            def gate(e, t1_rdy=t1_rdy, ntp=ntp):
                e.wait_ge(t1ld[ntp].h, t1_rdy)
                e.wait_ge(ld.h, ld_all)
            ctx.t(gate)
            ctx.s(gate)
            ln2_ev = plan_ln_tile(ctx, s,
                                  (lambda mc, ntp=ntp: tt[ntp][:, mc * 512:(mc + 1) * 512]),
                                  (lambda mc: xm[:, mc * 512:(mc + 1) * 512]),
                                  gamma=ln2_fm, beta=zeros8,
                                  bf_chunk=(lambda mc: xm_bf[:, mc * 512:(mc + 1) * 512]))

            for sub in range(4):
                ir, psr, wfree = s.ps.acquire()
                def rmm(e, sub=sub, psr=psr, wfree=wfree, ln2_ev=ln2_ev):
                    e.wait_ge(ev.h, ln2_ev)
                    pwait(e, wfree)
                    for k in range(8):
                        e.matmul(psr[:, 0:7], xm[:, k * 512 + sub * 128:k * 512 + sub * 128 + 128],
                                 rw_sb[:, k * 7:(k + 1) * 7], start=(k == 0), stop=False)
                    e.matmul(psr[:, 0:7], s.onesrow[0:1, :], rb_sb[0:1, :],
                             start=False, stop=True).then_inc(mm.h, 1)
                ctx.t(rmm)
                mmv = mm.inc()
                def sig(e, psr=psr, mmv=mmv):
                    e.wait_ge(mm.h, mmv)
                    e.activation(probs[:, :], psr[:, 0:7], AF.Sigmoid).then_inc(sa.h, 1)
                ctx.s(sig)
                sav = sa.inc()
                def top2(e, sav=sav):
                    e.wait_ge(sa.h, sav)
                    s.vself.edge(e, e.reduce_max(m1[:, :], probs[:, :], axis=AX.X))
                    s.vself.edge(e, e.tensor_single_scalar(scr7[:, :], probs[:, :], m1[:, :], ALU.is_equal))
                    s.vself.edge(e, e.scalar_tensor_tensor(scr7[:, :], scr7[:, :], -1e30, probs[:, :],
                                                           ALU.mult, ALU.add))
                    s.vself.edge(e, e.reduce_max(m2[:, :], scr7[:, :], axis=AX.X))
                    s.vself.edge(e, e.tensor_single_scalar(scr7[:, :], probs[:, :], m2[:, :], ALU.is_ge))
                    s.vself.edge(e, e.tensor_mul(scr7[:, :], scr7[:, :], probs[:, :]))
                    s.vself.edge(e, e.tensor_add(m1[:, :], m1[:, :], m2[:, :]))
                    s.vself.edge(e, e.reciprocal(m1[:, :], m1[:, :]))
                    e.tensor_scalar_mul(wgt_f[:, :], scr7[:, :], m1[:, :]).then_inc(ev.h, 1)
                ctx.v(top2)
                w_done = ev.inc()
                it, pst, wfree2 = s.ps.acquire()
                it2, pst2, wfree3 = s.ps.acquire()
                def wtr(e, pst=pst, pst2=pst2, wfree2=wfree2, wfree3=wfree3,
                        w_done=w_done, ident_done=ident_done):
                    e.wait_ge(ev.h, max(w_done, ident_done))
                    pwait(e, wfree2)
                    pwait(e, wfree3)
                    for ee in range(7):
                        pp = pst if ee < 4 else pst2
                        ins = e.transpose(pp[0:1, (ee % 4) * 128:(ee % 4 + 1) * 128],
                                          wgt_f[:, ee:ee + 1], identf[:, :])
                    ins.then_inc(mm.h, 1)
                ctx.t(wtr)
                mmv2 = mm.inc()
                def wcp(e, sub=sub, pst=pst, pst2=pst2, mmv2=mmv2):
                    e.wait_ge(mm.h, mmv2)
                    for ee in range(7):
                        pp = pst if ee < 4 else pst2
                        ins = e.tensor_copy(wrow[0:1, ee * 512 + sub * 128:ee * 512 + (sub + 1) * 128],
                                            pp[0:1, (ee % 4) * 128:(ee % 4 + 1) * 128])
                    ins.then_inc(ev.h, 1)
                ctx.v(wcp)
                s.ps.release(ir, ev, ev.inc())
                s.ps.release(it, ev, ev.n)
                s.ps.release(it2, ev, ev.n)
            wrow_done = ev.n
            iw, psw, wfreew = s.ps.acquire()
            iw2, psw2, wfreew2 = s.ps.acquire()
            wbc_ev = {}
            for ee in range(7):
                pw = psw if ee % 2 == 0 else psw2
                def wbm(e, ee=ee, pw=pw, wrow_done=wrow_done, wfreew=wfreew, wfreew2=wfreew2,
                        slot_free=wbc_ev.get(ee - 2, 0)):
                    e.wait_ge(ev.h, wrow_done)
                    if ee < 2:
                        pwait(e, wfreew if ee == 0 else wfreew2)
                    if slot_free:
                        e.wait_ge(ev.h, slot_free)
                    e.matmul(pw[:, :], onesrow_bf[0:1, :], wrow[0:1, ee * 512:(ee + 1) * 512],
                             start=True, stop=True).then_inc(mm.h, 1)
                ctx.t(wbm)
                mmv3 = mm.inc()
                def wbc(e, ee=ee, pw=pw, mmv3=mmv3):
                    e.wait_ge(mm.h, mmv3)
                    e.tensor_copy(wb_sb[:, ee * 512:(ee + 1) * 512], pw[:, :]).then_inc(ev.h, 1)
                ctx.v(wbc)
                wbc_ev[ee] = ev.inc()
            s.ps.release(iw, ev, ev.n)
            s.ps.release(iw2, ev, ev.n)
            wb_done = ev.n

            for eidx in range(8):
                for m in range(16):
                    p2 = m % 2
                    def gl(e, nt=nt, eidx=eidx, m=m, p2=p2, war=g_war[p2]):
                        if war:
                            e.wait_ge(mm.h, war)
                        e.dma_start(out=gb[p2][:, :],
                                    in_=gT[eidx, :, m * 128:(m + 1) * 128]
                                    .rearrange("(k p) m -> p k m", p=128)).then_inc(gld[p2].h, 16)
                    ctx.g(gl)
                    g_cnt[p2] += 16
                    def ul(e, nt=nt, eidx=eidx, m=m, p2=p2, war=u_war[p2]):
                        if war:
                            e.wait_ge(mm.h, war)
                        e.dma_start(out=ub[p2][:, :],
                                    in_=uT[eidx, :, m * 128:(m + 1) * 128]
                                    .rearrange("(k p) m -> p k m", p=128)).then_inc(uld[p2].h, 16)
                    ctx.g(ul)
                    u_cnt[p2] += 16
                    ig, psg, wfg = s.ps.acquire()
                    iu, psu, wfu = s.ps.acquire()
                    def gmm(e, m=m, p2=p2, psg=psg, psu=psu, wfg=wfg, wfu=wfu,
                            gc=g_cnt[p2], uc=u_cnt[p2], ln2_ev=ln2_ev):
                        e.wait_ge(gld[p2].h, gc)
                        e.wait_ge(uld[p2].h, uc)
                        e.wait_ge(ev.h, ln2_ev)
                        pwait(e, wfg)
                        pwait(e, wfu)
                        for k in range(8):
                            e.matmul(psg[:, :], gb[p2][:, k * 128:(k + 1) * 128],
                                     xm_bf[:, k * 512:(k + 1) * 512],
                                     start=(k == 0), stop=(k == 7))
                        for k in range(8):
                            ins = e.matmul(psu[:, :], ub[p2][:, k * 128:(k + 1) * 128],
                                           xm_bf[:, k * 512:(k + 1) * 512],
                                           start=(k == 0), stop=(k == 7))
                        ins.then_inc(mm.h, 1)
                    ctx.t(gmm)
                    g_war[p2] = mm.inc()
                    u_war[p2] = g_war[p2]
                    mmv = g_war[p2]
                    def sil(e, m=m, p2=p2, psg=psg, mmv=mmv, hw=hum_ev[p2]):
                        e.wait_ge(mm.h, mmv)
                        if hw:
                            e.wait_ge(ev.h, hw)
                        e.activation(sg_sb[p2][:, :], psg[:, :], AF.Silu).then_inc(sa.h, 1)
                    ctx.s(sil)
                    sav = sa.inc()
                    def humul(e, m=m, p2=p2, psu=psu, sav=sav):
                        e.wait_ge(sa.h, sav)
                        e.tensor_mul(hu_bf[:, m * 512:(m + 1) * 512], psu[:, :],
                                     sg_sb[p2][:, :]).then_inc(ev.h, 1)
                    ctx.v(humul)
                    evv = ev.inc()
                    hum_ev[p2] = evv
                    s.ps.release(ig, ev, evv)
                    s.ps.release(iu, ev, evv)
                hu_done = ev.n
                for mc in range(8):
                    p2 = mc % 2
                    def dl(e, eidx=eidx, mc=mc, p2=p2, war=d_war[p2]):
                        if war:
                            e.wait_ge(mm.h, war)
                        e.dma_start(out=db[p2][:, :],
                                    in_=dT[eidx, :, mc * 128:(mc + 1) * 128]
                                    .rearrange("(k p) m -> p k m", p=128)).then_inc(dld[p2].h, 16)
                    ctx.g(dl)
                    d_cnt[p2] += 16
                    idp, psd, wfd = s.ps.acquire()
                    def dmm(e, mc=mc, p2=p2, psd=psd, wfd=wfd, dc=d_cnt[p2], hu_done=hu_done):
                        e.wait_ge(dld[p2].h, dc)
                        e.wait_ge(ev.h, hu_done)
                        pwait(e, wfd)
                        for k in range(16):
                            ins = e.matmul(psd[:, :], db[p2][:, k * 128:(k + 1) * 128],
                                           hu_bf[:, k * 512:(k + 1) * 512],
                                           start=(k == 0), stop=(k == 15))
                        ins.then_inc(mm.h, 1)
                    ctx.t(dmm)
                    d_war[p2] = mm.inc()
                    mmv = d_war[p2]
                    if eidx == 0:
                        def dev(e, mc=mc, psd=psd, mmv=mmv):
                            e.wait_ge(mm.h, mmv)
                            e.tensor_copy(macc[:, mc * 512:(mc + 1) * 512], psd[:, :]).then_inc(ev.h, 1)
                    else:
                        def dev(e, mc=mc, psd=psd, mmv=mmv, eidx=eidx, wb_done=wb_done):
                            e.wait_ge(mm.h, mmv)
                            e.wait_ge(ev.h, wb_done)
                            s.vself.edge(e, e.tensor_mul(s.lnt[mc % 2][:, :], psd[:, :],
                                         wb_sb[:, (eidx - 1) * 512:eidx * 512]))
                            e.tensor_add(macc[:, mc * 512:(mc + 1) * 512],
                                         macc[:, mc * 512:(mc + 1) * 512],
                                         s.lnt[mc % 2][:, :]).then_inc(ev.h, 1)
                    ctx.v(dev)
                    s.ps.release(idp, ev, ev.inc())
            moe_done = ev.n

            for mc in range(8):
                def t2f(e, mc=mc, ntp=ntp, moe_done=moe_done, xa_rdy=xa_rdy,
                        swar=st_war[ntp]):
                    e.wait_ge(ev.h, moe_done)
                    e.wait_ge(xald[ntp].h, xa_rdy)
                    if swar:
                        e.wait_ge(xst[ntp].h, swar)
                    s.vself.edge(e, e.tensor_add(tt[ntp][:, mc * 512:(mc + 1) * 512],
                                 tt[ntp][:, mc * 512:(mc + 1) * 512],
                                 macc[:, mc * 512:(mc + 1) * 512]))
                    e.tensor_add(tt[ntp][:, mc * 512:(mc + 1) * 512],
                                 tt[ntp][:, mc * 512:(mc + 1) * 512],
                                 xat[ntp][:, mc * 512:(mc + 1) * 512]).then_inc(ev.h, 1)
                ctx.v(t2f)
                ev.inc()
            pre_done = ev.n
            xa_war[ntp] = pre_done
            out_ev = plan_ln_tile(ctx, s,
                                  (lambda mc, ntp=ntp: tt[ntp][:, mc * 512:(mc + 1) * 512]),
                                  (lambda mc, ntp=ntp: tt[ntp][:, mc * 512:(mc + 1) * 512]),
                                  wait_ev=pre_done)
            t1_war[ntp] = out_ev
            def stf(e, nt=nt, ntp=ntp, out_ev=out_ev):
                e.wait_ge(ev.h, out_ev)
                e.dma_start(out=xb_o.rearrange("(c p) n -> p c n", p=128)[:, :, nt * 512:(nt + 1) * 512],
                            in_=tt[ntp][:, :]).then_inc(xst[ntp].h, 16)
            ctx.y(stf)
            st_war[ntp] = xst[ntp].inc(16)

        f0, f1 = xst[0].n, xst[1].n
        def finw(e):
            e.wait_ge(xst[0].h, f0)
            e.wait_ge(xst[1].h, f1)
        ctx.y(finw)
        ctx.finish()
    return nc



def build_C1():
    nc = bass.Bass("TRN2", target_bir_lowering=False)
    xb = nc.dram_tensor("xb", [D, NL], F32, kind="ExternalInput").ap()
    WkT = nc.dram_tensor("WkT", [D, D], F32, kind="ExternalInput").ap()
    WvT = nc.dram_tensor("WvT", [D, D], F32, kind="ExternalInput").ap()
    bk = nc.dram_tensor("bk", [1, D], F32, kind="ExternalInput").ap()
    bv = nc.dram_tensor("bv", [1, D], F32, kind="ExternalInput").ap()
    Ktm_o = nc.dram_tensor("Ktm", [NL, D], BF16, kind="ExternalOutput").ap()
    Vtm_o = nc.dram_tensor("Vtm", [NL, D], BF16, kind="ExternalOutput").ap()
    Sh_o = nc.dram_tensor("Sh", [128, 8 * 64], F32, kind="ExternalOutput").ap()

    es = contextlib.ExitStack()
    with es:
        ctx = Ctx(nc, es)
        s = S(ctx, n512=6)
        mm, ev, va, sa, ld, st = s.mm, s.ev, s.va, s.sa, s.ld, s.st
        xb_bf = ctx.sbuf("xb_bf", [128, NCH * NL], BF16)
        wk = ctx.sbuf("wk", [128, NCH * D], BF16)
        wv = ctx.sbuf("wv", [128, NCH * D], BF16)
        bk_sb = ctx.sbuf("bk_sb", [1, D], BF16)
        bv_sb = ctx.sbuf("bv_sb", [1, D], BF16)
        ktm = ctx.sbuf("ktm", [128, NT * D], BF16)
        vtm = ctx.sbuf("vtm", [128, NT * D], BF16)
        ssb = ctx.sbuf("ssb", [128, 8 * 64], F32)
        kst = [ctx.sem(f"kst{i}") for i in range(2)]
        pstate = [ctx.psum(f"pstate{i}", [128, 512]) for i in range(2)]

        dma_in(ctx, xb_bf[:, :], xb.rearrange("(c p) n -> p c n", p=128), ld)
        dma_in(ctx, wk[:, :], WkT.rearrange("(k p) m -> p k m", p=128), ld)
        dma_in(ctx, wv[:, :], WvT.rearrange("(k p) m -> p k m", p=128), ld)
        dma_in(ctx, bk_sb[:, :], bk, ld)
        dma_in(ctx, bv_sb[:, :], bv, ld)
        ld_all = ld.n

        for which, (wbuf2, bsb, dst, sto) in enumerate(
                [(wk, bk_sb, ktm, Ktm_o), (wv, bv_sb, vtm, Vtm_o)]):
            for ncid in range(NT):
                for half in range(2):
                    i, ps, wfree = s.ps.acquire()
                    def mmf(e, wbuf2=wbuf2, bsb=bsb, ncid=ncid, half=half, ps=ps, wfree=wfree):
                        e.wait_ge(ld.h, ld_all)
                        e.wait_ge(ev.h, s.init_ev)
                        pwait(e, wfree)
                        for k in range(NCH):
                            e.matmul(ps[:, :], xb_bf[:, k * NL + ncid * 128:k * NL + (ncid + 1) * 128],
                                     wbuf2[:, k * D + half * 512:k * D + (half + 1) * 512],
                                     start=(k == 0), stop=False)
                        e.matmul(ps[:, :], s.onesrow_bf[0:1, 0:128],
                                 bsb[0:1, half * 512:(half + 1) * 512],
                                 start=False, stop=True).then_inc(mm.h, 1)
                    ctx.t(mmf)
                    mmv = mm.inc()
                    def evf(e, dst=dst, ncid=ncid, half=half, ps=ps, mmv=mmv):
                        e.wait_ge(mm.h, mmv)
                        e.tensor_copy(dst[:, ncid * D + half * 512:ncid * D + (half + 1) * 512],
                                      ps[:, :]).then_inc(ev.h, 1)
                    ctx.v(evf)
                    s.ps.release(i, ev, ev.inc())
        kv_done = ev.n
        def kvst(e):
            e.wait_ge(ev.h, kv_done)
            for ncid in range(NT):
                e.dma_start(out=Ktm_o[ncid * 128:(ncid + 1) * 128, :],
                            in_=ktm[:, ncid * D:(ncid + 1) * D]).then_inc(kst[0].h, 16)
                e.dma_start(out=Vtm_o[ncid * 128:(ncid + 1) * 128, :],
                            in_=vtm[:, ncid * D:(ncid + 1) * D]).then_inc(kst[1].h, 16)
        ctx.y(kvst)
        kst[0].inc(16 * NT)
        kst[1].inc(16 * NT)

        sev_ev = {}
        for h in range(H):
            def smm(e, h=h, kv_done=kv_done, war=sev_ev.get(h - 2, 0)):
                e.wait_ge(ev.h, kv_done)
                if war:
                    e.wait_ge(ev.h, war)
                for ncid in range(NT):
                    ins = e.matmul(pstate[h % 2][0:64, (h % 8) * 64:(h % 8) * 64 + 64],
                                   ktm[:, ncid * D + h * 64:ncid * D + h * 64 + 64],
                                   vtm[:, ncid * D + h * 64:ncid * D + h * 64 + 64],
                                   start=(ncid == 0), stop=(ncid == NT - 1))
                ins.then_inc(mm.h, 1)
            ctx.t(smm)
            mmv = mm.inc()
            def sev(e, h=h, mmv=mmv):
                e.wait_ge(mm.h, mmv)
                e.tensor_copy(ssb[(h % 2) * 64:(h % 2) * 64 + 64, (h // 2) * 64:(h // 2) * 64 + 64],
                              pstate[h % 2][0:64, (h % 8) * 64:(h % 8) * 64 + 64]).then_inc(ev.h, 1)
            ctx.v(sev)
            sev_ev[h] = ev.inc()
        s_done = ev.n
        def sst(e):
            e.wait_ge(ev.h, s_done)
            e.dma_start(out=Sh_o, in_=ssb[:, :]).then_inc(st.h, 16)
        ctx.y(sst)
        fin = st.inc(16)
        k0, k1 = kst[0].n, kst[1].n
        def finw(e):
            e.wait_ge(st.h, fin)
            e.wait_ge(kst[0].h, k0)
            e.wait_ge(kst[1].h, k1)
        ctx.y(finw)
        ctx.finish()
    return nc



def build_C2():
    nc = bass.Bass("TRN2", target_bir_lowering=False)
    xb = nc.dram_tensor("xb", [D, NL], F32, kind="ExternalInput").ap()
    WqT = nc.dram_tensor("WqT", [D, D], F32, kind="ExternalInput").ap()
    WkT = nc.dram_tensor("WkT", [D, D], F32, kind="ExternalInput").ap()
    WoT = nc.dram_tensor("WoT", [D, D], F32, kind="ExternalInput").ap()
    bq = nc.dram_tensor("bq", [128, 8], F32, kind="ExternalInput").ap()
    bkf = nc.dram_tensor("bkf", [128, 8], F32, kind="ExternalInput").ap()
    bo = nc.dram_tensor("bo", [128, 8], F32, kind="ExternalInput").ap()
    Ktm = nc.dram_tensor("Ktm", [NL, D], BF16, kind="ExternalInput").ap()
    Vtm = nc.dram_tensor("Vtm", [NL, D], BF16, kind="ExternalInput").ap()
    Sin = nc.dram_tensor("Sin", [128, 8 * 64], F32, kind="ExternalInput").ap()
    trimask = nc.dram_tensor("trimask", [128, 128], BF16, kind="ExternalInput").ap()
    xc_o = nc.dram_tensor("xc", [D, NL], F32, kind="ExternalOutput").ap()

    es = contextlib.ExitStack()
    with es:
        ctx = Ctx(nc, es)
        s = S(ctx, n512=6)
        mm, ev, va, sa, ld, st = s.mm, s.ev, s.va, s.sa, s.ld, s.st
        xb_bf = ctx.sbuf("xb_bf", [128, NCH * NL], BF16)
        wmb = [ctx.sbuf(f"wmb{i}", [128, NCH * 128], BF16) for i in range(2)]
        qfm = ctx.sbuf("qfm", [128, NCH * NL], BF16)
        kfm = ctx.sbuf("kfm", [128, NCH * NL], BF16)
        yfm = xb_bf
        ktc = [ctx.sbuf(f"ktc{i}", [128, D], BF16) for i in range(2)]
        vtc = [ctx.sbuf(f"vtc{i}", [128, D], BF16) for i in range(2)]
        state = ctx.sbuf("state", [128, 8 * 64], F32)
        state_bf = ctx.sbuf("state_bf", [128, 8 * 64], BF16)
        ssb_bf = ctx.sbuf("ssb_bf", [128, 128], BF16)
        tmsk = ctx.sbuf("tmsk", [128, 128], BF16)
        bq_sb = ctx.sbuf("bq_sb", [128, 8], F32)
        bk_sb = ctx.sbuf("bk_sb", [128, 8], F32)
        bo_sb = ctx.sbuf("bo_sb", [128, 8], F32)
        xat = [ctx.sbuf(f"xat{i}", [128, 8 * 512], F32) for i in range(2)]
        rs = [ctx.sbuf(f"rs{i}", [128, 8 * 512], F32) for i in range(2)]
        kld = [ctx.sem(f"kld{i}") for i in range(2)]
        vld = [ctx.sem(f"vld{i}") for i in range(2)]
        xald = [ctx.sem(f"xald{i}") for i in range(2)]
        xst = [ctx.sem(f"xst{i}") for i in range(2)]

        wld = [ctx.sem(f"wld{i}") for i in range(2)]
        dma_in(ctx, xb_bf[:, :], xb.rearrange("(c p) n -> p c n", p=128), ld)
        dma_in(ctx, bq_sb[:, :], bq, ld)
        dma_in(ctx, bk_sb[:, :], bkf, ld)
        dma_in(ctx, bo_sb[:, :], bo, ld)
        dma_in(ctx, state[:, :], Sin, ld)
        dma_in(ctx, tmsk[:, :], trimask, ld)
        ld_all = ld.n
        def st_bf0(e):
            e.wait_ge(ld.h, ld_all)
            e.tensor_copy(state_bf[:, :], state[:, :]).then_inc(ev.h, 1)
        ctx.v(st_bf0)
        state_ready = ev.inc()

        w_war = [0, 0]
        w_cnt = [0, 0]
        for which, (wT3, bsb, dst) in enumerate([(WqT, bq_sb, qfm), (WkT, bk_sb, kfm)]):
            for mc in range(8):
                p3 = (which * 8 + mc) % 2
                def wl(e, wT3=wT3, mc=mc, p3=p3, war=w_war[p3]):
                    if war:
                        e.wait_ge(mm.h, war)
                    e.dma_start(out=wmb[p3][:, :],
                                in_=wT3.rearrange("(k p) m -> p k m", p=128)[:, :, mc * 128:(mc + 1) * 128]
                                ).then_inc(wld[p3].h, 16)
                ctx.g(wl)
                w_cnt[p3] += 16
                for nt in range(4):
                    i, ps, wfree = s.ps.acquire()
                    def mmf(e, p3=p3, mc=mc, nt=nt, ps=ps, wfree=wfree, wc=w_cnt[p3]):
                        e.wait_ge(ld.h, ld_all)
                        e.wait_ge(wld[p3].h, wc)
                        pwait(e, wfree)
                        for k in range(NCH):
                            ins = e.matmul(ps[:, :], wmb[p3][:, k * 128:(k + 1) * 128],
                                           xb_bf[:, k * NL + nt * 512:k * NL + (nt + 1) * 512],
                                           start=(k == 0), stop=(k == NCH - 1))
                        ins.then_inc(mm.h, 1)
                    ctx.t(mmf)
                    mmv = mm.inc()
                    w_war[p3] = mmv
                    def evf(e, bsb=bsb, dst=dst, mc=mc, nt=nt, ps=ps, mmv=mmv):
                        e.wait_ge(mm.h, mmv)
                        e.tensor_scalar_add(dst[:, mc * NL + nt * 512:mc * NL + (nt + 1) * 512],
                                            ps[:, :], bsb[:, mc:mc + 1]).then_inc(ev.h, 1)
                    ctx.v(evf)
                    s.ps.release(i, ev, ev.inc())
        qk_done = ev.n

        state_upd_ev = state_ready
        ktc_war = [0, 0]
        for c in range(NT):
            p = c % 2
            def kvl(e, c=c, p=p, war=ktc_war[p]):
                if war:
                    e.wait_ge(mm.h, war)
                e.dma_start(out=ktc[p][:, :], in_=Ktm[c * 128:(c + 1) * 128, :]).then_inc(kld[p].h, 16)
                e.dma_start(out=vtc[p][:, :], in_=Vtm[c * 128:(c + 1) * 128, :]).then_inc(vld[p].h, 16)
            ctx.g(kvl)
            k_rdy = kld[p].inc(16)
            v_rdy = vld[p].inc(16)
            last_mm = 0
            for h in range(H):
                rb = (h % 2) * 64
                hc = h // 2
                i1, ps1, wf1 = s.ps.acquire()
                def smm(e, rb=rb, hc=hc, c=c, ps1=ps1, wf1=wf1, qk_done=qk_done):
                    e.wait_ge(ev.h, qk_done)
                    pwait(e, wf1)
                    e.matmul(ps1[:, 0:128],
                             kfm[rb:rb + 64, hc * NL + c * 128:hc * NL + (c + 1) * 128],
                             qfm[rb:rb + 64, hc * NL + c * 128:hc * NL + (c + 1) * 128],
                             start=True, stop=True).then_inc(mm.h, 1)
                ctx.t(smm)
                mmv = mm.inc()
                def smsk(e, ps1=ps1, mmv=mmv):
                    e.wait_ge(mm.h, mmv)
                    e.tensor_mul(ssb_bf[:, :], ps1[:, 0:128], tmsk[:, :]).then_inc(ev.h, 1)
                ctx.v(smsk)
                sm_ev = ev.inc()
                s.ps.release(i1, ev, sm_ev)
                i2, ps2, wf2 = s.ps.acquire()
                def omm(e, h=h, rb=rb, hc=hc, c=c, p=p, ps2=ps2, wf2=wf2, sm_ev=sm_ev,
                        v_rdy=v_rdy, sup=state_upd_ev):
                    e.wait_ge(ev.h, sm_ev)
                    e.wait_ge(vld[p].h, v_rdy)
                    e.wait_ge(ev.h, sup)
                    pwait(e, wf2)
                    e.matmul(ps2[0:64, 0:128], vtc[p][:, h * 64:(h + 1) * 64],
                             ssb_bf[:, :], start=True, stop=False)
                    e.matmul(ps2[0:64, 0:128],
                             state_bf[rb:rb + 64, hc * 64:(hc + 1) * 64],
                             qfm[rb:rb + 64, hc * NL + c * 128:hc * NL + (c + 1) * 128],
                             start=False, stop=True).then_inc(mm.h, 1)
                ctx.t(omm)
                mmv2 = mm.inc()
                def yev(e, h=h, c=c, ps2=ps2, mmv2=mmv2):
                    e.wait_ge(mm.h, mmv2)
                    e.tensor_copy(yfm[(h % 2) * 64:(h % 2) * 64 + 64,
                                      (h // 2) * NL + c * 128:(h // 2) * NL + (c + 1) * 128],
                                  ps2[0:64, 0:128]).then_inc(ev.h, 1)
                ctx.v(yev)
                s.ps.release(i2, ev, ev.inc())
                i3, ps3, wf3 = s.ps.acquire()
                def dmm(e, h=h, c=c, p=p, ps3=ps3, wf3=wf3, k_rdy=k_rdy, v_rdy=v_rdy):
                    e.wait_ge(kld[p].h, k_rdy)
                    e.wait_ge(vld[p].h, v_rdy)
                    pwait(e, wf3)
                    e.matmul(ps3[0:64, 0:64], ktc[p][:, h * 64:(h + 1) * 64],
                             vtc[p][:, h * 64:(h + 1) * 64],
                             start=True, stop=True).then_inc(mm.h, 1)
                ctx.t(dmm)
                mmv3 = mm.inc()
                def supd(e, h=h, rb=rb, hc=hc, ps3=ps3, mmv3=mmv3, mmv2=mmv2):
                    e.wait_ge(mm.h, max(mmv3, mmv2))
                    s.vself.edge(e, e.tensor_add(state[rb:rb + 64, hc * 64:(hc + 1) * 64],
                                 state[rb:rb + 64, hc * 64:(hc + 1) * 64],
                                 ps3[0:64, 0:64]))
                    e.tensor_copy(state_bf[rb:rb + 64, hc * 64:(hc + 1) * 64],
                                  state[rb:rb + 64, hc * 64:(hc + 1) * 64]).then_inc(ev.h, 1)
                ctx.v(supd)
                s.ps.release(i3, ev, ev.inc())
                last_mm = mmv3
            state_upd_ev = ev.n
            ktc_war[p] = last_mm
        y_done = ev.n

        st_war = [0, 0]
        for nt in range(4):
            ntp = nt % 2
            def xaf(e, nt=nt, ntp=ntp, war=st_war[ntp]):
                if war:
                    e.wait_ge(xst[ntp].h, war)
                e.dma_start(out=xat[ntp][:, :],
                            in_=xb.rearrange("(c p) n -> p c n", p=128)[:, :, nt * 512:(nt + 1) * 512]
                            ).then_inc(xald[ntp].h, 16)
            ctx.g(xaf)
            x_rdy = xald[ntp].inc(16)
            for mc in range(8):
                p3 = (nt * 8 + mc) % 2
                def wol(e, mc=mc, p3=p3, war=w_war[p3]):
                    if war:
                        e.wait_ge(mm.h, war)
                    e.dma_start(out=wmb[p3][:, :],
                                in_=WoT.rearrange("(k p) m -> p k m", p=128)[:, :, mc * 128:(mc + 1) * 128]
                                ).then_inc(wld[p3].h, 16)
                ctx.g(wol)
                w_cnt[p3] += 16
                io, pso, wfree = s.ps.acquire()
                def pmm(e, mc=mc, nt=nt, p3=p3, pso=pso, wfree=wfree, y_done=y_done, wc=w_cnt[p3]):
                    e.wait_ge(wld[p3].h, wc)
                    e.wait_ge(ev.h, y_done)
                    pwait(e, wfree)
                    for k in range(8):
                        ins = e.matmul(pso[:, :], wmb[p3][:, k * 128:(k + 1) * 128],
                                       yfm[:, k * NL + nt * 512:k * NL + (nt + 1) * 512],
                                       start=(k == 0), stop=(k == 7))
                    ins.then_inc(mm.h, 1)
                ctx.t(pmm)
                mmv = mm.inc()
                w_war[p3] = mmv
                def pev(e, mc=mc, ntp=ntp, pso=pso, mmv=mmv, x_rdy=x_rdy, war=st_war[ntp]):
                    e.wait_ge(mm.h, mmv)
                    e.wait_ge(xald[ntp].h, x_rdy)
                    if war:
                        e.wait_ge(xst[ntp].h, war)
                    e.scalar_tensor_tensor(rs[ntp][:, mc * 512:(mc + 1) * 512], pso[:, :],
                                           bo_sb[:, mc:mc + 1], xat[ntp][:, mc * 512:(mc + 1) * 512],
                                           ALU.add, ALU.add).then_inc(ev.h, 1)
                ctx.v(pev)
                s.ps.release(io, ev, ev.inc())
            t_done = ev.n
            out_ev = plan_ln_tile(ctx, s,
                                  (lambda mc, ntp=ntp: rs[ntp][:, mc * 512:(mc + 1) * 512]),
                                  (lambda mc, ntp=ntp: rs[ntp][:, mc * 512:(mc + 1) * 512]),
                                  wait_ev=t_done)
            def stf(e, nt=nt, ntp=ntp, out_ev=out_ev):
                e.wait_ge(ev.h, out_ev)
                e.dma_start(out=xc_o.rearrange("(c p) n -> p c n", p=128)[:, :, nt * 512:(nt + 1) * 512],
                            in_=rs[ntp][:, :]).then_inc(xst[ntp].h, 16)
            ctx.y(stf)
            st_war[ntp] = xst[ntp].inc(16)
        f0, f1 = xst[0].n, xst[1].n
        def finw(e):
            e.wait_ge(xst[0].h, f0)
            e.wait_ge(xst[1].h, f1)
        ctx.y(finw)
        ctx.finish()
    return nc



def build_D1():
    nc = bass.Bass("TRN2", target_bir_lowering=False)
    xc = nc.dram_tensor("xc", [D, NL], F32, kind="ExternalInput").ap()
    rwT = nc.dram_tensor("rwT", [D, 8], F32, kind="ExternalInput").ap()
    ltri = nc.dram_tensor("ltri", [128, 128], BF16, kind="ExternalInput").ap()
    meta_o = nc.dram_tensor("meta", [NL, 16], F32, kind="ExternalOutput").ap()
    cnt_o = nc.dram_tensor("cnt", [1, 8], F32, kind="ExternalOutput").ap()

    es = contextlib.ExitStack()
    with es:
        ctx = Ctx(nc, es)
        s = S(ctx, n512=6)
        mm, ev, va, sa, ld, st = s.mm, s.ev, s.va, s.sa, s.ld, s.st
        xcb = ctx.sbuf("xcb", [128, NCH * NL], F32)
        rw_sb = ctx.sbuf("rw_sb", [128, 8 * 8], F32)
        lt_sb = ctx.sbuf("lt_sb", [128, 128], BF16)
        probs = ctx.sbuf("probs_sb", [128, 8], F32)
        m1 = ctx.sbuf("m1_sb", [128, 1], F32)
        ssum = ctx.sbuf("ssum", [128, 1], F32)
        oh_sb = ctx.sbuf("oh_sb", [128, 8], F32)
        oh_bf = ctx.sbuf("oh_bf", [128, 8], BF16)
        carry = ctx.sbuf("carry", [1, 8], F32)
        posn = ctx.sbuf("posn", [128, 8], F32)
        pc_sb = ctx.sbuf("pc_sb", [128, 2], F32)
        cnts = ctx.sbuf("cnts", [1, 8], F32)
        ones128_bf = ctx.sbuf("ones128_bf", [128, 1], BF16)

        dma_in(ctx, xcb[:, :], xc.rearrange("(c p) n -> p c n", p=128), ld)
        dma_in(ctx, rw_sb[:, :], rwT.rearrange("(k p) m -> p k m", p=128), ld)
        dma_in(ctx, lt_sb[:, :], ltri, ld)
        ld_all = ld.n
        ctx.v(lambda e: e.memset(carry[:, :], 0.0))
        ctx.v(lambda e: e.memset(ones128_bf[:, :], 1.0))

        st_cnt = 0
        carry_ev = [0]
        for sub in range(NT):
            ir, psr, wfree = s.ps.acquire()
            def rmm(e, sub=sub, psr=psr, wfree=wfree):
                e.wait_ge(ld.h, ld_all)
                pwait(e, wfree)
                for k in range(8):
                    ins = e.matmul(psr[:, 0:8], xcb[:, k * NL + sub * 128:k * NL + (sub + 1) * 128],
                                   rw_sb[:, k * 8:(k + 1) * 8], start=(k == 0), stop=(k == 7))
                ins.then_inc(mm.h, 1)
            ctx.t(rmm)
            mmv = mm.inc()
            def sm1(e, psr=psr, mmv=mmv, war=st_cnt):
                e.wait_ge(mm.h, mmv)
                if war:
                    e.wait_ge(st.h, war)
                s.vself.edge(e, e.reduce_max(m1[:, :], psr[:, 0:8], axis=AX.X))
                e.tensor_scalar(probs[:, :], psr[:, 0:8], m1[:, :], None, ALU.subtract).then_inc(va.h, 1)
            ctx.v(sm1)
            vav = va.inc()
            def sm2(e, vav=vav):
                e.wait_ge(va.h, vav)
                e.activation(probs[:, :], probs[:, :], AF.Exp).then_inc(sa.h, 1)
            ctx.s(sm2)
            sav = sa.inc()
            def sm3(e, sav=sav, psr=psr):
                e.wait_ge(sa.h, sav)
                s.vself.edge(e, e.reduce_sum(ssum[:, :], probs[:, :], axis=AX.X))
                s.vself.edge(e, e.reciprocal(ssum[:, :], ssum[:, :]))
                s.vself.edge(e, e.tensor_scalar_mul(probs[:, :], probs[:, :], ssum[:, :]))
                s.vself.edge(e, e.reduce_max(pc_sb[:, 1:2], probs[:, :], axis=AX.X))
                s.vself.edge(e, e.tensor_single_scalar(oh_sb[:, :], probs[:, :], pc_sb[:, 1:2], ALU.is_equal))
                e.tensor_copy(oh_bf[:, :], oh_sb[:, :]).then_inc(ev.h, 1)
            ctx.v(sm3)
            oh_done = ev.inc()
            s.ps.release(ir, ev, oh_done)
            ip, psp, wfree2 = s.ps.acquire()
            it2, pst2, wfree3 = s.ps.acquire()
            def pmm(e, sub=sub, psp=psp, pst2=pst2, wfree2=wfree2, wfree3=wfree3,
                    oh_done=oh_done, cwait=carry_ev[0]):
                e.wait_ge(ev.h, oh_done)
                if cwait:
                    e.wait_ge(ev.h, cwait)
                pwait(e, wfree2)
                pwait(e, wfree3)
                e.matmul(psp[:, 0:8], lt_sb[:, :], oh_bf[:, :], start=True, stop=False)
                e.matmul(psp[:, 0:8], s.onesrow[0:1, :], carry[0:1, :],
                         start=False, stop=True)
                e.matmul(pst2[0:1, 0:8], ones128_bf[:, :], oh_bf[:, :],
                         start=True, stop=True).then_inc(mm.h, 1)
            ctx.t(pmm)
            mmv2 = mm.inc()
            def pev(e, sub=sub, psp=psp, pst2=pst2, mmv2=mmv2):
                e.wait_ge(mm.h, mmv2)
                e.tensor_add(carry[0:1, :], carry[0:1, :], pst2[0:1, 0:8])
                s.vself.edge(e, e.tensor_copy(posn[:, :], psp[:, 0:8]))
                s.vself.edge(e, e.tensor_mul(posn[:, :], posn[:, :], oh_sb[:, :]))
                e.reduce_sum(pc_sb[:, 0:1], posn[:, :], axis=AX.X).then_inc(ev.h, 1)
            ctx.v(pev)
            pos_done = ev.inc()
            carry_ev[0] = pos_done
            s.ps.release(ip, ev, pos_done)
            s.ps.release(it2, ev, pos_done)
            def stf(e, sub=sub, pos_done=pos_done):
                e.wait_ge(ev.h, pos_done)
                e.dma_start(out=meta_o[sub * 128:(sub + 1) * 128, 0:8], in_=oh_sb[:, :]).then_inc(st.h, 16)
                e.dma_start(out=meta_o[sub * 128:(sub + 1) * 128, 8:10], in_=pc_sb[:, :]).then_inc(st.h, 16)
            ctx.y(stf)
            st_cnt = st.inc(32)
        final_carry = carry_ev[0]
        def cev(e, final_carry=final_carry):
            e.wait_ge(ev.h, final_carry)
            e.tensor_copy(cnts[:, :], carry[0:1, :]).then_inc(ev.h, 1)
        ctx.v(cev)
        c_done = ev.inc()
        def cst(e, c_done=c_done):
            e.wait_ge(ev.h, c_done)
            e.dma_start(out=cnt_o, in_=cnts[:, :]).then_inc(st.h, 16)
        ctx.y(cst)
        fin = st.inc(16)
        ctx.y(lambda e: e.wait_ge(st.h, fin))
        ctx.finish()
    return nc



def build_D2():
    CAP = 3072.0
    nc = bass.Bass("TRN2", target_bir_lowering=False)
    xc = nc.dram_tensor("xc", [D, NL], F32, kind="ExternalInput").ap()
    meta_i = nc.dram_tensor("meta", [NL, 16], F32, kind="ExternalInput").ap()
    cnts_i = nc.dram_tensor("cnts", [8, 8], F32, kind="ExternalInput").ap()
    pmask = nc.dram_tensor("pmask", [8, 1], F32, kind="ExternalInput").ap()
    w1 = nc.dram_tensor("w1", [8, D, 4096], F32, kind="ExternalInput").ap()
    b1 = nc.dram_tensor("b1", [8, 128, 32], F32, kind="ExternalInput").ap()
    w2 = nc.dram_tensor("w2", [8, 4096, D], F32, kind="ExternalInput").ap()
    b2 = nc.dram_tensor("b2", [8, 128, 8], F32, kind="ExternalInput").ap()
    out_o = nc.dram_tensor("out", [D, NL], F32, kind="ExternalOutput").ap()

    es = contextlib.ExitStack()
    with es:
        ctx = Ctx(nc, es)
        s = S(ctx, n512=8)
        mm, ev, va, sa, ld, st = s.mm, s.ev, s.va, s.sa, s.ld, s.st
        xct = [ctx.sbuf(f"xct{i}", [128, 8 * 512], F32) for i in range(2)]
        xm_bf = ctx.sbuf("xm_bf", [128, 8 * 512], BF16)
        hu_bf = ctx.sbuf("hu_bf", [128, 32 * 512], BF16)
        yacc = ctx.sbuf("yacc", [128, 8 * 512], F32)
        gb_sb = ctx.sbuf("gb_sb", [128, 8 * 512], F32)
        aux = ctx.sbuf("aux", [128, 4 * 16], F32)
        gate = ctx.sbuf("gate", [128, 4], F32)
        ob_sb = ctx.sbuf("ob_sb", [128, 8], F32)
        offr = ctx.sbuf("offr", [1, 8], F32)
        cnt_sb = ctx.sbuf("cnt_sb", [8, 8], F32)
        pm_sb = ctx.sbuf("pm_sb", [8, 1], F32)
        grow = ctx.sbuf("grow", [1, 8 * 512], BF16)
        ge_f = ctx.sbuf("ge_f", [128, 1], F32)
        b1_sb = ctx.sbuf("b1_sb", [128, 32], F32)
        b2_sb = ctx.sbuf("b2_sb", [128, 8], F32)
        identf = ctx.sbuf("identf", [128, 128], F32)
        coliota = ctx.sbuf("coliota", [128, 128], F32)
        rowv = ctx.sbuf("rowv", [128, 1], F32)
        w1b = [ctx.sbuf(f"w1b_{i}", [128, 8 * 128], BF16) for i in range(2)]
        w2b = [ctx.sbuf(f"w2b_{i}", [128, 32 * 128], BF16) for i in range(2)]
        g1ld = [ctx.sem(f"g1ld{i}") for i in range(2)]
        g2ld = [ctx.sem(f"g2ld{i}") for i in range(2)]
        bld = ctx.sem("bld")
        axld = ctx.sem("axld")
        xld = [ctx.sem(f"xld{i}") for i in range(2)]
        ast = [ctx.sem(f"ast{i}") for i in range(2)]
        iosem = ctx.sem("iosem2")

        ctx.g(lambda e: e.iota(coliota[:, :], [[1, 128]], channel_multiplier=0,
                               allow_small_or_imprecise_dtypes=True))
        def iot2(e):
            e.iota(rowv[:, :], [[0, 1]], channel_multiplier=1,
                   allow_small_or_imprecise_dtypes=True).then_inc(iosem.h, 1)
        ctx.g(iot2)
        io_done = iosem.inc(1)
        def mkident(e):
            e.wait_ge(iosem.h, io_done)
            e.tensor_single_scalar(identf[:, :], coliota[:, :], rowv[:, :],
                                   ALU.is_equal).then_inc(ev.h, 1)
        ctx.v(mkident)
        ident_done = ev.inc()

        dma_in(ctx, cnt_sb[:, :], cnts_i, ld)
        dma_in(ctx, pm_sb[:, :], pmask, ld)
        ld_all = ld.n
        ioff, psoff, wfo = s.ps.acquire()
        def offmm(e, psoff=psoff, wfo=wfo):
            e.wait_ge(ld.h, ld_all)
            pwait(e, wfo)
            e.matmul(psoff[0:1, 0:8], pm_sb[:, :], cnt_sb[:, :],
                     start=True, stop=True).then_inc(mm.h, 1)
        ctx.t(offmm)
        off_mm = mm.inc()
        def offev(e, psoff=psoff, off_mm=off_mm):
            e.wait_ge(mm.h, off_mm)
            e.tensor_copy(offr[:, :], psoff[0:1, 0:8]).then_inc(ev.h, 1)
        ctx.v(offev)
        off_done = ev.inc()
        s.ps.release(ioff, ev, off_done)
        iob, psob, wfb = s.ps.acquire()
        def obm(e, psob=psob, wfb=wfb, off_done=off_done):
            e.wait_ge(ev.h, off_done)
            pwait(e, wfb)
            e.matmul(psob[:, 0:8], s.onesrow[0:1, :], offr[0:1, :],
                     start=True, stop=True).then_inc(mm.h, 1)
        ctx.t(obm)
        ob_mm = mm.inc()
        def obev(e, psob=psob, ob_mm=ob_mm):
            e.wait_ge(mm.h, ob_mm)
            e.tensor_copy(ob_sb[:, :], psob[:, 0:8]).then_inc(ev.h, 1)
        ctx.v(obev)
        ob_done = ev.inc()
        s.ps.release(iob, ev, ob_done)

        aux_war = [0]
        ge_tr_mm = [0]
        bias_gate = [0, 0]
        g1war = [0, 0]
        g2war = [0, 0]
        g1cnt = [0, 0]
        g2cnt = [0, 0]
        xwar = [0, 0]
        swar = [0, 0]
        for nt in range(4):
            ntp = nt % 2
            def xcf(e, nt=nt, ntp=ntp, war=xwar[ntp], sw=swar[ntp]):
                if war:
                    e.wait_ge(ev.h, war)
                if sw:
                    e.wait_ge(ast[ntp].h, sw)
                e.dma_start(out=xct[ntp][:, :],
                            in_=xc.rearrange("(c p) n -> p c n", p=128)[:, :, nt * 512:(nt + 1) * 512]
                            ).then_inc(xld[ntp].h, 16)
            ctx.g(xcf)
            x_rdy = xld[ntp].inc(16)
            def auxf(e, nt=nt, war=aux_war[0]):
                if war:
                    e.wait_ge(ev.h, war)
                e.dma_start(out=aux[:, :].rearrange("p (a c) -> p a c", c=16),
                            in_=meta_i[nt * 512:(nt + 1) * 512, :].rearrange("(a p) c -> p a c", p=128)
                            ).then_inc(axld.h, 16)
            ctx.g(auxf)
            aux_rdy = axld.inc(16)
            def xcast(e, ntp=ntp, x_rdy=x_rdy):
                e.wait_ge(xld[ntp].h, x_rdy)
                e.tensor_copy(xm_bf[:, :], xct[ntp][:, :]).then_inc(ev.h, 1)
            ctx.v(xcast)
            xbf_done = ev.inc()
            for sub in range(4):
                def gf(e, sub=sub, aux_rdy=aux_rdy, ob_done=ob_done):
                    e.wait_ge(axld.h, aux_rdy)
                    e.wait_ge(ev.h, ob_done)
                    s.vself.edge(e, e.tensor_mul(s.lnt[0][:, 0:8], aux[:, sub * 16:sub * 16 + 8], ob_sb[:, :]))
                    s.vself.edge(e, e.reduce_sum(ge_f[:, :], s.lnt[0][:, 0:8], axis=AX.X))
                    s.vself.edge(e, e.tensor_add(ge_f[:, :], ge_f[:, :], aux[:, sub * 16 + 8:sub * 16 + 9]))
                    s.vself.edge(e, e.tensor_single_scalar(ge_f[:, :], ge_f[:, :], CAP, ALU.is_lt))
                    e.tensor_mul(gate[:, sub:sub + 1], ge_f[:, :],
                                 aux[:, sub * 16 + 9:sub * 16 + 10]).then_inc(ev.h, 1)
                ctx.v(gf)
                ev.inc()
            gates_done = ev.n

            for eidx in range(8):
                itr, pst, wtt = s.ps.acquire()
                for sub in range(4):
                    def get(e, eidx=eidx, sub=sub, gates_done=gates_done, ident_done=ident_done,
                            pst=pst, wtt=wtt, prev_tr=ge_tr_mm[0]):
                        e.wait_ge(ev.h, max(gates_done, ident_done))
                        if prev_tr:
                            e.wait_ge(mm.h, prev_tr)
                        if sub == 0:
                            pwait(e, wtt)
                        e.tensor_mul(ge_f[:, :], gate[:, sub:sub + 1],
                                     aux[:, sub * 16 + eidx:sub * 16 + eidx + 1]).then_inc(ev.h, 1)
                    ctx.v(get)
                    gev = ev.inc()
                    def gtr(e, sub=sub, pst=pst, gev=gev):
                        e.wait_ge(ev.h, gev)
                        e.transpose(pst[0:1, sub * 128:(sub + 1) * 128], ge_f[:, :],
                                    identf[:, :]).then_inc(mm.h, 1)
                    ctx.t(gtr)
                    ge_tr_mm[0] = mm.inc()
                tr_mm = mm.n
                def gcp(e, eidx=eidx, pst=pst, tr_mm=tr_mm):
                    e.wait_ge(mm.h, tr_mm)
                    with nc.allow_low_precision(reason="top-1 gate is a bf16 matmul-side operand"):
                        e.tensor_copy(grow[0:1, eidx * 512:(eidx + 1) * 512],
                                      pst[0:1, 0:512]).then_inc(ev.h, 1)
                ctx.v(gcp)
                s.ps.release(itr, ev, ev.inc())
            grow_done = ev.n
            aux_war[0] = grow_done
            for eidx in range(8):
                ib, psb, wfb2 = s.ps.acquire()
                def gbm(e, eidx=eidx, psb=psb, wfb2=wfb2, grow_done=grow_done):
                    e.wait_ge(ev.h, grow_done)
                    pwait(e, wfb2)
                    e.matmul(psb[:, :], s.onesrow_bf[0:1, :], grow[0:1, eidx * 512:(eidx + 1) * 512],
                             start=True, stop=True).then_inc(mm.h, 1)
                ctx.t(gbm)
                mmv = mm.inc()
                def gbe(e, eidx=eidx, psb=psb, mmv=mmv):
                    e.wait_ge(mm.h, mmv)
                    e.tensor_copy(gb_sb[:, eidx * 512:(eidx + 1) * 512], psb[:, :]).then_inc(ev.h, 1)
                ctx.v(gbe)
                s.ps.release(ib, ev, ev.inc())
            gb_done = ev.n

            for eidx in range(8):
                def bf_(e, eidx=eidx, g1=bias_gate[0], g2=bias_gate[1]):
                    if g1:
                        e.wait_ge(sa.h, g1)
                    if g2:
                        e.wait_ge(ev.h, g2)
                    e.dma_start(out=b1_sb[:, :], in_=b1[eidx]).then_inc(bld.h, 16)
                    e.dma_start(out=b2_sb[:, :], in_=b2[eidx]).then_inc(bld.h, 16)
                ctx.g(bf_)
                b_rdy = bld.inc(32)
                for m in range(32):
                    p2 = m % 2
                    def w1l(e, eidx=eidx, m=m, p2=p2, war=g1war[p2]):
                        if war:
                            e.wait_ge(mm.h, war)
                        e.dma_start(out=w1b[p2][:, :],
                                    in_=w1[eidx, :, m * 128:(m + 1) * 128]
                                    .rearrange("(k p) m -> p k m", p=128)).then_inc(g1ld[p2].h, 16)
                    ctx.g(w1l)
                    g1cnt[p2] += 16
                    i1, ps1, wf1 = s.ps.acquire()
                    def w1m(e, m=m, p2=p2, ps1=ps1, wf1=wf1, gc=g1cnt[p2], xbf_done=xbf_done):
                        e.wait_ge(g1ld[p2].h, gc)
                        e.wait_ge(ev.h, xbf_done)
                        pwait(e, wf1)
                        for k in range(8):
                            ins = e.matmul(ps1[:, :], w1b[p2][:, k * 128:(k + 1) * 128],
                                           xm_bf[:, k * 512:(k + 1) * 512],
                                           start=(k == 0), stop=(k == 7))
                        ins.then_inc(mm.h, 1)
                    ctx.t(w1m)
                    g1war[p2] = mm.inc()
                    mmv = g1war[p2]
                    def gel(e, m=m, ps1=ps1, mmv=mmv, b_rdy=b_rdy):
                        e.wait_ge(mm.h, mmv)
                        e.wait_ge(bld.h, b_rdy)
                        e.activation(hu_bf[:, m * 512:(m + 1) * 512], ps1[:, :], AF.Gelu,
                                     bias=b1_sb[:, m:m + 1]).then_inc(sa.h, 1)
                    ctx.s(gel)
                    s.ps.release(i1, sa, sa.inc())
                h_done = sa.n
                for mc in range(8):
                    p2 = mc % 2
                    def w2l(e, eidx=eidx, mc=mc, p2=p2, war=g2war[p2]):
                        if war:
                            e.wait_ge(mm.h, war)
                        e.dma_start(out=w2b[p2][:, :],
                                    in_=w2[eidx, :, mc * 128:(mc + 1) * 128]
                                    .rearrange("(k p) m -> p k m", p=128)).then_inc(g2ld[p2].h, 16)
                    ctx.g(w2l)
                    g2cnt[p2] += 16
                    i2, ps2, wf2 = s.ps.acquire()
                    def w2m(e, mc=mc, p2=p2, ps2=ps2, wf2=wf2, gc=g2cnt[p2], h_done=h_done):
                        e.wait_ge(g2ld[p2].h, gc)
                        e.wait_ge(sa.h, h_done)
                        pwait(e, wf2)
                        for k in range(32):
                            ins = e.matmul(ps2[:, :], w2b[p2][:, k * 128:(k + 1) * 128],
                                           hu_bf[:, k * 512:(k + 1) * 512],
                                           start=(k == 0), stop=(k == 31))
                        ins.then_inc(mm.h, 1)
                    ctx.t(w2m)
                    g2war[p2] = mm.inc()
                    mmv = g2war[p2]
                    if eidx == 0:
                        def dev(e, mc=mc, ps2=ps2, mmv=mmv, b_rdy=b_rdy, gb_done=gb_done):
                            e.wait_ge(mm.h, mmv)
                            e.wait_ge(ev.h, gb_done)
                            e.wait_ge(bld.h, b_rdy)
                            s.vself.edge(e, e.tensor_scalar_add(s.lnt[mc % 2][:, :], ps2[:, :], b2_sb[:, mc:mc + 1]))
                            e.tensor_mul(yacc[:, mc * 512:(mc + 1) * 512], s.lnt[mc % 2][:, :],
                                         gb_sb[:, 0:512]).then_inc(ev.h, 1)
                    else:
                        def dev(e, eidx=eidx, mc=mc, ps2=ps2, mmv=mmv, b_rdy=b_rdy, gb_done=gb_done):
                            e.wait_ge(mm.h, mmv)
                            e.wait_ge(ev.h, gb_done)
                            e.wait_ge(bld.h, b_rdy)
                            s.vself.edge(e, e.tensor_scalar_add(s.lnt[mc % 2][:, :], ps2[:, :], b2_sb[:, mc:mc + 1]))
                            s.vself.edge(e, e.tensor_mul(s.lnt[mc % 2][:, :], s.lnt[mc % 2][:, :],
                                         gb_sb[:, eidx * 512:(eidx + 1) * 512]))
                            e.tensor_add(yacc[:, mc * 512:(mc + 1) * 512],
                                         yacc[:, mc * 512:(mc + 1) * 512],
                                         s.lnt[mc % 2][:, :]).then_inc(ev.h, 1)
                    ctx.v(dev)
                    s.ps.release(i2, ev, ev.inc())
                bias_gate[0] = sa.n
                bias_gate[1] = ev.n
            y_done = ev.n

            for mc in range(8):
                def t2f(e, mc=mc, ntp=ntp, y_done=y_done, x_rdy=x_rdy, war=swar[ntp]):
                    e.wait_ge(ev.h, y_done)
                    e.wait_ge(xld[ntp].h, x_rdy)
                    if war:
                        e.wait_ge(ast[ntp].h, war)
                    e.tensor_add(xct[ntp][:, mc * 512:(mc + 1) * 512],
                                 xct[ntp][:, mc * 512:(mc + 1) * 512],
                                 yacc[:, mc * 512:(mc + 1) * 512]).then_inc(ev.h, 1)
                ctx.v(t2f)
                ev.inc()
            pre_done = ev.n
            xwar[ntp] = pre_done
            out_ev = plan_ln_tile(ctx, s,
                                  (lambda mc, ntp=ntp: xct[ntp][:, mc * 512:(mc + 1) * 512]),
                                  (lambda mc, ntp=ntp: xct[ntp][:, mc * 512:(mc + 1) * 512]),
                                  wait_ev=pre_done)
            def stf(e, nt=nt, ntp=ntp, out_ev=out_ev):
                e.wait_ge(ev.h, out_ev)
                e.dma_start(out=out_o.rearrange("(c p) n -> p c n", p=128)[:, :, nt * 512:(nt + 1) * 512],
                            in_=xct[ntp][:, :]).then_inc(ast[ntp].h, 16)
            ctx.y(stf)
            swar[ntp] = ast[ntp].inc(16)
        f0, f1 = ast[0].n, ast[1].n
        def finw(e):
            e.wait_ge(ast[0].h, f0)
            e.wait_ge(ast[1].h, f1)
        ctx.y(finw)
        ctx.finish()
    return nc



def _fm(a):
    return np.ascontiguousarray(a.T)


_NC_CACHE = {}

def _get(name, fn):
    if name not in _NC_CACHE:
        _NC_CACHE[name] = fn()
    return _NC_CACHE[name]


def run_A1(x_shards, wqkvT, projm):
    nc = _get("A1", build_A1)
    in_maps = [{"x": x_shards[c], "wqkvT": wqkvT, "projm": projm} for c in range(NCORE)]
    res = run_bass_kernel_spmd(nc, in_maps, core_ids=list(range(NCORE)))
    return res.results


def _bias_fm(b, c):
    return np.ascontiguousarray(np.asarray(b, dtype=np.float32).reshape(c, 128).T)


def stage_A(inp, x_shards):
    wqkvT = np.ascontiguousarray(np.asarray(inp["L0_attn_w_qkv"]).T)
    projm = np.ascontiguousarray(np.asarray(inp["L0_attn_proj"]).reshape(D, 256))
    r1 = run_A1(x_shards, wqkvT, projm)
    w_outT = np.ascontiguousarray(np.asarray(inp["L0_attn_w_out"]).T)
    w1T = np.ascontiguousarray(np.asarray(inp["L0_attn_w1"]).T)
    w2T = np.ascontiguousarray(np.asarray(inp["L0_attn_w2"]).T)
    common = {
        "w_outT": w_outT, "w1T": w1T, "w2T": w2T,
        "b_out": _bias_fm(inp["L0_attn_b_out"], 8),
        "gamma": _bias_fm(inp["L0_attn_gamma"], 8),
        "beta": _bias_fm(inp["L0_attn_beta"], 8),
        "b1": _bias_fm(inp["L0_attn_b1"], 32),
        "b2": _bias_fm(inp["L0_attn_b2"], 8),
    }
    nc = _get("A2", build_A2)
    in_maps = []
    for c in range(NCORE):
        m = dict(common)
        m["x"] = x_shards[c]
        m["qf"] = np.asarray(r1[c]["qf"])
        m["kvx_a"] = np.asarray(r1[c]["kvx"])
        m["kvx_b"] = np.asarray(r1[c ^ 1]["kvx"])
        in_maps.append(m)
    res = run_bass_kernel_spmd(nc, in_maps, core_ids=list(range(NCORE)))
    return [np.asarray(res.results[c]["xa"]) for c in range(NCORE)]


def _bf(a):
    import ml_dtypes
    return np.ascontiguousarray(np.asarray(a).astype(ml_dtypes.bfloat16))


def _ct(a):
    return np.ascontiguousarray(np.asarray(a, dtype=np.float32))


def _fm_shard(xf, c):
    return np.ascontiguousarray(xf[c * NL:(c + 1) * NL].T)


def stage_B(inp, xa_shards):
    import ml_dtypes
    rq = np.asarray(inp["L0_ffn_rope_q_w"]).reshape(16, 32, 256)
    rk = np.asarray(inp["L0_ffn_rope_k_w"]).reshape(16, 32, 1024)
    inv_freq = 1.0 / 10000.0 ** (np.arange(0, 32, 2, dtype=np.float64) / 32)
    tpos = np.arange(T, dtype=np.float64)
    fr = tpos[:, None] * inv_freq[None, :]
    cosT = np.ascontiguousarray(np.tile(np.cos(fr).T, (8, 1)).astype(np.float32))
    sinT = np.ascontiguousarray(np.tile(np.sin(fr).T, (8, 1)).astype(np.float32))
    common = {
        "ln1w": _bias_fm(inp["L0_ffn_ln1_w"], 8),
        "kvdT": _ct(np.asarray(inp["L0_ffn_kv_d"]).T),
        "qdT": _ct(np.asarray(inp["L0_ffn_q_d"]).T),
        "kunT": _ct(np.asarray(inp["L0_ffn_k_u"]).reshape(16, 64, 256)[:, 32:].reshape(512, 256).T),
        "qunT": _ct(np.asarray(inp["L0_ffn_q_u"]).reshape(16, 64, 256)[:, 32:].reshape(512, 256).T),
        "vuT": _ct(np.asarray(inp["L0_ffn_v_u"]).T),
        "rqaT": _ct(rq[:, :16].reshape(256, 256).T),
        "rqbT": _ct(rq[:, 16:].reshape(256, 256).T),
        "rkaT": _ct(rk[:, :16].reshape(256, 1024).T),
        "rkbT": _ct(rk[:, 16:].reshape(256, 1024).T),
    }
    nc = _get("B1", build_B1)
    in_maps = []
    for c in range(NCORE):
        m = dict(common)
        m["xa"] = xa_shards[c]
        half = c % 2
        m["cosf"] = np.ascontiguousarray(cosT[:, half * NL:(half + 1) * NL])
        m["sinf"] = np.ascontiguousarray(sinT[:, half * NL:(half + 1) * NL])
        in_maps.append(m)
    r1 = run_bass_kernel_spmd(nc, in_maps, core_ids=list(range(NCORE))).results
    Ks = [np.asarray(r1[c]["K"]) for c in range(NCORE)]
    Qs = [np.asarray(r1[c]["Q"]) for c in range(NCORE)]
    Vs = [np.asarray(r1[c]["V"]) for c in range(NCORE)]

    stair = (np.arange(2176)[None, :] >= (np.arange(128)[:, None] + 1023)).astype(ml_dtypes.bfloat16)
    stair = np.ascontiguousarray(stair)
    owT = _ct(np.asarray(inp["L0_ffn_o_w"]).T)
    t1s = [None] * NCORE
    for odd in (False, True):
        nc2 = _get(f"B2{'o' if odd else 'e'}", lambda odd=odd: build_B2(odd))
        cores = [1, 3, 5, 7] if odd else [0, 2, 4, 6]
        maps = []
        for c in cores:
            b = c // 2
            if odd:
                Kf = np.concatenate([Ks[2 * b], Ks[2 * b + 1]], axis=1)
                Vf = np.concatenate([Vs[2 * b], Vs[2 * b + 1]], axis=0)
            else:
                Kf, Vf = Ks[2 * b], Vs[2 * b]
            maps.append({"xa": xa_shards[c], "K": np.ascontiguousarray(Kf),
                         "Q": Qs[c], "V": np.ascontiguousarray(Vf),
                         "stair": stair, "owT": owT})
        r2 = run_bass_kernel_spmd(nc2, maps, core_ids=cores).results
        for i, c in enumerate(cores):
            t1s[c] = np.asarray(r2[i]["t1"])

    gT = np.concatenate([np.asarray(inp["L0_ffn_shared_gate"]),
                         np.asarray(inp["L0_ffn_routed_gate"])], axis=0).transpose(0, 2, 1)
    uT = np.concatenate([np.asarray(inp["L0_ffn_shared_up"]),
                         np.asarray(inp["L0_ffn_routed_up"])], axis=0).transpose(0, 2, 1)
    dT = np.concatenate([np.asarray(inp["L0_ffn_shared_down"]),
                         np.asarray(inp["L0_ffn_routed_down"])], axis=0).transpose(0, 2, 1)
    common3 = {
        "ln2w": _bias_fm(inp["L0_ffn_ln2_w"], 8),
        "rwT": _ct(np.asarray(inp["L0_ffn_router_w"]).T),
        "rbias": _ct(np.asarray(inp["L0_ffn_routing_bias"]).reshape(1, 7)),
        "gT": _ct(gT), "uT": _ct(uT), "dT": _ct(dT),
    }
    nc3 = _get("B3", build_B3)
    maps = []
    for c in range(NCORE):
        m = dict(common3)
        m["t1"] = t1s[c]
        m["xa"] = xa_shards[c]
        maps.append(m)
    r3 = run_bass_kernel_spmd(nc3, maps, core_ids=list(range(NCORE))).results
    return [np.asarray(r3[c]["xb"]) for c in range(NCORE)]


def stage_C(inp, xb_shards):
    import ml_dtypes
    common1 = {
        "WkT": _ct(np.asarray(inp["L1_attn_Wk"]).T),
        "WvT": _ct(np.asarray(inp["L1_attn_Wv"]).T),
        "bk": _ct(np.asarray(inp["L1_attn_bk"]).reshape(1, D)),
        "bv": _ct(np.asarray(inp["L1_attn_bv"]).reshape(1, D)),
    }
    nc1 = _get("C1", build_C1)
    maps = [dict(common1, xb=xb_shards[c]) for c in range(NCORE)]
    r1 = run_bass_kernel_spmd(nc1, maps, core_ids=list(range(NCORE))).results
    trimask = (np.arange(128)[None, :] >= np.arange(128)[:, None]).astype(ml_dtypes.bfloat16)
    common2 = {
        "WqT": _ct(np.asarray(inp["L1_attn_Wq"]).T),
        "WkT": _ct(np.asarray(inp["L1_attn_Wk"]).T),
        "WoT": _ct(np.asarray(inp["L1_attn_Wo"]).T),
        "bq": _bias_fm(inp["L1_attn_bq"], 8),
        "bkf": _bias_fm(inp["L1_attn_bk"], 8),
        "bo": _bias_fm(inp["L1_attn_bo"], 8),
        "trimask": np.ascontiguousarray(trimask),
    }
    nc2 = _get("C2", build_C2)
    maps = []
    for c in range(NCORE):
        m = dict(common2)
        m["xb"] = xb_shards[c]
        m["Ktm"] = np.asarray(r1[c]["Ktm"])
        m["Vtm"] = np.asarray(r1[c]["Vtm"])
        m["Sin"] = (np.asarray(r1[c - 1]["Sh"]) if c % 2 == 1
                    else np.zeros((128, 512), np.float32))
        maps.append(m)
    r2 = run_bass_kernel_spmd(nc2, maps, core_ids=list(range(NCORE))).results
    return [np.asarray(r2[c]["xc"]) for c in range(NCORE)]


def stage_D(inp, xc_shards):
    import ml_dtypes
    ltri = (np.arange(128)[:, None] < np.arange(128)[None, :]).astype(ml_dtypes.bfloat16)
    common1 = {
        "rwT": _ct(np.asarray(inp["L1_ffn_router_w"]).T),
        "ltri": np.ascontiguousarray(ltri),
    }
    nc1 = _get("D1", build_D1)
    maps = [dict(common1, xc=xc_shards[c]) for c in range(NCORE)]
    r1 = run_bass_kernel_spmd(nc1, maps, core_ids=list(range(NCORE))).results
    cnts_all = np.ascontiguousarray(
        np.concatenate([np.asarray(r1[c]["cnt"]) for c in range(NCORE)], axis=0))
    b1 = np.asarray(inp["L1_ffn_b1"]).reshape(8, 32, 128).transpose(0, 2, 1)
    b2 = np.asarray(inp["L1_ffn_b2"]).reshape(8, 8, 128).transpose(0, 2, 1)
    common2 = {
        "cnts": cnts_all,
        "w1": _ct(np.asarray(inp["L1_ffn_w1"])),
        "b1": _ct(b1),
        "w2": _ct(np.asarray(inp["L1_ffn_w2"])),
        "b2": _ct(b2),
    }
    nc2 = _get("D2", build_D2)
    maps = []
    for c in range(NCORE):
        m = dict(common2)
        m["xc"] = xc_shards[c]
        m["meta"] = np.asarray(r1[c]["meta"])
        m["pmask"] = np.ascontiguousarray((np.arange(8) < c).astype(np.float32).reshape(8, 1))
        maps.append(m)
    r2 = run_bass_kernel_spmd(nc2, maps, core_ids=list(range(NCORE))).results
    return [np.asarray(r2[c]["out"]) for c in range(NCORE)]


def kernel(**inputs):
    x = np.asarray(inputs["L0_attn_x"], dtype=np.float32).reshape(B * T, D)
    x_shards = [_fm_shard(x, c) for c in range(NCORE)]
    xa = stage_A(inputs, x_shards)
    xb = stage_B(inputs, xa)
    xc = stage_C(inputs, xb)
    outs = stage_D(inputs, xc)
    full = np.concatenate([o.T for o in outs], axis=0)
    return np.ascontiguousarray(full.reshape(B, T, D), dtype=np.float32)


# version 02
# speedup vs baseline: 1.2261x; 1.2261x over previous
import contextlib
import numpy as np
import concourse.bass as bass
import concourse.mybir as mybir
from concourse.bass_utils import run_bass_kernel_spmd

F32 = mybir.dt.float32
BF16 = mybir.dt.bfloat16
I32 = mybir.dt.int32
AF = mybir.ActivationFunctionType
ALU = mybir.AluOpType
AX = mybir.AxisListType

NCORE = 8
B, T, D = 4, 4096, 1024
NL = 2048
NCH = D // 128
NT = NL // 128
H, HD = 16, 64


class Sem:
    __slots__ = ("h", "n")
    def __init__(self, h):
        self.h = h
        self.n = 0
    def inc(self, k=1):
        self.n += k
        return self.n

class Pool:
    def __init__(self, slots, ev=None):
        self.slots = slots
        self.free_at = [None] * len(slots)
        self.rr = 0
    def acquire(self):
        i = self.rr % len(self.slots)
        self.rr += 1
        return i, self.slots[i], self.free_at[i]
    def release(self, i, sem, val=None):
        if val is None:
            raise TypeError("release(i, sem, val)")
        self.free_at[i] = (sem, val)


def pwait(e, free):
    if free is not None:
        e.wait_ge(free[0].h, free[1])


class VSelf:
    def __init__(self, sem):
        self.sem = sem
        self.n = 0
    def edge(self, e, ins):
        self.n += 1
        ins.then_inc(self.sem.h, 1)
        e.wait_ge(self.sem.h, self.n)

class Ctx:
    def __init__(self, nc, es):
        self.nc = nc
        self.es = es
        self.ops = {e: [] for e in ("g", "t", "v", "s", "y")}
        self._semi = 0
    def sbuf(self, name, shape, dtype):
        return self.es.enter_context(self.nc.sbuf_tensor(name, shape, dtype))
    def psum(self, name, shape, dtype=F32):
        return self.es.enter_context(self.nc.psum_tensor(name, shape, dtype))
    def sem(self, name=None):
        self._semi += 1
        return Sem(self.es.enter_context(self.nc.semaphore(name or f"s{self._semi}")))
    def g(self, f): self.ops["g"].append(f)
    def t(self, f): self.ops["t"].append(f)
    def v(self, f): self.ops["v"].append(f)
    def s(self, f): self.ops["s"].append(f)
    def y(self, f): self.ops["y"].append(f)
    def finish(self):
        blk = self.es.enter_context(self.nc.Block())
        ops = self.ops
        @blk.gpsimd
        def _(e):
            for f in ops["g"]: f(e)
        @blk.tensor
        def _(e):
            for f in ops["t"]: f(e)
        @blk.vector
        def _(e):
            for f in ops["v"]: f(e)
        @blk.scalar
        def _(e):
            for f in ops["s"]: f(e)
        @blk.sync
        def _(e):
            for f in ops["y"]: f(e)


def dma_in(ctx, dst_ap, src_ap, ld, queue="g"):
    def f(e):
        e.dma_start(out=dst_ap, in_=src_ap).then_inc(ld.h, 16)
    getattr(ctx, queue)(f)
    return ld.inc(16)


class EluEvac:
    def __init__(self, ctx, width, mm, va, sa, ev, tag):
        self.ctx = ctx
        self.w = width
        self.u = [ctx.sbuf(f"elu_u{tag}_{i}", [128, width], F32) for i in range(2)]
        self.e = [ctx.sbuf(f"elu_e{tag}_{i}", [128, width], F32) for i in range(2)]
        self.mm, self.va, self.sa, self.ev = mm, va, sa, ev
        self.i = 0
        self.sa_of = {}
        self.ev_of = {}
    def plan(self, ps_ap, out_ap, mmv, out_war=None):
        ctx, idx, p = self.ctx, self.i, self.i % 2
        self.i += 1
        u, eb = self.u[p], self.e[p]
        va, sa, ev, mm = self.va, self.sa, self.ev, self.mm
        war_sa = self.sa_of.get(idx - 2)
        def ev1(e):
            e.wait_ge(mm.h, mmv)
            if war_sa:
                e.wait_ge(sa.h, war_sa)
            e.tensor_scalar_min(u[:, :], ps_ap, 0.0).then_inc(va.h, 1)
        ctx.v(ev1)
        vav = va.inc()
        war_ev = self.ev_of.get(idx - 2)
        def ev2(e):
            e.wait_ge(va.h, vav)
            if war_ev:
                e.wait_ge(ev.h, war_ev)
            e.activation(eb[:, :], u[:, :], AF.Exp).then_inc(sa.h, 1)
        ctx.s(ev2)
        sav = sa.inc()
        self.sa_of[idx] = sav
        def ev3(e):
            e.wait_ge(sa.h, sav)
            if out_war is not None:
                e.wait_ge(out_war[0].h, out_war[1])
            e.scalar_tensor_tensor(out_ap, ps_ap, 0.0, eb[:, :], ALU.max, ALU.add).then_inc(ev.h, 1)
        ctx.v(ev3)
        evv = ev.inc()
        self.ev_of[idx] = evv
        return evv



def build_A1():
    nc = bass.Bass("TRN2", target_bir_lowering=False)
    x = nc.dram_tensor("x", [D, NL], F32, kind="ExternalInput").ap()
    wqkvT = nc.dram_tensor("wqkvT", [D, 3 * D], F32, kind="ExternalInput").ap()
    projm = nc.dram_tensor("projm", [D, 256], F32, kind="ExternalInput").ap()
    qf_o = nc.dram_tensor("qf", [128, 32 * NL], BF16, kind="ExternalOutput").ap()
    kvx_o = nc.dram_tensor("kvx", [128, H * 130], F32, kind="ExternalOutput").ap()

    es = contextlib.ExitStack()
    with es:
        ctx = Ctx(nc, es)
        x_bf = ctx.sbuf("x_bf", [128, NCH * NL], BF16)
        wqk = ctx.sbuf("wqk", [128, NCH * 2048], BF16)
        pj = ctx.sbuf("pj", [128, NCH * 256], BF16)
        qk_bf = ctx.sbuf("qk_bf", [128, 16 * NL], BF16)
        vaug = ctx.sbuf("vaug", [128, NT * H * 65], BF16)
        kf_h = ctx.sbuf("kf_h", [128, NT * 256], BF16)
        qf_t = ctx.sbuf("qf_t", [128, 4 * 512], BF16)
        kvx_sb = ctx.sbuf("kvx_sb", [128, H * 130], F32)
        wv = wqk

        ld = ctx.sem("ld")
        mm = ctx.sem("mm")
        ev = ctx.sem("ev")
        va = ctx.sem("va")
        sa = ctx.sem("sa")
        st = ctx.sem("st")
        qst = [ctx.sem(f"qst{i}") for i in range(4)]

        ps512 = Pool([ctx.psum(f"p5_{i}", [128, 512]) for i in range(4)], ev)
        pkf = Pool([ctx.psum(f"pf_{i}", [128, 512]) for i in range(2)], ev)
        pkv = Pool([ctx.psum(f"pv_{i}", [128, 512]) for i in range(2)], ev)

        elu_q = EluEvac(ctx, 512, mm, va, sa, ev, "q")
        elu_k = EluEvac(ctx, 256, mm, va, sa, ev, "k")

        ld_x = dma_in(ctx, x_bf[:, :], x.rearrange("(c p) n -> p c n", p=128), ld)
        ld_wqk = dma_in(ctx, wqk[:, :], wqkvT.rearrange("(k p) m -> p k m", p=128)[:, :, 0:2048], ld)
        ld_pj = dma_in(ctx, pj[:, :], projm.rearrange("(k p) m -> p k m", p=128), ld)
        ld_all = ld.n
        ctx.v(lambda e: e.memset(
            vaug[:, :].rearrange("p (a h c) -> p a h c", h=H, c=65)[:, :, :, 64:65], 1.0))

        qk_ev = {}
        for m in range(16):
            for nt in range(4):
                i, ps, wfree = ps512.acquire()
                def mmf(e, m=m, nt=nt, ps=ps, wfree=wfree):
                    e.wait_ge(ld.h, ld_all)
                    pwait(e, wfree)
                    for k in range(NCH):
                        ins = e.matmul(ps[:, :], wqk[:, k * 2048 + m * 128:k * 2048 + (m + 1) * 128],
                                       x_bf[:, k * NL + nt * 512:k * NL + (nt + 1) * 512],
                                       start=(k == 0), stop=(k == NCH - 1))
                    ins.then_inc(mm.h, 1)
                ctx.t(mmf)
                mmv = mm.inc()
                def evf(e, ps=ps, m=m, nt=nt, mmv=mmv):
                    e.wait_ge(mm.h, mmv)
                    e.tensor_copy(qk_bf[:, m * NL + nt * 512:m * NL + (nt + 1) * 512],
                                  ps[:, :]).then_inc(ev.h, 1)
                ctx.v(evf)
                qk_ev[(m, nt)] = ev.inc()
                ps512.release(i, ev, qk_ev[(m, nt)])

        qk_mm_done = mm.n
        def ld_wv_f(e):
            e.wait_ge(mm.h, qk_mm_done)
            e.dma_start(out=wv[:, 0:NCH * 1024],
                        in_=wqkvT.rearrange("(k p) m -> p k m", p=128)[:, :, 2048:3072]).then_inc(ld.h, 16)
        ctx.g(ld_wv_f)
        ld_wv = ld.inc(16)

        for mc in range(NT):
            for nt in range(2):
                i, ps, wfree = ps512.acquire()
                def mmf(e, mc=mc, nt=nt, ps=ps, wfree=wfree):
                    e.wait_ge(ld.h, ld_wv)
                    pwait(e, wfree)
                    for k in range(NCH):
                        ins = e.matmul(ps[:, :], x_bf[:, k * NL + mc * 128:k * NL + (mc + 1) * 128],
                                       wv[:, k * 1024 + nt * 512:k * 1024 + (nt + 1) * 512],
                                       start=(k == 0), stop=(k == NCH - 1))
                    ins.then_inc(mm.h, 1)
                ctx.t(mmf)
                mmv = mm.inc()
                dst = vaug[:, :].rearrange("p (a h c) -> p a h c", h=H, c=65)[:, mc, nt * 8:(nt + 1) * 8, 0:64]
                def evf(e, ps=ps, dst=dst, mmv=mmv):
                    e.wait_ge(mm.h, mmv)
                    e.tensor_copy(dst, ps[:, :]).then_inc(ev.h, 1)
                ctx.v(evf)
                ps512.release(i, ev, ev.inc())
        qkv_done = ev.n

        qf_i = 0
        qf_war = {}
        for h in range(H):
            rb = (h % 2) * 64
            qc = h // 2
            for fc in range(2):
                for nt in range(4):
                    i, ps, wfree = ps512.acquire()
                    def mmf(e, rb=rb, qc=qc, fc=fc, nt=nt, ps=ps, wfree=wfree):
                        e.wait_ge(ld.h, ld_all)
                        e.wait_ge(ev.h, qkv_done)
                        pwait(e, wfree)
                        e.matmul(ps[:, :], pj[rb:rb + 64, qc * 256 + fc * 128:qc * 256 + (fc + 1) * 128],
                                 qk_bf[rb:rb + 64, qc * NL + nt * 512:qc * NL + (nt + 1) * 512],
                                 start=True, stop=True).then_inc(mm.h, 1)
                    ctx.t(mmf)
                    mmv = mm.inc()
                    col = (h * 2 + fc) * NL + nt * 512
                    qslot = qf_i % 4
                    war = qf_war.get(qslot)
                    evv = elu_q.plan(ps[:, :], qf_t[:, qslot * 512:(qslot + 1) * 512], mmv,
                                     out_war=(qst[qslot], war) if war else None)
                    ps512.release(i, ev, evv)
                    def stf(e, qslot=qslot, col=col, evv=evv):
                        e.wait_ge(ev.h, evv)
                        e.dma_start(out=qf_o[:, col:col + 512],
                                    in_=qf_t[:, qslot * 512:(qslot + 1) * 512]).then_inc(qst[qslot].h, 16)
                    ctx.y(stf)
                    qf_war[qslot] = qst[qslot].inc(16)
                    qf_i += 1

        prev_kv_mm = 0
        for h in range(H):
            rb = (h % 2) * 64
            kc = 8 + h // 2
            for mc in range(NT):
                i, ps, wfree = pkf.acquire()
                def mmf(e, rb=rb, kc=kc, h=h, mc=mc, ps=ps, wfree=wfree, pkm=prev_kv_mm):
                    e.wait_ge(ev.h, qkv_done)
                    pwait(e, wfree)
                    if pkm:
                        e.wait_ge(mm.h, pkm)
                    e.matmul(ps[:, 0:256],
                             qk_bf[rb:rb + 64, kc * NL + mc * 128:kc * NL + (mc + 1) * 128],
                             pj[rb:rb + 64, (h // 2) * 256:(h // 2) * 256 + 256],
                             start=True, stop=True).then_inc(mm.h, 1)
                ctx.t(mmf)
                mmv = mm.inc()
                evv = elu_k.plan(ps[:, 0:256], kf_h[:, mc * 256:(mc + 1) * 256], mmv)
                pkf.release(i, ev, evv)
            kf_done = ev.n

            ik, psv, wfree = pkv.acquire()
            def kvmm(e, h=h, psv=psv, wfree=wfree, kf_done=kf_done):
                e.wait_ge(ev.h, kf_done)
                pwait(e, wfree)
                for fc in range(2):
                    for mc in range(NT):
                        ins = e.matmul(psv[:, fc * 65:(fc + 1) * 65],
                                       kf_h[:, mc * 256 + fc * 128:mc * 256 + fc * 128 + 128],
                                       vaug[:, (mc * H + h) * 65:(mc * H + h) * 65 + 65],
                                       start=(mc == 0), stop=(mc == NT - 1))
                ins.then_inc(mm.h, 1)
            ctx.t(kvmm)
            prev_kv_mm = mm.inc()
            def kvev(e, h=h, psv=psv, mmv=prev_kv_mm):
                e.wait_ge(mm.h, mmv)
                e.tensor_copy(kvx_sb[:, h * 130:(h + 1) * 130], psv[:, 0:130]).then_inc(ev.h, 1)
            ctx.v(kvev)
            pkv.release(ik, ev, ev.inc())

        all_ev = ev.n
        def stores(e):
            e.wait_ge(ev.h, all_ev)
            e.dma_start(out=kvx_o, in_=kvx_sb[:, :]).then_inc(st.h, 16)
        ctx.y(stores)
        fin = st.inc(16)
        qfin = [q.n for q in qst]
        def finw(e):
            e.wait_ge(st.h, fin)
            for q, v_ in zip(qst, qfin):
                e.wait_ge(q.h, v_)
        ctx.y(finw)

        ctx.finish()
    return nc




class S:
    def __init__(self, ctx, n512=4):
        self.ctx = ctx
        self.ld = ctx.sem("ld")
        self.mm = ctx.sem("mm")
        self.ev = ctx.sem("ev")
        self.va = ctx.sem("va")
        self.sa = ctx.sem("sa")
        self.st = ctx.sem("st")
        self.ps = Pool([ctx.psum(f"ps_{i}", [128, 512]) for i in range(n512)], self.ev)
        self.ones128 = ctx.sbuf("ones128", [128, 1], F32)
        self.onesrow = ctx.sbuf("onesrow", [1, 128], F32)
        self.onesrow_bf = ctx.sbuf("s_onesrow_bf", [1, 128], BF16)
        self.sq = [ctx.sbuf(f"sq_{i}", [128, 512], F32) for i in range(2)]
        self.rowbuf = ctx.sbuf("rowbuf", [1, 4 * 512], F32)
        self.lnt = [ctx.sbuf(f"lnt_{i}", [128, 512], F32) for i in range(2)]
        self._sq_war = {}
        self._sq_i = 0
        self.eps_ln = ctx.sbuf("eps_ln", [1, 1], F32)
        self.vself = VSelf(ctx.sem("vself"))
        def _init(e):
            e.memset(self.ones128[:, :], 1.0)
            e.memset(self.onesrow[:, :], 1.0)
            e.memset(self.onesrow_bf[:, :], 1.0)
            e.memset(self.eps_ln[:, :], 1e-5).then_inc(self.ev.h, 1)
        ctx.v(_init)
        self.init_ev = self.ev.inc()


def plan_ln_tile(ctx, s, src_chunk, dst_chunk, *, gamma=None, beta=None,
                 bf_chunk=None, eps=1e-5, nch=8, dim=None, wait_ev=0, wait_mm=0):
    dim = dim or nch * 128
    mm, ev, va, sa = s.mm, s.ev, s.va, s.sa
    i_sum, ps_sum, w_sum = s.ps.acquire()
    i_sq, ps_sq, w_sq = s.ps.acquire()
    def sumf(e, w_sum=w_sum):
        e.wait_ge(ev.h, max(wait_ev, s.init_ev))
        if wait_mm:
            e.wait_ge(mm.h, wait_mm)
        pwait(e, w_sum)
        for mc in range(nch):
            ins = e.matmul(ps_sum[0:1, :], s.ones128[:, :], src_chunk(mc),
                           start=(mc == 0), stop=(mc == nch - 1))
        ins.then_inc(mm.h, 1)
    ctx.t(sumf)
    mm.inc()
    for mc in range(nch):
        p = s._sq_i % 2
        s._sq_i += 1
        war = s._sq_war.get(s._sq_i - 3)
        def sqf(e, mc=mc, p=p, war=war):
            if war:
                e.wait_ge(mm.h, war)
            if wait_ev:
                e.wait_ge(ev.h, wait_ev)
            e.activation(s.sq[p][:, :], src_chunk(mc), AF.Square).then_inc(sa.h, 1)
        ctx.s(sqf)
        sav = sa.inc()
        def sqmm(e, mc=mc, p=p, sav=sav, w_sq=w_sq):
            e.wait_ge(sa.h, sav)
            if mc == 0:
                pwait(e, w_sq)
            e.matmul(ps_sq[0:1, :], s.ones128[:, :], s.sq[p][:, :],
                     start=(mc == 0), stop=(mc == nch - 1)).then_inc(mm.h, 1)
        ctx.t(sqmm)
        s._sq_war[s._sq_i - 1] = mm.inc()
    mm_sq = mm.n
    R = s.rowbuf
    mu, var, rstd, bsh = R[0:1, 0:512], R[0:1, 512:1024], R[0:1, 1024:1536], R[0:1, 1536:2048]
    def rowv(e):
        e.wait_ge(mm.h, mm_sq)
        s.vself.edge(e, e.tensor_scalar_mul(mu, ps_sum[0:1, :], 1.0 / dim))
        s.vself.edge(e, e.tensor_mul(var, mu, mu))
        e.scalar_tensor_tensor(var, ps_sq[0:1, :], 1.0 / dim, var,
                               ALU.mult, ALU.subtract).then_inc(ev.h, 1)
    ctx.v(rowv)
    stats_free = ev.inc()
    s.ps.release(i_sum, ev, stats_free)
    s.ps.release(i_sq, ev, stats_free)
    def rows_(e):
        e.wait_ge(ev.h, stats_free)
        e.activation(rstd, var, AF.Sqrt, bias=s.eps_ln[0:1, 0:1]).then_inc(sa.h, 1)
    ctx.s(rows_)
    sav = sa.inc()
    def rowv3(e, sav=sav):
        e.wait_ge(sa.h, sav)
        s.vself.edge(e, e.reciprocal(rstd, rstd))
        e.scalar_tensor_tensor(bsh, mu, -1.0, rstd, ALU.mult, ALU.mult).then_inc(ev.h, 1)
    ctx.v(rowv3)
    rows_done = ev.inc()
    i_a, ps_a, w_a = s.ps.acquire()
    i_b, ps_b, w_b = s.ps.acquire()
    def bcast(e, w_a=w_a, w_b=w_b):
        e.wait_ge(ev.h, rows_done)
        pwait(e, w_a)
        pwait(e, w_b)
        e.matmul(ps_a[:, :], s.onesrow[:, :], rstd, start=True, stop=True)
        e.matmul(ps_b[:, :], s.onesrow[:, :], bsh, start=True, stop=True).then_inc(mm.h, 1)
    ctx.t(bcast)
    mm_ab = mm.inc()
    last_ev = None
    for mc in range(nch):
        p = mc % 2
        t1 = s.lnt[p]
        def app(e, mc=mc, t1=t1):
            e.wait_ge(mm.h, mm_ab)
            s.vself.edge(e, e.tensor_mul(t1[:, :], src_chunk(mc), ps_a[:, :]))
            s.vself.edge(e, e.tensor_add(t1[:, :], t1[:, :], ps_b[:, :]))
            dst = dst_chunk(mc) if dst_chunk is not None else bf_chunk(mc)
            if gamma is not None:
                ins = e.tensor_scalar(dst, t1[:, :], gamma[:, mc:mc + 1],
                                      beta[:, mc:mc + 1], ALU.mult, ALU.add)
            else:
                ins = e.tensor_copy(dst, t1[:, :])
            if bf_chunk is not None and dst_chunk is not None:
                s.vself.edge(e, ins)
                ins = e.tensor_copy(bf_chunk(mc), dst_chunk(mc))
            ins.then_inc(ev.h, 1)
        ctx.v(app)
        last_ev = ev.inc()
    s.ps.release(i_a, ev, last_ev)
    s.ps.release(i_b, ev, last_ev)
    return last_ev




def build_A2():
    nc = bass.Bass("TRN2", target_bir_lowering=False)
    x = nc.dram_tensor("x", [D, NL], F32, kind="ExternalInput").ap()
    qf = nc.dram_tensor("qf", [128, 32 * NL], BF16, kind="ExternalInput").ap()
    kvx_a = nc.dram_tensor("kvx_a", [128, H * 130], F32, kind="ExternalInput").ap()
    kvx_b = nc.dram_tensor("kvx_b", [128, H * 130], F32, kind="ExternalInput").ap()
    w_outT = nc.dram_tensor("w_outT", [D, D], F32, kind="ExternalInput").ap()
    b_out = nc.dram_tensor("b_out", [128, 8], F32, kind="ExternalInput").ap()
    gamma = nc.dram_tensor("gamma", [128, 8], F32, kind="ExternalInput").ap()
    beta = nc.dram_tensor("beta", [128, 8], F32, kind="ExternalInput").ap()
    w1T = nc.dram_tensor("w1T", [D, 4 * D], F32, kind="ExternalInput").ap()
    b1 = nc.dram_tensor("b1", [128, 32], F32, kind="ExternalInput").ap()
    w2T = nc.dram_tensor("w2T", [4 * D, D], F32, kind="ExternalInput").ap()
    b2 = nc.dram_tensor("b2", [128, 8], F32, kind="ExternalInput").ap()
    xa_o = nc.dram_tensor("xa", [D, NL], F32, kind="ExternalOutput").ap()

    es = contextlib.ExitStack()
    with es:
        ctx = Ctx(nc, es)
        s = S(ctx, n512=8)
        mm, ev, va, sa, ld, st = s.mm, s.ev, s.va, s.sa, s.ld, s.st

        x0 = [ctx.sbuf(f"x0_{i}", [128, 8 * 512], F32) for i in range(2)]
        rs = [ctx.sbuf(f"rs_{i}", [128, 8 * 512], F32) for i in range(2)]
        attn_bf = ctx.sbuf("attn_bf", [128, 8 * 512], BF16)
        x1_bf = ctx.sbuf("x1_bf", [128, 8 * 512], BF16)
        h_bf = ctx.sbuf("h_bf", [128, 32 * 512], BF16)
        wout = ctx.sbuf("wout", [128, 8 * 1024], BF16)
        w1b = [ctx.sbuf(f"w1b_{i}", [128, 8 * 128], BF16) for i in range(2)]
        w2b = [ctx.sbuf(f"w2b_{i}", [128, 32 * 128], BF16) for i in range(2)]
        qft = [ctx.sbuf(f"qft_{i}", [128, 512], BF16) for i in range(8)]
        kvt_a = ctx.sbuf("kvt_a", [128, H * 130], F32)
        kvt_b = ctx.sbuf("kvt_b", [128, H * 130], F32)
        kv_bf = ctx.sbuf("kv_bf", [128, H * 130], BF16)
        bo_fm = ctx.sbuf("bo_fm", [128, 8], F32)
        g_fm = ctx.sbuf("g_fm", [128, 8], F32)
        be_fm = ctx.sbuf("be_fm", [128, 8], F32)
        b2_fm = ctx.sbuf("b2_fm", [128, 8], F32)
        b1_fm = ctx.sbuf("b1_fm", [128, 32], F32)
        onesrow_bf = ctx.sbuf("onesrow_bf", [1, 128], BF16)
        zr_bf = ctx.sbuf("zr_bf", [1, 2 * 512], BF16)
        zb_sb = ctx.sbuf("zb_sb", [128, 512], F32)

        x0ld = [ctx.sem(f"x0ld{i}") for i in range(2)]
        qld = [ctx.sem(f"qld{i}") for i in range(8)]
        w1ld = [ctx.sem(f"w1ld{i}") for i in range(2)]
        w2ld = [ctx.sem(f"w2ld{i}") for i in range(2)]
        xst = [ctx.sem(f"xst{i}") for i in range(2)]

        ctx.v(lambda e: e.memset(onesrow_bf[:, :], 1.0))

        dma_in(ctx, wout[:, :], w_outT.rearrange("(k p) m -> p k m", p=128), ld)
        dma_in(ctx, kvt_a[:, :], kvx_a, ld)
        dma_in(ctx, kvt_b[:, :], kvx_b, ld)
        dma_in(ctx, bo_fm[:, :], b_out, ld)
        dma_in(ctx, g_fm[:, :], gamma, ld)
        dma_in(ctx, be_fm[:, :], beta, ld)
        dma_in(ctx, b2_fm[:, :], b2, ld)
        dma_in(ctx, b1_fm[:, :], b1, ld)
        ld_all = ld.n
        def kvadd(e):
            e.wait_ge(ld.h, ld_all)
            e.tensor_add(kv_bf[:, :], kvt_a[:, :], kvt_b[:, :]).then_inc(ev.h, 1)
        ctx.v(kvadd)
        kv_ready = ev.inc()

        qf_i = 0
        qslot_war = [0] * 8
        qld_cnt = [0] * 8
        x0_war = [0, 0]
        store_war = [0, 0]
        w1_war = [0, 0]
        w2_war = [0, 0]
        w1_cnt = [0, 0]
        w2_cnt = [0, 0]

        for nt in range(4):
            ntp = nt % 2
            def x0f(e, nt=nt, ntp=ntp):
                e.dma_start(out=x0[ntp][:, :],
                            in_=x.rearrange("(c p) n -> p c n", p=128)[:, :, nt * 512:(nt + 1) * 512]
                            ).then_inc(x0ld[ntp].h, 16)
            if nt >= 2:
                war_val = x0_war[ntp]
                def x0g(e, war_val=war_val, f=x0f):
                    e.wait_ge(ev.h, war_val)
                    f(e)
                ctx.g(x0g)
            else:
                ctx.g(x0f)
            x0_rdy = x0ld[ntp].inc(16)

            for hp in range(8):
                h0, h1 = 2 * hp, 2 * hp + 1
                tiles = {}
                for (hh, fc) in [(h0, 0), (h0, 1), (h1, 0), (h1, 1)]:
                    sl = qf_i % 8
                    qf_i += 1
                    war = qslot_war[sl]
                    def qlf(e, hh=hh, fc=fc, sl=sl, war=war, nt=nt):
                        if war:
                            e.wait_ge(mm.h, war)
                        e.dma_start(out=qft[sl][:, :],
                                    in_=qf[:, (hh * 2 + fc) * NL + nt * 512:(hh * 2 + fc) * NL + (nt + 1) * 512]
                                    ).then_inc(qld[sl].h, 16)
                    ctx.g(qlf)
                    qld_cnt[sl] += 16
                    tiles[(hh, fc)] = (sl, qld_cnt[sl])
                zrow_done = []
                for j, hh in enumerate((h0, h1)):
                    iz, psz, wfree = s.ps.acquire()
                    def zmm(e, hh=hh, psz=psz, wfree=wfree, tiles=tiles):
                        e.wait_ge(ev.h, kv_ready)
                        pwait(e, wfree)
                        for fc in range(2):
                            sl, cnt = tiles[(hh, fc)]
                            e.wait_ge(qld[sl].h, cnt)
                            ins = e.matmul(psz[0:1, :], kv_bf[:, hh * 130 + fc * 65 + 64:hh * 130 + fc * 65 + 65],
                                           qft[sl][:, :], start=(fc == 0), stop=(fc == 1))
                        ins.then_inc(mm.h, 1)
                    ctx.t(zmm)
                    mmv = mm.inc()
                    def zrv(e, j=j, psz=psz, mmv=mmv):
                        e.wait_ge(mm.h, mmv)
                        s.vself.edge(e, e.tensor_scalar_add(s.rowbuf[0:1, 0:512], psz[0:1, :], 1e-8))
                        with nc.allow_low_precision(reason="z normalizer is a bf16 matmul operand"):
                            e.reciprocal(zr_bf[0:1, j * 512:(j + 1) * 512], s.rowbuf[0:1, 0:512]).then_inc(ev.h, 1)
                    ctx.v(zrv)
                    s.ps.release(iz, ev, ev.inc())
                zr_done = ev.n
                izb, pzb, wfree = s.ps.acquire()
                def zbm(e, pzb=pzb, wfree=wfree, zr_done=zr_done):
                    e.wait_ge(ev.h, zr_done)
                    pwait(e, wfree)
                    e.matmul(pzb[0:64, :], onesrow_bf[0:1, 0:64], zr_bf[0:1, 0:512], start=True, stop=True)
                    e.matmul(pzb[64:128, :], onesrow_bf[0:1, 0:64], zr_bf[0:1, 512:1024],
                             start=True, stop=True).then_inc(mm.h, 1)
                ctx.t(zbm)
                mm_zb = mm.inc()
                def zbe(e, pzb=pzb, mm_zb=mm_zb):
                    e.wait_ge(mm.h, mm_zb)
                    e.tensor_copy(zb_sb[:, :], pzb[:, :]).then_inc(ev.h, 1)
                ctx.v(zbe)
                s.ps.release(izb, ev, ev.inc())
                zb_done = ev.n
                ia, psa, wfree = s.ps.acquire()
                def amm(e, h0=h0, h1=h1, psa=psa, wfree=wfree, tiles=tiles):
                    e.wait_ge(ev.h, kv_ready)
                    pwait(e, wfree)
                    for j, hh in enumerate((h0, h1)):
                        for fc in range(2):
                            sl, cnt = tiles[(hh, fc)]
                            e.wait_ge(qld[sl].h, cnt)
                            ins = e.matmul(psa[j * 64:(j + 1) * 64, :],
                                           kv_bf[:, hh * 130 + fc * 65:hh * 130 + fc * 65 + 64],
                                           qft[sl][:, :], start=(fc == 0), stop=(fc == 1))
                    ins.then_inc(mm.h, 1)
                ctx.t(amm)
                mm_a = mm.inc()
                for (hh, fc) in tiles:
                    qslot_war[tiles[(hh, fc)][0]] = mm_a
                def aev(e, hp=hp, psa=psa, mm_a=mm_a, zb_done=zb_done):
                    e.wait_ge(mm.h, mm_a)
                    e.wait_ge(ev.h, zb_done)
                    e.tensor_mul(attn_bf[:, hp * 512:(hp + 1) * 512], psa[:, :], zb_sb[:, :]).then_inc(ev.h, 1)
                ctx.v(aev)
                s.ps.release(ia, ev, ev.inc())
            attn_done = ev.n

            for mc in range(8):
                io, pso, wfree = s.ps.acquire()
                def omm(e, mc=mc, pso=pso, wfree=wfree, attn_done=attn_done):
                    e.wait_ge(ld.h, ld_all)
                    e.wait_ge(ev.h, attn_done)
                    pwait(e, wfree)
                    for k in range(8):
                        ins = e.matmul(pso[:, :], wout[:, k * 1024 + mc * 128:k * 1024 + (mc + 1) * 128],
                                       attn_bf[:, k * 512:(k + 1) * 512],
                                       start=(k == 0), stop=(k == 7))
                    ins.then_inc(mm.h, 1)
                ctx.t(omm)
                mmv = mm.inc()
                def oev(e, mc=mc, pso=pso, mmv=mmv, ntp=ntp, x0_rdy=x0_rdy, swar=store_war[ntp]):
                    e.wait_ge(mm.h, mmv)
                    e.wait_ge(x0ld[ntp].h, x0_rdy)
                    if swar:
                        e.wait_ge(xst[ntp].h, swar)
                    e.scalar_tensor_tensor(rs[ntp][:, mc * 512:(mc + 1) * 512], pso[:, :],
                                           bo_fm[:, mc:mc + 1], x0[ntp][:, mc * 512:(mc + 1) * 512],
                                           ALU.add, ALU.add).then_inc(ev.h, 1)
                ctx.v(oev)
                s.ps.release(io, ev, ev.inc())
            r1_done = ev.n

            ln1_ev = plan_ln_tile(ctx, s,
                                  (lambda mc, ntp=ntp: rs[ntp][:, mc * 512:(mc + 1) * 512]),
                                  (lambda mc, ntp=ntp: rs[ntp][:, mc * 512:(mc + 1) * 512]),
                                  gamma=g_fm, beta=be_fm,
                                  bf_chunk=(lambda mc: x1_bf[:, mc * 512:(mc + 1) * 512]),
                                  wait_ev=r1_done)

            for m in range(32):
                p = m % 2
                gi = nt * 32 + m
                war = w1_war[p]
                def w1l(e, m=m, p=p, war=war):
                    if war:
                        e.wait_ge(mm.h, war)
                    e.dma_start(out=w1b[p][:, :],
                                in_=w1T.rearrange("(k p) m -> p k m", p=128)[:, :, m * 128:(m + 1) * 128]
                                ).then_inc(w1ld[p].h, 16)
                ctx.g(w1l)
                w1_cnt[p] += 16
                i1, ps1, wfree = s.ps.acquire()
                def w1m(e, m=m, p=p, ps1=ps1, wfree=wfree, cnt=w1_cnt[p], ln1_ev=ln1_ev):
                    e.wait_ge(w1ld[p].h, cnt)
                    e.wait_ge(ev.h, ln1_ev)
                    pwait(e, wfree)
                    for k in range(8):
                        ins = e.matmul(ps1[:, :], w1b[p][:, k * 128:(k + 1) * 128],
                                       x1_bf[:, k * 512:(k + 1) * 512],
                                       start=(k == 0), stop=(k == 7))
                    ins.then_inc(mm.h, 1)
                ctx.t(w1m)
                w1_war[p] = mm.inc()
                mmv = w1_war[p]
                def gel(e, m=m, ps1=ps1, mmv=mmv):
                    e.wait_ge(mm.h, mmv)
                    e.activation(h_bf[:, m * 512:(m + 1) * 512], ps1[:, :], AF.Gelu,
                                 bias=b1_fm[:, m:m + 1]).then_inc(sa.h, 1)
                ctx.s(gel)
                s.ps.release(i1, sa, sa.inc())
            gelu_done = sa.n

            for m2 in range(8):
                p = m2 % 2
                war = w2_war[p]
                def w2l(e, m2=m2, p=p, war=war):
                    if war:
                        e.wait_ge(mm.h, war)
                    e.dma_start(out=w2b[p][:, :],
                                in_=w2T.rearrange("(k p) m -> p k m", p=128)[:, :, m2 * 128:(m2 + 1) * 128]
                                ).then_inc(w2ld[p].h, 16)
                ctx.g(w2l)
                w2_cnt[p] += 16
                i2, ps2, wfree = s.ps.acquire()
                def w2m(e, m2=m2, p=p, ps2=ps2, wfree=wfree, cnt=w2_cnt[p], gelu_done=gelu_done):
                    e.wait_ge(w2ld[p].h, cnt)
                    e.wait_ge(sa.h, gelu_done)
                    pwait(e, wfree)
                    for k in range(32):
                        ins = e.matmul(ps2[:, :], w2b[p][:, k * 128:(k + 1) * 128],
                                       h_bf[:, k * 512:(k + 1) * 512],
                                       start=(k == 0), stop=(k == 31))
                    ins.then_inc(mm.h, 1)
                ctx.t(w2m)
                w2_war[p] = mm.inc()
                mmv = w2_war[p]
                def x2e(e, m2=m2, ps2=ps2, mmv=mmv, ntp=ntp):
                    e.wait_ge(mm.h, mmv)
                    e.scalar_tensor_tensor(rs[ntp][:, m2 * 512:(m2 + 1) * 512], ps2[:, :],
                                           b2_fm[:, m2:m2 + 1], rs[ntp][:, m2 * 512:(m2 + 1) * 512],
                                           ALU.add, ALU.add).then_inc(ev.h, 1)
                ctx.v(x2e)
                s.ps.release(i2, ev, ev.inc())
            x2_done = ev.n

            ln2_ev = plan_ln_tile(ctx, s,
                                  (lambda mc, ntp=ntp: rs[ntp][:, mc * 512:(mc + 1) * 512]),
                                  (lambda mc, ntp=ntp: rs[ntp][:, mc * 512:(mc + 1) * 512]),
                                  gamma=g_fm, beta=be_fm, wait_ev=x2_done)

            for mc in range(8):
                def padd(e, mc=mc, ntp=ntp, ln2_ev=ln2_ev):
                    e.wait_ge(ev.h, ln2_ev)
                    e.tensor_add(rs[ntp][:, mc * 512:(mc + 1) * 512],
                                 rs[ntp][:, mc * 512:(mc + 1) * 512],
                                 x0[ntp][:, mc * 512:(mc + 1) * 512]).then_inc(ev.h, 1)
                ctx.v(padd)
                ev.inc()
            x0_war[ntp] = ev.n
            pre_done = ev.n
            out_ev = plan_ln_tile(ctx, s,
                                  (lambda mc, ntp=ntp: rs[ntp][:, mc * 512:(mc + 1) * 512]),
                                  (lambda mc, ntp=ntp: rs[ntp][:, mc * 512:(mc + 1) * 512]),
                                  wait_ev=pre_done)
            def stf(e, nt=nt, ntp=ntp, out_ev=out_ev):
                e.wait_ge(ev.h, out_ev)
                e.dma_start(out=xa_o.rearrange("(c p) n -> p c n", p=128)[:, :, nt * 512:(nt + 1) * 512],
                            in_=rs[ntp][:, :]).then_inc(xst[ntp].h, 16)
            ctx.y(stf)
            store_war[ntp] = xst[ntp].inc(16)

        f0, f1 = xst[0].n, xst[1].n
        def finw(e):
            e.wait_ge(xst[0].h, f0)
            e.wait_ge(xst[1].h, f1)
        ctx.y(finw)
        ctx.finish()
    return nc




def build_B1():
    nc = bass.Bass("TRN2", target_bir_lowering=False)
    xa = nc.dram_tensor("xa", [D, NL], F32, kind="ExternalInput").ap()
    ln1w = nc.dram_tensor("ln1w", [128, 8], F32, kind="ExternalInput").ap()
    wdefs = {
        "kvd": (nc.dram_tensor("kvdT", [D, 256], F32, kind="ExternalInput").ap(), 8, 256),
        "qd": (nc.dram_tensor("qdT", [D, 256], F32, kind="ExternalInput").ap(), 8, 256),
        "kun": (nc.dram_tensor("kunT", [256, 512], F32, kind="ExternalInput").ap(), 2, 512),
        "qun": (nc.dram_tensor("qunT", [256, 512], F32, kind="ExternalInput").ap(), 2, 512),
        "vu": (nc.dram_tensor("vuT", [256, D], F32, kind="ExternalInput").ap(), 2, 1024),
        "rqa": (nc.dram_tensor("rqaT", [256, 256], F32, kind="ExternalInput").ap(), 2, 256),
        "rqb": (nc.dram_tensor("rqbT", [256, 256], F32, kind="ExternalInput").ap(), 2, 256),
        "rka": (nc.dram_tensor("rkaT", [D, 256], F32, kind="ExternalInput").ap(), 8, 256),
        "rkb": (nc.dram_tensor("rkbT", [D, 256], F32, kind="ExternalInput").ap(), 8, 256),
    }
    cosf = nc.dram_tensor("cosf", [128, NL], F32, kind="ExternalInput").ap()
    sinf = nc.dram_tensor("sinf", [128, NL], F32, kind="ExternalInput").ap()
    K_o = nc.dram_tensor("K", [D, NL], BF16, kind="ExternalOutput").ap()
    Q_o = nc.dram_tensor("Q", [D, NL], BF16, kind="ExternalOutput").ap()
    V_o = nc.dram_tensor("V", [NL, D], BF16, kind="ExternalOutput").ap()

    es = contextlib.ExitStack()
    with es:
        ctx = Ctx(nc, es)
        s = S(ctx, n512=8)
        mm, ev, va, sa, ld, st = s.mm, s.ev, s.va, s.sa, s.ld, s.st

        xat = [ctx.sbuf(f"xat{i}", [128, 8 * 512], F32) for i in range(2)]
        xl_bf = ctx.sbuf("xl_bf", [128, NCH * NL], BF16)
        lat_bf = ctx.sbuf("lat_bf", [128, 4 * NL], BF16)
        nopet = [ctx.sbuf(f"nopet{i}", [128, NL], BF16) for i in range(2)]
        vtm = [ctx.sbuf(f"vtm{i}", [128, 1024], BF16) for i in range(2)]
        ra = ctx.sbuf("ra", [128, 2 * 1024], F32)
        rb = ctx.sbuf("rb", [128, 2 * 1024], F32)
        lo = ctx.sbuf("lo", [128, 2 * 1024], BF16)
        hi = ctx.sbuf("hi", [128, 2 * 1024], BF16)
        tA = ctx.sbuf("tA", [128, 1024], F32)
        tB = ctx.sbuf("tB", [128, 1024], F32)
        tC = ctx.sbuf("tC", [128, 1024], F32)
        tD = ctx.sbuf("tD", [128, 1024], F32)
        cs = ctx.sbuf("cs", [128, NL], F32)
        sn = ctx.sbuf("sn", [128, NL], F32)
        ln1_fm = ctx.sbuf("ln1_fm", [128, 8], F32)
        zeros8 = ctx.sbuf("zeros8", [128, 8], F32)
        x0ld = [ctx.sem(f"x0ld{i}") for i in range(2)]
        nst = [ctx.sem(f"nst{i}") for i in range(2)]
        vsm = [ctx.sem(f"vsm{i}") for i in range(2)]
        rst = ctx.sem("rst")

        ctx.v(lambda e: e.memset(zeros8[:, :], 0.0))

        off = {}
        cur = 0
        for nm, (wT, kch, M) in wdefs.items():
            off[nm] = (cur, M)
            cur += kch * M
        wbuf = ctx.sbuf("wbuf", [128, cur], BF16)
        for nm, (wT, kch, M) in wdefs.items():
            dma_in(ctx, wbuf[:, off[nm][0]:off[nm][0] + kch * M],
                   wT.rearrange("(k p) m -> p k m", p=128), ld)
        dma_in(ctx, cs[:, :], cosf, ld)
        dma_in(ctx, sn[:, :], sinf, ld)
        dma_in(ctx, ln1_fm[:, :], ln1w, ld)
        ld_all = ld.n

        def wt(nm, k, m):
            base, M = off[nm]
            c = base + k * M + m * 128
            return wbuf[:, c:c + 128]

        ln_done = [0, 0]
        for nt in range(4):
            ntp = nt % 2
            war = ln_done[ntp]
            def xaf(e, nt=nt, ntp=ntp, war=war):
                if war:
                    e.wait_ge(ev.h, war)
                e.dma_start(out=xat[ntp][:, :],
                            in_=xa.rearrange("(c p) n -> p c n", p=128)[:, :, nt * 512:(nt + 1) * 512]
                            ).then_inc(x0ld[ntp].h, 16)
            ctx.g(xaf)
            x0_rdy = x0ld[ntp].inc(16)
            def gate(e, x0_rdy=x0_rdy, ntp=ntp):
                e.wait_ge(x0ld[ntp].h, x0_rdy)
                e.wait_ge(ld.h, ld_all)
            ctx.t(gate)
            ctx.s(gate)
            lev = plan_ln_tile(ctx, s,
                               (lambda mc, ntp=ntp: xat[ntp][:, mc * 512:(mc + 1) * 512]),
                               None,
                               gamma=ln1_fm, beta=zeros8,
                               bf_chunk=(lambda mc, nt=nt: xl_bf[:, mc * NL + nt * 512:mc * NL + nt * 512 + 512]))
            ln_done[ntp] = lev
        xl_done = max(ln_done)

        for ci, nm in enumerate(["kvd", "kvd", "qd", "qd"]):
            mch = ci % 2
            for nt in range(4):
                i, ps, wfree = s.ps.acquire()
                def mmf(e, nm=nm, mch=mch, nt=nt, ps=ps, wfree=wfree):
                    e.wait_ge(ev.h, xl_done)
                    pwait(e, wfree)
                    for k in range(8):
                        ins = e.matmul(ps[:, :], wt(nm, k, mch),
                                       xl_bf[:, k * NL + nt * 512:k * NL + (nt + 1) * 512],
                                       start=(k == 0), stop=(k == 7))
                    ins.then_inc(mm.h, 1)
                ctx.t(mmf)
                mmv = mm.inc()
                def evf(e, ci=ci, nt=nt, ps=ps, mmv=mmv):
                    e.wait_ge(mm.h, mmv)
                    e.tensor_copy(lat_bf[:, ci * NL + nt * 512:ci * NL + (nt + 1) * 512],
                                  ps[:, :]).then_inc(ev.h, 1)
                ctx.v(evf)
                s.ps.release(i, ev, ev.inc())
        lat_done = ev.n

        nst_war = [0, 0]
        for half, (nm, lat0, O) in enumerate([("kun", 0, K_o), ("qun", 2, Q_o)]):
            for mch in range(4):
                cp = (half * 4 + mch) % 2
                war = nst_war[cp]
                for nt in range(4):
                    i, ps, wfree = s.ps.acquire()
                    def mmf(e, nm=nm, lat0=lat0, mch=mch, nt=nt, ps=ps, wfree=wfree):
                        e.wait_ge(ev.h, lat_done)
                        pwait(e, wfree)
                        for k in range(2):
                            ins = e.matmul(ps[:, :], wt(nm, k, mch),
                                           lat_bf[:, (lat0 + k) * NL + nt * 512:(lat0 + k) * NL + (nt + 1) * 512],
                                           start=(k == 0), stop=(k == 1))
                        ins.then_inc(mm.h, 1)
                    ctx.t(mmf)
                    mmv = mm.inc()
                    def evf(e, cp=cp, nt=nt, ps=ps, mmv=mmv, war=war):
                        e.wait_ge(mm.h, mmv)
                        if war:
                            e.wait_ge(nst[cp].h, war)
                        e.tensor_copy(nopet[cp][:, nt * 512:(nt + 1) * 512],
                                      ps[:, :]).then_inc(ev.h, 1)
                    ctx.v(evf)
                    s.ps.release(i, ev, ev.inc())
                ch_done = ev.n
                def nope_st(e, mch=mch, cp=cp, ch_done=ch_done, O=O):
                    e.wait_ge(ev.h, ch_done)
                    for j in range(4):
                        hh = mch * 4 + j
                        e.dma_start(out=O[hh * 64 + 32:hh * 64 + 64, :],
                                    in_=nopet[cp][j * 32:(j + 1) * 32, :]).then_inc(nst[cp].h, 16)
                ctx.y(nope_st)
                nst_war[cp] = nst[cp].inc(16 * 4)

        vst_war = [0, 0]
        for ncid in range(NT):
            p = ncid % 2
            i, ps, wfree = s.ps.acquire()
            i2, ps2, wfree2 = s.ps.acquire()
            def mmf(e, ncid=ncid, ps=ps, ps2=ps2, wfree=wfree, wfree2=wfree2):
                e.wait_ge(ev.h, lat_done)
                pwait(e, wfree)
                pwait(e, wfree2)
                for half2, pp in ((0, ps), (1, ps2)):
                    for k in range(2):
                        ins = e.matmul(pp[:, :],
                                       lat_bf[:, k * NL + ncid * 128:k * NL + (ncid + 1) * 128],
                                       wbuf[:, off["vu"][0] + k * 1024 + half2 * 512:
                                            off["vu"][0] + k * 1024 + (half2 + 1) * 512],
                                       start=(k == 0), stop=(k == 1))
                ins.then_inc(mm.h, 1)
            ctx.t(mmf)
            mmv = mm.inc()
            war = vst_war[p]
            def evf(e, p=p, ps=ps, ps2=ps2, mmv=mmv, war=war):
                e.wait_ge(mm.h, mmv)
                if war:
                    e.wait_ge(vsm[p].h, war)
                e.tensor_copy(vtm[p][:, 0:512], ps[:, :])
                e.tensor_copy(vtm[p][:, 512:1024], ps2[:, :]).then_inc(ev.h, 1)
            ctx.v(evf)
            evv = ev.inc()
            s.ps.release(i, ev, evv)
            s.ps.release(i2, ev, evv)
            def vst(e, ncid=ncid, p=p, evv=evv):
                e.wait_ge(ev.h, evv)
                e.dma_start(out=V_o[ncid * 128:(ncid + 1) * 128, :], in_=vtm[p][:, :]).then_inc(vsm[p].h, 16)
            ctx.y(vst)
            vst_war[p] = vsm[p].inc(16)

        rope_st_war = 0
        for which, (nma, nmb, kch, lat0, O) in enumerate([
                ("rqa", "rqb", 2, 2, Q_o), ("rka", "rkb", 8, None, K_o)]):
            for half in range(2):
                for nm, dstbuf in ((nma, ra), (nmb, rb)):
                    for mch in range(2):
                        for sub in range(2):
                            i, ps, wfree = s.ps.acquire()
                            def mmf(e, nm=nm, mch=mch, half=half, sub=sub, ps=ps,
                                    wfree=wfree, kch=kch, lat0=lat0):
                                e.wait_ge(ev.h, xl_done if lat0 is None else lat_done)
                                pwait(e, wfree)
                                for k in range(kch):
                                    col = (lat0 + k if lat0 is not None else k) * NL + half * 1024 + sub * 512
                                    rhs = (lat_bf if lat0 is not None else xl_bf)[:, col:col + 512]
                                    ins = e.matmul(ps[:, :], wt(nm, k, mch), rhs,
                                                   start=(k == 0), stop=(k == kch - 1))
                                ins.then_inc(mm.h, 1)
                            ctx.t(mmf)
                            mmv = mm.inc()
                            def evf(e, dstbuf=dstbuf, mch=mch, sub=sub, ps=ps, mmv=mmv,
                                    war=rope_st_war):
                                e.wait_ge(mm.h, mmv)
                                if war:
                                    e.wait_ge(rst.h, war)
                                e.tensor_copy(dstbuf[:, mch * 1024 + sub * 512:mch * 1024 + (sub + 1) * 512],
                                              ps[:, :]).then_inc(ev.h, 1)
                            ctx.v(evf)
                            s.ps.release(i, ev, ev.inc())
                ab_done = ev.n
                for c in range(2):
                    def rop(e, c=c, half=half, ab_done=ab_done):
                        e.wait_ge(ev.h, ab_done)
                        csl = cs[:, half * 1024:(half + 1) * 1024]
                        snl = sn[:, half * 1024:(half + 1) * 1024]
                        rau = ra[:, c * 1024:(c + 1) * 1024]
                        rbu = rb[:, c * 1024:(c + 1) * 1024]
                        e.tensor_mul(tA[:, :], rau, csl)
                        e.tensor_mul(tB[:, :], rbu, snl)
                        e.tensor_mul(tC[:, :], rbu, csl)
                        s.vself.edge(e, e.tensor_mul(tD[:, :], rau, snl))
                        e.tensor_sub(lo[:, c * 1024:(c + 1) * 1024], tA[:, :], tB[:, :])
                        e.tensor_add(hi[:, c * 1024:(c + 1) * 1024], tC[:, :],
                                     tD[:, :]).then_inc(ev.h, 1)
                    ctx.v(rop)
                    ev.inc()
                rope_done = ev.n
                def rstf(e, half=half, rope_done=rope_done, O=O):
                    e.wait_ge(ev.h, rope_done)
                    for c in range(2):
                        for j in range(8):
                            hh = c * 8 + j
                            e.dma_start(out=O[hh * 64 + 0:hh * 64 + 16, half * 1024:(half + 1) * 1024],
                                        in_=lo[j * 16:(j + 1) * 16, c * 1024:(c + 1) * 1024]).then_inc(rst.h, 16)
                            e.dma_start(out=O[hh * 64 + 16:hh * 64 + 32, half * 1024:(half + 1) * 1024],
                                        in_=hi[j * 16:(j + 1) * 16, c * 1024:(c + 1) * 1024]).then_inc(rst.h, 16)
                ctx.y(rstf)
                rope_st_war = rst.inc(16 * 32)

        fins = [(nst[0], nst[0].n), (nst[1], nst[1].n), (vsm[0], vsm[0].n),
                (vsm[1], vsm[1].n), (rst, rst.n)]
        def finw(e):
            for sm, v_ in fins:
                e.wait_ge(sm.h, v_)
        ctx.y(finw)
        ctx.finish()
    return nc



def build_B2(odd, debug=False):
    KT = 32 if odd else 16
    off_q = 2048 if odd else 0
    nc = bass.Bass("TRN2", target_bir_lowering=False)
    xa = nc.dram_tensor("xa", [D, NL], F32, kind="ExternalInput").ap()
    Kd = nc.dram_tensor("K", [D, KT * 128], BF16, kind="ExternalInput").ap()
    Qd = nc.dram_tensor("Q", [D, NL], BF16, kind="ExternalInput").ap()
    Vd = nc.dram_tensor("V", [KT * 128, D], BF16, kind="ExternalInput").ap()
    stair = nc.dram_tensor("stair", [128, 2176], BF16, kind="ExternalInput").ap()
    owT = nc.dram_tensor("owT", [D, D], F32, kind="ExternalInput").ap()
    t1_o = nc.dram_tensor("t1", [D, NL], F32, kind="ExternalOutput").ap()
    if debug:
        yfm_o = nc.dram_tensor("Yfm_dbg", [128, NCH * NL], BF16, kind="ExternalOutput").ap()

    es = contextlib.ExitStack()
    with es:
        ctx = Ctx(nc, es)
        s = S(ctx, n512=4)
        mm, ev, va, sa, ld, st = s.mm, s.ev, s.va, s.sa, s.ld, s.st

        Kb = [ctx.sbuf(f"Kb{i}", [128, KT * 128], BF16) for i in range(2)]
        Qb = [ctx.sbuf(f"Qb{i}", [128, NL], BF16) for i in range(2)]
        aug = [ctx.sbuf(f"aug{i}", [128, KT * 130], BF16) for i in range(2)]
        Pt = [[ctx.sbuf(f"Pt{h}_{i}", [128, 1024], BF16) for i in range(2)] for h in range(2)]
        stb = ctx.sbuf("stb", [128, 2176], BF16)
        zrow = ctx.sbuf("zrow", [1, 4 * 512], F32)
        zb_sb = ctx.sbuf("zb_sb", [128, 1024], F32)
        Yfm = ctx.sbuf("Yfm", [128, NCH * NL], BF16)
        owb = ctx.sbuf("owb", [128, NCH * D], BF16)
        xat = [ctx.sbuf(f"xat{i}", [128, 8 * 512], F32) for i in range(2)]
        rs = [ctx.sbuf(f"rs{i}", [128, 8 * 512], F32) for i in range(2)]
        psA = [ctx.psum(f"psA{h}", [128, 1024], F32) for h in range(2)]

        kld = [ctx.sem(f"kld{i}") for i in range(2)]
        qld = [ctx.sem(f"qld{i}") for i in range(2)]
        vld = [ctx.sem(f"vld{i}") for i in range(2)]
        x0ld = [ctx.sem(f"x0ld{i}") for i in range(2)]
        yst2 = [ctx.sem(f"yst2_{i}") for i in range(2)]
        tst2 = [ctx.sem(f"tst2_{i}") for i in range(2)]

        dma_in(ctx, stb[:, :], stair, ld)
        dma_in(ctx, owb[:, :], owT.rearrange("(k p) m -> p k m", p=128), ld)
        ld_all = ld.n
        def ones_init(e):
            e.memset(aug[0][:, :].rearrange("q (a j c) -> q a j c", j=2, c=65)[:, :, :, 64:65], 1.0)
            e.memset(aug[1][:, :].rearrange("q (a j c) -> q a j c", j=2, c=65)[:, :, :, 64:65],
                     1.0).then_inc(ev.h, 1)
        ctx.v(ones_init)
        ones_done = ev.inc()

        kt_hi = {0: (KT - 8), 1: KT}
        def need_mask(strip, kt):
            d = kt * 128 - strip * 1024 - off_q
            return -127 < d <= 1023
        def stair_col(strip, kt):
            d = kt * 128 - strip * 1024 - off_q
            return 1023 - d

        hp_war = [(0, 0)] * 2
        last_exp = {0: 0, 1: 0}
        last_pv = {}
        zb_done = {0: 0, 1: 0}
        for hp in range(8):
            p = hp % 2
            war_mm, war_ev = hp_war[p]
            def kqv_load(e, hp=hp, p=p, war_mm=war_mm, war_ev=war_ev):
                if war_mm:
                    e.wait_ge(mm.h, war_mm)
                if war_ev:
                    e.wait_ge(ev.h, war_ev)
                e.dma_start(out=Kb[p][:, :], in_=Kd[hp * 128:(hp + 1) * 128, :]).then_inc(kld[p].h, 16)
                e.dma_start(out=Qb[p][:, :], in_=Qd[hp * 128:(hp + 1) * 128, :]).then_inc(qld[p].h, 16)
                for j in range(2):
                    e.dma_start(out=aug[p][:, :].rearrange("q (a j c) -> q a j c", j=2, c=65)[:, :, j, 0:64],
                                in_=Vd[:, hp * 128 + j * 64:hp * 128 + (j + 1) * 64]
                                .rearrange("(a q) c -> q a c", q=128)
                                ).then_inc(vld[p].h, 16)
            ctx.g(kqv_load)
            k_rdy = kld[p].inc(16)
            q_rdy = qld[p].inc(16)
            v_rdy = vld[p].inc(32)
            aug_rdy = ones_done

            for strip in range(2):
                nk = kt_hi[strip]
                pvs = {}
                for h in range(2):
                    for half in range(2):
                        pvs[(h, half)] = s.ps.acquire()
                mask_ev = {}
                for kt in range(nk):
                    first, last = (kt == 0), (kt == nk - 1)
                    for h in range(2):
                        ktp = kt % 2
                        def smm(e, p=p, h=h, kt=kt, strip=strip, k_rdy=k_rdy, q_rdy=q_rdy,
                                first=first, esa=last_exp[h], zbd=zb_done[h],
                                frees=[pvs[(h, hf)][2] for hf in range(2)] if first else None):
                            e.wait_ge(kld[p].h, k_rdy)
                            e.wait_ge(qld[p].h, q_rdy)
                            if esa:
                                e.wait_ge(sa.h, esa)
                            if zbd:
                                e.wait_ge(ev.h, zbd)
                            if first:
                                for w in frees:
                                    pwait(e, w)
                            for half in range(2):
                                ins = e.matmul(psA[h][:, half * 512:(half + 1) * 512],
                                               Kb[p][h * 64:(h + 1) * 64, kt * 128:(kt + 1) * 128],
                                               Qb[p][h * 64:(h + 1) * 64,
                                                     strip * 1024 + half * 512:strip * 1024 + (half + 1) * 512],
                                               start=True, stop=True)
                            ins.then_inc(mm.h, 1)
                        ctx.t(smm)
                        smm_v = mm.inc()
                        def expf(e, h=h, ktp=ktp, smm_v=smm_v, pvw=last_pv.get((h, ktp))):
                            e.wait_ge(mm.h, smm_v)
                            if pvw:
                                e.wait_ge(mm.h, pvw)
                            e.activation(Pt[h][ktp][:, :], psA[h][:, :], AF.Exp,
                                         scale=0.125).then_inc(sa.h, 1)
                        ctx.s(expf)
                        sav = sa.inc()
                        last_exp[h] = sav
                        if need_mask(strip, kt):
                            c0 = stair_col(strip, kt)
                            def mskf(e, h=h, ktp=ktp, c0=c0, sav=sav):
                                e.wait_ge(sa.h, sav)
                                e.wait_ge(ld.h, ld_all)
                                e.tensor_mul(Pt[h][ktp][:, :], Pt[h][ktp][:, :],
                                             stb[:, c0:c0 + 1024]).then_inc(ev.h, 1)
                            ctx.v(mskf)
                            mask_ev[(kt, h)] = ev.inc()
                        def pvf(e, p=p, h=h, kt=kt, ktp=ktp, first=first, last=last,
                                sav=sav, mev=mask_ev.get((kt, h)), aug_rdy=aug_rdy,
                                v_rdy=v_rdy,
                                s0=pvs[(h, 0)][1], s1=pvs[(h, 1)][1]):
                            if mev:
                                e.wait_ge(ev.h, mev)
                            else:
                                e.wait_ge(sa.h, sav)
                            e.wait_ge(ev.h, aug_rdy)
                            e.wait_ge(vld[p].h, v_rdy)
                            for half, sl in ((0, s0), (1, s1)):
                                ins = e.matmul(sl[0:65, :],
                                               aug[p][:, (kt * 2 + h) * 65:(kt * 2 + h) * 65 + 65],
                                               Pt[h][ktp][:, half * 512:(half + 1) * 512],
                                               start=first, stop=last)
                            ins.then_inc(mm.h, 1)
                        ctx.t(pvf)
                        last_pv[(h, ktp)] = mm.inc()
                pv_all = mm.n
                def zrec(e, pv_all=pv_all, sls=[pvs[(h, hf)][1] for h in range(2) for hf in range(2)]):
                    e.wait_ge(mm.h, pv_all)
                    for i4, sl in enumerate(sls):
                        ins = e.reciprocal(zrow[0:1, i4 * 512:(i4 + 1) * 512], sl[64:65, :])
                    ins.then_inc(ev.h, 1)
                ctx.v(zrec)
                zr_done = ev.inc()
                def zbm(e, zr_done=zr_done, e0=last_exp[0], e1=last_exp[1]):
                    e.wait_ge(ev.h, zr_done)
                    e.wait_ge(sa.h, max(e0, e1))
                    for h in range(2):
                        for half in range(2):
                            i4 = h * 2 + half
                            ins = e.matmul(psA[0][h * 64:(h + 1) * 64, half * 512:(half + 1) * 512],
                                           s.onesrow[0:1, 0:64],
                                           zrow[0:1, i4 * 512:(i4 + 1) * 512],
                                           start=True, stop=True)
                    ins.then_inc(mm.h, 1)
                ctx.t(zbm)
                zb_mm = mm.inc()
                def zbc(e, zb_mm=zb_mm):
                    e.wait_ge(mm.h, zb_mm)
                    e.tensor_copy(zb_sb[:, :], psA[0][:, :]).then_inc(ev.h, 1)
                ctx.v(zbc)
                zb_done[0] = ev.inc()
                zb_done[1] = zb_done[0]
                evs = []
                for h in range(2):
                    for half in range(2):
                        def yev(e, h=h, half=half, hp=hp, strip=strip,
                                sl=pvs[(h, half)][1], zbd=zb_done[0]):
                            e.wait_ge(ev.h, zbd)
                            col = hp * NL + strip * 1024 + half * 512
                            e.tensor_mul(Yfm[h * 64:(h + 1) * 64, col:col + 512],
                                         sl[0:64, :],
                                         zb_sb[h * 64:(h + 1) * 64, half * 512:(half + 1) * 512]
                                         ).then_inc(ev.h, 1)
                        ctx.v(yev)
                        evs.append(ev.inc())
                for h in range(2):
                    for half in range(2):
                        s.ps.release(pvs[(h, half)][0], ev, evs[-1])
            hp_war[p] = (mm.n, 0)
        attn_done_ev = ev.n

        st_war = [0, 0]
        for nt in range(4):
            ntp = nt % 2
            war = st_war[ntp]
            def xaf(e, nt=nt, ntp=ntp, war=war):
                if war:
                    e.wait_ge(tst2[ntp].h, war)
                e.dma_start(out=xat[ntp][:, :],
                            in_=xa.rearrange("(c p) n -> p c n", p=128)[:, :, nt * 512:(nt + 1) * 512]
                            ).then_inc(x0ld[ntp].h, 16)
            ctx.g(xaf)
            x_rdy = x0ld[ntp].inc(16)
            for mc in range(8):
                io, pso, wfree = s.ps.acquire()
                def omm(e, mc=mc, nt=nt, pso=pso, wfree=wfree):
                    e.wait_ge(ld.h, ld_all)
                    e.wait_ge(ev.h, attn_done_ev)
                    pwait(e, wfree)
                    for k in range(8):
                        ins = e.matmul(pso[:, :], owb[:, k * D + mc * 128:k * D + (mc + 1) * 128],
                                       Yfm[:, k * NL + nt * 512:k * NL + (nt + 1) * 512],
                                       start=(k == 0), stop=(k == 7))
                    ins.then_inc(mm.h, 1)
                ctx.t(omm)
                mmv = mm.inc()
                def oev(e, mc=mc, ntp=ntp, pso=pso, mmv=mmv, x_rdy=x_rdy, war=war):
                    e.wait_ge(mm.h, mmv)
                    e.wait_ge(x0ld[ntp].h, x_rdy)
                    if war:
                        e.wait_ge(tst2[ntp].h, war)
                    e.tensor_add(rs[ntp][:, mc * 512:(mc + 1) * 512], pso[:, :],
                                 xat[ntp][:, mc * 512:(mc + 1) * 512]).then_inc(ev.h, 1)
                ctx.v(oev)
                s.ps.release(io, ev, ev.inc())
            t1_done = ev.n
            def tst(e, nt=nt, ntp=ntp, t1_done=t1_done):
                e.wait_ge(ev.h, t1_done)
                e.dma_start(out=t1_o.rearrange("(c p) n -> p c n", p=128)[:, :, nt * 512:(nt + 1) * 512],
                            in_=rs[ntp][:, :]).then_inc(tst2[ntp].h, 16)
            ctx.y(tst)
            st_war[ntp] = tst2[ntp].inc(16)

        if debug:
            def ydbg(e):
                e.wait_ge(ev.h, attn_done_ev)
                e.dma_start(out=yfm_o, in_=Yfm[:, :]).then_inc(st.h, 16)
            ctx.y(ydbg)
            st.inc(16)
        fin = st.n
        tf0, tf1 = tst2[0].n, tst2[1].n
        def finw(e):
            e.wait_ge(st.h, fin)
            e.wait_ge(tst2[0].h, tf0)
            e.wait_ge(tst2[1].h, tf1)
        ctx.y(finw)
        ctx.finish()
    return nc



def build_B3():
    nc = bass.Bass("TRN2", target_bir_lowering=False)
    t1 = nc.dram_tensor("t1", [D, NL], F32, kind="ExternalInput").ap()
    xa = nc.dram_tensor("xa", [D, NL], F32, kind="ExternalInput").ap()
    ln2w = nc.dram_tensor("ln2w", [128, 8], F32, kind="ExternalInput").ap()
    rwT = nc.dram_tensor("rwT", [D, 7], F32, kind="ExternalInput").ap()
    rbias = nc.dram_tensor("rbias", [1, 7], F32, kind="ExternalInput").ap()
    gT = nc.dram_tensor("gT", [8, D, 2048], F32, kind="ExternalInput").ap()
    uT = nc.dram_tensor("uT", [8, D, 2048], F32, kind="ExternalInput").ap()
    dT = nc.dram_tensor("dT", [8, 2048, D], F32, kind="ExternalInput").ap()
    xb_o = nc.dram_tensor("xb", [D, NL], F32, kind="ExternalOutput").ap()

    es = contextlib.ExitStack()
    with es:
        ctx = Ctx(nc, es)
        s = S(ctx, n512=8)
        mm, ev, va, sa, ld, st = s.mm, s.ev, s.va, s.sa, s.ld, s.st

        tt = [ctx.sbuf(f"tt{i}", [128, 8 * 512], F32) for i in range(2)]
        xat = [ctx.sbuf(f"xat{i}", [128, 8 * 512], F32) for i in range(2)]
        xm = ctx.sbuf("xm", [128, 8 * 512], F32)
        xm_bf = ctx.sbuf("xm_bf", [128, 8 * 512], BF16)
        hu_bf = ctx.sbuf("hu_bf", [128, 16 * 512], BF16)
        macc = ctx.sbuf("macc", [128, 8 * 512], F32)
        wb_sb = ctx.sbuf("wb_sb", [128, 7 * 512], BF16)
        probs = ctx.sbuf("probs", [128, 7], F32)
        scr7 = ctx.sbuf("scr7", [128, 7], F32)
        m1 = ctx.sbuf("m1", [128, 1], F32)
        m2 = ctx.sbuf("m2", [128, 1], F32)
        wgt_f = ctx.sbuf("wgt_f", [128, 7], F32)
        wrow = ctx.sbuf("wrow", [1, 7 * 512], BF16)
        sg_sb = [ctx.sbuf(f"sg_sb{i}", [128, 512], F32) for i in range(2)]
        rw_sb = ctx.sbuf("rw_sb", [128, 8 * 7], F32)
        rb_sb = ctx.sbuf("rb_sb", [1, 7], F32)
        ln2_fm = ctx.sbuf("ln2_fm", [128, 8], F32)
        zeros8 = ctx.sbuf("zeros8", [128, 8], F32)
        identb = ctx.sbuf("identb", [128, 128], BF16)
        identf = ctx.sbuf("identf", [128, 128], F32)
        coliota = ctx.sbuf("coliota", [128, 128], F32)
        rowv = ctx.sbuf("rowv", [128, 1], F32)
        onesrow_bf = ctx.sbuf("onesrow_bf", [1, 128], BF16)
        gld = [ctx.sem(f"gld{i}") for i in range(2)]
        uld = [ctx.sem(f"uld{i}") for i in range(2)]
        dld = [ctx.sem(f"dld{i}") for i in range(2)]
        t1ld = [ctx.sem(f"t1ld{i}") for i in range(2)]
        xald = [ctx.sem(f"xald{i}") for i in range(2)]
        xst = [ctx.sem(f"xst{i}") for i in range(2)]
        gb = [ctx.sbuf(f"gb{i}", [128, 8 * 512], BF16) for i in range(2)]
        ub = [ctx.sbuf(f"ub{i}", [128, 8 * 512], BF16) for i in range(2)]
        db = [ctx.sbuf(f"db{i}", [128, 16 * 256], BF16) for i in range(2)]

        ctx.v(lambda e: e.memset(zeros8[:, :], 0.0))
        ctx.v(lambda e: e.memset(onesrow_bf[:, :], 1.0))
        ctx.g(lambda e: e.iota(coliota[:, :], [[1, 128]], channel_multiplier=0,
                               allow_small_or_imprecise_dtypes=True))
        iosem = ctx.sem("iosem")
        def iot2(e):
            e.iota(rowv[:, :], [[0, 1]], channel_multiplier=1,
                   allow_small_or_imprecise_dtypes=True).then_inc(iosem.h, 1)
        ctx.g(iot2)
        ld_iota = iosem.inc(1)
        def mkident(e):
            e.wait_ge(iosem.h, ld_iota)
            e.tensor_single_scalar(identf[:, :], coliota[:, :], rowv[:, :],
                                   ALU.is_equal).then_inc(va.h, 1)
        ctx.v(mkident)
        idva = va.inc()
        def mkident2(e, idva=idva):
            e.wait_ge(va.h, idva)
            e.tensor_copy(identb[:, :], identf[:, :]).then_inc(ev.h, 1)
        ctx.v(mkident2)
        ident_done = ev.inc()

        dma_in(ctx, rw_sb[:, :], rwT.rearrange("(k p) m -> p k m", p=128), ld)
        dma_in(ctx, rb_sb[:, :], rbias, ld)
        dma_in(ctx, ln2_fm[:, :], ln2w, ld)
        ld_all = ld.n

        g_war = [0, 0]
        u_war = [0, 0]
        d_war = [0, 0]
        hum_ev = [0, 0]
        g_cnt = [0, 0]
        u_cnt = [0, 0]
        d_cnt = [0, 0]
        t1_war = [0, 0]
        xa_war = [0, 0]
        st_war = [0, 0]

        for nt in range(4):
            ntp = nt % 2
            def t1f(e, nt=nt, ntp=ntp, war=t1_war[ntp], sw=st_war[ntp]):
                if war:
                    e.wait_ge(ev.h, war)
                if sw:
                    e.wait_ge(xst[ntp].h, sw)
                e.dma_start(out=tt[ntp][:, :],
                            in_=t1.rearrange("(c p) n -> p c n", p=128)[:, :, nt * 512:(nt + 1) * 512]
                            ).then_inc(t1ld[ntp].h, 16)
            ctx.g(t1f)
            t1_rdy = t1ld[ntp].inc(16)
            def xaf(e, nt=nt, ntp=ntp, war=xa_war[ntp]):
                if war:
                    e.wait_ge(ev.h, war)
                e.dma_start(out=xat[ntp][:, :],
                            in_=xa.rearrange("(c p) n -> p c n", p=128)[:, :, nt * 512:(nt + 1) * 512]
                            ).then_inc(xald[ntp].h, 16)
            ctx.g(xaf)
            xa_rdy = xald[ntp].inc(16)

            def gate(e, t1_rdy=t1_rdy, ntp=ntp):
                e.wait_ge(t1ld[ntp].h, t1_rdy)
                e.wait_ge(ld.h, ld_all)
            ctx.t(gate)
            ctx.s(gate)
            ln2_ev = plan_ln_tile(ctx, s,
                                  (lambda mc, ntp=ntp: tt[ntp][:, mc * 512:(mc + 1) * 512]),
                                  (lambda mc: xm[:, mc * 512:(mc + 1) * 512]),
                                  gamma=ln2_fm, beta=zeros8,
                                  bf_chunk=(lambda mc: xm_bf[:, mc * 512:(mc + 1) * 512]))

            for sub in range(4):
                ir, psr, wfree = s.ps.acquire()
                def rmm(e, sub=sub, psr=psr, wfree=wfree, ln2_ev=ln2_ev):
                    e.wait_ge(ev.h, ln2_ev)
                    pwait(e, wfree)
                    for k in range(8):
                        e.matmul(psr[:, 0:7], xm[:, k * 512 + sub * 128:k * 512 + sub * 128 + 128],
                                 rw_sb[:, k * 7:(k + 1) * 7], start=(k == 0), stop=False)
                    e.matmul(psr[:, 0:7], s.onesrow[0:1, :], rb_sb[0:1, :],
                             start=False, stop=True).then_inc(mm.h, 1)
                ctx.t(rmm)
                mmv = mm.inc()
                def sig(e, psr=psr, mmv=mmv):
                    e.wait_ge(mm.h, mmv)
                    e.activation(probs[:, :], psr[:, 0:7], AF.Sigmoid).then_inc(sa.h, 1)
                ctx.s(sig)
                sav = sa.inc()
                def top2(e, sav=sav):
                    e.wait_ge(sa.h, sav)
                    s.vself.edge(e, e.reduce_max(m1[:, :], probs[:, :], axis=AX.X))
                    s.vself.edge(e, e.tensor_single_scalar(scr7[:, :], probs[:, :], m1[:, :], ALU.is_equal))
                    s.vself.edge(e, e.scalar_tensor_tensor(scr7[:, :], scr7[:, :], -1e30, probs[:, :],
                                                           ALU.mult, ALU.add))
                    s.vself.edge(e, e.reduce_max(m2[:, :], scr7[:, :], axis=AX.X))
                    s.vself.edge(e, e.tensor_single_scalar(scr7[:, :], probs[:, :], m2[:, :], ALU.is_ge))
                    s.vself.edge(e, e.tensor_mul(scr7[:, :], scr7[:, :], probs[:, :]))
                    s.vself.edge(e, e.tensor_add(m1[:, :], m1[:, :], m2[:, :]))
                    s.vself.edge(e, e.reciprocal(m1[:, :], m1[:, :]))
                    e.tensor_scalar_mul(wgt_f[:, :], scr7[:, :], m1[:, :]).then_inc(ev.h, 1)
                ctx.v(top2)
                w_done = ev.inc()
                it, pst, wfree2 = s.ps.acquire()
                it2, pst2, wfree3 = s.ps.acquire()
                def wtr(e, pst=pst, pst2=pst2, wfree2=wfree2, wfree3=wfree3,
                        w_done=w_done, ident_done=ident_done):
                    e.wait_ge(ev.h, max(w_done, ident_done))
                    pwait(e, wfree2)
                    pwait(e, wfree3)
                    for ee in range(7):
                        pp = pst if ee < 4 else pst2
                        ins = e.transpose(pp[0:1, (ee % 4) * 128:(ee % 4 + 1) * 128],
                                          wgt_f[:, ee:ee + 1], identf[:, :])
                    ins.then_inc(mm.h, 1)
                ctx.t(wtr)
                mmv2 = mm.inc()
                def wcp(e, sub=sub, pst=pst, pst2=pst2, mmv2=mmv2):
                    e.wait_ge(mm.h, mmv2)
                    for ee in range(7):
                        pp = pst if ee < 4 else pst2
                        ins = e.tensor_copy(wrow[0:1, ee * 512 + sub * 128:ee * 512 + (sub + 1) * 128],
                                            pp[0:1, (ee % 4) * 128:(ee % 4 + 1) * 128])
                    ins.then_inc(ev.h, 1)
                ctx.v(wcp)
                s.ps.release(ir, ev, ev.inc())
                s.ps.release(it, ev, ev.n)
                s.ps.release(it2, ev, ev.n)
            wrow_done = ev.n
            iw, psw, wfreew = s.ps.acquire()
            iw2, psw2, wfreew2 = s.ps.acquire()
            wbc_ev = {}
            for ee in range(7):
                pw = psw if ee % 2 == 0 else psw2
                def wbm(e, ee=ee, pw=pw, wrow_done=wrow_done, wfreew=wfreew, wfreew2=wfreew2,
                        slot_free=wbc_ev.get(ee - 2, 0)):
                    e.wait_ge(ev.h, wrow_done)
                    if ee < 2:
                        pwait(e, wfreew if ee == 0 else wfreew2)
                    if slot_free:
                        e.wait_ge(ev.h, slot_free)
                    e.matmul(pw[:, :], onesrow_bf[0:1, :], wrow[0:1, ee * 512:(ee + 1) * 512],
                             start=True, stop=True).then_inc(mm.h, 1)
                ctx.t(wbm)
                mmv3 = mm.inc()
                def wbc(e, ee=ee, pw=pw, mmv3=mmv3):
                    e.wait_ge(mm.h, mmv3)
                    with nc.allow_low_precision(reason="top2 gate weights carried at bf16"):
                        e.tensor_copy(wb_sb[:, ee * 512:(ee + 1) * 512], pw[:, :]).then_inc(ev.h, 1)
                ctx.v(wbc)
                wbc_ev[ee] = ev.inc()
            s.ps.release(iw, ev, ev.n)
            s.ps.release(iw2, ev, ev.n)
            wb_done = ev.n

            for eidx in range(8):
                for m in range(16):
                    p2 = (m // 4) % 2
                    if m % 4 == 0:
                        def gl(e, nt=nt, eidx=eidx, m0=m, p2=p2, war=g_war[p2]):
                            if war:
                                e.wait_ge(mm.h, war)
                            e.dma_start(out=gb[p2][:, :],
                                        in_=gT[eidx, :, m0 * 128:(m0 + 4) * 128]
                                        .rearrange("(k p) m -> p k m", p=128)).then_inc(gld[p2].h, 16)
                        ctx.g(gl)
                        g_cnt[p2] += 16
                        def ul(e, nt=nt, eidx=eidx, m0=m, p2=p2, war=u_war[p2]):
                            if war:
                                e.wait_ge(mm.h, war)
                            e.dma_start(out=ub[p2][:, :],
                                        in_=uT[eidx, :, m0 * 128:(m0 + 4) * 128]
                                        .rearrange("(k p) m -> p k m", p=128)).then_inc(uld[p2].h, 16)
                        ctx.g(ul)
                        u_cnt[p2] += 16
                    ig, psg, wfg = s.ps.acquire()
                    iu, psu, wfu = s.ps.acquire()
                    def gmm(e, m=m, p2=p2, psg=psg, psu=psu, wfg=wfg, wfu=wfu,
                            gc=g_cnt[p2], uc=u_cnt[p2], ln2_ev=ln2_ev):
                        e.wait_ge(gld[p2].h, gc)
                        e.wait_ge(uld[p2].h, uc)
                        e.wait_ge(ev.h, ln2_ev)
                        pwait(e, wfg)
                        pwait(e, wfu)
                        mq = (m % 4) * 128
                        for k in range(8):
                            e.matmul(psg[:, :], gb[p2][:, k * 512 + mq:k * 512 + mq + 128],
                                     xm_bf[:, k * 512:(k + 1) * 512],
                                     start=(k == 0), stop=(k == 7))
                        for k in range(8):
                            ins = e.matmul(psu[:, :], ub[p2][:, k * 512 + mq:k * 512 + mq + 128],
                                           xm_bf[:, k * 512:(k + 1) * 512],
                                           start=(k == 0), stop=(k == 7))
                        ins.then_inc(mm.h, 1)
                    ctx.t(gmm)
                    g_war[p2] = mm.inc()
                    u_war[p2] = g_war[p2]
                    mmv = g_war[p2]
                    def sil(e, m=m, p2=p2, psg=psg, mmv=mmv, hw=hum_ev[p2]):
                        e.wait_ge(mm.h, mmv)
                        if hw:
                            e.wait_ge(ev.h, hw)
                        e.activation(sg_sb[p2][:, :], psg[:, :], AF.Silu).then_inc(sa.h, 1)
                    ctx.s(sil)
                    sav = sa.inc()
                    def humul(e, m=m, p2=p2, psu=psu, sav=sav):
                        e.wait_ge(sa.h, sav)
                        e.tensor_mul(hu_bf[:, m * 512:(m + 1) * 512], psu[:, :],
                                     sg_sb[p2][:, :]).then_inc(ev.h, 1)
                    ctx.v(humul)
                    evv = ev.inc()
                    hum_ev[p2] = evv
                    s.ps.release(ig, ev, evv)
                    s.ps.release(iu, ev, evv)
                hu_done = ev.n
                for mc in range(8):
                    p2 = (mc // 2) % 2
                    if mc % 2 == 0:
                        def dl(e, eidx=eidx, mc0=mc, p2=p2, war=d_war[p2]):
                            if war:
                                e.wait_ge(mm.h, war)
                            e.dma_start(out=db[p2][:, :],
                                        in_=dT[eidx, :, mc0 * 128:(mc0 + 2) * 128]
                                        .rearrange("(k p) m -> p k m", p=128)).then_inc(dld[p2].h, 16)
                        ctx.g(dl)
                        d_cnt[p2] += 16
                    idp, psd, wfd = s.ps.acquire()
                    def dmm(e, mc=mc, p2=p2, psd=psd, wfd=wfd, dc=d_cnt[p2], hu_done=hu_done):
                        e.wait_ge(dld[p2].h, dc)
                        e.wait_ge(ev.h, hu_done)
                        pwait(e, wfd)
                        mq = (mc % 2) * 128
                        for k in range(16):
                            ins = e.matmul(psd[:, :], db[p2][:, k * 256 + mq:k * 256 + mq + 128],
                                           hu_bf[:, k * 512:(k + 1) * 512],
                                           start=(k == 0), stop=(k == 15))
                        ins.then_inc(mm.h, 1)
                    ctx.t(dmm)
                    d_war[p2] = mm.inc()
                    mmv = d_war[p2]
                    if eidx == 0:
                        def dev(e, mc=mc, psd=psd, mmv=mmv):
                            e.wait_ge(mm.h, mmv)
                            e.tensor_copy(macc[:, mc * 512:(mc + 1) * 512], psd[:, :]).then_inc(ev.h, 1)
                    else:
                        def dev(e, mc=mc, psd=psd, mmv=mmv, eidx=eidx, wb_done=wb_done):
                            e.wait_ge(mm.h, mmv)
                            e.wait_ge(ev.h, wb_done)
                            s.vself.edge(e, e.tensor_mul(s.lnt[mc % 2][:, :], psd[:, :],
                                         wb_sb[:, (eidx - 1) * 512:eidx * 512]))
                            e.tensor_add(macc[:, mc * 512:(mc + 1) * 512],
                                         macc[:, mc * 512:(mc + 1) * 512],
                                         s.lnt[mc % 2][:, :]).then_inc(ev.h, 1)
                    ctx.v(dev)
                    s.ps.release(idp, ev, ev.inc())
            moe_done = ev.n

            for mc in range(8):
                def t2f(e, mc=mc, ntp=ntp, moe_done=moe_done, xa_rdy=xa_rdy,
                        swar=st_war[ntp]):
                    e.wait_ge(ev.h, moe_done)
                    e.wait_ge(xald[ntp].h, xa_rdy)
                    if swar:
                        e.wait_ge(xst[ntp].h, swar)
                    s.vself.edge(e, e.tensor_add(tt[ntp][:, mc * 512:(mc + 1) * 512],
                                 tt[ntp][:, mc * 512:(mc + 1) * 512],
                                 macc[:, mc * 512:(mc + 1) * 512]))
                    e.tensor_add(tt[ntp][:, mc * 512:(mc + 1) * 512],
                                 tt[ntp][:, mc * 512:(mc + 1) * 512],
                                 xat[ntp][:, mc * 512:(mc + 1) * 512]).then_inc(ev.h, 1)
                ctx.v(t2f)
                ev.inc()
            pre_done = ev.n
            xa_war[ntp] = pre_done
            out_ev = plan_ln_tile(ctx, s,
                                  (lambda mc, ntp=ntp: tt[ntp][:, mc * 512:(mc + 1) * 512]),
                                  (lambda mc, ntp=ntp: tt[ntp][:, mc * 512:(mc + 1) * 512]),
                                  wait_ev=pre_done)
            t1_war[ntp] = out_ev
            def stf(e, nt=nt, ntp=ntp, out_ev=out_ev):
                e.wait_ge(ev.h, out_ev)
                e.dma_start(out=xb_o.rearrange("(c p) n -> p c n", p=128)[:, :, nt * 512:(nt + 1) * 512],
                            in_=tt[ntp][:, :]).then_inc(xst[ntp].h, 16)
            ctx.y(stf)
            st_war[ntp] = xst[ntp].inc(16)

        f0, f1 = xst[0].n, xst[1].n
        def finw(e):
            e.wait_ge(xst[0].h, f0)
            e.wait_ge(xst[1].h, f1)
        ctx.y(finw)
        ctx.finish()
    return nc



def build_C1():
    nc = bass.Bass("TRN2", target_bir_lowering=False)
    xb = nc.dram_tensor("xb", [D, NL], F32, kind="ExternalInput").ap()
    WkT = nc.dram_tensor("WkT", [D, D], F32, kind="ExternalInput").ap()
    WvT = nc.dram_tensor("WvT", [D, D], F32, kind="ExternalInput").ap()
    bk = nc.dram_tensor("bk", [1, D], F32, kind="ExternalInput").ap()
    bv = nc.dram_tensor("bv", [1, D], F32, kind="ExternalInput").ap()
    Ktm_o = nc.dram_tensor("Ktm", [NL, D], BF16, kind="ExternalOutput").ap()
    Vtm_o = nc.dram_tensor("Vtm", [NL, D], BF16, kind="ExternalOutput").ap()
    Sh_o = nc.dram_tensor("Sh", [128, 8 * 64], F32, kind="ExternalOutput").ap()

    es = contextlib.ExitStack()
    with es:
        ctx = Ctx(nc, es)
        s = S(ctx, n512=6)
        mm, ev, va, sa, ld, st = s.mm, s.ev, s.va, s.sa, s.ld, s.st
        xb_bf = ctx.sbuf("xb_bf", [128, NCH * NL], BF16)
        wk = ctx.sbuf("wk", [128, NCH * D], BF16)
        wv = ctx.sbuf("wv", [128, NCH * D], BF16)
        bk_sb = ctx.sbuf("bk_sb", [1, D], BF16)
        bv_sb = ctx.sbuf("bv_sb", [1, D], BF16)
        ktm = ctx.sbuf("ktm", [128, NT * D], BF16)
        vtm = ctx.sbuf("vtm", [128, NT * D], BF16)
        ssb = ctx.sbuf("ssb", [128, 8 * 64], F32)
        kst = [ctx.sem(f"kst{i}") for i in range(2)]
        pstate = [ctx.psum(f"pstate{i}", [128, 512]) for i in range(2)]

        dma_in(ctx, xb_bf[:, :], xb.rearrange("(c p) n -> p c n", p=128), ld)
        dma_in(ctx, wk[:, :], WkT.rearrange("(k p) m -> p k m", p=128), ld)
        dma_in(ctx, wv[:, :], WvT.rearrange("(k p) m -> p k m", p=128), ld)
        dma_in(ctx, bk_sb[:, :], bk, ld)
        dma_in(ctx, bv_sb[:, :], bv, ld)
        ld_all = ld.n

        for which, (wbuf2, bsb, dst, sto) in enumerate(
                [(wk, bk_sb, ktm, Ktm_o), (wv, bv_sb, vtm, Vtm_o)]):
            for ncid in range(NT):
                for half in range(2):
                    i, ps, wfree = s.ps.acquire()
                    def mmf(e, wbuf2=wbuf2, bsb=bsb, ncid=ncid, half=half, ps=ps, wfree=wfree):
                        e.wait_ge(ld.h, ld_all)
                        e.wait_ge(ev.h, s.init_ev)
                        pwait(e, wfree)
                        for k in range(NCH):
                            e.matmul(ps[:, :], xb_bf[:, k * NL + ncid * 128:k * NL + (ncid + 1) * 128],
                                     wbuf2[:, k * D + half * 512:k * D + (half + 1) * 512],
                                     start=(k == 0), stop=False)
                        e.matmul(ps[:, :], s.onesrow_bf[0:1, 0:128],
                                 bsb[0:1, half * 512:(half + 1) * 512],
                                 start=False, stop=True).then_inc(mm.h, 1)
                    ctx.t(mmf)
                    mmv = mm.inc()
                    def evf(e, dst=dst, ncid=ncid, half=half, ps=ps, mmv=mmv):
                        e.wait_ge(mm.h, mmv)
                        e.tensor_copy(dst[:, ncid * D + half * 512:ncid * D + (half + 1) * 512],
                                      ps[:, :]).then_inc(ev.h, 1)
                    ctx.v(evf)
                    s.ps.release(i, ev, ev.inc())
        kv_done = ev.n
        def kvst(e):
            e.wait_ge(ev.h, kv_done)
            for ncid in range(NT):
                e.dma_start(out=Ktm_o[ncid * 128:(ncid + 1) * 128, :],
                            in_=ktm[:, ncid * D:(ncid + 1) * D]).then_inc(kst[0].h, 16)
                e.dma_start(out=Vtm_o[ncid * 128:(ncid + 1) * 128, :],
                            in_=vtm[:, ncid * D:(ncid + 1) * D]).then_inc(kst[1].h, 16)
        ctx.y(kvst)
        kst[0].inc(16 * NT)
        kst[1].inc(16 * NT)

        sev_ev = {}
        for h in range(H):
            def smm(e, h=h, kv_done=kv_done, war=sev_ev.get(h - 2, 0)):
                e.wait_ge(ev.h, kv_done)
                if war:
                    e.wait_ge(ev.h, war)
                for ncid in range(NT):
                    ins = e.matmul(pstate[h % 2][0:64, (h % 8) * 64:(h % 8) * 64 + 64],
                                   ktm[:, ncid * D + h * 64:ncid * D + h * 64 + 64],
                                   vtm[:, ncid * D + h * 64:ncid * D + h * 64 + 64],
                                   start=(ncid == 0), stop=(ncid == NT - 1))
                ins.then_inc(mm.h, 1)
            ctx.t(smm)
            mmv = mm.inc()
            def sev(e, h=h, mmv=mmv):
                e.wait_ge(mm.h, mmv)
                e.tensor_copy(ssb[(h % 2) * 64:(h % 2) * 64 + 64, (h // 2) * 64:(h // 2) * 64 + 64],
                              pstate[h % 2][0:64, (h % 8) * 64:(h % 8) * 64 + 64]).then_inc(ev.h, 1)
            ctx.v(sev)
            sev_ev[h] = ev.inc()
        s_done = ev.n
        def sst(e):
            e.wait_ge(ev.h, s_done)
            e.dma_start(out=Sh_o, in_=ssb[:, :]).then_inc(st.h, 16)
        ctx.y(sst)
        fin = st.inc(16)
        k0, k1 = kst[0].n, kst[1].n
        def finw(e):
            e.wait_ge(st.h, fin)
            e.wait_ge(kst[0].h, k0)
            e.wait_ge(kst[1].h, k1)
        ctx.y(finw)
        ctx.finish()
    return nc



def build_C2():
    nc = bass.Bass("TRN2", target_bir_lowering=False)
    xb = nc.dram_tensor("xb", [D, NL], F32, kind="ExternalInput").ap()
    WqT = nc.dram_tensor("WqT", [D, D], F32, kind="ExternalInput").ap()
    WkT = nc.dram_tensor("WkT", [D, D], F32, kind="ExternalInput").ap()
    WoT = nc.dram_tensor("WoT", [D, D], F32, kind="ExternalInput").ap()
    bq = nc.dram_tensor("bq", [128, 8], F32, kind="ExternalInput").ap()
    bkf = nc.dram_tensor("bkf", [128, 8], F32, kind="ExternalInput").ap()
    bo = nc.dram_tensor("bo", [128, 8], F32, kind="ExternalInput").ap()
    Ktm = nc.dram_tensor("Ktm", [NL, D], BF16, kind="ExternalInput").ap()
    Vtm = nc.dram_tensor("Vtm", [NL, D], BF16, kind="ExternalInput").ap()
    Sin = nc.dram_tensor("Sin", [128, 8 * 64], F32, kind="ExternalInput").ap()
    trimask = nc.dram_tensor("trimask", [128, 128], BF16, kind="ExternalInput").ap()
    xc_o = nc.dram_tensor("xc", [D, NL], F32, kind="ExternalOutput").ap()

    es = contextlib.ExitStack()
    with es:
        ctx = Ctx(nc, es)
        s = S(ctx, n512=6)
        mm, ev, va, sa, ld, st = s.mm, s.ev, s.va, s.sa, s.ld, s.st
        xb_bf = ctx.sbuf("xb_bf", [128, NCH * NL], BF16)
        wmb = [ctx.sbuf(f"wmb{i}", [128, NCH * 128], BF16) for i in range(2)]
        qfm = ctx.sbuf("qfm", [128, NCH * NL], BF16)
        kfm = ctx.sbuf("kfm", [128, NCH * NL], BF16)
        yfm = xb_bf
        ktc = [ctx.sbuf(f"ktc{i}", [128, D], BF16) for i in range(2)]
        vtc = [ctx.sbuf(f"vtc{i}", [128, D], BF16) for i in range(2)]
        state = ctx.sbuf("state", [128, 8 * 64], F32)
        state_bf = ctx.sbuf("state_bf", [128, 8 * 64], BF16)
        ssb_bf = ctx.sbuf("ssb_bf", [128, 128], BF16)
        tmsk = ctx.sbuf("tmsk", [128, 128], BF16)
        bq_sb = ctx.sbuf("bq_sb", [128, 8], F32)
        bk_sb = ctx.sbuf("bk_sb", [128, 8], F32)
        bo_sb = ctx.sbuf("bo_sb", [128, 8], F32)
        xat = [ctx.sbuf(f"xat{i}", [128, 8 * 512], F32) for i in range(2)]
        rs = [ctx.sbuf(f"rs{i}", [128, 8 * 512], F32) for i in range(2)]
        kld = [ctx.sem(f"kld{i}") for i in range(2)]
        vld = [ctx.sem(f"vld{i}") for i in range(2)]
        xald = [ctx.sem(f"xald{i}") for i in range(2)]
        xst = [ctx.sem(f"xst{i}") for i in range(2)]

        wld = [ctx.sem(f"wld{i}") for i in range(2)]
        dma_in(ctx, xb_bf[:, :], xb.rearrange("(c p) n -> p c n", p=128), ld)
        dma_in(ctx, bq_sb[:, :], bq, ld)
        dma_in(ctx, bk_sb[:, :], bkf, ld)
        dma_in(ctx, bo_sb[:, :], bo, ld)
        dma_in(ctx, state[:, :], Sin, ld)
        dma_in(ctx, tmsk[:, :], trimask, ld)
        ld_all = ld.n
        def st_bf0(e):
            e.wait_ge(ld.h, ld_all)
            e.tensor_copy(state_bf[:, :], state[:, :]).then_inc(ev.h, 1)
        ctx.v(st_bf0)
        state_ready = ev.inc()

        w_war = [0, 0]
        w_cnt = [0, 0]
        for which, (wT3, bsb, dst) in enumerate([(WqT, bq_sb, qfm), (WkT, bk_sb, kfm)]):
            for mc in range(8):
                p3 = (which * 8 + mc) % 2
                def wl(e, wT3=wT3, mc=mc, p3=p3, war=w_war[p3]):
                    if war:
                        e.wait_ge(mm.h, war)
                    e.dma_start(out=wmb[p3][:, :],
                                in_=wT3.rearrange("(k p) m -> p k m", p=128)[:, :, mc * 128:(mc + 1) * 128]
                                ).then_inc(wld[p3].h, 16)
                ctx.g(wl)
                w_cnt[p3] += 16
                for nt in range(4):
                    i, ps, wfree = s.ps.acquire()
                    def mmf(e, p3=p3, mc=mc, nt=nt, ps=ps, wfree=wfree, wc=w_cnt[p3]):
                        e.wait_ge(ld.h, ld_all)
                        e.wait_ge(wld[p3].h, wc)
                        pwait(e, wfree)
                        for k in range(NCH):
                            ins = e.matmul(ps[:, :], wmb[p3][:, k * 128:(k + 1) * 128],
                                           xb_bf[:, k * NL + nt * 512:k * NL + (nt + 1) * 512],
                                           start=(k == 0), stop=(k == NCH - 1))
                        ins.then_inc(mm.h, 1)
                    ctx.t(mmf)
                    mmv = mm.inc()
                    w_war[p3] = mmv
                    def evf(e, bsb=bsb, dst=dst, mc=mc, nt=nt, ps=ps, mmv=mmv):
                        e.wait_ge(mm.h, mmv)
                        e.tensor_scalar_add(dst[:, mc * NL + nt * 512:mc * NL + (nt + 1) * 512],
                                            ps[:, :], bsb[:, mc:mc + 1]).then_inc(ev.h, 1)
                    ctx.v(evf)
                    s.ps.release(i, ev, ev.inc())
        qk_done = ev.n

        state_upd_ev = state_ready
        ktc_war = [0, 0]
        for c in range(NT):
            p = c % 2
            def kvl(e, c=c, p=p, war=ktc_war[p]):
                if war:
                    e.wait_ge(mm.h, war)
                e.dma_start(out=ktc[p][:, :], in_=Ktm[c * 128:(c + 1) * 128, :]).then_inc(kld[p].h, 16)
                e.dma_start(out=vtc[p][:, :], in_=Vtm[c * 128:(c + 1) * 128, :]).then_inc(vld[p].h, 16)
            ctx.g(kvl)
            k_rdy = kld[p].inc(16)
            v_rdy = vld[p].inc(16)
            last_mm = 0
            for h in range(H):
                rb = (h % 2) * 64
                hc = h // 2
                i1, ps1, wf1 = s.ps.acquire()
                def smm(e, rb=rb, hc=hc, c=c, ps1=ps1, wf1=wf1, qk_done=qk_done):
                    e.wait_ge(ev.h, qk_done)
                    pwait(e, wf1)
                    e.matmul(ps1[:, 0:128],
                             kfm[rb:rb + 64, hc * NL + c * 128:hc * NL + (c + 1) * 128],
                             qfm[rb:rb + 64, hc * NL + c * 128:hc * NL + (c + 1) * 128],
                             start=True, stop=True).then_inc(mm.h, 1)
                ctx.t(smm)
                mmv = mm.inc()
                def smsk(e, ps1=ps1, mmv=mmv):
                    e.wait_ge(mm.h, mmv)
                    e.tensor_mul(ssb_bf[:, :], ps1[:, 0:128], tmsk[:, :]).then_inc(ev.h, 1)
                ctx.v(smsk)
                sm_ev = ev.inc()
                s.ps.release(i1, ev, sm_ev)
                i2, ps2, wf2 = s.ps.acquire()
                def omm(e, h=h, rb=rb, hc=hc, c=c, p=p, ps2=ps2, wf2=wf2, sm_ev=sm_ev,
                        v_rdy=v_rdy, sup=state_upd_ev):
                    e.wait_ge(ev.h, sm_ev)
                    e.wait_ge(vld[p].h, v_rdy)
                    e.wait_ge(ev.h, sup)
                    pwait(e, wf2)
                    e.matmul(ps2[0:64, 0:128], vtc[p][:, h * 64:(h + 1) * 64],
                             ssb_bf[:, :], start=True, stop=False)
                    e.matmul(ps2[0:64, 0:128],
                             state_bf[rb:rb + 64, hc * 64:(hc + 1) * 64],
                             qfm[rb:rb + 64, hc * NL + c * 128:hc * NL + (c + 1) * 128],
                             start=False, stop=True).then_inc(mm.h, 1)
                ctx.t(omm)
                mmv2 = mm.inc()
                def yev(e, h=h, c=c, ps2=ps2, mmv2=mmv2):
                    e.wait_ge(mm.h, mmv2)
                    e.tensor_copy(yfm[(h % 2) * 64:(h % 2) * 64 + 64,
                                      (h // 2) * NL + c * 128:(h // 2) * NL + (c + 1) * 128],
                                  ps2[0:64, 0:128]).then_inc(ev.h, 1)
                ctx.v(yev)
                s.ps.release(i2, ev, ev.inc())
                i3, ps3, wf3 = s.ps.acquire()
                def dmm(e, h=h, c=c, p=p, ps3=ps3, wf3=wf3, k_rdy=k_rdy, v_rdy=v_rdy):
                    e.wait_ge(kld[p].h, k_rdy)
                    e.wait_ge(vld[p].h, v_rdy)
                    pwait(e, wf3)
                    e.matmul(ps3[0:64, 0:64], ktc[p][:, h * 64:(h + 1) * 64],
                             vtc[p][:, h * 64:(h + 1) * 64],
                             start=True, stop=True).then_inc(mm.h, 1)
                ctx.t(dmm)
                mmv3 = mm.inc()
                def supd(e, h=h, rb=rb, hc=hc, ps3=ps3, mmv3=mmv3, mmv2=mmv2):
                    e.wait_ge(mm.h, max(mmv3, mmv2))
                    s.vself.edge(e, e.tensor_add(state[rb:rb + 64, hc * 64:(hc + 1) * 64],
                                 state[rb:rb + 64, hc * 64:(hc + 1) * 64],
                                 ps3[0:64, 0:64]))
                    e.tensor_copy(state_bf[rb:rb + 64, hc * 64:(hc + 1) * 64],
                                  state[rb:rb + 64, hc * 64:(hc + 1) * 64]).then_inc(ev.h, 1)
                ctx.v(supd)
                s.ps.release(i3, ev, ev.inc())
                last_mm = mmv3
            state_upd_ev = ev.n
            ktc_war[p] = last_mm
        y_done = ev.n

        st_war = [0, 0]
        for nt in range(4):
            ntp = nt % 2
            def xaf(e, nt=nt, ntp=ntp, war=st_war[ntp]):
                if war:
                    e.wait_ge(xst[ntp].h, war)
                e.dma_start(out=xat[ntp][:, :],
                            in_=xb.rearrange("(c p) n -> p c n", p=128)[:, :, nt * 512:(nt + 1) * 512]
                            ).then_inc(xald[ntp].h, 16)
            ctx.g(xaf)
            x_rdy = xald[ntp].inc(16)
            for mc in range(8):
                p3 = (nt * 8 + mc) % 2
                def wol(e, mc=mc, p3=p3, war=w_war[p3]):
                    if war:
                        e.wait_ge(mm.h, war)
                    e.dma_start(out=wmb[p3][:, :],
                                in_=WoT.rearrange("(k p) m -> p k m", p=128)[:, :, mc * 128:(mc + 1) * 128]
                                ).then_inc(wld[p3].h, 16)
                ctx.g(wol)
                w_cnt[p3] += 16
                io, pso, wfree = s.ps.acquire()
                def pmm(e, mc=mc, nt=nt, p3=p3, pso=pso, wfree=wfree, y_done=y_done, wc=w_cnt[p3]):
                    e.wait_ge(wld[p3].h, wc)
                    e.wait_ge(ev.h, y_done)
                    pwait(e, wfree)
                    for k in range(8):
                        ins = e.matmul(pso[:, :], wmb[p3][:, k * 128:(k + 1) * 128],
                                       yfm[:, k * NL + nt * 512:k * NL + (nt + 1) * 512],
                                       start=(k == 0), stop=(k == 7))
                    ins.then_inc(mm.h, 1)
                ctx.t(pmm)
                mmv = mm.inc()
                w_war[p3] = mmv
                def pev(e, mc=mc, ntp=ntp, pso=pso, mmv=mmv, x_rdy=x_rdy, war=st_war[ntp]):
                    e.wait_ge(mm.h, mmv)
                    e.wait_ge(xald[ntp].h, x_rdy)
                    if war:
                        e.wait_ge(xst[ntp].h, war)
                    e.scalar_tensor_tensor(rs[ntp][:, mc * 512:(mc + 1) * 512], pso[:, :],
                                           bo_sb[:, mc:mc + 1], xat[ntp][:, mc * 512:(mc + 1) * 512],
                                           ALU.add, ALU.add).then_inc(ev.h, 1)
                ctx.v(pev)
                s.ps.release(io, ev, ev.inc())
            t_done = ev.n
            out_ev = plan_ln_tile(ctx, s,
                                  (lambda mc, ntp=ntp: rs[ntp][:, mc * 512:(mc + 1) * 512]),
                                  (lambda mc, ntp=ntp: rs[ntp][:, mc * 512:(mc + 1) * 512]),
                                  wait_ev=t_done)
            def stf(e, nt=nt, ntp=ntp, out_ev=out_ev):
                e.wait_ge(ev.h, out_ev)
                e.dma_start(out=xc_o.rearrange("(c p) n -> p c n", p=128)[:, :, nt * 512:(nt + 1) * 512],
                            in_=rs[ntp][:, :]).then_inc(xst[ntp].h, 16)
            ctx.y(stf)
            st_war[ntp] = xst[ntp].inc(16)
        f0, f1 = xst[0].n, xst[1].n
        def finw(e):
            e.wait_ge(xst[0].h, f0)
            e.wait_ge(xst[1].h, f1)
        ctx.y(finw)
        ctx.finish()
    return nc



def build_D1():
    nc = bass.Bass("TRN2", target_bir_lowering=False)
    xc = nc.dram_tensor("xc", [D, NL], F32, kind="ExternalInput").ap()
    rwT = nc.dram_tensor("rwT", [D, 8], F32, kind="ExternalInput").ap()
    ltri = nc.dram_tensor("ltri", [128, 128], BF16, kind="ExternalInput").ap()
    meta_o = nc.dram_tensor("meta", [NL, 16], F32, kind="ExternalOutput").ap()
    cnt_o = nc.dram_tensor("cnt", [1, 8], F32, kind="ExternalOutput").ap()

    es = contextlib.ExitStack()
    with es:
        ctx = Ctx(nc, es)
        s = S(ctx, n512=6)
        mm, ev, va, sa, ld, st = s.mm, s.ev, s.va, s.sa, s.ld, s.st
        xcb = ctx.sbuf("xcb", [128, NCH * NL], F32)
        rw_sb = ctx.sbuf("rw_sb", [128, 8 * 8], F32)
        lt_sb = ctx.sbuf("lt_sb", [128, 128], BF16)
        probs = ctx.sbuf("probs_sb", [128, 8], F32)
        m1 = ctx.sbuf("m1_sb", [128, 1], F32)
        ssum = ctx.sbuf("ssum", [128, 1], F32)
        oh_sb = ctx.sbuf("oh_sb", [128, 8], F32)
        oh_bf = ctx.sbuf("oh_bf", [128, 8], BF16)
        carry = ctx.sbuf("carry", [1, 8], F32)
        posn = ctx.sbuf("posn", [128, 8], F32)
        pc_sb = ctx.sbuf("pc_sb", [128, 2], F32)
        cnts = ctx.sbuf("cnts", [1, 8], F32)
        ones128_bf = ctx.sbuf("ones128_bf", [128, 1], BF16)

        dma_in(ctx, xcb[:, :], xc.rearrange("(c p) n -> p c n", p=128), ld)
        dma_in(ctx, rw_sb[:, :], rwT.rearrange("(k p) m -> p k m", p=128), ld)
        dma_in(ctx, lt_sb[:, :], ltri, ld)
        ld_all = ld.n
        ctx.v(lambda e: e.memset(carry[:, :], 0.0))
        ctx.v(lambda e: e.memset(ones128_bf[:, :], 1.0))

        st_cnt = 0
        carry_ev = [0]
        for sub in range(NT):
            ir, psr, wfree = s.ps.acquire()
            def rmm(e, sub=sub, psr=psr, wfree=wfree):
                e.wait_ge(ld.h, ld_all)
                pwait(e, wfree)
                for k in range(8):
                    ins = e.matmul(psr[:, 0:8], xcb[:, k * NL + sub * 128:k * NL + (sub + 1) * 128],
                                   rw_sb[:, k * 8:(k + 1) * 8], start=(k == 0), stop=(k == 7))
                ins.then_inc(mm.h, 1)
            ctx.t(rmm)
            mmv = mm.inc()
            def sm1(e, psr=psr, mmv=mmv, war=st_cnt):
                e.wait_ge(mm.h, mmv)
                if war:
                    e.wait_ge(st.h, war)
                s.vself.edge(e, e.reduce_max(m1[:, :], psr[:, 0:8], axis=AX.X))
                e.tensor_scalar(probs[:, :], psr[:, 0:8], m1[:, :], None, ALU.subtract).then_inc(va.h, 1)
            ctx.v(sm1)
            vav = va.inc()
            def sm2(e, vav=vav):
                e.wait_ge(va.h, vav)
                e.activation(probs[:, :], probs[:, :], AF.Exp).then_inc(sa.h, 1)
            ctx.s(sm2)
            sav = sa.inc()
            def sm3(e, sav=sav, psr=psr):
                e.wait_ge(sa.h, sav)
                s.vself.edge(e, e.reduce_sum(ssum[:, :], probs[:, :], axis=AX.X))
                s.vself.edge(e, e.reciprocal(ssum[:, :], ssum[:, :]))
                s.vself.edge(e, e.tensor_scalar_mul(probs[:, :], probs[:, :], ssum[:, :]))
                s.vself.edge(e, e.reduce_max(pc_sb[:, 1:2], probs[:, :], axis=AX.X))
                s.vself.edge(e, e.tensor_single_scalar(oh_sb[:, :], probs[:, :], pc_sb[:, 1:2], ALU.is_equal))
                e.tensor_copy(oh_bf[:, :], oh_sb[:, :]).then_inc(ev.h, 1)
            ctx.v(sm3)
            oh_done = ev.inc()
            s.ps.release(ir, ev, oh_done)
            ip, psp, wfree2 = s.ps.acquire()
            it2, pst2, wfree3 = s.ps.acquire()
            def pmm(e, sub=sub, psp=psp, pst2=pst2, wfree2=wfree2, wfree3=wfree3,
                    oh_done=oh_done, cwait=carry_ev[0]):
                e.wait_ge(ev.h, oh_done)
                if cwait:
                    e.wait_ge(ev.h, cwait)
                pwait(e, wfree2)
                pwait(e, wfree3)
                e.matmul(psp[:, 0:8], lt_sb[:, :], oh_bf[:, :], start=True, stop=False)
                e.matmul(psp[:, 0:8], s.onesrow[0:1, :], carry[0:1, :],
                         start=False, stop=True)
                e.matmul(pst2[0:1, 0:8], ones128_bf[:, :], oh_bf[:, :],
                         start=True, stop=True).then_inc(mm.h, 1)
            ctx.t(pmm)
            mmv2 = mm.inc()
            def pev(e, sub=sub, psp=psp, pst2=pst2, mmv2=mmv2):
                e.wait_ge(mm.h, mmv2)
                e.tensor_add(carry[0:1, :], carry[0:1, :], pst2[0:1, 0:8])
                s.vself.edge(e, e.tensor_copy(posn[:, :], psp[:, 0:8]))
                s.vself.edge(e, e.tensor_mul(posn[:, :], posn[:, :], oh_sb[:, :]))
                e.reduce_sum(pc_sb[:, 0:1], posn[:, :], axis=AX.X).then_inc(ev.h, 1)
            ctx.v(pev)
            pos_done = ev.inc()
            carry_ev[0] = pos_done
            s.ps.release(ip, ev, pos_done)
            s.ps.release(it2, ev, pos_done)
            def stf(e, sub=sub, pos_done=pos_done):
                e.wait_ge(ev.h, pos_done)
                e.dma_start(out=meta_o[sub * 128:(sub + 1) * 128, 0:8], in_=oh_sb[:, :]).then_inc(st.h, 16)
                e.dma_start(out=meta_o[sub * 128:(sub + 1) * 128, 8:10], in_=pc_sb[:, :]).then_inc(st.h, 16)
            ctx.y(stf)
            st_cnt = st.inc(32)
        final_carry = carry_ev[0]
        def cev(e, final_carry=final_carry):
            e.wait_ge(ev.h, final_carry)
            e.tensor_copy(cnts[:, :], carry[0:1, :]).then_inc(ev.h, 1)
        ctx.v(cev)
        c_done = ev.inc()
        def cst(e, c_done=c_done):
            e.wait_ge(ev.h, c_done)
            e.dma_start(out=cnt_o, in_=cnts[:, :]).then_inc(st.h, 16)
        ctx.y(cst)
        fin = st.inc(16)
        ctx.y(lambda e: e.wait_ge(st.h, fin))
        ctx.finish()
    return nc



def build_D2():
    CAP = 3072.0
    nc = bass.Bass("TRN2", target_bir_lowering=False)
    xc = nc.dram_tensor("xc", [D, NL], F32, kind="ExternalInput").ap()
    meta_i = nc.dram_tensor("meta", [NL, 16], F32, kind="ExternalInput").ap()
    cnts_i = nc.dram_tensor("cnts", [8, 8], F32, kind="ExternalInput").ap()
    pmask = nc.dram_tensor("pmask", [8, 1], F32, kind="ExternalInput").ap()
    w1 = nc.dram_tensor("w1", [8, D, 4096], F32, kind="ExternalInput").ap()
    b1 = nc.dram_tensor("b1", [8, 128, 32], F32, kind="ExternalInput").ap()
    w2 = nc.dram_tensor("w2", [8, 4096, D], F32, kind="ExternalInput").ap()
    b2 = nc.dram_tensor("b2", [8, 128, 8], F32, kind="ExternalInput").ap()
    out_o = nc.dram_tensor("out", [D, NL], F32, kind="ExternalOutput").ap()

    es = contextlib.ExitStack()
    with es:
        ctx = Ctx(nc, es)
        s = S(ctx, n512=8)
        mm, ev, va, sa, ld, st = s.mm, s.ev, s.va, s.sa, s.ld, s.st
        xct = [ctx.sbuf(f"xct{i}", [128, 8 * 512], F32) for i in range(2)]
        xm_bf = ctx.sbuf("xm_bf", [128, 8 * 512], BF16)
        hu_bf = ctx.sbuf("hu_bf", [128, 32 * 512], BF16)
        yacc = ctx.sbuf("yacc", [128, 8 * 512], F32)
        gb_sb = ctx.sbuf("gb_sb", [128, 8 * 512], F32)
        aux = ctx.sbuf("aux", [128, 4 * 16], F32)
        gate = ctx.sbuf("gate", [128, 4], F32)
        ob_sb = ctx.sbuf("ob_sb", [128, 8], F32)
        offr = ctx.sbuf("offr", [1, 8], F32)
        cnt_sb = ctx.sbuf("cnt_sb", [8, 8], F32)
        pm_sb = ctx.sbuf("pm_sb", [8, 1], F32)
        grow = ctx.sbuf("grow", [1, 8 * 512], BF16)
        ge_f = ctx.sbuf("ge_f", [128, 1], F32)
        b1_sb = ctx.sbuf("b1_sb", [128, 32], F32)
        b2_sb = ctx.sbuf("b2_sb", [128, 8], F32)
        identf = ctx.sbuf("identf", [128, 128], F32)
        coliota = ctx.sbuf("coliota", [128, 128], F32)
        rowv = ctx.sbuf("rowv", [128, 1], F32)
        w1b = [ctx.sbuf(f"w1b_{i}", [128, 8 * 512], BF16) for i in range(2)]
        w2b = [ctx.sbuf(f"w2b_{i}", [128, 32 * 256], BF16) for i in range(2)]
        g1ld = [ctx.sem(f"g1ld{i}") for i in range(2)]
        g2ld = [ctx.sem(f"g2ld{i}") for i in range(2)]
        bld = ctx.sem("bld")
        axld = ctx.sem("axld")
        xld = [ctx.sem(f"xld{i}") for i in range(2)]
        ast = [ctx.sem(f"ast{i}") for i in range(2)]
        iosem = ctx.sem("iosem2")

        ctx.g(lambda e: e.iota(coliota[:, :], [[1, 128]], channel_multiplier=0,
                               allow_small_or_imprecise_dtypes=True))
        def iot2(e):
            e.iota(rowv[:, :], [[0, 1]], channel_multiplier=1,
                   allow_small_or_imprecise_dtypes=True).then_inc(iosem.h, 1)
        ctx.g(iot2)
        io_done = iosem.inc(1)
        def mkident(e):
            e.wait_ge(iosem.h, io_done)
            e.tensor_single_scalar(identf[:, :], coliota[:, :], rowv[:, :],
                                   ALU.is_equal).then_inc(ev.h, 1)
        ctx.v(mkident)
        ident_done = ev.inc()

        dma_in(ctx, cnt_sb[:, :], cnts_i, ld)
        dma_in(ctx, pm_sb[:, :], pmask, ld)
        ld_all = ld.n
        ioff, psoff, wfo = s.ps.acquire()
        def offmm(e, psoff=psoff, wfo=wfo):
            e.wait_ge(ld.h, ld_all)
            pwait(e, wfo)
            e.matmul(psoff[0:1, 0:8], pm_sb[:, :], cnt_sb[:, :],
                     start=True, stop=True).then_inc(mm.h, 1)
        ctx.t(offmm)
        off_mm = mm.inc()
        def offev(e, psoff=psoff, off_mm=off_mm):
            e.wait_ge(mm.h, off_mm)
            e.tensor_copy(offr[:, :], psoff[0:1, 0:8]).then_inc(ev.h, 1)
        ctx.v(offev)
        off_done = ev.inc()
        s.ps.release(ioff, ev, off_done)
        iob, psob, wfb = s.ps.acquire()
        def obm(e, psob=psob, wfb=wfb, off_done=off_done):
            e.wait_ge(ev.h, off_done)
            pwait(e, wfb)
            e.matmul(psob[:, 0:8], s.onesrow[0:1, :], offr[0:1, :],
                     start=True, stop=True).then_inc(mm.h, 1)
        ctx.t(obm)
        ob_mm = mm.inc()
        def obev(e, psob=psob, ob_mm=ob_mm):
            e.wait_ge(mm.h, ob_mm)
            e.tensor_copy(ob_sb[:, :], psob[:, 0:8]).then_inc(ev.h, 1)
        ctx.v(obev)
        ob_done = ev.inc()
        s.ps.release(iob, ev, ob_done)

        aux_war = [0]
        ge_tr_mm = [0]
        bias_gate = [0, 0]
        g1war = [0, 0]
        g2war = [0, 0]
        g1cnt = [0, 0]
        g2cnt = [0, 0]
        xwar = [0, 0]
        swar = [0, 0]
        for nt in range(4):
            ntp = nt % 2
            def xcf(e, nt=nt, ntp=ntp, war=xwar[ntp], sw=swar[ntp]):
                if war:
                    e.wait_ge(ev.h, war)
                if sw:
                    e.wait_ge(ast[ntp].h, sw)
                e.dma_start(out=xct[ntp][:, :],
                            in_=xc.rearrange("(c p) n -> p c n", p=128)[:, :, nt * 512:(nt + 1) * 512]
                            ).then_inc(xld[ntp].h, 16)
            ctx.g(xcf)
            x_rdy = xld[ntp].inc(16)
            def auxf(e, nt=nt, war=aux_war[0]):
                if war:
                    e.wait_ge(ev.h, war)
                e.dma_start(out=aux[:, :].rearrange("p (a c) -> p a c", c=16),
                            in_=meta_i[nt * 512:(nt + 1) * 512, :].rearrange("(a p) c -> p a c", p=128)
                            ).then_inc(axld.h, 16)
            ctx.g(auxf)
            aux_rdy = axld.inc(16)
            def xcast(e, ntp=ntp, x_rdy=x_rdy):
                e.wait_ge(xld[ntp].h, x_rdy)
                e.tensor_copy(xm_bf[:, :], xct[ntp][:, :]).then_inc(ev.h, 1)
            ctx.v(xcast)
            xbf_done = ev.inc()
            for sub in range(4):
                def gf(e, sub=sub, aux_rdy=aux_rdy, ob_done=ob_done):
                    e.wait_ge(axld.h, aux_rdy)
                    e.wait_ge(ev.h, ob_done)
                    s.vself.edge(e, e.tensor_mul(s.lnt[0][:, 0:8], aux[:, sub * 16:sub * 16 + 8], ob_sb[:, :]))
                    s.vself.edge(e, e.reduce_sum(ge_f[:, :], s.lnt[0][:, 0:8], axis=AX.X))
                    s.vself.edge(e, e.tensor_add(ge_f[:, :], ge_f[:, :], aux[:, sub * 16 + 8:sub * 16 + 9]))
                    s.vself.edge(e, e.tensor_single_scalar(ge_f[:, :], ge_f[:, :], CAP, ALU.is_lt))
                    e.tensor_mul(gate[:, sub:sub + 1], ge_f[:, :],
                                 aux[:, sub * 16 + 9:sub * 16 + 10]).then_inc(ev.h, 1)
                ctx.v(gf)
                ev.inc()
            gates_done = ev.n

            for eidx in range(8):
                itr, pst, wtt = s.ps.acquire()
                for sub in range(4):
                    def get(e, eidx=eidx, sub=sub, gates_done=gates_done, ident_done=ident_done,
                            pst=pst, wtt=wtt, prev_tr=ge_tr_mm[0]):
                        e.wait_ge(ev.h, max(gates_done, ident_done))
                        if prev_tr:
                            e.wait_ge(mm.h, prev_tr)
                        if sub == 0:
                            pwait(e, wtt)
                        e.tensor_mul(ge_f[:, :], gate[:, sub:sub + 1],
                                     aux[:, sub * 16 + eidx:sub * 16 + eidx + 1]).then_inc(ev.h, 1)
                    ctx.v(get)
                    gev = ev.inc()
                    def gtr(e, sub=sub, pst=pst, gev=gev):
                        e.wait_ge(ev.h, gev)
                        e.transpose(pst[0:1, sub * 128:(sub + 1) * 128], ge_f[:, :],
                                    identf[:, :]).then_inc(mm.h, 1)
                    ctx.t(gtr)
                    ge_tr_mm[0] = mm.inc()
                tr_mm = mm.n
                def gcp(e, eidx=eidx, pst=pst, tr_mm=tr_mm):
                    e.wait_ge(mm.h, tr_mm)
                    with nc.allow_low_precision(reason="top-1 gate is a bf16 matmul-side operand"):
                        e.tensor_copy(grow[0:1, eidx * 512:(eidx + 1) * 512],
                                      pst[0:1, 0:512]).then_inc(ev.h, 1)
                ctx.v(gcp)
                s.ps.release(itr, ev, ev.inc())
            grow_done = ev.n
            aux_war[0] = grow_done
            for eidx in range(8):
                ib, psb, wfb2 = s.ps.acquire()
                def gbm(e, eidx=eidx, psb=psb, wfb2=wfb2, grow_done=grow_done):
                    e.wait_ge(ev.h, grow_done)
                    pwait(e, wfb2)
                    e.matmul(psb[:, :], s.onesrow_bf[0:1, :], grow[0:1, eidx * 512:(eidx + 1) * 512],
                             start=True, stop=True).then_inc(mm.h, 1)
                ctx.t(gbm)
                mmv = mm.inc()
                def gbe(e, eidx=eidx, psb=psb, mmv=mmv):
                    e.wait_ge(mm.h, mmv)
                    e.tensor_copy(gb_sb[:, eidx * 512:(eidx + 1) * 512], psb[:, :]).then_inc(ev.h, 1)
                ctx.v(gbe)
                s.ps.release(ib, ev, ev.inc())
            gb_done = ev.n

            for eidx in range(8):
                def bf_(e, eidx=eidx, g1=bias_gate[0], g2=bias_gate[1]):
                    if g1:
                        e.wait_ge(sa.h, g1)
                    if g2:
                        e.wait_ge(ev.h, g2)
                    e.dma_start(out=b1_sb[:, :], in_=b1[eidx]).then_inc(bld.h, 16)
                    e.dma_start(out=b2_sb[:, :], in_=b2[eidx]).then_inc(bld.h, 16)
                ctx.g(bf_)
                b_rdy = bld.inc(32)
                for m in range(32):
                    p2 = (m // 4) % 2
                    if m % 4 == 0:
                        def w1l(e, eidx=eidx, m0=m, p2=p2, war=g1war[p2]):
                            if war:
                                e.wait_ge(mm.h, war)
                            e.dma_start(out=w1b[p2][:, :],
                                        in_=w1[eidx, :, m0 * 128:(m0 + 4) * 128]
                                        .rearrange("(k p) m -> p k m", p=128)).then_inc(g1ld[p2].h, 16)
                        ctx.g(w1l)
                        g1cnt[p2] += 16
                    i1, ps1, wf1 = s.ps.acquire()
                    def w1m(e, m=m, p2=p2, ps1=ps1, wf1=wf1, gc=g1cnt[p2], xbf_done=xbf_done):
                        e.wait_ge(g1ld[p2].h, gc)
                        e.wait_ge(ev.h, xbf_done)
                        pwait(e, wf1)
                        for k in range(8):
                            ins = e.matmul(ps1[:, :], w1b[p2][:, k * 512 + (m % 4) * 128:k * 512 + (m % 4 + 1) * 128],
                                           xm_bf[:, k * 512:(k + 1) * 512],
                                           start=(k == 0), stop=(k == 7))
                        ins.then_inc(mm.h, 1)
                    ctx.t(w1m)
                    g1war[p2] = mm.inc()
                    mmv = g1war[p2]
                    def gel(e, m=m, ps1=ps1, mmv=mmv, b_rdy=b_rdy):
                        e.wait_ge(mm.h, mmv)
                        e.wait_ge(bld.h, b_rdy)
                        e.activation(hu_bf[:, m * 512:(m + 1) * 512], ps1[:, :], AF.Gelu,
                                     bias=b1_sb[:, m:m + 1]).then_inc(sa.h, 1)
                    ctx.s(gel)
                    s.ps.release(i1, sa, sa.inc())
                h_done = sa.n
                for mc in range(8):
                    p2 = (mc // 2) % 2
                    if mc % 2 == 0:
                        def w2l(e, eidx=eidx, mc0=mc, p2=p2, war=g2war[p2]):
                            if war:
                                e.wait_ge(mm.h, war)
                            e.dma_start(out=w2b[p2][:, :],
                                        in_=w2[eidx, :, mc0 * 128:(mc0 + 2) * 128]
                                        .rearrange("(k p) m -> p k m", p=128)).then_inc(g2ld[p2].h, 16)
                        ctx.g(w2l)
                        g2cnt[p2] += 16
                    i2, ps2, wf2 = s.ps.acquire()
                    def w2m(e, mc=mc, p2=p2, ps2=ps2, wf2=wf2, gc=g2cnt[p2], h_done=h_done):
                        e.wait_ge(g2ld[p2].h, gc)
                        e.wait_ge(sa.h, h_done)
                        pwait(e, wf2)
                        for k in range(32):
                            ins = e.matmul(ps2[:, :], w2b[p2][:, k * 256 + (mc % 2) * 128:k * 256 + (mc % 2 + 1) * 128],
                                           hu_bf[:, k * 512:(k + 1) * 512],
                                           start=(k == 0), stop=(k == 31))
                        ins.then_inc(mm.h, 1)
                    ctx.t(w2m)
                    g2war[p2] = mm.inc()
                    mmv = g2war[p2]
                    if eidx == 0:
                        def dev(e, mc=mc, ps2=ps2, mmv=mmv, b_rdy=b_rdy, gb_done=gb_done):
                            e.wait_ge(mm.h, mmv)
                            e.wait_ge(ev.h, gb_done)
                            e.wait_ge(bld.h, b_rdy)
                            s.vself.edge(e, e.tensor_scalar_add(s.lnt[mc % 2][:, :], ps2[:, :], b2_sb[:, mc:mc + 1]))
                            e.tensor_mul(yacc[:, mc * 512:(mc + 1) * 512], s.lnt[mc % 2][:, :],
                                         gb_sb[:, 0:512]).then_inc(ev.h, 1)
                    else:
                        def dev(e, eidx=eidx, mc=mc, ps2=ps2, mmv=mmv, b_rdy=b_rdy, gb_done=gb_done):
                            e.wait_ge(mm.h, mmv)
                            e.wait_ge(ev.h, gb_done)
                            e.wait_ge(bld.h, b_rdy)
                            s.vself.edge(e, e.tensor_scalar_add(s.lnt[mc % 2][:, :], ps2[:, :], b2_sb[:, mc:mc + 1]))
                            s.vself.edge(e, e.tensor_mul(s.lnt[mc % 2][:, :], s.lnt[mc % 2][:, :],
                                         gb_sb[:, eidx * 512:(eidx + 1) * 512]))
                            e.tensor_add(yacc[:, mc * 512:(mc + 1) * 512],
                                         yacc[:, mc * 512:(mc + 1) * 512],
                                         s.lnt[mc % 2][:, :]).then_inc(ev.h, 1)
                    ctx.v(dev)
                    s.ps.release(i2, ev, ev.inc())
                bias_gate[0] = sa.n
                bias_gate[1] = ev.n
            y_done = ev.n

            for mc in range(8):
                def t2f(e, mc=mc, ntp=ntp, y_done=y_done, x_rdy=x_rdy, war=swar[ntp]):
                    e.wait_ge(ev.h, y_done)
                    e.wait_ge(xld[ntp].h, x_rdy)
                    if war:
                        e.wait_ge(ast[ntp].h, war)
                    e.tensor_add(xct[ntp][:, mc * 512:(mc + 1) * 512],
                                 xct[ntp][:, mc * 512:(mc + 1) * 512],
                                 yacc[:, mc * 512:(mc + 1) * 512]).then_inc(ev.h, 1)
                ctx.v(t2f)
                ev.inc()
            pre_done = ev.n
            xwar[ntp] = pre_done
            out_ev = plan_ln_tile(ctx, s,
                                  (lambda mc, ntp=ntp: xct[ntp][:, mc * 512:(mc + 1) * 512]),
                                  (lambda mc, ntp=ntp: xct[ntp][:, mc * 512:(mc + 1) * 512]),
                                  wait_ev=pre_done)
            def stf(e, nt=nt, ntp=ntp, out_ev=out_ev):
                e.wait_ge(ev.h, out_ev)
                e.dma_start(out=out_o.rearrange("(c p) n -> p c n", p=128)[:, :, nt * 512:(nt + 1) * 512],
                            in_=xct[ntp][:, :]).then_inc(ast[ntp].h, 16)
            ctx.y(stf)
            swar[ntp] = ast[ntp].inc(16)
        f0, f1 = ast[0].n, ast[1].n
        def finw(e):
            e.wait_ge(ast[0].h, f0)
            e.wait_ge(ast[1].h, f1)
        ctx.y(finw)
        ctx.finish()
    return nc



def _fm(a):
    return np.ascontiguousarray(a.T)


_NC_CACHE = {}

def _get(name, fn):
    if name not in _NC_CACHE:
        _NC_CACHE[name] = fn()
    return _NC_CACHE[name]


def run_A1(x_shards, wqkvT, projm):
    nc = _get("A1", build_A1)
    in_maps = [{"x": x_shards[c], "wqkvT": wqkvT, "projm": projm} for c in range(NCORE)]
    res = run_bass_kernel_spmd(nc, in_maps, core_ids=list(range(NCORE)))
    return res.results


def _bias_fm(b, c):
    return np.ascontiguousarray(np.asarray(b, dtype=np.float32).reshape(c, 128).T)


def stage_A(inp, x_shards):
    wqkvT = np.ascontiguousarray(np.asarray(inp["L0_attn_w_qkv"]).T)
    projm = np.ascontiguousarray(np.asarray(inp["L0_attn_proj"]).reshape(D, 256))
    r1 = run_A1(x_shards, wqkvT, projm)
    w_outT = np.ascontiguousarray(np.asarray(inp["L0_attn_w_out"]).T)
    w1T = np.ascontiguousarray(np.asarray(inp["L0_attn_w1"]).T)
    w2T = np.ascontiguousarray(np.asarray(inp["L0_attn_w2"]).T)
    common = {
        "w_outT": w_outT, "w1T": w1T, "w2T": w2T,
        "b_out": _bias_fm(inp["L0_attn_b_out"], 8),
        "gamma": _bias_fm(inp["L0_attn_gamma"], 8),
        "beta": _bias_fm(inp["L0_attn_beta"], 8),
        "b1": _bias_fm(inp["L0_attn_b1"], 32),
        "b2": _bias_fm(inp["L0_attn_b2"], 8),
    }
    nc = _get("A2", build_A2)
    in_maps = []
    for c in range(NCORE):
        m = dict(common)
        m["x"] = x_shards[c]
        m["qf"] = np.asarray(r1[c]["qf"])
        m["kvx_a"] = np.asarray(r1[c]["kvx"])
        m["kvx_b"] = np.asarray(r1[c ^ 1]["kvx"])
        in_maps.append(m)
    res = run_bass_kernel_spmd(nc, in_maps, core_ids=list(range(NCORE)))
    return [np.asarray(res.results[c]["xa"]) for c in range(NCORE)]


def _bf(a):
    import ml_dtypes
    return np.ascontiguousarray(np.asarray(a).astype(ml_dtypes.bfloat16))


def _ct(a):
    return np.ascontiguousarray(np.asarray(a, dtype=np.float32))


def _fm_shard(xf, c):
    return np.ascontiguousarray(xf[c * NL:(c + 1) * NL].T)


def stage_B(inp, xa_shards):
    import ml_dtypes
    rq = np.asarray(inp["L0_ffn_rope_q_w"]).reshape(16, 32, 256)
    rk = np.asarray(inp["L0_ffn_rope_k_w"]).reshape(16, 32, 1024)
    inv_freq = 1.0 / 10000.0 ** (np.arange(0, 32, 2, dtype=np.float64) / 32)
    tpos = np.arange(T, dtype=np.float64)
    fr = tpos[:, None] * inv_freq[None, :]
    cosT = np.ascontiguousarray(np.tile(np.cos(fr).T, (8, 1)).astype(np.float32))
    sinT = np.ascontiguousarray(np.tile(np.sin(fr).T, (8, 1)).astype(np.float32))
    common = {
        "ln1w": _bias_fm(inp["L0_ffn_ln1_w"], 8),
        "kvdT": _ct(np.asarray(inp["L0_ffn_kv_d"]).T),
        "qdT": _ct(np.asarray(inp["L0_ffn_q_d"]).T),
        "kunT": _ct(np.asarray(inp["L0_ffn_k_u"]).reshape(16, 64, 256)[:, 32:].reshape(512, 256).T),
        "qunT": _ct(np.asarray(inp["L0_ffn_q_u"]).reshape(16, 64, 256)[:, 32:].reshape(512, 256).T),
        "vuT": _ct(np.asarray(inp["L0_ffn_v_u"]).T),
        "rqaT": _ct(rq[:, :16].reshape(256, 256).T),
        "rqbT": _ct(rq[:, 16:].reshape(256, 256).T),
        "rkaT": _ct(rk[:, :16].reshape(256, 1024).T),
        "rkbT": _ct(rk[:, 16:].reshape(256, 1024).T),
    }
    nc = _get("B1", build_B1)
    in_maps = []
    for c in range(NCORE):
        m = dict(common)
        m["xa"] = xa_shards[c]
        half = c % 2
        m["cosf"] = np.ascontiguousarray(cosT[:, half * NL:(half + 1) * NL])
        m["sinf"] = np.ascontiguousarray(sinT[:, half * NL:(half + 1) * NL])
        in_maps.append(m)
    r1 = run_bass_kernel_spmd(nc, in_maps, core_ids=list(range(NCORE))).results
    Ks = [np.asarray(r1[c]["K"]) for c in range(NCORE)]
    Qs = [np.asarray(r1[c]["Q"]) for c in range(NCORE)]
    Vs = [np.asarray(r1[c]["V"]) for c in range(NCORE)]

    stair = (np.arange(2176)[None, :] >= (np.arange(128)[:, None] + 1023)).astype(ml_dtypes.bfloat16)
    stair = np.ascontiguousarray(stair)
    owT = _ct(np.asarray(inp["L0_ffn_o_w"]).T)
    t1s = [None] * NCORE
    for odd in (False, True):
        nc2 = _get(f"B2{'o' if odd else 'e'}", lambda odd=odd: build_B2(odd))
        cores = [1, 3, 5, 7] if odd else [0, 2, 4, 6]
        maps = []
        for c in cores:
            b = c // 2
            if odd:
                Kf = np.concatenate([Ks[2 * b], Ks[2 * b + 1]], axis=1)
                Vf = np.concatenate([Vs[2 * b], Vs[2 * b + 1]], axis=0)
            else:
                Kf, Vf = Ks[2 * b], Vs[2 * b]
            maps.append({"xa": xa_shards[c], "K": np.ascontiguousarray(Kf),
                         "Q": Qs[c], "V": np.ascontiguousarray(Vf),
                         "stair": stair, "owT": owT})
        r2 = run_bass_kernel_spmd(nc2, maps, core_ids=cores).results
        for i, c in enumerate(cores):
            t1s[c] = np.asarray(r2[i]["t1"])

    gT = np.concatenate([np.asarray(inp["L0_ffn_shared_gate"]),
                         np.asarray(inp["L0_ffn_routed_gate"])], axis=0).transpose(0, 2, 1)
    uT = np.concatenate([np.asarray(inp["L0_ffn_shared_up"]),
                         np.asarray(inp["L0_ffn_routed_up"])], axis=0).transpose(0, 2, 1)
    dT = np.concatenate([np.asarray(inp["L0_ffn_shared_down"]),
                         np.asarray(inp["L0_ffn_routed_down"])], axis=0).transpose(0, 2, 1)
    common3 = {
        "ln2w": _bias_fm(inp["L0_ffn_ln2_w"], 8),
        "rwT": _ct(np.asarray(inp["L0_ffn_router_w"]).T),
        "rbias": _ct(np.asarray(inp["L0_ffn_routing_bias"]).reshape(1, 7)),
        "gT": _ct(gT), "uT": _ct(uT), "dT": _ct(dT),
    }
    nc3 = _get("B3", build_B3)
    maps = []
    for c in range(NCORE):
        m = dict(common3)
        m["t1"] = t1s[c]
        m["xa"] = xa_shards[c]
        maps.append(m)
    r3 = run_bass_kernel_spmd(nc3, maps, core_ids=list(range(NCORE))).results
    return [np.asarray(r3[c]["xb"]) for c in range(NCORE)]


def stage_C(inp, xb_shards):
    import ml_dtypes
    common1 = {
        "WkT": _ct(np.asarray(inp["L1_attn_Wk"]).T),
        "WvT": _ct(np.asarray(inp["L1_attn_Wv"]).T),
        "bk": _ct(np.asarray(inp["L1_attn_bk"]).reshape(1, D)),
        "bv": _ct(np.asarray(inp["L1_attn_bv"]).reshape(1, D)),
    }
    nc1 = _get("C1", build_C1)
    maps = [dict(common1, xb=xb_shards[c]) for c in range(NCORE)]
    r1 = run_bass_kernel_spmd(nc1, maps, core_ids=list(range(NCORE))).results
    trimask = (np.arange(128)[None, :] >= np.arange(128)[:, None]).astype(ml_dtypes.bfloat16)
    common2 = {
        "WqT": _ct(np.asarray(inp["L1_attn_Wq"]).T),
        "WkT": _ct(np.asarray(inp["L1_attn_Wk"]).T),
        "WoT": _ct(np.asarray(inp["L1_attn_Wo"]).T),
        "bq": _bias_fm(inp["L1_attn_bq"], 8),
        "bkf": _bias_fm(inp["L1_attn_bk"], 8),
        "bo": _bias_fm(inp["L1_attn_bo"], 8),
        "trimask": np.ascontiguousarray(trimask),
    }
    nc2 = _get("C2", build_C2)
    maps = []
    for c in range(NCORE):
        m = dict(common2)
        m["xb"] = xb_shards[c]
        m["Ktm"] = np.asarray(r1[c]["Ktm"])
        m["Vtm"] = np.asarray(r1[c]["Vtm"])
        m["Sin"] = (np.asarray(r1[c - 1]["Sh"]) if c % 2 == 1
                    else np.zeros((128, 512), np.float32))
        maps.append(m)
    r2 = run_bass_kernel_spmd(nc2, maps, core_ids=list(range(NCORE))).results
    return [np.asarray(r2[c]["xc"]) for c in range(NCORE)]


def stage_D(inp, xc_shards):
    import ml_dtypes
    ltri = (np.arange(128)[:, None] < np.arange(128)[None, :]).astype(ml_dtypes.bfloat16)
    common1 = {
        "rwT": _ct(np.asarray(inp["L1_ffn_router_w"]).T),
        "ltri": np.ascontiguousarray(ltri),
    }
    nc1 = _get("D1", build_D1)
    maps = [dict(common1, xc=xc_shards[c]) for c in range(NCORE)]
    r1 = run_bass_kernel_spmd(nc1, maps, core_ids=list(range(NCORE))).results
    cnts_all = np.ascontiguousarray(
        np.concatenate([np.asarray(r1[c]["cnt"]) for c in range(NCORE)], axis=0))
    b1 = np.asarray(inp["L1_ffn_b1"]).reshape(8, 32, 128).transpose(0, 2, 1)
    b2 = np.asarray(inp["L1_ffn_b2"]).reshape(8, 8, 128).transpose(0, 2, 1)
    common2 = {
        "cnts": cnts_all,
        "w1": _ct(np.asarray(inp["L1_ffn_w1"])),
        "b1": _ct(b1),
        "w2": _ct(np.asarray(inp["L1_ffn_w2"])),
        "b2": _ct(b2),
    }
    nc2 = _get("D2", build_D2)
    maps = []
    for c in range(NCORE):
        m = dict(common2)
        m["xc"] = xc_shards[c]
        m["meta"] = np.asarray(r1[c]["meta"])
        m["pmask"] = np.ascontiguousarray((np.arange(8) < c).astype(np.float32).reshape(8, 1))
        maps.append(m)
    r2 = run_bass_kernel_spmd(nc2, maps, core_ids=list(range(NCORE))).results
    return [np.asarray(r2[c]["out"]) for c in range(NCORE)]


def kernel(**inputs):
    x = np.asarray(inputs["L0_attn_x"], dtype=np.float32).reshape(B * T, D)
    x_shards = [_fm_shard(x, c) for c in range(NCORE)]
    xa = stage_A(inputs, x_shards)
    xb = stage_B(inputs, xa)
    xc = stage_C(inputs, xb)
    outs = stage_D(inputs, xc)
    full = np.concatenate([o.T for o in outs], axis=0)
    return np.ascontiguousarray(full.reshape(B, T, D), dtype=np.float32)


# version 03 (final)
# speedup vs baseline: 1.2777x; 1.0421x over previous
import contextlib
import numpy as np
import concourse.bass as bass
import concourse.mybir as mybir
from concourse.bass_utils import run_bass_kernel_spmd

F32 = mybir.dt.float32
BF16 = mybir.dt.bfloat16
I32 = mybir.dt.int32
AF = mybir.ActivationFunctionType
ALU = mybir.AluOpType
AX = mybir.AxisListType

NCORE = 8
B, T, D = 4, 4096, 1024
NL = 2048
NCH = D // 128
NT = NL // 128
H, HD = 16, 64


class Sem:
    __slots__ = ("h", "n")
    def __init__(self, h):
        self.h = h
        self.n = 0
    def inc(self, k=1):
        self.n += k
        return self.n

class Pool:
    def __init__(self, slots, ev=None):
        self.slots = slots
        self.free_at = [None] * len(slots)
        self.rr = 0
    def acquire(self):
        i = self.rr % len(self.slots)
        self.rr += 1
        return i, self.slots[i], self.free_at[i]
    def release(self, i, sem, val=None):
        if val is None:
            raise TypeError("release(i, sem, val)")
        self.free_at[i] = (sem, val)


def pwait(e, free):
    if free is not None:
        e.wait_ge(free[0].h, free[1])


class VSelf:
    def __init__(self, sem):
        self.sem = sem
        self.n = 0
    def edge(self, e, ins):
        self.n += 1
        ins.then_inc(self.sem.h, 1)
        e.wait_ge(self.sem.h, self.n)

class Ctx:
    def __init__(self, nc, es):
        self.nc = nc
        self.es = es
        self.ops = {e: [] for e in ("g", "t", "v", "s", "y")}
        self._semi = 0
    def sbuf(self, name, shape, dtype):
        return self.es.enter_context(self.nc.sbuf_tensor(name, shape, dtype))
    def psum(self, name, shape, dtype=F32):
        return self.es.enter_context(self.nc.psum_tensor(name, shape, dtype))
    def sem(self, name=None):
        self._semi += 1
        return Sem(self.es.enter_context(self.nc.semaphore(name or f"s{self._semi}")))
    def g(self, f): self.ops["g"].append(f)
    def t(self, f): self.ops["t"].append(f)
    def v(self, f): self.ops["v"].append(f)
    def s(self, f): self.ops["s"].append(f)
    def y(self, f): self.ops["y"].append(f)
    def finish(self):
        blk = self.es.enter_context(self.nc.Block())
        ops = self.ops
        @blk.gpsimd
        def _(e):
            for f in ops["g"]: f(e)
        @blk.tensor
        def _(e):
            for f in ops["t"]: f(e)
        @blk.vector
        def _(e):
            for f in ops["v"]: f(e)
        @blk.scalar
        def _(e):
            for f in ops["s"]: f(e)
        @blk.sync
        def _(e):
            for f in ops["y"]: f(e)


def dma_in(ctx, dst_ap, src_ap, ld, queue="g"):
    def f(e):
        e.dma_start(out=dst_ap, in_=src_ap).then_inc(ld.h, 16)
    getattr(ctx, queue)(f)
    return ld.inc(16)


class EluEvac:
    def __init__(self, ctx, width, mm, va, sa, ev, tag):
        self.ctx = ctx
        self.w = width
        self.u = [ctx.sbuf(f"elu_u{tag}_{i}", [128, width], F32) for i in range(2)]
        self.e = [ctx.sbuf(f"elu_e{tag}_{i}", [128, width], F32) for i in range(2)]
        self.mm, self.va, self.sa, self.ev = mm, va, sa, ev
        self.i = 0
        self.sa_of = {}
        self.ev_of = {}
    def plan(self, ps_ap, out_ap, mmv, out_war=None):
        ctx, idx, p = self.ctx, self.i, self.i % 2
        self.i += 1
        u, eb = self.u[p], self.e[p]
        va, sa, ev, mm = self.va, self.sa, self.ev, self.mm
        war_sa = self.sa_of.get(idx - 2)
        def ev1(e):
            e.wait_ge(mm.h, mmv)
            if war_sa:
                e.wait_ge(sa.h, war_sa)
            e.tensor_scalar_min(u[:, :], ps_ap, 0.0).then_inc(va.h, 1)
        ctx.v(ev1)
        vav = va.inc()
        war_ev = self.ev_of.get(idx - 2)
        def ev2(e):
            e.wait_ge(va.h, vav)
            if war_ev:
                e.wait_ge(ev.h, war_ev)
            e.activation(eb[:, :], u[:, :], AF.Exp).then_inc(sa.h, 1)
        ctx.s(ev2)
        sav = sa.inc()
        self.sa_of[idx] = sav
        def ev3(e):
            e.wait_ge(sa.h, sav)
            if out_war is not None:
                e.wait_ge(out_war[0].h, out_war[1])
            e.scalar_tensor_tensor(out_ap, ps_ap, 0.0, eb[:, :], ALU.max, ALU.add).then_inc(ev.h, 1)
        ctx.v(ev3)
        evv = ev.inc()
        self.ev_of[idx] = evv
        return evv



def build_A1():
    nc = bass.Bass("TRN2", target_bir_lowering=False)
    x = nc.dram_tensor("x", [D, NL], F32, kind="ExternalInput").ap()
    wqkvT = nc.dram_tensor("wqkvT", [D, 3 * D], F32, kind="ExternalInput").ap()
    projm = nc.dram_tensor("projm", [D, 256], F32, kind="ExternalInput").ap()
    qf_o = nc.dram_tensor("qf", [128, 32 * NL], BF16, kind="ExternalOutput").ap()
    kvx_o = nc.dram_tensor("kvx", [128, H * 130], F32, kind="ExternalOutput").ap()

    es = contextlib.ExitStack()
    with es:
        ctx = Ctx(nc, es)
        x_bf = ctx.sbuf("x_bf", [128, NCH * NL], BF16)
        wqk = ctx.sbuf("wqk", [128, NCH * 2048], BF16)
        pj = ctx.sbuf("pj", [128, NCH * 256], BF16)
        qk_bf = ctx.sbuf("qk_bf", [128, 16 * NL], BF16)
        vaug = ctx.sbuf("vaug", [128, NT * H * 65], BF16)
        kf_h = ctx.sbuf("kf_h", [128, NT * 256], BF16)
        qf_t = ctx.sbuf("qf_t", [128, 4 * 512], BF16)
        kvx_sb = ctx.sbuf("kvx_sb", [128, H * 130], F32)
        wv = wqk

        ld = ctx.sem("ld")
        mm = ctx.sem("mm")
        ev = ctx.sem("ev")
        va = ctx.sem("va")
        sa = ctx.sem("sa")
        st = ctx.sem("st")
        qst = [ctx.sem(f"qst{i}") for i in range(4)]

        ps512 = Pool([ctx.psum(f"p5_{i}", [128, 512]) for i in range(4)], ev)
        pkf = Pool([ctx.psum(f"pf_{i}", [128, 512]) for i in range(2)], ev)
        pkv = Pool([ctx.psum(f"pv_{i}", [128, 512]) for i in range(2)], ev)

        elu_q = EluEvac(ctx, 512, mm, va, sa, ev, "q")
        elu_k = EluEvac(ctx, 256, mm, va, sa, ev, "k")

        ld_x = dma_in(ctx, x_bf[:, :], x.rearrange("(c p) n -> p c n", p=128), ld)
        ld_wqk = dma_in(ctx, wqk[:, :], wqkvT.rearrange("(k p) m -> p k m", p=128)[:, :, 0:2048], ld)
        ld_pj = dma_in(ctx, pj[:, :], projm.rearrange("(k p) m -> p k m", p=128), ld)
        ld_all = ld.n
        ctx.v(lambda e: e.memset(
            vaug[:, :].rearrange("p (a h c) -> p a h c", h=H, c=65)[:, :, :, 64:65], 1.0))

        qk_ev = {}
        for m in range(16):
            for nt in range(4):
                i, ps, wfree = ps512.acquire()
                def mmf(e, m=m, nt=nt, ps=ps, wfree=wfree):
                    e.wait_ge(ld.h, ld_all)
                    pwait(e, wfree)
                    for k in range(NCH):
                        ins = e.matmul(ps[:, :], wqk[:, k * 2048 + m * 128:k * 2048 + (m + 1) * 128],
                                       x_bf[:, k * NL + nt * 512:k * NL + (nt + 1) * 512],
                                       start=(k == 0), stop=(k == NCH - 1))
                    ins.then_inc(mm.h, 1)
                ctx.t(mmf)
                mmv = mm.inc()
                def evf(e, ps=ps, m=m, nt=nt, mmv=mmv):
                    e.wait_ge(mm.h, mmv)
                    e.tensor_copy(qk_bf[:, m * NL + nt * 512:m * NL + (nt + 1) * 512],
                                  ps[:, :]).then_inc(ev.h, 1)
                ctx.v(evf)
                qk_ev[(m, nt)] = ev.inc()
                ps512.release(i, ev, qk_ev[(m, nt)])

        qk_mm_done = mm.n
        def ld_wv_f(e):
            e.wait_ge(mm.h, qk_mm_done)
            e.dma_start(out=wv[:, 0:NCH * 1024],
                        in_=wqkvT.rearrange("(k p) m -> p k m", p=128)[:, :, 2048:3072]).then_inc(ld.h, 16)
        ctx.g(ld_wv_f)
        ld_wv = ld.inc(16)

        for mc in range(NT):
            for nt in range(2):
                i, ps, wfree = ps512.acquire()
                def mmf(e, mc=mc, nt=nt, ps=ps, wfree=wfree):
                    e.wait_ge(ld.h, ld_wv)
                    pwait(e, wfree)
                    for k in range(NCH):
                        ins = e.matmul(ps[:, :], x_bf[:, k * NL + mc * 128:k * NL + (mc + 1) * 128],
                                       wv[:, k * 1024 + nt * 512:k * 1024 + (nt + 1) * 512],
                                       start=(k == 0), stop=(k == NCH - 1))
                    ins.then_inc(mm.h, 1)
                ctx.t(mmf)
                mmv = mm.inc()
                dst = vaug[:, :].rearrange("p (a h c) -> p a h c", h=H, c=65)[:, mc, nt * 8:(nt + 1) * 8, 0:64]
                def evf(e, ps=ps, dst=dst, mmv=mmv):
                    e.wait_ge(mm.h, mmv)
                    e.tensor_copy(dst, ps[:, :]).then_inc(ev.h, 1)
                ctx.v(evf)
                ps512.release(i, ev, ev.inc())
        qkv_done = ev.n

        qf_i = 0
        qf_war = {}
        for h in range(H):
            rb = (h % 2) * 64
            qc = h // 2
            for fc in range(2):
                for nt in range(4):
                    i, ps, wfree = ps512.acquire()
                    def mmf(e, rb=rb, qc=qc, fc=fc, nt=nt, ps=ps, wfree=wfree):
                        e.wait_ge(ld.h, ld_all)
                        e.wait_ge(ev.h, qkv_done)
                        pwait(e, wfree)
                        e.matmul(ps[:, :], pj[rb:rb + 64, qc * 256 + fc * 128:qc * 256 + (fc + 1) * 128],
                                 qk_bf[rb:rb + 64, qc * NL + nt * 512:qc * NL + (nt + 1) * 512],
                                 start=True, stop=True).then_inc(mm.h, 1)
                    ctx.t(mmf)
                    mmv = mm.inc()
                    col = (h * 2 + fc) * NL + nt * 512
                    qslot = qf_i % 4
                    war = qf_war.get(qslot)
                    evv = elu_q.plan(ps[:, :], qf_t[:, qslot * 512:(qslot + 1) * 512], mmv,
                                     out_war=(qst[qslot], war) if war else None)
                    ps512.release(i, ev, evv)
                    def stf(e, qslot=qslot, col=col, evv=evv):
                        e.wait_ge(ev.h, evv)
                        e.dma_start(out=qf_o[:, col:col + 512],
                                    in_=qf_t[:, qslot * 512:(qslot + 1) * 512]).then_inc(qst[qslot].h, 16)
                    ctx.y(stf)
                    qf_war[qslot] = qst[qslot].inc(16)
                    qf_i += 1

        prev_kv_mm = 0
        for h in range(H):
            rb = (h % 2) * 64
            kc = 8 + h // 2
            for mc in range(NT):
                i, ps, wfree = pkf.acquire()
                def mmf(e, rb=rb, kc=kc, h=h, mc=mc, ps=ps, wfree=wfree, pkm=prev_kv_mm):
                    e.wait_ge(ev.h, qkv_done)
                    pwait(e, wfree)
                    if pkm:
                        e.wait_ge(mm.h, pkm)
                    e.matmul(ps[:, 0:256],
                             qk_bf[rb:rb + 64, kc * NL + mc * 128:kc * NL + (mc + 1) * 128],
                             pj[rb:rb + 64, (h // 2) * 256:(h // 2) * 256 + 256],
                             start=True, stop=True).then_inc(mm.h, 1)
                ctx.t(mmf)
                mmv = mm.inc()
                evv = elu_k.plan(ps[:, 0:256], kf_h[:, mc * 256:(mc + 1) * 256], mmv)
                pkf.release(i, ev, evv)
            kf_done = ev.n

            ik, psv, wfree = pkv.acquire()
            def kvmm(e, h=h, psv=psv, wfree=wfree, kf_done=kf_done):
                e.wait_ge(ev.h, kf_done)
                pwait(e, wfree)
                for fc in range(2):
                    for mc in range(NT):
                        ins = e.matmul(psv[:, fc * 65:(fc + 1) * 65],
                                       kf_h[:, mc * 256 + fc * 128:mc * 256 + fc * 128 + 128],
                                       vaug[:, (mc * H + h) * 65:(mc * H + h) * 65 + 65],
                                       start=(mc == 0), stop=(mc == NT - 1))
                ins.then_inc(mm.h, 1)
            ctx.t(kvmm)
            prev_kv_mm = mm.inc()
            def kvev(e, h=h, psv=psv, mmv=prev_kv_mm):
                e.wait_ge(mm.h, mmv)
                e.tensor_copy(kvx_sb[:, h * 130:(h + 1) * 130], psv[:, 0:130]).then_inc(ev.h, 1)
            ctx.v(kvev)
            pkv.release(ik, ev, ev.inc())

        all_ev = ev.n
        def stores(e):
            e.wait_ge(ev.h, all_ev)
            e.dma_start(out=kvx_o, in_=kvx_sb[:, :]).then_inc(st.h, 16)
        ctx.y(stores)
        fin = st.inc(16)
        qfin = [q.n for q in qst]
        def finw(e):
            e.wait_ge(st.h, fin)
            for q, v_ in zip(qst, qfin):
                e.wait_ge(q.h, v_)
        ctx.y(finw)

        ctx.finish()
    return nc




class S:
    def __init__(self, ctx, n512=4):
        self.ctx = ctx
        self.ld = ctx.sem("ld")
        self.mm = ctx.sem("mm")
        self.ev = ctx.sem("ev")
        self.va = ctx.sem("va")
        self.sa = ctx.sem("sa")
        self.st = ctx.sem("st")
        self.ps = Pool([ctx.psum(f"ps_{i}", [128, 512]) for i in range(n512)], self.ev)
        self.ones128 = ctx.sbuf("ones128", [128, 1], F32)
        self.onesrow = ctx.sbuf("onesrow", [1, 128], F32)
        self.onesrow_bf = ctx.sbuf("s_onesrow_bf", [1, 128], BF16)
        self.sq = [ctx.sbuf(f"sq_{i}", [128, 512], F32) for i in range(2)]
        self.rowbuf = ctx.sbuf("rowbuf", [1, 4 * 512], F32)
        self.lnt = [ctx.sbuf(f"lnt_{i}", [128, 512], F32) for i in range(2)]
        self._sq_war = {}
        self._sq_i = 0
        self.eps_ln = ctx.sbuf("eps_ln", [1, 1], F32)
        self.vself = VSelf(ctx.sem("vself"))
        def _init(e):
            e.memset(self.ones128[:, :], 1.0)
            e.memset(self.onesrow[:, :], 1.0)
            e.memset(self.onesrow_bf[:, :], 1.0)
            e.memset(self.eps_ln[:, :], 1e-5).then_inc(self.ev.h, 1)
        ctx.v(_init)
        self.init_ev = self.ev.inc()


def plan_ln_tile(ctx, s, src_chunk, dst_chunk, *, gamma=None, beta=None,
                 bf_chunk=None, eps=1e-5, nch=8, dim=None, wait_ev=0, wait_mm=0):
    dim = dim or nch * 128
    mm, ev, va, sa = s.mm, s.ev, s.va, s.sa
    i_sum, ps_sum, w_sum = s.ps.acquire()
    i_sq, ps_sq, w_sq = s.ps.acquire()
    def sumf(e, w_sum=w_sum):
        e.wait_ge(ev.h, max(wait_ev, s.init_ev))
        if wait_mm:
            e.wait_ge(mm.h, wait_mm)
        pwait(e, w_sum)
        for mc in range(nch):
            ins = e.matmul(ps_sum[0:1, :], s.ones128[:, :], src_chunk(mc),
                           start=(mc == 0), stop=(mc == nch - 1))
        ins.then_inc(mm.h, 1)
    ctx.t(sumf)
    mm.inc()
    for mc in range(nch):
        p = s._sq_i % 2
        s._sq_i += 1
        war = s._sq_war.get(s._sq_i - 3)
        def sqf(e, mc=mc, p=p, war=war):
            if war:
                e.wait_ge(mm.h, war)
            if wait_ev:
                e.wait_ge(ev.h, wait_ev)
            e.activation(s.sq[p][:, :], src_chunk(mc), AF.Square).then_inc(sa.h, 1)
        ctx.s(sqf)
        sav = sa.inc()
        def sqmm(e, mc=mc, p=p, sav=sav, w_sq=w_sq):
            e.wait_ge(sa.h, sav)
            if mc == 0:
                pwait(e, w_sq)
            e.matmul(ps_sq[0:1, :], s.ones128[:, :], s.sq[p][:, :],
                     start=(mc == 0), stop=(mc == nch - 1)).then_inc(mm.h, 1)
        ctx.t(sqmm)
        s._sq_war[s._sq_i - 1] = mm.inc()
    mm_sq = mm.n
    R = s.rowbuf
    mu, var, rstd, bsh = R[0:1, 0:512], R[0:1, 512:1024], R[0:1, 1024:1536], R[0:1, 1536:2048]
    def rowv(e):
        e.wait_ge(mm.h, mm_sq)
        s.vself.edge(e, e.tensor_scalar_mul(mu, ps_sum[0:1, :], 1.0 / dim))
        s.vself.edge(e, e.tensor_mul(var, mu, mu))
        e.scalar_tensor_tensor(var, ps_sq[0:1, :], 1.0 / dim, var,
                               ALU.mult, ALU.subtract).then_inc(ev.h, 1)
    ctx.v(rowv)
    stats_free = ev.inc()
    s.ps.release(i_sum, ev, stats_free)
    s.ps.release(i_sq, ev, stats_free)
    def rows_(e):
        e.wait_ge(ev.h, stats_free)
        e.activation(rstd, var, AF.Sqrt, bias=s.eps_ln[0:1, 0:1]).then_inc(sa.h, 1)
    ctx.s(rows_)
    sav = sa.inc()
    def rowv3(e, sav=sav):
        e.wait_ge(sa.h, sav)
        s.vself.edge(e, e.reciprocal(rstd, rstd))
        e.scalar_tensor_tensor(bsh, mu, -1.0, rstd, ALU.mult, ALU.mult).then_inc(ev.h, 1)
    ctx.v(rowv3)
    rows_done = ev.inc()
    i_a, ps_a, w_a = s.ps.acquire()
    i_b, ps_b, w_b = s.ps.acquire()
    def bcast(e, w_a=w_a, w_b=w_b):
        e.wait_ge(ev.h, rows_done)
        pwait(e, w_a)
        pwait(e, w_b)
        e.matmul(ps_a[:, :], s.onesrow[:, :], rstd, start=True, stop=True)
        e.matmul(ps_b[:, :], s.onesrow[:, :], bsh, start=True, stop=True).then_inc(mm.h, 1)
    ctx.t(bcast)
    mm_ab = mm.inc()
    last_ev = None
    for mc in range(nch):
        p = mc % 2
        t1 = s.lnt[p]
        def app(e, mc=mc, t1=t1):
            e.wait_ge(mm.h, mm_ab)
            s.vself.edge(e, e.tensor_mul(t1[:, :], src_chunk(mc), ps_a[:, :]))
            s.vself.edge(e, e.tensor_add(t1[:, :], t1[:, :], ps_b[:, :]))
            dst = dst_chunk(mc) if dst_chunk is not None else bf_chunk(mc)
            if gamma is not None:
                ins = e.tensor_scalar(dst, t1[:, :], gamma[:, mc:mc + 1],
                                      beta[:, mc:mc + 1], ALU.mult, ALU.add)
            else:
                ins = e.tensor_copy(dst, t1[:, :])
            if bf_chunk is not None and dst_chunk is not None:
                s.vself.edge(e, ins)
                ins = e.tensor_copy(bf_chunk(mc), dst_chunk(mc))
            ins.then_inc(ev.h, 1)
        ctx.v(app)
        last_ev = ev.inc()
    s.ps.release(i_a, ev, last_ev)
    s.ps.release(i_b, ev, last_ev)
    return last_ev




def build_A2():
    nc = bass.Bass("TRN2", target_bir_lowering=False)
    x = nc.dram_tensor("x", [D, NL], F32, kind="ExternalInput").ap()
    qf = nc.dram_tensor("qf", [128, 32 * NL], BF16, kind="ExternalInput").ap()
    kvx_a = nc.dram_tensor("kvx_a", [128, H * 130], F32, kind="ExternalInput").ap()
    kvx_b = nc.dram_tensor("kvx_b", [128, H * 130], F32, kind="ExternalInput").ap()
    w_outT = nc.dram_tensor("w_outT", [D, D], F32, kind="ExternalInput").ap()
    b_out = nc.dram_tensor("b_out", [128, 8], F32, kind="ExternalInput").ap()
    gamma = nc.dram_tensor("gamma", [128, 8], F32, kind="ExternalInput").ap()
    beta = nc.dram_tensor("beta", [128, 8], F32, kind="ExternalInput").ap()
    w1T = nc.dram_tensor("w1T", [D, 4 * D], F32, kind="ExternalInput").ap()
    b1 = nc.dram_tensor("b1", [128, 32], F32, kind="ExternalInput").ap()
    w2T = nc.dram_tensor("w2T", [4 * D, D], F32, kind="ExternalInput").ap()
    b2 = nc.dram_tensor("b2", [128, 8], F32, kind="ExternalInput").ap()
    xa_o = nc.dram_tensor("xa", [D, NL], F32, kind="ExternalOutput").ap()

    es = contextlib.ExitStack()
    with es:
        ctx = Ctx(nc, es)
        s = S(ctx, n512=8)
        mm, ev, va, sa, ld, st = s.mm, s.ev, s.va, s.sa, s.ld, s.st

        x0 = [ctx.sbuf(f"x0_{i}", [128, 8 * 512], F32) for i in range(2)]
        rs = [ctx.sbuf(f"rs_{i}", [128, 8 * 512], F32) for i in range(2)]
        attn_bf = ctx.sbuf("attn_bf", [128, 8 * 512], BF16)
        x1_bf = ctx.sbuf("x1_bf", [128, 8 * 512], BF16)
        h_bf = ctx.sbuf("h_bf", [128, 32 * 512], BF16)
        wout = ctx.sbuf("wout", [128, 8 * 1024], BF16)
        w1b = [ctx.sbuf(f"w1b_{i}", [128, 8 * 128], BF16) for i in range(2)]
        w2b = [ctx.sbuf(f"w2b_{i}", [128, 32 * 128], BF16) for i in range(2)]
        qft = [ctx.sbuf(f"qft_{i}", [128, 512], BF16) for i in range(8)]
        kvt_a = ctx.sbuf("kvt_a", [128, H * 130], F32)
        kvt_b = ctx.sbuf("kvt_b", [128, H * 130], F32)
        kv_bf = ctx.sbuf("kv_bf", [128, H * 130], BF16)
        bo_fm = ctx.sbuf("bo_fm", [128, 8], F32)
        g_fm = ctx.sbuf("g_fm", [128, 8], F32)
        be_fm = ctx.sbuf("be_fm", [128, 8], F32)
        b2_fm = ctx.sbuf("b2_fm", [128, 8], F32)
        b1_fm = ctx.sbuf("b1_fm", [128, 32], F32)
        onesrow_bf = ctx.sbuf("onesrow_bf", [1, 128], BF16)
        zr_bf = ctx.sbuf("zr_bf", [1, 2 * 512], BF16)
        zb_sb = ctx.sbuf("zb_sb", [128, 512], F32)

        x0ld = [ctx.sem(f"x0ld{i}") for i in range(2)]
        qld = [ctx.sem(f"qld{i}") for i in range(8)]
        w1ld = [ctx.sem(f"w1ld{i}") for i in range(2)]
        w2ld = [ctx.sem(f"w2ld{i}") for i in range(2)]
        xst = [ctx.sem(f"xst{i}") for i in range(2)]

        ctx.v(lambda e: e.memset(onesrow_bf[:, :], 1.0))

        dma_in(ctx, wout[:, :], w_outT.rearrange("(k p) m -> p k m", p=128), ld)
        dma_in(ctx, kvt_a[:, :], kvx_a, ld)
        dma_in(ctx, kvt_b[:, :], kvx_b, ld)
        dma_in(ctx, bo_fm[:, :], b_out, ld)
        dma_in(ctx, g_fm[:, :], gamma, ld)
        dma_in(ctx, be_fm[:, :], beta, ld)
        dma_in(ctx, b2_fm[:, :], b2, ld)
        dma_in(ctx, b1_fm[:, :], b1, ld)
        ld_all = ld.n
        def kvadd(e):
            e.wait_ge(ld.h, ld_all)
            e.tensor_add(kv_bf[:, :], kvt_a[:, :], kvt_b[:, :]).then_inc(ev.h, 1)
        ctx.v(kvadd)
        kv_ready = ev.inc()

        qf_i = 0
        qslot_war = [0] * 8
        qld_cnt = [0] * 8
        x0_war = [0, 0]
        store_war = [0, 0]
        w1_war = [0, 0]
        w2_war = [0, 0]
        w1_cnt = [0, 0]
        w2_cnt = [0, 0]

        for nt in range(4):
            ntp = nt % 2
            def x0f(e, nt=nt, ntp=ntp):
                e.dma_start(out=x0[ntp][:, :],
                            in_=x.rearrange("(c p) n -> p c n", p=128)[:, :, nt * 512:(nt + 1) * 512]
                            ).then_inc(x0ld[ntp].h, 16)
            if nt >= 2:
                war_val = x0_war[ntp]
                def x0g(e, war_val=war_val, f=x0f):
                    e.wait_ge(ev.h, war_val)
                    f(e)
                ctx.g(x0g)
            else:
                ctx.g(x0f)
            x0_rdy = x0ld[ntp].inc(16)

            for hp in range(8):
                h0, h1 = 2 * hp, 2 * hp + 1
                tiles = {}
                for (hh, fc) in [(h0, 0), (h0, 1), (h1, 0), (h1, 1)]:
                    sl = qf_i % 8
                    qf_i += 1
                    war = qslot_war[sl]
                    def qlf(e, hh=hh, fc=fc, sl=sl, war=war, nt=nt):
                        if war:
                            e.wait_ge(mm.h, war)
                        e.dma_start(out=qft[sl][:, :],
                                    in_=qf[:, (hh * 2 + fc) * NL + nt * 512:(hh * 2 + fc) * NL + (nt + 1) * 512]
                                    ).then_inc(qld[sl].h, 16)
                    ctx.g(qlf)
                    qld_cnt[sl] += 16
                    tiles[(hh, fc)] = (sl, qld_cnt[sl])
                zrow_done = []
                for j, hh in enumerate((h0, h1)):
                    iz, psz, wfree = s.ps.acquire()
                    def zmm(e, hh=hh, psz=psz, wfree=wfree, tiles=tiles):
                        e.wait_ge(ev.h, kv_ready)
                        pwait(e, wfree)
                        for fc in range(2):
                            sl, cnt = tiles[(hh, fc)]
                            e.wait_ge(qld[sl].h, cnt)
                            ins = e.matmul(psz[0:1, :], kv_bf[:, hh * 130 + fc * 65 + 64:hh * 130 + fc * 65 + 65],
                                           qft[sl][:, :], start=(fc == 0), stop=(fc == 1))
                        ins.then_inc(mm.h, 1)
                    ctx.t(zmm)
                    mmv = mm.inc()
                    def zrv(e, j=j, psz=psz, mmv=mmv):
                        e.wait_ge(mm.h, mmv)
                        s.vself.edge(e, e.tensor_scalar_add(s.rowbuf[0:1, 0:512], psz[0:1, :], 1e-8))
                        with nc.allow_low_precision(reason="z normalizer is a bf16 matmul operand"):
                            e.reciprocal(zr_bf[0:1, j * 512:(j + 1) * 512], s.rowbuf[0:1, 0:512]).then_inc(ev.h, 1)
                    ctx.v(zrv)
                    s.ps.release(iz, ev, ev.inc())
                zr_done = ev.n
                izb, pzb, wfree = s.ps.acquire()
                def zbm(e, pzb=pzb, wfree=wfree, zr_done=zr_done):
                    e.wait_ge(ev.h, zr_done)
                    pwait(e, wfree)
                    e.matmul(pzb[0:64, :], onesrow_bf[0:1, 0:64], zr_bf[0:1, 0:512], start=True, stop=True)
                    e.matmul(pzb[64:128, :], onesrow_bf[0:1, 0:64], zr_bf[0:1, 512:1024],
                             start=True, stop=True).then_inc(mm.h, 1)
                ctx.t(zbm)
                mm_zb = mm.inc()
                def zbe(e, pzb=pzb, mm_zb=mm_zb):
                    e.wait_ge(mm.h, mm_zb)
                    e.tensor_copy(zb_sb[:, :], pzb[:, :]).then_inc(ev.h, 1)
                ctx.v(zbe)
                s.ps.release(izb, ev, ev.inc())
                zb_done = ev.n
                ia, psa, wfree = s.ps.acquire()
                def amm(e, h0=h0, h1=h1, psa=psa, wfree=wfree, tiles=tiles):
                    e.wait_ge(ev.h, kv_ready)
                    pwait(e, wfree)
                    for j, hh in enumerate((h0, h1)):
                        for fc in range(2):
                            sl, cnt = tiles[(hh, fc)]
                            e.wait_ge(qld[sl].h, cnt)
                            ins = e.matmul(psa[j * 64:(j + 1) * 64, :],
                                           kv_bf[:, hh * 130 + fc * 65:hh * 130 + fc * 65 + 64],
                                           qft[sl][:, :], start=(fc == 0), stop=(fc == 1))
                    ins.then_inc(mm.h, 1)
                ctx.t(amm)
                mm_a = mm.inc()
                for (hh, fc) in tiles:
                    qslot_war[tiles[(hh, fc)][0]] = mm_a
                def aev(e, hp=hp, psa=psa, mm_a=mm_a, zb_done=zb_done):
                    e.wait_ge(mm.h, mm_a)
                    e.wait_ge(ev.h, zb_done)
                    e.tensor_mul(attn_bf[:, hp * 512:(hp + 1) * 512], psa[:, :], zb_sb[:, :]).then_inc(ev.h, 1)
                ctx.v(aev)
                s.ps.release(ia, ev, ev.inc())
            attn_done = ev.n

            for mc in range(8):
                io, pso, wfree = s.ps.acquire()
                def omm(e, mc=mc, pso=pso, wfree=wfree, attn_done=attn_done):
                    e.wait_ge(ld.h, ld_all)
                    e.wait_ge(ev.h, attn_done)
                    pwait(e, wfree)
                    for k in range(8):
                        ins = e.matmul(pso[:, :], wout[:, k * 1024 + mc * 128:k * 1024 + (mc + 1) * 128],
                                       attn_bf[:, k * 512:(k + 1) * 512],
                                       start=(k == 0), stop=(k == 7))
                    ins.then_inc(mm.h, 1)
                ctx.t(omm)
                mmv = mm.inc()
                def oev(e, mc=mc, pso=pso, mmv=mmv, ntp=ntp, x0_rdy=x0_rdy, swar=store_war[ntp]):
                    e.wait_ge(mm.h, mmv)
                    e.wait_ge(x0ld[ntp].h, x0_rdy)
                    if swar:
                        e.wait_ge(xst[ntp].h, swar)
                    e.scalar_tensor_tensor(rs[ntp][:, mc * 512:(mc + 1) * 512], pso[:, :],
                                           bo_fm[:, mc:mc + 1], x0[ntp][:, mc * 512:(mc + 1) * 512],
                                           ALU.add, ALU.add).then_inc(ev.h, 1)
                ctx.v(oev)
                s.ps.release(io, ev, ev.inc())
            r1_done = ev.n

            ln1_ev = plan_ln_tile(ctx, s,
                                  (lambda mc, ntp=ntp: rs[ntp][:, mc * 512:(mc + 1) * 512]),
                                  (lambda mc, ntp=ntp: rs[ntp][:, mc * 512:(mc + 1) * 512]),
                                  gamma=g_fm, beta=be_fm,
                                  bf_chunk=(lambda mc: x1_bf[:, mc * 512:(mc + 1) * 512]),
                                  wait_ev=r1_done)

            for m in range(32):
                p = m % 2
                gi = nt * 32 + m
                war = w1_war[p]
                def w1l(e, m=m, p=p, war=war):
                    if war:
                        e.wait_ge(mm.h, war)
                    e.dma_start(out=w1b[p][:, :],
                                in_=w1T.rearrange("(k p) m -> p k m", p=128)[:, :, m * 128:(m + 1) * 128]
                                ).then_inc(w1ld[p].h, 16)
                ctx.g(w1l)
                w1_cnt[p] += 16
                i1, ps1, wfree = s.ps.acquire()
                def w1m(e, m=m, p=p, ps1=ps1, wfree=wfree, cnt=w1_cnt[p], ln1_ev=ln1_ev):
                    e.wait_ge(w1ld[p].h, cnt)
                    e.wait_ge(ev.h, ln1_ev)
                    pwait(e, wfree)
                    for k in range(8):
                        ins = e.matmul(ps1[:, :], w1b[p][:, k * 128:(k + 1) * 128],
                                       x1_bf[:, k * 512:(k + 1) * 512],
                                       start=(k == 0), stop=(k == 7))
                    ins.then_inc(mm.h, 1)
                ctx.t(w1m)
                w1_war[p] = mm.inc()
                mmv = w1_war[p]
                def gel(e, m=m, ps1=ps1, mmv=mmv):
                    e.wait_ge(mm.h, mmv)
                    e.activation(h_bf[:, m * 512:(m + 1) * 512], ps1[:, :], AF.Gelu,
                                 bias=b1_fm[:, m:m + 1]).then_inc(sa.h, 1)
                ctx.s(gel)
                s.ps.release(i1, sa, sa.inc())
            gelu_done = sa.n

            for m2 in range(8):
                p = m2 % 2
                war = w2_war[p]
                def w2l(e, m2=m2, p=p, war=war):
                    if war:
                        e.wait_ge(mm.h, war)
                    e.dma_start(out=w2b[p][:, :],
                                in_=w2T.rearrange("(k p) m -> p k m", p=128)[:, :, m2 * 128:(m2 + 1) * 128]
                                ).then_inc(w2ld[p].h, 16)
                ctx.g(w2l)
                w2_cnt[p] += 16
                i2, ps2, wfree = s.ps.acquire()
                def w2m(e, m2=m2, p=p, ps2=ps2, wfree=wfree, cnt=w2_cnt[p], gelu_done=gelu_done):
                    e.wait_ge(w2ld[p].h, cnt)
                    e.wait_ge(sa.h, gelu_done)
                    pwait(e, wfree)
                    for k in range(32):
                        ins = e.matmul(ps2[:, :], w2b[p][:, k * 128:(k + 1) * 128],
                                       h_bf[:, k * 512:(k + 1) * 512],
                                       start=(k == 0), stop=(k == 31))
                    ins.then_inc(mm.h, 1)
                ctx.t(w2m)
                w2_war[p] = mm.inc()
                mmv = w2_war[p]
                def x2e(e, m2=m2, ps2=ps2, mmv=mmv, ntp=ntp):
                    e.wait_ge(mm.h, mmv)
                    e.scalar_tensor_tensor(rs[ntp][:, m2 * 512:(m2 + 1) * 512], ps2[:, :],
                                           b2_fm[:, m2:m2 + 1], rs[ntp][:, m2 * 512:(m2 + 1) * 512],
                                           ALU.add, ALU.add).then_inc(ev.h, 1)
                ctx.v(x2e)
                s.ps.release(i2, ev, ev.inc())
            x2_done = ev.n

            ln2_ev = plan_ln_tile(ctx, s,
                                  (lambda mc, ntp=ntp: rs[ntp][:, mc * 512:(mc + 1) * 512]),
                                  (lambda mc, ntp=ntp: rs[ntp][:, mc * 512:(mc + 1) * 512]),
                                  gamma=g_fm, beta=be_fm, wait_ev=x2_done)

            for mc in range(8):
                def padd(e, mc=mc, ntp=ntp, ln2_ev=ln2_ev):
                    e.wait_ge(ev.h, ln2_ev)
                    e.tensor_add(rs[ntp][:, mc * 512:(mc + 1) * 512],
                                 rs[ntp][:, mc * 512:(mc + 1) * 512],
                                 x0[ntp][:, mc * 512:(mc + 1) * 512]).then_inc(ev.h, 1)
                ctx.v(padd)
                ev.inc()
            x0_war[ntp] = ev.n
            pre_done = ev.n
            out_ev = plan_ln_tile(ctx, s,
                                  (lambda mc, ntp=ntp: rs[ntp][:, mc * 512:(mc + 1) * 512]),
                                  (lambda mc, ntp=ntp: rs[ntp][:, mc * 512:(mc + 1) * 512]),
                                  wait_ev=pre_done)
            def stf(e, nt=nt, ntp=ntp, out_ev=out_ev):
                e.wait_ge(ev.h, out_ev)
                e.dma_start(out=xa_o.rearrange("(c p) n -> p c n", p=128)[:, :, nt * 512:(nt + 1) * 512],
                            in_=rs[ntp][:, :]).then_inc(xst[ntp].h, 16)
            ctx.y(stf)
            store_war[ntp] = xst[ntp].inc(16)

        f0, f1 = xst[0].n, xst[1].n
        def finw(e):
            e.wait_ge(xst[0].h, f0)
            e.wait_ge(xst[1].h, f1)
        ctx.y(finw)
        ctx.finish()
    return nc




def build_B1():
    nc = bass.Bass("TRN2", target_bir_lowering=False)
    xa = nc.dram_tensor("xa", [D, NL], F32, kind="ExternalInput").ap()
    ln1w = nc.dram_tensor("ln1w", [128, 8], F32, kind="ExternalInput").ap()
    wdefs = {
        "kvd": (nc.dram_tensor("kvdT", [D, 256], F32, kind="ExternalInput").ap(), 8, 256),
        "qd": (nc.dram_tensor("qdT", [D, 256], F32, kind="ExternalInput").ap(), 8, 256),
        "kun": (nc.dram_tensor("kunT", [256, 512], F32, kind="ExternalInput").ap(), 2, 512),
        "qun": (nc.dram_tensor("qunT", [256, 512], F32, kind="ExternalInput").ap(), 2, 512),
        "vu": (nc.dram_tensor("vuT", [256, D], F32, kind="ExternalInput").ap(), 2, 1024),
        "rqa": (nc.dram_tensor("rqaT", [256, 256], F32, kind="ExternalInput").ap(), 2, 256),
        "rqb": (nc.dram_tensor("rqbT", [256, 256], F32, kind="ExternalInput").ap(), 2, 256),
        "rka": (nc.dram_tensor("rkaT", [D, 256], F32, kind="ExternalInput").ap(), 8, 256),
        "rkb": (nc.dram_tensor("rkbT", [D, 256], F32, kind="ExternalInput").ap(), 8, 256),
    }
    cosf = nc.dram_tensor("cosf", [128, NL], F32, kind="ExternalInput").ap()
    sinf = nc.dram_tensor("sinf", [128, NL], F32, kind="ExternalInput").ap()
    K_o = nc.dram_tensor("K", [D, NL], BF16, kind="ExternalOutput").ap()
    Q_o = nc.dram_tensor("Q", [D, NL], BF16, kind="ExternalOutput").ap()
    V_o = nc.dram_tensor("V", [NL, D], BF16, kind="ExternalOutput").ap()

    es = contextlib.ExitStack()
    with es:
        ctx = Ctx(nc, es)
        s = S(ctx, n512=8)
        mm, ev, va, sa, ld, st = s.mm, s.ev, s.va, s.sa, s.ld, s.st

        xat = [ctx.sbuf(f"xat{i}", [128, 8 * 512], F32) for i in range(2)]
        xl_bf = ctx.sbuf("xl_bf", [128, NCH * NL], BF16)
        lat_bf = ctx.sbuf("lat_bf", [128, 4 * NL], BF16)
        nopet = [ctx.sbuf(f"nopet{i}", [128, NL], BF16) for i in range(2)]
        vtm = [ctx.sbuf(f"vtm{i}", [128, 1024], BF16) for i in range(2)]
        ra = ctx.sbuf("ra", [128, 2 * 1024], F32)
        rb = ctx.sbuf("rb", [128, 2 * 1024], F32)
        lo = ctx.sbuf("lo", [128, 2 * 1024], BF16)
        hi = ctx.sbuf("hi", [128, 2 * 1024], BF16)
        tA = ctx.sbuf("tA", [128, 1024], F32)
        tB = ctx.sbuf("tB", [128, 1024], F32)
        tC = ctx.sbuf("tC", [128, 1024], F32)
        tD = ctx.sbuf("tD", [128, 1024], F32)
        cs = ctx.sbuf("cs", [128, NL], F32)
        sn = ctx.sbuf("sn", [128, NL], F32)
        ln1_fm = ctx.sbuf("ln1_fm", [128, 8], F32)
        zeros8 = ctx.sbuf("zeros8", [128, 8], F32)
        x0ld = [ctx.sem(f"x0ld{i}") for i in range(2)]
        nst = [ctx.sem(f"nst{i}") for i in range(2)]
        vsm = [ctx.sem(f"vsm{i}") for i in range(2)]
        rst = ctx.sem("rst")

        ctx.v(lambda e: e.memset(zeros8[:, :], 0.0))

        off = {}
        cur = 0
        for nm, (wT, kch, M) in wdefs.items():
            off[nm] = (cur, M)
            cur += kch * M
        wbuf = ctx.sbuf("wbuf", [128, cur], BF16)
        for nm, (wT, kch, M) in wdefs.items():
            dma_in(ctx, wbuf[:, off[nm][0]:off[nm][0] + kch * M],
                   wT.rearrange("(k p) m -> p k m", p=128), ld)
        dma_in(ctx, cs[:, :], cosf, ld)
        dma_in(ctx, sn[:, :], sinf, ld)
        dma_in(ctx, ln1_fm[:, :], ln1w, ld)
        ld_all = ld.n

        def wt(nm, k, m):
            base, M = off[nm]
            c = base + k * M + m * 128
            return wbuf[:, c:c + 128]

        ln_done = [0, 0]
        for nt in range(4):
            ntp = nt % 2
            war = ln_done[ntp]
            def xaf(e, nt=nt, ntp=ntp, war=war):
                if war:
                    e.wait_ge(ev.h, war)
                e.dma_start(out=xat[ntp][:, :],
                            in_=xa.rearrange("(c p) n -> p c n", p=128)[:, :, nt * 512:(nt + 1) * 512]
                            ).then_inc(x0ld[ntp].h, 16)
            ctx.g(xaf)
            x0_rdy = x0ld[ntp].inc(16)
            def gate(e, x0_rdy=x0_rdy, ntp=ntp):
                e.wait_ge(x0ld[ntp].h, x0_rdy)
                e.wait_ge(ld.h, ld_all)
            ctx.t(gate)
            ctx.s(gate)
            lev = plan_ln_tile(ctx, s,
                               (lambda mc, ntp=ntp: xat[ntp][:, mc * 512:(mc + 1) * 512]),
                               None,
                               gamma=ln1_fm, beta=zeros8,
                               bf_chunk=(lambda mc, nt=nt: xl_bf[:, mc * NL + nt * 512:mc * NL + nt * 512 + 512]))
            ln_done[ntp] = lev
        xl_done = max(ln_done)

        for ci, nm in enumerate(["kvd", "kvd", "qd", "qd"]):
            mch = ci % 2
            for nt in range(4):
                i, ps, wfree = s.ps.acquire()
                def mmf(e, nm=nm, mch=mch, nt=nt, ps=ps, wfree=wfree):
                    e.wait_ge(ev.h, xl_done)
                    pwait(e, wfree)
                    for k in range(8):
                        ins = e.matmul(ps[:, :], wt(nm, k, mch),
                                       xl_bf[:, k * NL + nt * 512:k * NL + (nt + 1) * 512],
                                       start=(k == 0), stop=(k == 7))
                    ins.then_inc(mm.h, 1)
                ctx.t(mmf)
                mmv = mm.inc()
                def evf(e, ci=ci, nt=nt, ps=ps, mmv=mmv):
                    e.wait_ge(mm.h, mmv)
                    e.tensor_copy(lat_bf[:, ci * NL + nt * 512:ci * NL + (nt + 1) * 512],
                                  ps[:, :]).then_inc(ev.h, 1)
                ctx.v(evf)
                s.ps.release(i, ev, ev.inc())
        lat_done = ev.n

        nst_war = [0, 0]
        for half, (nm, lat0, O) in enumerate([("kun", 0, K_o), ("qun", 2, Q_o)]):
            for mch in range(4):
                cp = (half * 4 + mch) % 2
                war = nst_war[cp]
                for nt in range(4):
                    i, ps, wfree = s.ps.acquire()
                    def mmf(e, nm=nm, lat0=lat0, mch=mch, nt=nt, ps=ps, wfree=wfree):
                        e.wait_ge(ev.h, lat_done)
                        pwait(e, wfree)
                        for k in range(2):
                            ins = e.matmul(ps[:, :], wt(nm, k, mch),
                                           lat_bf[:, (lat0 + k) * NL + nt * 512:(lat0 + k) * NL + (nt + 1) * 512],
                                           start=(k == 0), stop=(k == 1))
                        ins.then_inc(mm.h, 1)
                    ctx.t(mmf)
                    mmv = mm.inc()
                    def evf(e, cp=cp, nt=nt, ps=ps, mmv=mmv, war=war):
                        e.wait_ge(mm.h, mmv)
                        if war:
                            e.wait_ge(nst[cp].h, war)
                        e.tensor_copy(nopet[cp][:, nt * 512:(nt + 1) * 512],
                                      ps[:, :]).then_inc(ev.h, 1)
                    ctx.v(evf)
                    s.ps.release(i, ev, ev.inc())
                ch_done = ev.n
                def nope_st(e, mch=mch, cp=cp, ch_done=ch_done, O=O):
                    e.wait_ge(ev.h, ch_done)
                    for j in range(4):
                        hh = mch * 4 + j
                        e.dma_start(out=O[hh * 64 + 32:hh * 64 + 64, :],
                                    in_=nopet[cp][j * 32:(j + 1) * 32, :]).then_inc(nst[cp].h, 16)
                ctx.y(nope_st)
                nst_war[cp] = nst[cp].inc(16 * 4)

        vst_war = [0, 0]
        for ncid in range(NT):
            p = ncid % 2
            i, ps, wfree = s.ps.acquire()
            i2, ps2, wfree2 = s.ps.acquire()
            def mmf(e, ncid=ncid, ps=ps, ps2=ps2, wfree=wfree, wfree2=wfree2):
                e.wait_ge(ev.h, lat_done)
                pwait(e, wfree)
                pwait(e, wfree2)
                for half2, pp in ((0, ps), (1, ps2)):
                    for k in range(2):
                        ins = e.matmul(pp[:, :],
                                       lat_bf[:, k * NL + ncid * 128:k * NL + (ncid + 1) * 128],
                                       wbuf[:, off["vu"][0] + k * 1024 + half2 * 512:
                                            off["vu"][0] + k * 1024 + (half2 + 1) * 512],
                                       start=(k == 0), stop=(k == 1))
                ins.then_inc(mm.h, 1)
            ctx.t(mmf)
            mmv = mm.inc()
            war = vst_war[p]
            def evf(e, p=p, ps=ps, ps2=ps2, mmv=mmv, war=war):
                e.wait_ge(mm.h, mmv)
                if war:
                    e.wait_ge(vsm[p].h, war)
                e.tensor_copy(vtm[p][:, 0:512], ps[:, :])
                e.tensor_copy(vtm[p][:, 512:1024], ps2[:, :]).then_inc(ev.h, 1)
            ctx.v(evf)
            evv = ev.inc()
            s.ps.release(i, ev, evv)
            s.ps.release(i2, ev, evv)
            def vst(e, ncid=ncid, p=p, evv=evv):
                e.wait_ge(ev.h, evv)
                e.dma_start(out=V_o[ncid * 128:(ncid + 1) * 128, :], in_=vtm[p][:, :]).then_inc(vsm[p].h, 16)
            ctx.y(vst)
            vst_war[p] = vsm[p].inc(16)

        rope_st_war = 0
        for which, (nma, nmb, kch, lat0, O) in enumerate([
                ("rqa", "rqb", 2, 2, Q_o), ("rka", "rkb", 8, None, K_o)]):
            for half in range(2):
                for nm, dstbuf in ((nma, ra), (nmb, rb)):
                    for mch in range(2):
                        for sub in range(2):
                            i, ps, wfree = s.ps.acquire()
                            def mmf(e, nm=nm, mch=mch, half=half, sub=sub, ps=ps,
                                    wfree=wfree, kch=kch, lat0=lat0):
                                e.wait_ge(ev.h, xl_done if lat0 is None else lat_done)
                                pwait(e, wfree)
                                for k in range(kch):
                                    col = (lat0 + k if lat0 is not None else k) * NL + half * 1024 + sub * 512
                                    rhs = (lat_bf if lat0 is not None else xl_bf)[:, col:col + 512]
                                    ins = e.matmul(ps[:, :], wt(nm, k, mch), rhs,
                                                   start=(k == 0), stop=(k == kch - 1))
                                ins.then_inc(mm.h, 1)
                            ctx.t(mmf)
                            mmv = mm.inc()
                            def evf(e, dstbuf=dstbuf, mch=mch, sub=sub, ps=ps, mmv=mmv,
                                    war=rope_st_war):
                                e.wait_ge(mm.h, mmv)
                                if war:
                                    e.wait_ge(rst.h, war)
                                e.tensor_copy(dstbuf[:, mch * 1024 + sub * 512:mch * 1024 + (sub + 1) * 512],
                                              ps[:, :]).then_inc(ev.h, 1)
                            ctx.v(evf)
                            s.ps.release(i, ev, ev.inc())
                ab_done = ev.n
                for c in range(2):
                    def rop(e, c=c, half=half, ab_done=ab_done):
                        e.wait_ge(ev.h, ab_done)
                        csl = cs[:, half * 1024:(half + 1) * 1024]
                        snl = sn[:, half * 1024:(half + 1) * 1024]
                        rau = ra[:, c * 1024:(c + 1) * 1024]
                        rbu = rb[:, c * 1024:(c + 1) * 1024]
                        e.tensor_mul(tA[:, :], rau, csl)
                        e.tensor_mul(tB[:, :], rbu, snl)
                        e.tensor_mul(tC[:, :], rbu, csl)
                        s.vself.edge(e, e.tensor_mul(tD[:, :], rau, snl))
                        e.tensor_sub(lo[:, c * 1024:(c + 1) * 1024], tA[:, :], tB[:, :])
                        e.tensor_add(hi[:, c * 1024:(c + 1) * 1024], tC[:, :],
                                     tD[:, :]).then_inc(ev.h, 1)
                    ctx.v(rop)
                    ev.inc()
                rope_done = ev.n
                def rstf(e, half=half, rope_done=rope_done, O=O):
                    e.wait_ge(ev.h, rope_done)
                    for c in range(2):
                        for j in range(8):
                            hh = c * 8 + j
                            e.dma_start(out=O[hh * 64 + 0:hh * 64 + 16, half * 1024:(half + 1) * 1024],
                                        in_=lo[j * 16:(j + 1) * 16, c * 1024:(c + 1) * 1024]).then_inc(rst.h, 16)
                            e.dma_start(out=O[hh * 64 + 16:hh * 64 + 32, half * 1024:(half + 1) * 1024],
                                        in_=hi[j * 16:(j + 1) * 16, c * 1024:(c + 1) * 1024]).then_inc(rst.h, 16)
                ctx.y(rstf)
                rope_st_war = rst.inc(16 * 32)

        fins = [(nst[0], nst[0].n), (nst[1], nst[1].n), (vsm[0], vsm[0].n),
                (vsm[1], vsm[1].n), (rst, rst.n)]
        def finw(e):
            for sm, v_ in fins:
                e.wait_ge(sm.h, v_)
        ctx.y(finw)
        ctx.finish()
    return nc



def build_B2(odd, debug=False):
    KT = 32 if odd else 16
    off_q = 2048 if odd else 0
    nc = bass.Bass("TRN2", target_bir_lowering=False)
    xa = nc.dram_tensor("xa", [D, NL], F32, kind="ExternalInput").ap()
    Kd = nc.dram_tensor("K", [D, KT * 128], BF16, kind="ExternalInput").ap()
    Qd = nc.dram_tensor("Q", [D, NL], BF16, kind="ExternalInput").ap()
    Vd = nc.dram_tensor("V", [KT * 128, D], BF16, kind="ExternalInput").ap()
    stair = nc.dram_tensor("stair", [128, 2176], BF16, kind="ExternalInput").ap()
    owT = nc.dram_tensor("owT", [D, D], F32, kind="ExternalInput").ap()
    t1_o = nc.dram_tensor("t1", [D, NL], F32, kind="ExternalOutput").ap()
    if debug:
        yfm_o = nc.dram_tensor("Yfm_dbg", [128, NCH * NL], BF16, kind="ExternalOutput").ap()

    es = contextlib.ExitStack()
    with es:
        ctx = Ctx(nc, es)
        s = S(ctx, n512=4)
        mm, ev, va, sa, ld, st = s.mm, s.ev, s.va, s.sa, s.ld, s.st

        Kb = [ctx.sbuf(f"Kb{i}", [128, KT * 128], BF16) for i in range(2)]
        Qb = [ctx.sbuf(f"Qb{i}", [128, NL], BF16) for i in range(2)]
        aug = [ctx.sbuf(f"aug{i}", [128, KT * 130], BF16) for i in range(2)]
        Pt = [[ctx.sbuf(f"Pt{h}_{i}", [128, 1024], BF16) for i in range(2)] for h in range(2)]
        stb = ctx.sbuf("stb", [128, 2176], BF16)
        zrow = ctx.sbuf("zrow", [1, 4 * 512], F32)
        zb_sb = ctx.sbuf("zb_sb", [128, 1024], F32)
        Yfm = ctx.sbuf("Yfm", [128, NCH * NL], BF16)
        owb = ctx.sbuf("owb", [128, NCH * D], BF16)
        xat = [ctx.sbuf(f"xat{i}", [128, 8 * 512], F32) for i in range(2)]
        rs = [ctx.sbuf(f"rs{i}", [128, 8 * 512], F32) for i in range(2)]
        psA = [ctx.psum(f"psA{h}", [128, 1024], F32) for h in range(2)]

        kld = [ctx.sem(f"kld{i}") for i in range(2)]
        qld = [ctx.sem(f"qld{i}") for i in range(2)]
        vld = [ctx.sem(f"vld{i}") for i in range(2)]
        x0ld = [ctx.sem(f"x0ld{i}") for i in range(2)]
        yst2 = [ctx.sem(f"yst2_{i}") for i in range(2)]
        tst2 = [ctx.sem(f"tst2_{i}") for i in range(2)]

        dma_in(ctx, stb[:, :], stair, ld)
        dma_in(ctx, owb[:, :], owT.rearrange("(k p) m -> p k m", p=128), ld)
        ld_all = ld.n
        def ones_init(e):
            e.memset(aug[0][:, :].rearrange("q (a j c) -> q a j c", j=2, c=65)[:, :, :, 64:65], 1.0)
            e.memset(aug[1][:, :].rearrange("q (a j c) -> q a j c", j=2, c=65)[:, :, :, 64:65],
                     1.0).then_inc(ev.h, 1)
        ctx.v(ones_init)
        ones_done = ev.inc()

        kt_hi = {0: (KT - 8), 1: KT}
        def need_mask(strip, kt):
            d = kt * 128 - strip * 1024 - off_q
            return -127 < d <= 1023
        def stair_col(strip, kt):
            d = kt * 128 - strip * 1024 - off_q
            return 1023 - d

        hp_war = [(0, 0)] * 2
        last_exp = {0: 0, 1: 0}
        last_pv = {}
        zb_done = {0: 0, 1: 0}
        for hp in range(8):
            p = hp % 2
            war_mm, war_ev = hp_war[p]
            def kqv_load(e, hp=hp, p=p, war_mm=war_mm, war_ev=war_ev):
                if war_mm:
                    e.wait_ge(mm.h, war_mm)
                if war_ev:
                    e.wait_ge(ev.h, war_ev)
                e.dma_start(out=Kb[p][:, :], in_=Kd[hp * 128:(hp + 1) * 128, :]).then_inc(kld[p].h, 16)
                e.dma_start(out=Qb[p][:, :], in_=Qd[hp * 128:(hp + 1) * 128, :]).then_inc(qld[p].h, 16)
                for j in range(2):
                    e.dma_start(out=aug[p][:, :].rearrange("q (a j c) -> q a j c", j=2, c=65)[:, :, j, 0:64],
                                in_=Vd[:, hp * 128 + j * 64:hp * 128 + (j + 1) * 64]
                                .rearrange("(a q) c -> q a c", q=128)
                                ).then_inc(vld[p].h, 16)
            ctx.g(kqv_load)
            k_rdy = kld[p].inc(16)
            q_rdy = qld[p].inc(16)
            v_rdy = vld[p].inc(32)
            aug_rdy = ones_done

            for strip in range(2):
                nk = kt_hi[strip]
                pvs = {}
                for h in range(2):
                    for half in range(2):
                        pvs[(h, half)] = s.ps.acquire()
                mask_ev = {}
                for kt in range(nk):
                    first, last = (kt == 0), (kt == nk - 1)
                    for h in range(2):
                        ktp = kt % 2
                        def smm(e, p=p, h=h, kt=kt, strip=strip, k_rdy=k_rdy, q_rdy=q_rdy,
                                first=first, esa=last_exp[h], zbd=zb_done[h],
                                frees=[pvs[(h, hf)][2] for hf in range(2)] if first else None):
                            e.wait_ge(kld[p].h, k_rdy)
                            e.wait_ge(qld[p].h, q_rdy)
                            if esa:
                                e.wait_ge(sa.h, esa)
                            if zbd:
                                e.wait_ge(ev.h, zbd)
                            if first:
                                for w in frees:
                                    pwait(e, w)
                            for half in range(2):
                                ins = e.matmul(psA[h][:, half * 512:(half + 1) * 512],
                                               Kb[p][h * 64:(h + 1) * 64, kt * 128:(kt + 1) * 128],
                                               Qb[p][h * 64:(h + 1) * 64,
                                                     strip * 1024 + half * 512:strip * 1024 + (half + 1) * 512],
                                               start=True, stop=True)
                            ins.then_inc(mm.h, 1)
                        ctx.t(smm)
                        smm_v = mm.inc()
                        def expf(e, h=h, ktp=ktp, smm_v=smm_v, pvw=last_pv.get((h, ktp))):
                            e.wait_ge(mm.h, smm_v)
                            if pvw:
                                e.wait_ge(mm.h, pvw)
                            e.activation(Pt[h][ktp][:, :], psA[h][:, :], AF.Exp,
                                         scale=0.125).then_inc(sa.h, 1)
                        ctx.s(expf)
                        sav = sa.inc()
                        last_exp[h] = sav
                        if need_mask(strip, kt):
                            c0 = stair_col(strip, kt)
                            def mskf(e, h=h, ktp=ktp, c0=c0, sav=sav):
                                e.wait_ge(sa.h, sav)
                                e.wait_ge(ld.h, ld_all)
                                e.tensor_mul(Pt[h][ktp][:, :], Pt[h][ktp][:, :],
                                             stb[:, c0:c0 + 1024]).then_inc(ev.h, 1)
                            ctx.v(mskf)
                            mask_ev[(kt, h)] = ev.inc()
                        def pvf(e, p=p, h=h, kt=kt, ktp=ktp, first=first, last=last,
                                sav=sav, mev=mask_ev.get((kt, h)), aug_rdy=aug_rdy,
                                v_rdy=v_rdy,
                                s0=pvs[(h, 0)][1], s1=pvs[(h, 1)][1]):
                            if mev:
                                e.wait_ge(ev.h, mev)
                            else:
                                e.wait_ge(sa.h, sav)
                            e.wait_ge(ev.h, aug_rdy)
                            e.wait_ge(vld[p].h, v_rdy)
                            for half, sl in ((0, s0), (1, s1)):
                                ins = e.matmul(sl[0:65, :],
                                               aug[p][:, (kt * 2 + h) * 65:(kt * 2 + h) * 65 + 65],
                                               Pt[h][ktp][:, half * 512:(half + 1) * 512],
                                               start=first, stop=last)
                            ins.then_inc(mm.h, 1)
                        ctx.t(pvf)
                        last_pv[(h, ktp)] = mm.inc()
                pv_all = mm.n
                def zrec(e, pv_all=pv_all, sls=[pvs[(h, hf)][1] for h in range(2) for hf in range(2)]):
                    e.wait_ge(mm.h, pv_all)
                    for i4, sl in enumerate(sls):
                        ins = e.reciprocal(zrow[0:1, i4 * 512:(i4 + 1) * 512], sl[64:65, :])
                    ins.then_inc(ev.h, 1)
                ctx.v(zrec)
                zr_done = ev.inc()
                def zbm(e, zr_done=zr_done, e0=last_exp[0], e1=last_exp[1]):
                    e.wait_ge(ev.h, zr_done)
                    e.wait_ge(sa.h, max(e0, e1))
                    for h in range(2):
                        for half in range(2):
                            i4 = h * 2 + half
                            ins = e.matmul(psA[0][h * 64:(h + 1) * 64, half * 512:(half + 1) * 512],
                                           s.onesrow[0:1, 0:64],
                                           zrow[0:1, i4 * 512:(i4 + 1) * 512],
                                           start=True, stop=True)
                    ins.then_inc(mm.h, 1)
                ctx.t(zbm)
                zb_mm = mm.inc()
                def zbc(e, zb_mm=zb_mm):
                    e.wait_ge(mm.h, zb_mm)
                    e.tensor_copy(zb_sb[:, :], psA[0][:, :]).then_inc(ev.h, 1)
                ctx.v(zbc)
                zb_done[0] = ev.inc()
                zb_done[1] = zb_done[0]
                evs = []
                for h in range(2):
                    for half in range(2):
                        def yev(e, h=h, half=half, hp=hp, strip=strip,
                                sl=pvs[(h, half)][1], zbd=zb_done[0]):
                            e.wait_ge(ev.h, zbd)
                            col = hp * NL + strip * 1024 + half * 512
                            e.tensor_mul(Yfm[h * 64:(h + 1) * 64, col:col + 512],
                                         sl[0:64, :],
                                         zb_sb[h * 64:(h + 1) * 64, half * 512:(half + 1) * 512]
                                         ).then_inc(ev.h, 1)
                        ctx.v(yev)
                        evs.append(ev.inc())
                for h in range(2):
                    for half in range(2):
                        s.ps.release(pvs[(h, half)][0], ev, evs[-1])
            hp_war[p] = (mm.n, 0)
        attn_done_ev = ev.n

        st_war = [0, 0]
        for nt in range(4):
            ntp = nt % 2
            war = st_war[ntp]
            def xaf(e, nt=nt, ntp=ntp, war=war):
                if war:
                    e.wait_ge(tst2[ntp].h, war)
                e.dma_start(out=xat[ntp][:, :],
                            in_=xa.rearrange("(c p) n -> p c n", p=128)[:, :, nt * 512:(nt + 1) * 512]
                            ).then_inc(x0ld[ntp].h, 16)
            ctx.g(xaf)
            x_rdy = x0ld[ntp].inc(16)
            for mc in range(8):
                io, pso, wfree = s.ps.acquire()
                def omm(e, mc=mc, nt=nt, pso=pso, wfree=wfree):
                    e.wait_ge(ld.h, ld_all)
                    e.wait_ge(ev.h, attn_done_ev)
                    pwait(e, wfree)
                    for k in range(8):
                        ins = e.matmul(pso[:, :], owb[:, k * D + mc * 128:k * D + (mc + 1) * 128],
                                       Yfm[:, k * NL + nt * 512:k * NL + (nt + 1) * 512],
                                       start=(k == 0), stop=(k == 7))
                    ins.then_inc(mm.h, 1)
                ctx.t(omm)
                mmv = mm.inc()
                def oev(e, mc=mc, ntp=ntp, pso=pso, mmv=mmv, x_rdy=x_rdy, war=war):
                    e.wait_ge(mm.h, mmv)
                    e.wait_ge(x0ld[ntp].h, x_rdy)
                    if war:
                        e.wait_ge(tst2[ntp].h, war)
                    e.tensor_add(rs[ntp][:, mc * 512:(mc + 1) * 512], pso[:, :],
                                 xat[ntp][:, mc * 512:(mc + 1) * 512]).then_inc(ev.h, 1)
                ctx.v(oev)
                s.ps.release(io, ev, ev.inc())
            t1_done = ev.n
            def tst(e, nt=nt, ntp=ntp, t1_done=t1_done):
                e.wait_ge(ev.h, t1_done)
                e.dma_start(out=t1_o.rearrange("(c p) n -> p c n", p=128)[:, :, nt * 512:(nt + 1) * 512],
                            in_=rs[ntp][:, :]).then_inc(tst2[ntp].h, 16)
            ctx.y(tst)
            st_war[ntp] = tst2[ntp].inc(16)

        if debug:
            def ydbg(e):
                e.wait_ge(ev.h, attn_done_ev)
                e.dma_start(out=yfm_o, in_=Yfm[:, :]).then_inc(st.h, 16)
            ctx.y(ydbg)
            st.inc(16)
        fin = st.n
        tf0, tf1 = tst2[0].n, tst2[1].n
        def finw(e):
            e.wait_ge(st.h, fin)
            e.wait_ge(tst2[0].h, tf0)
            e.wait_ge(tst2[1].h, tf1)
        ctx.y(finw)
        ctx.finish()
    return nc



def build_B3():
    nc = bass.Bass("TRN2", target_bir_lowering=False)
    t1 = nc.dram_tensor("t1", [D, NL], F32, kind="ExternalInput").ap()
    xa = nc.dram_tensor("xa", [D, NL], F32, kind="ExternalInput").ap()
    ln2w = nc.dram_tensor("ln2w", [128, 8], F32, kind="ExternalInput").ap()
    rwT = nc.dram_tensor("rwT", [D, 7], F32, kind="ExternalInput").ap()
    rbias = nc.dram_tensor("rbias", [1, 7], F32, kind="ExternalInput").ap()
    gT = nc.dram_tensor("gT", [8, D, 2048], F32, kind="ExternalInput").ap()
    uT = nc.dram_tensor("uT", [8, D, 2048], F32, kind="ExternalInput").ap()
    dT = nc.dram_tensor("dT", [8, 2048, D], F32, kind="ExternalInput").ap()
    xb_o = nc.dram_tensor("xb", [D, NL], F32, kind="ExternalOutput").ap()

    es = contextlib.ExitStack()
    with es:
        ctx = Ctx(nc, es)
        s = S(ctx, n512=8)
        mm, ev, va, sa, ld, st = s.mm, s.ev, s.va, s.sa, s.ld, s.st

        tt = [ctx.sbuf(f"tt{i}", [128, 8 * 512], F32) for i in range(2)]
        xat = [ctx.sbuf(f"xat{i}", [128, 8 * 512], F32) for i in range(2)]
        xm = ctx.sbuf("xm", [128, 8 * 512], F32)
        xm_bf = ctx.sbuf("xm_bf", [128, 8 * 512], BF16)
        hu_bf = ctx.sbuf("hu_bf", [128, 16 * 512], BF16)
        macc = ctx.sbuf("macc", [128, 8 * 512], F32)
        wb_sb = ctx.sbuf("wb_sb", [128, 7 * 512], BF16)
        probs = ctx.sbuf("probs", [128, 7], F32)
        scr7 = ctx.sbuf("scr7", [128, 7], F32)
        m1 = ctx.sbuf("m1", [128, 1], F32)
        m2 = ctx.sbuf("m2", [128, 1], F32)
        wgt_f = ctx.sbuf("wgt_f", [128, 7], F32)
        wrow = ctx.sbuf("wrow", [1, 7 * 512], BF16)
        sg_sb = [ctx.sbuf(f"sg_sb{i}", [128, 512], F32) for i in range(2)]
        rw_sb = ctx.sbuf("rw_sb", [128, 8 * 7], F32)
        rb_sb = ctx.sbuf("rb_sb", [1, 7], F32)
        ln2_fm = ctx.sbuf("ln2_fm", [128, 8], F32)
        zeros8 = ctx.sbuf("zeros8", [128, 8], F32)
        identb = ctx.sbuf("identb", [128, 128], BF16)
        identf = ctx.sbuf("identf", [128, 128], F32)
        coliota = ctx.sbuf("coliota", [128, 128], F32)
        rowv = ctx.sbuf("rowv", [128, 1], F32)
        onesrow_bf = ctx.sbuf("onesrow_bf", [1, 128], BF16)
        gld = [ctx.sem(f"gld{i}") for i in range(2)]
        uld = [ctx.sem(f"uld{i}") for i in range(2)]
        dld = [ctx.sem(f"dld{i}") for i in range(2)]
        t1ld = [ctx.sem(f"t1ld{i}") for i in range(2)]
        xald = [ctx.sem(f"xald{i}") for i in range(2)]
        xst = [ctx.sem(f"xst{i}") for i in range(2)]
        gb = [ctx.sbuf(f"gb{i}", [128, 8 * 512], BF16) for i in range(2)]
        ub = [ctx.sbuf(f"ub{i}", [128, 8 * 512], BF16) for i in range(2)]
        db = [ctx.sbuf(f"db{i}", [128, 16 * 256], BF16) for i in range(2)]

        ctx.v(lambda e: e.memset(zeros8[:, :], 0.0))
        ctx.v(lambda e: e.memset(onesrow_bf[:, :], 1.0))
        ctx.g(lambda e: e.iota(coliota[:, :], [[1, 128]], channel_multiplier=0,
                               allow_small_or_imprecise_dtypes=True))
        iosem = ctx.sem("iosem")
        def iot2(e):
            e.iota(rowv[:, :], [[0, 1]], channel_multiplier=1,
                   allow_small_or_imprecise_dtypes=True).then_inc(iosem.h, 1)
        ctx.g(iot2)
        ld_iota = iosem.inc(1)
        def mkident(e):
            e.wait_ge(iosem.h, ld_iota)
            e.tensor_single_scalar(identf[:, :], coliota[:, :], rowv[:, :],
                                   ALU.is_equal).then_inc(va.h, 1)
        ctx.v(mkident)
        idva = va.inc()
        def mkident2(e, idva=idva):
            e.wait_ge(va.h, idva)
            e.tensor_copy(identb[:, :], identf[:, :]).then_inc(ev.h, 1)
        ctx.v(mkident2)
        ident_done = ev.inc()

        dma_in(ctx, rw_sb[:, :], rwT.rearrange("(k p) m -> p k m", p=128), ld)
        dma_in(ctx, rb_sb[:, :], rbias, ld)
        dma_in(ctx, ln2_fm[:, :], ln2w, ld)
        ld_all = ld.n

        g_war = [0, 0]
        u_war = [0, 0]
        d_war = [0, 0]
        hum_ev = [0, 0]
        g_cnt = [0, 0]
        u_cnt = [0, 0]
        d_cnt = [0, 0]
        t1_war = [0, 0]
        xa_war = [0, 0]
        st_war = [0, 0]

        for nt in range(4):
            ntp = nt % 2
            def t1f(e, nt=nt, ntp=ntp, war=t1_war[ntp], sw=st_war[ntp]):
                if war:
                    e.wait_ge(ev.h, war)
                if sw:
                    e.wait_ge(xst[ntp].h, sw)
                e.dma_start(out=tt[ntp][:, :],
                            in_=t1.rearrange("(c p) n -> p c n", p=128)[:, :, nt * 512:(nt + 1) * 512]
                            ).then_inc(t1ld[ntp].h, 16)
            ctx.g(t1f)
            t1_rdy = t1ld[ntp].inc(16)
            def xaf(e, nt=nt, ntp=ntp, war=xa_war[ntp]):
                if war:
                    e.wait_ge(ev.h, war)
                e.dma_start(out=xat[ntp][:, :],
                            in_=xa.rearrange("(c p) n -> p c n", p=128)[:, :, nt * 512:(nt + 1) * 512]
                            ).then_inc(xald[ntp].h, 16)
            ctx.g(xaf)
            xa_rdy = xald[ntp].inc(16)

            def gate(e, t1_rdy=t1_rdy, ntp=ntp):
                e.wait_ge(t1ld[ntp].h, t1_rdy)
                e.wait_ge(ld.h, ld_all)
            ctx.t(gate)
            ctx.s(gate)
            ln2_ev = plan_ln_tile(ctx, s,
                                  (lambda mc, ntp=ntp: tt[ntp][:, mc * 512:(mc + 1) * 512]),
                                  (lambda mc: xm[:, mc * 512:(mc + 1) * 512]),
                                  gamma=ln2_fm, beta=zeros8,
                                  bf_chunk=(lambda mc: xm_bf[:, mc * 512:(mc + 1) * 512]))

            for sub in range(4):
                ir, psr, wfree = s.ps.acquire()
                def rmm(e, sub=sub, psr=psr, wfree=wfree, ln2_ev=ln2_ev):
                    e.wait_ge(ev.h, ln2_ev)
                    pwait(e, wfree)
                    for k in range(8):
                        e.matmul(psr[:, 0:7], xm[:, k * 512 + sub * 128:k * 512 + sub * 128 + 128],
                                 rw_sb[:, k * 7:(k + 1) * 7], start=(k == 0), stop=False)
                    e.matmul(psr[:, 0:7], s.onesrow[0:1, :], rb_sb[0:1, :],
                             start=False, stop=True).then_inc(mm.h, 1)
                ctx.t(rmm)
                mmv = mm.inc()
                def sig(e, psr=psr, mmv=mmv):
                    e.wait_ge(mm.h, mmv)
                    e.activation(probs[:, :], psr[:, 0:7], AF.Sigmoid).then_inc(sa.h, 1)
                ctx.s(sig)
                sav = sa.inc()
                def top2(e, sav=sav):
                    e.wait_ge(sa.h, sav)
                    s.vself.edge(e, e.reduce_max(m1[:, :], probs[:, :], axis=AX.X))
                    s.vself.edge(e, e.tensor_single_scalar(scr7[:, :], probs[:, :], m1[:, :], ALU.is_equal))
                    s.vself.edge(e, e.scalar_tensor_tensor(scr7[:, :], scr7[:, :], -1e30, probs[:, :],
                                                           ALU.mult, ALU.add))
                    s.vself.edge(e, e.reduce_max(m2[:, :], scr7[:, :], axis=AX.X))
                    s.vself.edge(e, e.tensor_single_scalar(scr7[:, :], probs[:, :], m2[:, :], ALU.is_ge))
                    s.vself.edge(e, e.tensor_mul(scr7[:, :], scr7[:, :], probs[:, :]))
                    s.vself.edge(e, e.tensor_add(m1[:, :], m1[:, :], m2[:, :]))
                    s.vself.edge(e, e.reciprocal(m1[:, :], m1[:, :]))
                    e.tensor_scalar_mul(wgt_f[:, :], scr7[:, :], m1[:, :]).then_inc(ev.h, 1)
                ctx.v(top2)
                w_done = ev.inc()
                it, pst, wfree2 = s.ps.acquire()
                it2, pst2, wfree3 = s.ps.acquire()
                def wtr(e, pst=pst, pst2=pst2, wfree2=wfree2, wfree3=wfree3,
                        w_done=w_done, ident_done=ident_done):
                    e.wait_ge(ev.h, max(w_done, ident_done))
                    pwait(e, wfree2)
                    pwait(e, wfree3)
                    for ee in range(7):
                        pp = pst if ee < 4 else pst2
                        ins = e.transpose(pp[0:1, (ee % 4) * 128:(ee % 4 + 1) * 128],
                                          wgt_f[:, ee:ee + 1], identf[:, :])
                    ins.then_inc(mm.h, 1)
                ctx.t(wtr)
                mmv2 = mm.inc()
                def wcp(e, sub=sub, pst=pst, pst2=pst2, mmv2=mmv2):
                    e.wait_ge(mm.h, mmv2)
                    for ee in range(7):
                        pp = pst if ee < 4 else pst2
                        ins = e.tensor_copy(wrow[0:1, ee * 512 + sub * 128:ee * 512 + (sub + 1) * 128],
                                            pp[0:1, (ee % 4) * 128:(ee % 4 + 1) * 128])
                    ins.then_inc(ev.h, 1)
                ctx.v(wcp)
                s.ps.release(ir, ev, ev.inc())
                s.ps.release(it, ev, ev.n)
                s.ps.release(it2, ev, ev.n)
            wrow_done = ev.n
            iw, psw, wfreew = s.ps.acquire()
            iw2, psw2, wfreew2 = s.ps.acquire()
            wbc_ev = {}
            for ee in range(7):
                pw = psw if ee % 2 == 0 else psw2
                def wbm(e, ee=ee, pw=pw, wrow_done=wrow_done, wfreew=wfreew, wfreew2=wfreew2,
                        slot_free=wbc_ev.get(ee - 2, 0)):
                    e.wait_ge(ev.h, wrow_done)
                    if ee < 2:
                        pwait(e, wfreew if ee == 0 else wfreew2)
                    if slot_free:
                        e.wait_ge(ev.h, slot_free)
                    e.matmul(pw[:, :], onesrow_bf[0:1, :], wrow[0:1, ee * 512:(ee + 1) * 512],
                             start=True, stop=True).then_inc(mm.h, 1)
                ctx.t(wbm)
                mmv3 = mm.inc()
                def wbc(e, ee=ee, pw=pw, mmv3=mmv3):
                    e.wait_ge(mm.h, mmv3)
                    with nc.allow_low_precision(reason="top2 gate weights carried at bf16"):
                        e.tensor_copy(wb_sb[:, ee * 512:(ee + 1) * 512], pw[:, :]).then_inc(ev.h, 1)
                ctx.v(wbc)
                wbc_ev[ee] = ev.inc()
            s.ps.release(iw, ev, ev.n)
            s.ps.release(iw2, ev, ev.n)
            wb_done = ev.n

            for eidx in range(8):
                for m in range(16):
                    p2 = (m // 4) % 2
                    if m % 4 == 0:
                        def gl(e, nt=nt, eidx=eidx, m0=m, p2=p2, war=g_war[p2]):
                            if war:
                                e.wait_ge(mm.h, war)
                            e.dma_start(out=gb[p2][:, :],
                                        in_=gT[eidx, :, m0 * 128:(m0 + 4) * 128]
                                        .rearrange("(k p) m -> p k m", p=128)).then_inc(gld[p2].h, 16)
                        ctx.g(gl)
                        g_cnt[p2] += 16
                        def ul(e, nt=nt, eidx=eidx, m0=m, p2=p2, war=u_war[p2]):
                            if war:
                                e.wait_ge(mm.h, war)
                            e.dma_start(out=ub[p2][:, :],
                                        in_=uT[eidx, :, m0 * 128:(m0 + 4) * 128]
                                        .rearrange("(k p) m -> p k m", p=128)).then_inc(uld[p2].h, 16)
                        ctx.g(ul)
                        u_cnt[p2] += 16
                    ig, psg, wfg = s.ps.acquire()
                    iu, psu, wfu = s.ps.acquire()
                    def gmm(e, m=m, p2=p2, psg=psg, psu=psu, wfg=wfg, wfu=wfu,
                            gc=g_cnt[p2], uc=u_cnt[p2], ln2_ev=ln2_ev):
                        e.wait_ge(gld[p2].h, gc)
                        e.wait_ge(uld[p2].h, uc)
                        e.wait_ge(ev.h, ln2_ev)
                        pwait(e, wfg)
                        pwait(e, wfu)
                        mq = (m % 4) * 128
                        for k in range(8):
                            e.matmul(psg[:, :], gb[p2][:, k * 512 + mq:k * 512 + mq + 128],
                                     xm_bf[:, k * 512:(k + 1) * 512],
                                     start=(k == 0), stop=(k == 7))
                        for k in range(8):
                            ins = e.matmul(psu[:, :], ub[p2][:, k * 512 + mq:k * 512 + mq + 128],
                                           xm_bf[:, k * 512:(k + 1) * 512],
                                           start=(k == 0), stop=(k == 7))
                        ins.then_inc(mm.h, 1)
                    ctx.t(gmm)
                    g_war[p2] = mm.inc()
                    u_war[p2] = g_war[p2]
                    mmv = g_war[p2]
                    def sil(e, m=m, p2=p2, psg=psg, mmv=mmv, hw=hum_ev[p2]):
                        e.wait_ge(mm.h, mmv)
                        if hw:
                            e.wait_ge(ev.h, hw)
                        e.activation(sg_sb[p2][:, :], psg[:, :], AF.Silu).then_inc(sa.h, 1)
                    ctx.s(sil)
                    sav = sa.inc()
                    def humul(e, m=m, p2=p2, psu=psu, sav=sav):
                        e.wait_ge(sa.h, sav)
                        e.tensor_mul(hu_bf[:, m * 512:(m + 1) * 512], psu[:, :],
                                     sg_sb[p2][:, :]).then_inc(ev.h, 1)
                    ctx.v(humul)
                    evv = ev.inc()
                    hum_ev[p2] = evv
                    s.ps.release(ig, ev, evv)
                    s.ps.release(iu, ev, evv)
                hu_done = ev.n
                for mc in range(8):
                    p2 = (mc // 2) % 2
                    if mc % 2 == 0:
                        def dl(e, eidx=eidx, mc0=mc, p2=p2, war=d_war[p2]):
                            if war:
                                e.wait_ge(mm.h, war)
                            e.dma_start(out=db[p2][:, :],
                                        in_=dT[eidx, :, mc0 * 128:(mc0 + 2) * 128]
                                        .rearrange("(k p) m -> p k m", p=128)).then_inc(dld[p2].h, 16)
                        ctx.g(dl)
                        d_cnt[p2] += 16
                    idp, psd, wfd = s.ps.acquire()
                    def dmm(e, mc=mc, p2=p2, psd=psd, wfd=wfd, dc=d_cnt[p2], hu_done=hu_done):
                        e.wait_ge(dld[p2].h, dc)
                        e.wait_ge(ev.h, hu_done)
                        pwait(e, wfd)
                        mq = (mc % 2) * 128
                        for k in range(16):
                            ins = e.matmul(psd[:, :], db[p2][:, k * 256 + mq:k * 256 + mq + 128],
                                           hu_bf[:, k * 512:(k + 1) * 512],
                                           start=(k == 0), stop=(k == 15))
                        ins.then_inc(mm.h, 1)
                    ctx.t(dmm)
                    d_war[p2] = mm.inc()
                    mmv = d_war[p2]
                    if eidx == 0:
                        def dev(e, mc=mc, psd=psd, mmv=mmv):
                            e.wait_ge(mm.h, mmv)
                            e.tensor_copy(macc[:, mc * 512:(mc + 1) * 512], psd[:, :]).then_inc(ev.h, 1)
                    else:
                        def dev(e, mc=mc, psd=psd, mmv=mmv, eidx=eidx, wb_done=wb_done):
                            e.wait_ge(mm.h, mmv)
                            e.wait_ge(ev.h, wb_done)
                            s.vself.edge(e, e.tensor_mul(s.lnt[mc % 2][:, :], psd[:, :],
                                         wb_sb[:, (eidx - 1) * 512:eidx * 512]))
                            e.tensor_add(macc[:, mc * 512:(mc + 1) * 512],
                                         macc[:, mc * 512:(mc + 1) * 512],
                                         s.lnt[mc % 2][:, :]).then_inc(ev.h, 1)
                    ctx.v(dev)
                    s.ps.release(idp, ev, ev.inc())
            moe_done = ev.n

            for mc in range(8):
                def t2f(e, mc=mc, ntp=ntp, moe_done=moe_done, xa_rdy=xa_rdy,
                        swar=st_war[ntp]):
                    e.wait_ge(ev.h, moe_done)
                    e.wait_ge(xald[ntp].h, xa_rdy)
                    if swar:
                        e.wait_ge(xst[ntp].h, swar)
                    s.vself.edge(e, e.tensor_add(tt[ntp][:, mc * 512:(mc + 1) * 512],
                                 tt[ntp][:, mc * 512:(mc + 1) * 512],
                                 macc[:, mc * 512:(mc + 1) * 512]))
                    e.tensor_add(tt[ntp][:, mc * 512:(mc + 1) * 512],
                                 tt[ntp][:, mc * 512:(mc + 1) * 512],
                                 xat[ntp][:, mc * 512:(mc + 1) * 512]).then_inc(ev.h, 1)
                ctx.v(t2f)
                ev.inc()
            pre_done = ev.n
            xa_war[ntp] = pre_done
            out_ev = plan_ln_tile(ctx, s,
                                  (lambda mc, ntp=ntp: tt[ntp][:, mc * 512:(mc + 1) * 512]),
                                  (lambda mc, ntp=ntp: tt[ntp][:, mc * 512:(mc + 1) * 512]),
                                  wait_ev=pre_done)
            t1_war[ntp] = out_ev
            def stf(e, nt=nt, ntp=ntp, out_ev=out_ev):
                e.wait_ge(ev.h, out_ev)
                e.dma_start(out=xb_o.rearrange("(c p) n -> p c n", p=128)[:, :, nt * 512:(nt + 1) * 512],
                            in_=tt[ntp][:, :]).then_inc(xst[ntp].h, 16)
            ctx.y(stf)
            st_war[ntp] = xst[ntp].inc(16)

        f0, f1 = xst[0].n, xst[1].n
        def finw(e):
            e.wait_ge(xst[0].h, f0)
            e.wait_ge(xst[1].h, f1)
        ctx.y(finw)
        ctx.finish()
    return nc



def build_C1():
    nc = bass.Bass("TRN2", target_bir_lowering=False)
    xb = nc.dram_tensor("xb", [D, NL], F32, kind="ExternalInput").ap()
    WkT = nc.dram_tensor("WkT", [D, D], F32, kind="ExternalInput").ap()
    WvT = nc.dram_tensor("WvT", [D, D], F32, kind="ExternalInput").ap()
    bk = nc.dram_tensor("bk", [1, D], F32, kind="ExternalInput").ap()
    bv = nc.dram_tensor("bv", [1, D], F32, kind="ExternalInput").ap()
    Ktm_o = nc.dram_tensor("Ktm", [NL, D], BF16, kind="ExternalOutput").ap()
    Vtm_o = nc.dram_tensor("Vtm", [NL, D], BF16, kind="ExternalOutput").ap()
    Sh_o = nc.dram_tensor("Sh", [128, 8 * 64], F32, kind="ExternalOutput").ap()

    es = contextlib.ExitStack()
    with es:
        ctx = Ctx(nc, es)
        s = S(ctx, n512=6)
        mm, ev, va, sa, ld, st = s.mm, s.ev, s.va, s.sa, s.ld, s.st
        xb_bf = ctx.sbuf("xb_bf", [128, NCH * NL], BF16)
        wk = ctx.sbuf("wk", [128, NCH * D], BF16)
        wv = ctx.sbuf("wv", [128, NCH * D], BF16)
        bk_sb = ctx.sbuf("bk_sb", [1, D], BF16)
        bv_sb = ctx.sbuf("bv_sb", [1, D], BF16)
        ktm = ctx.sbuf("ktm", [128, NT * D], BF16)
        vtm = ctx.sbuf("vtm", [128, NT * D], BF16)
        ssb = ctx.sbuf("ssb", [128, 8 * 64], F32)
        kst = [ctx.sem(f"kst{i}") for i in range(2)]
        pstate = [ctx.psum(f"pstate{i}", [128, 512]) for i in range(2)]

        dma_in(ctx, xb_bf[:, :], xb.rearrange("(c p) n -> p c n", p=128), ld)
        dma_in(ctx, wk[:, :], WkT.rearrange("(k p) m -> p k m", p=128), ld)
        dma_in(ctx, wv[:, :], WvT.rearrange("(k p) m -> p k m", p=128), ld)
        dma_in(ctx, bk_sb[:, :], bk, ld)
        dma_in(ctx, bv_sb[:, :], bv, ld)
        ld_all = ld.n

        for which, (wbuf2, bsb, dst, sto) in enumerate(
                [(wk, bk_sb, ktm, Ktm_o), (wv, bv_sb, vtm, Vtm_o)]):
            for ncid in range(NT):
                for half in range(2):
                    i, ps, wfree = s.ps.acquire()
                    def mmf(e, wbuf2=wbuf2, bsb=bsb, ncid=ncid, half=half, ps=ps, wfree=wfree):
                        e.wait_ge(ld.h, ld_all)
                        e.wait_ge(ev.h, s.init_ev)
                        pwait(e, wfree)
                        for k in range(NCH):
                            e.matmul(ps[:, :], xb_bf[:, k * NL + ncid * 128:k * NL + (ncid + 1) * 128],
                                     wbuf2[:, k * D + half * 512:k * D + (half + 1) * 512],
                                     start=(k == 0), stop=False)
                        e.matmul(ps[:, :], s.onesrow_bf[0:1, 0:128],
                                 bsb[0:1, half * 512:(half + 1) * 512],
                                 start=False, stop=True).then_inc(mm.h, 1)
                    ctx.t(mmf)
                    mmv = mm.inc()
                    def evf(e, dst=dst, ncid=ncid, half=half, ps=ps, mmv=mmv):
                        e.wait_ge(mm.h, mmv)
                        e.tensor_copy(dst[:, ncid * D + half * 512:ncid * D + (half + 1) * 512],
                                      ps[:, :]).then_inc(ev.h, 1)
                    ctx.v(evf)
                    s.ps.release(i, ev, ev.inc())
        kv_done = ev.n
        def kvst(e):
            e.wait_ge(ev.h, kv_done)
            for ncid in range(NT):
                e.dma_start(out=Ktm_o[ncid * 128:(ncid + 1) * 128, :],
                            in_=ktm[:, ncid * D:(ncid + 1) * D]).then_inc(kst[0].h, 16)
                e.dma_start(out=Vtm_o[ncid * 128:(ncid + 1) * 128, :],
                            in_=vtm[:, ncid * D:(ncid + 1) * D]).then_inc(kst[1].h, 16)
        ctx.y(kvst)
        kst[0].inc(16 * NT)
        kst[1].inc(16 * NT)

        sev_ev = {}
        for h in range(H):
            def smm(e, h=h, kv_done=kv_done, war=sev_ev.get(h - 2, 0)):
                e.wait_ge(ev.h, kv_done)
                if war:
                    e.wait_ge(ev.h, war)
                for ncid in range(NT):
                    ins = e.matmul(pstate[h % 2][0:64, (h % 8) * 64:(h % 8) * 64 + 64],
                                   ktm[:, ncid * D + h * 64:ncid * D + h * 64 + 64],
                                   vtm[:, ncid * D + h * 64:ncid * D + h * 64 + 64],
                                   start=(ncid == 0), stop=(ncid == NT - 1))
                ins.then_inc(mm.h, 1)
            ctx.t(smm)
            mmv = mm.inc()
            def sev(e, h=h, mmv=mmv):
                e.wait_ge(mm.h, mmv)
                e.tensor_copy(ssb[(h % 2) * 64:(h % 2) * 64 + 64, (h // 2) * 64:(h // 2) * 64 + 64],
                              pstate[h % 2][0:64, (h % 8) * 64:(h % 8) * 64 + 64]).then_inc(ev.h, 1)
            ctx.v(sev)
            sev_ev[h] = ev.inc()
        s_done = ev.n
        def sst(e):
            e.wait_ge(ev.h, s_done)
            e.dma_start(out=Sh_o, in_=ssb[:, :]).then_inc(st.h, 16)
        ctx.y(sst)
        fin = st.inc(16)
        k0, k1 = kst[0].n, kst[1].n
        def finw(e):
            e.wait_ge(st.h, fin)
            e.wait_ge(kst[0].h, k0)
            e.wait_ge(kst[1].h, k1)
        ctx.y(finw)
        ctx.finish()
    return nc



def build_C2():
    nc = bass.Bass("TRN2", target_bir_lowering=False)
    xb = nc.dram_tensor("xb", [D, NL], F32, kind="ExternalInput").ap()
    WqT = nc.dram_tensor("WqT", [D, D], F32, kind="ExternalInput").ap()
    WkT = nc.dram_tensor("WkT", [D, D], F32, kind="ExternalInput").ap()
    WoT = nc.dram_tensor("WoT", [D, D], F32, kind="ExternalInput").ap()
    bq = nc.dram_tensor("bq", [128, 8], F32, kind="ExternalInput").ap()
    bkf = nc.dram_tensor("bkf", [128, 8], F32, kind="ExternalInput").ap()
    bo = nc.dram_tensor("bo", [128, 8], F32, kind="ExternalInput").ap()
    Ktm = nc.dram_tensor("Ktm", [NL, D], BF16, kind="ExternalInput").ap()
    Vtm = nc.dram_tensor("Vtm", [NL, D], BF16, kind="ExternalInput").ap()
    Sin = nc.dram_tensor("Sin", [128, 8 * 64], F32, kind="ExternalInput").ap()
    trimask = nc.dram_tensor("trimask", [128, 128], BF16, kind="ExternalInput").ap()
    xc_o = nc.dram_tensor("xc", [D, NL], F32, kind="ExternalOutput").ap()

    es = contextlib.ExitStack()
    with es:
        ctx = Ctx(nc, es)
        s = S(ctx, n512=6)
        mm, ev, va, sa, ld, st = s.mm, s.ev, s.va, s.sa, s.ld, s.st
        xb_bf = ctx.sbuf("xb_bf", [128, NCH * NL], BF16)
        wmb = [ctx.sbuf(f"wmb{i}", [128, NCH * 128], BF16) for i in range(2)]
        qfm = ctx.sbuf("qfm", [128, NCH * NL], BF16)
        kfm = ctx.sbuf("kfm", [128, NCH * NL], BF16)
        yfm = xb_bf
        ktc = [ctx.sbuf(f"ktc{i}", [128, D], BF16) for i in range(2)]
        vtc = [ctx.sbuf(f"vtc{i}", [128, D], BF16) for i in range(2)]
        state = ctx.sbuf("state", [128, 8 * 64], F32)
        state_bf = ctx.sbuf("state_bf", [128, 8 * 64], BF16)
        ssb_bf = ctx.sbuf("ssb_bf", [128, 128], BF16)
        tmsk = ctx.sbuf("tmsk", [128, 128], BF16)
        bq_sb = ctx.sbuf("bq_sb", [128, 8], F32)
        bk_sb = ctx.sbuf("bk_sb", [128, 8], F32)
        bo_sb = ctx.sbuf("bo_sb", [128, 8], F32)
        xat = [ctx.sbuf(f"xat{i}", [128, 8 * 512], F32) for i in range(2)]
        rs = [ctx.sbuf(f"rs{i}", [128, 8 * 512], F32) for i in range(2)]
        kld = [ctx.sem(f"kld{i}") for i in range(2)]
        vld = [ctx.sem(f"vld{i}") for i in range(2)]
        xald = [ctx.sem(f"xald{i}") for i in range(2)]
        xst = [ctx.sem(f"xst{i}") for i in range(2)]

        wld = [ctx.sem(f"wld{i}") for i in range(2)]
        dma_in(ctx, xb_bf[:, :], xb.rearrange("(c p) n -> p c n", p=128), ld)
        dma_in(ctx, bq_sb[:, :], bq, ld)
        dma_in(ctx, bk_sb[:, :], bkf, ld)
        dma_in(ctx, bo_sb[:, :], bo, ld)
        dma_in(ctx, state[:, :], Sin, ld)
        dma_in(ctx, tmsk[:, :], trimask, ld)
        ld_all = ld.n
        def st_bf0(e):
            e.wait_ge(ld.h, ld_all)
            e.tensor_copy(state_bf[:, :], state[:, :]).then_inc(ev.h, 1)
        ctx.v(st_bf0)
        state_ready = ev.inc()

        w_war = [0, 0]
        w_cnt = [0, 0]
        for which, (wT3, bsb, dst) in enumerate([(WqT, bq_sb, qfm), (WkT, bk_sb, kfm)]):
            for mc in range(8):
                p3 = (which * 8 + mc) % 2
                def wl(e, wT3=wT3, mc=mc, p3=p3, war=w_war[p3]):
                    if war:
                        e.wait_ge(mm.h, war)
                    e.dma_start(out=wmb[p3][:, :],
                                in_=wT3.rearrange("(k p) m -> p k m", p=128)[:, :, mc * 128:(mc + 1) * 128]
                                ).then_inc(wld[p3].h, 16)
                ctx.g(wl)
                w_cnt[p3] += 16
                for nt in range(4):
                    i, ps, wfree = s.ps.acquire()
                    def mmf(e, p3=p3, mc=mc, nt=nt, ps=ps, wfree=wfree, wc=w_cnt[p3]):
                        e.wait_ge(ld.h, ld_all)
                        e.wait_ge(wld[p3].h, wc)
                        pwait(e, wfree)
                        for k in range(NCH):
                            ins = e.matmul(ps[:, :], wmb[p3][:, k * 128:(k + 1) * 128],
                                           xb_bf[:, k * NL + nt * 512:k * NL + (nt + 1) * 512],
                                           start=(k == 0), stop=(k == NCH - 1))
                        ins.then_inc(mm.h, 1)
                    ctx.t(mmf)
                    mmv = mm.inc()
                    w_war[p3] = mmv
                    def evf(e, bsb=bsb, dst=dst, mc=mc, nt=nt, ps=ps, mmv=mmv):
                        e.wait_ge(mm.h, mmv)
                        e.tensor_scalar_add(dst[:, mc * NL + nt * 512:mc * NL + (nt + 1) * 512],
                                            ps[:, :], bsb[:, mc:mc + 1]).then_inc(ev.h, 1)
                    ctx.v(evf)
                    s.ps.release(i, ev, ev.inc())
        qk_done = ev.n

        state_upd_ev = state_ready
        ktc_war = [0, 0]
        for c in range(NT):
            p = c % 2
            def kvl(e, c=c, p=p, war=ktc_war[p]):
                if war:
                    e.wait_ge(mm.h, war)
                e.dma_start(out=ktc[p][:, :], in_=Ktm[c * 128:(c + 1) * 128, :]).then_inc(kld[p].h, 16)
                e.dma_start(out=vtc[p][:, :], in_=Vtm[c * 128:(c + 1) * 128, :]).then_inc(vld[p].h, 16)
            ctx.g(kvl)
            k_rdy = kld[p].inc(16)
            v_rdy = vld[p].inc(16)
            last_mm = 0
            for h in range(H):
                rb = (h % 2) * 64
                hc = h // 2
                i1, ps1, wf1 = s.ps.acquire()
                def smm(e, rb=rb, hc=hc, c=c, ps1=ps1, wf1=wf1, qk_done=qk_done):
                    e.wait_ge(ev.h, qk_done)
                    pwait(e, wf1)
                    e.matmul(ps1[:, 0:128],
                             kfm[rb:rb + 64, hc * NL + c * 128:hc * NL + (c + 1) * 128],
                             qfm[rb:rb + 64, hc * NL + c * 128:hc * NL + (c + 1) * 128],
                             start=True, stop=True).then_inc(mm.h, 1)
                ctx.t(smm)
                mmv = mm.inc()
                def smsk(e, ps1=ps1, mmv=mmv):
                    e.wait_ge(mm.h, mmv)
                    e.tensor_mul(ssb_bf[:, :], ps1[:, 0:128], tmsk[:, :]).then_inc(ev.h, 1)
                ctx.v(smsk)
                sm_ev = ev.inc()
                s.ps.release(i1, ev, sm_ev)
                i2, ps2, wf2 = s.ps.acquire()
                def omm(e, h=h, rb=rb, hc=hc, c=c, p=p, ps2=ps2, wf2=wf2, sm_ev=sm_ev,
                        v_rdy=v_rdy, sup=state_upd_ev):
                    e.wait_ge(ev.h, sm_ev)
                    e.wait_ge(vld[p].h, v_rdy)
                    e.wait_ge(ev.h, sup)
                    pwait(e, wf2)
                    e.matmul(ps2[0:64, 0:128], vtc[p][:, h * 64:(h + 1) * 64],
                             ssb_bf[:, :], start=True, stop=False)
                    e.matmul(ps2[0:64, 0:128],
                             state_bf[rb:rb + 64, hc * 64:(hc + 1) * 64],
                             qfm[rb:rb + 64, hc * NL + c * 128:hc * NL + (c + 1) * 128],
                             start=False, stop=True).then_inc(mm.h, 1)
                ctx.t(omm)
                mmv2 = mm.inc()
                def yev(e, h=h, c=c, ps2=ps2, mmv2=mmv2):
                    e.wait_ge(mm.h, mmv2)
                    e.tensor_copy(yfm[(h % 2) * 64:(h % 2) * 64 + 64,
                                      (h // 2) * NL + c * 128:(h // 2) * NL + (c + 1) * 128],
                                  ps2[0:64, 0:128]).then_inc(ev.h, 1)
                ctx.v(yev)
                s.ps.release(i2, ev, ev.inc())
                i3, ps3, wf3 = s.ps.acquire()
                def dmm(e, h=h, c=c, p=p, ps3=ps3, wf3=wf3, k_rdy=k_rdy, v_rdy=v_rdy):
                    e.wait_ge(kld[p].h, k_rdy)
                    e.wait_ge(vld[p].h, v_rdy)
                    pwait(e, wf3)
                    e.matmul(ps3[0:64, 0:64], ktc[p][:, h * 64:(h + 1) * 64],
                             vtc[p][:, h * 64:(h + 1) * 64],
                             start=True, stop=True).then_inc(mm.h, 1)
                ctx.t(dmm)
                mmv3 = mm.inc()
                def supd(e, h=h, rb=rb, hc=hc, ps3=ps3, mmv3=mmv3, mmv2=mmv2):
                    e.wait_ge(mm.h, max(mmv3, mmv2))
                    s.vself.edge(e, e.tensor_add(state[rb:rb + 64, hc * 64:(hc + 1) * 64],
                                 state[rb:rb + 64, hc * 64:(hc + 1) * 64],
                                 ps3[0:64, 0:64]))
                    e.tensor_copy(state_bf[rb:rb + 64, hc * 64:(hc + 1) * 64],
                                  state[rb:rb + 64, hc * 64:(hc + 1) * 64]).then_inc(ev.h, 1)
                ctx.v(supd)
                s.ps.release(i3, ev, ev.inc())
                last_mm = mmv3
            state_upd_ev = ev.n
            ktc_war[p] = last_mm
        y_done = ev.n

        st_war = [0, 0]
        for nt in range(4):
            ntp = nt % 2
            def xaf(e, nt=nt, ntp=ntp, war=st_war[ntp]):
                if war:
                    e.wait_ge(xst[ntp].h, war)
                e.dma_start(out=xat[ntp][:, :],
                            in_=xb.rearrange("(c p) n -> p c n", p=128)[:, :, nt * 512:(nt + 1) * 512]
                            ).then_inc(xald[ntp].h, 16)
            ctx.g(xaf)
            x_rdy = xald[ntp].inc(16)
            for mc in range(8):
                p3 = (nt * 8 + mc) % 2
                def wol(e, mc=mc, p3=p3, war=w_war[p3]):
                    if war:
                        e.wait_ge(mm.h, war)
                    e.dma_start(out=wmb[p3][:, :],
                                in_=WoT.rearrange("(k p) m -> p k m", p=128)[:, :, mc * 128:(mc + 1) * 128]
                                ).then_inc(wld[p3].h, 16)
                ctx.g(wol)
                w_cnt[p3] += 16
                io, pso, wfree = s.ps.acquire()
                def pmm(e, mc=mc, nt=nt, p3=p3, pso=pso, wfree=wfree, y_done=y_done, wc=w_cnt[p3]):
                    e.wait_ge(wld[p3].h, wc)
                    e.wait_ge(ev.h, y_done)
                    pwait(e, wfree)
                    for k in range(8):
                        ins = e.matmul(pso[:, :], wmb[p3][:, k * 128:(k + 1) * 128],
                                       yfm[:, k * NL + nt * 512:k * NL + (nt + 1) * 512],
                                       start=(k == 0), stop=(k == 7))
                    ins.then_inc(mm.h, 1)
                ctx.t(pmm)
                mmv = mm.inc()
                w_war[p3] = mmv
                def pev(e, mc=mc, ntp=ntp, pso=pso, mmv=mmv, x_rdy=x_rdy, war=st_war[ntp]):
                    e.wait_ge(mm.h, mmv)
                    e.wait_ge(xald[ntp].h, x_rdy)
                    if war:
                        e.wait_ge(xst[ntp].h, war)
                    e.scalar_tensor_tensor(rs[ntp][:, mc * 512:(mc + 1) * 512], pso[:, :],
                                           bo_sb[:, mc:mc + 1], xat[ntp][:, mc * 512:(mc + 1) * 512],
                                           ALU.add, ALU.add).then_inc(ev.h, 1)
                ctx.v(pev)
                s.ps.release(io, ev, ev.inc())
            t_done = ev.n
            out_ev = plan_ln_tile(ctx, s,
                                  (lambda mc, ntp=ntp: rs[ntp][:, mc * 512:(mc + 1) * 512]),
                                  (lambda mc, ntp=ntp: rs[ntp][:, mc * 512:(mc + 1) * 512]),
                                  wait_ev=t_done)
            def stf(e, nt=nt, ntp=ntp, out_ev=out_ev):
                e.wait_ge(ev.h, out_ev)
                e.dma_start(out=xc_o.rearrange("(c p) n -> p c n", p=128)[:, :, nt * 512:(nt + 1) * 512],
                            in_=rs[ntp][:, :]).then_inc(xst[ntp].h, 16)
            ctx.y(stf)
            st_war[ntp] = xst[ntp].inc(16)
        f0, f1 = xst[0].n, xst[1].n
        def finw(e):
            e.wait_ge(xst[0].h, f0)
            e.wait_ge(xst[1].h, f1)
        ctx.y(finw)
        ctx.finish()
    return nc



def build_D1():
    nc = bass.Bass("TRN2", target_bir_lowering=False)
    xc = nc.dram_tensor("xc", [D, NL], F32, kind="ExternalInput").ap()
    rwT = nc.dram_tensor("rwT", [D, 8], F32, kind="ExternalInput").ap()
    ltri = nc.dram_tensor("ltri", [128, 128], BF16, kind="ExternalInput").ap()
    meta_o = nc.dram_tensor("meta", [NL, 16], F32, kind="ExternalOutput").ap()
    cnt_o = nc.dram_tensor("cnt", [1, 8], F32, kind="ExternalOutput").ap()

    es = contextlib.ExitStack()
    with es:
        ctx = Ctx(nc, es)
        s = S(ctx, n512=6)
        mm, ev, va, sa, ld, st = s.mm, s.ev, s.va, s.sa, s.ld, s.st
        xcb = ctx.sbuf("xcb", [128, NCH * NL], F32)
        rw_sb = ctx.sbuf("rw_sb", [128, 8 * 8], F32)
        lt_sb = ctx.sbuf("lt_sb", [128, 128], BF16)
        probs = ctx.sbuf("probs_sb", [128, 8], F32)
        m1 = ctx.sbuf("m1_sb", [128, 1], F32)
        ssum = ctx.sbuf("ssum", [128, 1], F32)
        oh_sb = ctx.sbuf("oh_sb", [128, 8], F32)
        oh_bf = ctx.sbuf("oh_bf", [128, 8], BF16)
        carry = ctx.sbuf("carry", [1, 8], F32)
        posn = ctx.sbuf("posn", [128, 8], F32)
        pc_sb = ctx.sbuf("pc_sb", [128, 2], F32)
        cnts = ctx.sbuf("cnts", [1, 8], F32)
        ones128_bf = ctx.sbuf("ones128_bf", [128, 1], BF16)

        dma_in(ctx, xcb[:, :], xc.rearrange("(c p) n -> p c n", p=128), ld)
        dma_in(ctx, rw_sb[:, :], rwT.rearrange("(k p) m -> p k m", p=128), ld)
        dma_in(ctx, lt_sb[:, :], ltri, ld)
        ld_all = ld.n
        ctx.v(lambda e: e.memset(carry[:, :], 0.0))
        ctx.v(lambda e: e.memset(ones128_bf[:, :], 1.0))

        st_cnt = 0
        carry_ev = [0]
        for sub in range(NT):
            ir, psr, wfree = s.ps.acquire()
            def rmm(e, sub=sub, psr=psr, wfree=wfree):
                e.wait_ge(ld.h, ld_all)
                pwait(e, wfree)
                for k in range(8):
                    ins = e.matmul(psr[:, 0:8], xcb[:, k * NL + sub * 128:k * NL + (sub + 1) * 128],
                                   rw_sb[:, k * 8:(k + 1) * 8], start=(k == 0), stop=(k == 7))
                ins.then_inc(mm.h, 1)
            ctx.t(rmm)
            mmv = mm.inc()
            def sm1(e, psr=psr, mmv=mmv, war=st_cnt):
                e.wait_ge(mm.h, mmv)
                if war:
                    e.wait_ge(st.h, war)
                s.vself.edge(e, e.reduce_max(m1[:, :], psr[:, 0:8], axis=AX.X))
                e.tensor_scalar(probs[:, :], psr[:, 0:8], m1[:, :], None, ALU.subtract).then_inc(va.h, 1)
            ctx.v(sm1)
            vav = va.inc()
            def sm2(e, vav=vav):
                e.wait_ge(va.h, vav)
                e.activation(probs[:, :], probs[:, :], AF.Exp).then_inc(sa.h, 1)
            ctx.s(sm2)
            sav = sa.inc()
            def sm3(e, sav=sav, psr=psr):
                e.wait_ge(sa.h, sav)
                s.vself.edge(e, e.reduce_sum(ssum[:, :], probs[:, :], axis=AX.X))
                s.vself.edge(e, e.reciprocal(ssum[:, :], ssum[:, :]))
                s.vself.edge(e, e.tensor_scalar_mul(probs[:, :], probs[:, :], ssum[:, :]))
                s.vself.edge(e, e.reduce_max(pc_sb[:, 1:2], probs[:, :], axis=AX.X))
                s.vself.edge(e, e.tensor_single_scalar(oh_sb[:, :], probs[:, :], pc_sb[:, 1:2], ALU.is_equal))
                e.tensor_copy(oh_bf[:, :], oh_sb[:, :]).then_inc(ev.h, 1)
            ctx.v(sm3)
            oh_done = ev.inc()
            s.ps.release(ir, ev, oh_done)
            ip, psp, wfree2 = s.ps.acquire()
            it2, pst2, wfree3 = s.ps.acquire()
            def pmm(e, sub=sub, psp=psp, pst2=pst2, wfree2=wfree2, wfree3=wfree3,
                    oh_done=oh_done, cwait=carry_ev[0]):
                e.wait_ge(ev.h, oh_done)
                if cwait:
                    e.wait_ge(ev.h, cwait)
                pwait(e, wfree2)
                pwait(e, wfree3)
                e.matmul(psp[:, 0:8], lt_sb[:, :], oh_bf[:, :], start=True, stop=False)
                e.matmul(psp[:, 0:8], s.onesrow[0:1, :], carry[0:1, :],
                         start=False, stop=True)
                e.matmul(pst2[0:1, 0:8], ones128_bf[:, :], oh_bf[:, :],
                         start=True, stop=True).then_inc(mm.h, 1)
            ctx.t(pmm)
            mmv2 = mm.inc()
            def pev(e, sub=sub, psp=psp, pst2=pst2, mmv2=mmv2):
                e.wait_ge(mm.h, mmv2)
                e.tensor_add(carry[0:1, :], carry[0:1, :], pst2[0:1, 0:8])
                s.vself.edge(e, e.tensor_copy(posn[:, :], psp[:, 0:8]))
                s.vself.edge(e, e.tensor_mul(posn[:, :], posn[:, :], oh_sb[:, :]))
                e.reduce_sum(pc_sb[:, 0:1], posn[:, :], axis=AX.X).then_inc(ev.h, 1)
            ctx.v(pev)
            pos_done = ev.inc()
            carry_ev[0] = pos_done
            s.ps.release(ip, ev, pos_done)
            s.ps.release(it2, ev, pos_done)
            def stf(e, sub=sub, pos_done=pos_done):
                e.wait_ge(ev.h, pos_done)
                e.dma_start(out=meta_o[sub * 128:(sub + 1) * 128, 0:8], in_=oh_sb[:, :]).then_inc(st.h, 16)
                e.dma_start(out=meta_o[sub * 128:(sub + 1) * 128, 8:10], in_=pc_sb[:, :]).then_inc(st.h, 16)
            ctx.y(stf)
            st_cnt = st.inc(32)
        final_carry = carry_ev[0]
        def cev(e, final_carry=final_carry):
            e.wait_ge(ev.h, final_carry)
            e.tensor_copy(cnts[:, :], carry[0:1, :]).then_inc(ev.h, 1)
        ctx.v(cev)
        c_done = ev.inc()
        def cst(e, c_done=c_done):
            e.wait_ge(ev.h, c_done)
            e.dma_start(out=cnt_o, in_=cnts[:, :]).then_inc(st.h, 16)
        ctx.y(cst)
        fin = st.inc(16)
        ctx.y(lambda e: e.wait_ge(st.h, fin))
        ctx.finish()
    return nc



def build_D2():
    CAP = 3072.0
    NTW = 1024
    NSUB = NTW // 128
    nc = bass.Bass("TRN2", target_bir_lowering=False)
    xc = nc.dram_tensor("xc", [D, NL], F32, kind="ExternalInput").ap()
    meta_i = nc.dram_tensor("meta", [NL, 16], F32, kind="ExternalInput").ap()
    cnts_i = nc.dram_tensor("cnts", [8, 8], F32, kind="ExternalInput").ap()
    pmask = nc.dram_tensor("pmask", [8, 1], F32, kind="ExternalInput").ap()
    w1 = nc.dram_tensor("w1", [8, D, 4096], F32, kind="ExternalInput").ap()
    b1 = nc.dram_tensor("b1", [8, 128, 32], F32, kind="ExternalInput").ap()
    w2 = nc.dram_tensor("w2", [8, 4096, D], F32, kind="ExternalInput").ap()
    b2 = nc.dram_tensor("b2", [8, 128, 8], F32, kind="ExternalInput").ap()
    out_o = nc.dram_tensor("out", [D, NL], F32, kind="ExternalOutput").ap()

    es = contextlib.ExitStack()
    with es:
        ctx = Ctx(nc, es)
        s = S(ctx, n512=8)
        mm, ev, va, sa, ld, st = s.mm, s.ev, s.va, s.sa, s.ld, s.st
        xct = ctx.sbuf("xct", [128, 8 * NTW], F32)
        xm_bf = ctx.sbuf("xm_bf", [128, 8 * NTW], BF16)
        hu_bf = ctx.sbuf("hu_bf", [128, 32 * NTW], BF16)
        yacc = ctx.sbuf("yacc", [128, 8 * NTW], F32)
        gb_sb = [ctx.sbuf(f"gb_sb{i}", [128, NTW], BF16) for i in range(2)]
        aux = ctx.sbuf("aux", [128, NSUB * 16], F32)
        gate = ctx.sbuf("gate", [128, NSUB], F32)
        ob_sb = ctx.sbuf("ob_sb", [128, 8], F32)
        offr = ctx.sbuf("offr", [1, 8], F32)
        cnt_sb = ctx.sbuf("cnt_sb", [8, 8], F32)
        pm_sb = ctx.sbuf("pm_sb", [8, 1], F32)
        grow = [ctx.sbuf(f"grow{i}", [1, NTW], BF16) for i in range(2)]
        ge_f = ctx.sbuf("ge_f", [128, 1], F32)
        b1_sb = ctx.sbuf("b1_sb", [128, 32], F32)
        b2_sb = ctx.sbuf("b2_sb", [128, 8], F32)
        identf = ctx.sbuf("identf", [128, 128], F32)
        coliota = ctx.sbuf("coliota", [128, 128], F32)
        rowv = ctx.sbuf("rowv", [128, 1], F32)
        w1b = [ctx.sbuf(f"w1b_{i}", [128, 8 * 256], BF16) for i in range(2)]
        w2b = [ctx.sbuf(f"w2b_{i}", [128, 32 * 128], BF16) for i in range(2)]
        g1ld = [ctx.sem(f"g1ld{i}") for i in range(2)]
        g2ld = [ctx.sem(f"g2ld{i}") for i in range(2)]
        bld = ctx.sem("bld")
        axld = ctx.sem("axld")
        xld = ctx.sem("xld")
        ast = ctx.sem("ast")
        iosem = ctx.sem("iosem2")

        ctx.g(lambda e: e.iota(coliota[:, :], [[1, 128]], channel_multiplier=0,
                               allow_small_or_imprecise_dtypes=True))
        def iot2(e):
            e.iota(rowv[:, :], [[0, 1]], channel_multiplier=1,
                   allow_small_or_imprecise_dtypes=True).then_inc(iosem.h, 1)
        ctx.g(iot2)
        io_done = iosem.inc(1)
        def mkident(e):
            e.wait_ge(iosem.h, io_done)
            e.tensor_single_scalar(identf[:, :], coliota[:, :], rowv[:, :],
                                   ALU.is_equal).then_inc(ev.h, 1)
        ctx.v(mkident)
        ident_done = ev.inc()

        dma_in(ctx, cnt_sb[:, :], cnts_i, ld)
        dma_in(ctx, pm_sb[:, :], pmask, ld)
        ld_all = ld.n
        ioff, psoff, wfo = s.ps.acquire()
        def offmm(e, psoff=psoff, wfo=wfo):
            e.wait_ge(ld.h, ld_all)
            pwait(e, wfo)
            e.matmul(psoff[0:1, 0:8], pm_sb[:, :], cnt_sb[:, :],
                     start=True, stop=True).then_inc(mm.h, 1)
        ctx.t(offmm)
        off_mm = mm.inc()
        def offev(e, psoff=psoff, off_mm=off_mm):
            e.wait_ge(mm.h, off_mm)
            e.tensor_copy(offr[:, :], psoff[0:1, 0:8]).then_inc(ev.h, 1)
        ctx.v(offev)
        off_done = ev.inc()
        s.ps.release(ioff, ev, off_done)
        iob, psob, wfb = s.ps.acquire()
        def obm(e, psob=psob, wfb=wfb, off_done=off_done):
            e.wait_ge(ev.h, off_done)
            pwait(e, wfb)
            e.matmul(psob[:, 0:8], s.onesrow[0:1, :], offr[0:1, :],
                     start=True, stop=True).then_inc(mm.h, 1)
        ctx.t(obm)
        ob_mm = mm.inc()
        def obev(e, psob=psob, ob_mm=ob_mm):
            e.wait_ge(mm.h, ob_mm)
            e.tensor_copy(ob_sb[:, :], psob[:, 0:8]).then_inc(ev.h, 1)
        ctx.v(obev)
        ob_done = ev.inc()
        s.ps.release(iob, ev, ob_done)

        aux_war = [0]
        gb_war = [0, 0]
        gb_tr = [0]
        bias_gate = [0, 0]
        g1war = [0, 0]
        g2war = [0, 0]
        g1cnt = [0, 0]
        g2cnt = [0, 0]
        xwar = [0]
        swar = [0]
        for nt in range(2048 // NTW):
            def xcf(e, nt=nt, war=xwar[0], sw=swar[0]):
                if war:
                    e.wait_ge(ev.h, war)
                if sw:
                    e.wait_ge(ast.h, sw)
                e.dma_start(out=xct[:, :],
                            in_=xc.rearrange("(c p) n -> p c n", p=128)[:, :, nt * NTW:(nt + 1) * NTW]
                            ).then_inc(xld.h, 16)
            ctx.g(xcf)
            x_rdy = xld.inc(16)
            def auxf(e, nt=nt, war=aux_war[0]):
                if war:
                    e.wait_ge(ev.h, war)
                e.dma_start(out=aux[:, :].rearrange("p (a c) -> p a c", c=16),
                            in_=meta_i[nt * NTW:(nt + 1) * NTW, :].rearrange("(a p) c -> p a c", p=128)
                            ).then_inc(axld.h, 16)
            ctx.g(auxf)
            aux_rdy = axld.inc(16)
            def xcast(e, x_rdy=x_rdy):
                e.wait_ge(xld.h, x_rdy)
                e.tensor_copy(xm_bf[:, :], xct[:, :]).then_inc(ev.h, 1)
            ctx.v(xcast)
            xbf_done = ev.inc()
            for sub in range(NSUB):
                def gf(e, sub=sub, aux_rdy=aux_rdy, ob_done=ob_done):
                    e.wait_ge(axld.h, aux_rdy)
                    e.wait_ge(ev.h, ob_done)
                    s.vself.edge(e, e.tensor_mul(s.lnt[0][:, 0:8], aux[:, sub * 16:sub * 16 + 8], ob_sb[:, :]))
                    s.vself.edge(e, e.reduce_sum(ge_f[:, :], s.lnt[0][:, 0:8], axis=AX.X))
                    s.vself.edge(e, e.tensor_add(ge_f[:, :], ge_f[:, :], aux[:, sub * 16 + 8:sub * 16 + 9]))
                    s.vself.edge(e, e.tensor_single_scalar(ge_f[:, :], ge_f[:, :], CAP, ALU.is_lt))
                    e.tensor_mul(gate[:, sub:sub + 1], ge_f[:, :],
                                 aux[:, sub * 16 + 9:sub * 16 + 10]).then_inc(ev.h, 1)
                ctx.v(gf)
                ev.inc()
            gates_done = ev.n
            aux_war[0] = gates_done

            for eidx in range(8):
                gp = eidx % 2
                itr, pst, wtt = s.ps.acquire()
                gcp_ev = [0]
                for sub in range(NSUB):
                    def get(e, eidx=eidx, sub=sub, gates_done=gates_done, ident_done=ident_done,
                            pst=pst, wtt=wtt, prev_tr=gb_tr[0]):
                        e.wait_ge(ev.h, max(gates_done, ident_done))
                        if prev_tr:
                            e.wait_ge(mm.h, prev_tr)
                        if sub == 0:
                            pwait(e, wtt)
                        e.tensor_mul(ge_f[:, :], gate[:, sub:sub + 1],
                                     aux[:, sub * 16 + eidx:sub * 16 + eidx + 1]).then_inc(ev.h, 1)
                    ctx.v(get)
                    gev = ev.inc()
                    def gtr(e, sub=sub, pst=pst, gev=gev, gw=gcp_ev[0]):
                        e.wait_ge(ev.h, gev)
                        if gw and sub % 4 == 0:
                            e.wait_ge(ev.h, gw)
                        e.transpose(pst[0:1, (sub % 4) * 128:(sub % 4 + 1) * 128], ge_f[:, :],
                                    identf[:, :]).then_inc(mm.h, 1)
                    ctx.t(gtr)
                    gb_tr[0] = mm.inc()
                    if sub % 4 == 3:
                        def gcp(e, eidx=eidx, gp=gp, sub=sub, pst=pst, tr_mm=gb_tr[0],
                                war=gb_war[gp]):
                            e.wait_ge(mm.h, tr_mm)
                            if war and sub == 3:
                                e.wait_ge(ev.h, war)
                            with nc.allow_low_precision(reason="top-1 gate is a bf16 matmul-side operand"):
                                e.tensor_copy(grow[gp][0:1, (sub - 3) * 128:(sub + 1) * 128],
                                              pst[0:1, 0:512]).then_inc(ev.h, 1)
                        ctx.v(gcp)
                        gcp_ev[0] = ev.inc()
                grow_done = ev.n
                ib, psb, wfb2 = s.ps.acquire()
                ib2, psb2, wfb3 = s.ps.acquire()
                def gbm(e, gp=gp, psb=psb, psb2=psb2, wfb2=wfb2, wfb3=wfb3, grow_done=grow_done):
                    e.wait_ge(ev.h, grow_done)
                    pwait(e, wfb2)
                    pwait(e, wfb3)
                    e.matmul(psb[:, :], s.onesrow_bf[0:1, :], grow[gp][0:1, 0:512],
                             start=True, stop=True)
                    e.matmul(psb2[:, :], s.onesrow_bf[0:1, :], grow[gp][0:1, 512:1024],
                             start=True, stop=True).then_inc(mm.h, 1)
                ctx.t(gbm)
                mmv = mm.inc()
                def gbe(e, gp=gp, psb=psb, psb2=psb2, mmv=mmv):
                    e.wait_ge(mm.h, mmv)
                    with nc.allow_low_precision(reason="top-1 gate carried at bf16"):
                        e.tensor_copy(gb_sb[gp][:, 0:512], psb[:, :])
                        e.tensor_copy(gb_sb[gp][:, 512:1024], psb2[:, :]).then_inc(ev.h, 1)
                ctx.v(gbe)
                gb_done = ev.inc()
                s.ps.release(ib, ev, gb_done)
                s.ps.release(ib2, ev, gb_done)

                def bf_(e, eidx=eidx, g1=bias_gate[0], g2=bias_gate[1]):
                    if g1:
                        e.wait_ge(sa.h, g1)
                    if g2:
                        e.wait_ge(ev.h, g2)
                    e.dma_start(out=b1_sb[:, :], in_=b1[eidx]).then_inc(bld.h, 16)
                    e.dma_start(out=b2_sb[:, :], in_=b2[eidx]).then_inc(bld.h, 16)
                ctx.g(bf_)
                b_rdy = bld.inc(32)
                for m in range(32):
                    p2 = (m // 2) % 2
                    if m % 2 == 0:
                        def w1l(e, eidx=eidx, m0=m, p2=p2, war=g1war[p2]):
                            if war:
                                e.wait_ge(mm.h, war)
                            e.dma_start(out=w1b[p2][:, :],
                                        in_=w1[eidx, :, m0 * 128:(m0 + 2) * 128]
                                        .rearrange("(k p) m -> p k m", p=128)).then_inc(g1ld[p2].h, 16)
                        ctx.g(w1l)
                        g1cnt[p2] += 16
                    for half in range(NTW // 512):
                        i1, ps1, wf1 = s.ps.acquire()
                        def w1m(e, m=m, p2=p2, half=half, ps1=ps1, wf1=wf1, gc=g1cnt[p2],
                                xbf_done=xbf_done):
                            e.wait_ge(g1ld[p2].h, gc)
                            e.wait_ge(ev.h, xbf_done)
                            pwait(e, wf1)
                            mq = (m % 2) * 128
                            for k in range(8):
                                ins = e.matmul(ps1[:, :], w1b[p2][:, k * 256 + mq:k * 256 + mq + 128],
                                               xm_bf[:, k * NTW + half * 512:k * NTW + (half + 1) * 512],
                                               start=(k == 0), stop=(k == 7))
                            ins.then_inc(mm.h, 1)
                        ctx.t(w1m)
                        g1war[p2] = mm.inc()
                        mmv = g1war[p2]
                        def gel(e, m=m, half=half, ps1=ps1, mmv=mmv, b_rdy=b_rdy):
                            e.wait_ge(mm.h, mmv)
                            e.wait_ge(bld.h, b_rdy)
                            e.activation(hu_bf[:, m * NTW + half * 512:m * NTW + (half + 1) * 512],
                                         ps1[:, :], AF.Gelu,
                                         bias=b1_sb[:, m:m + 1]).then_inc(sa.h, 1)
                        ctx.s(gel)
                        s.ps.release(i1, sa, sa.inc())
                h_done = sa.n
                for mc in range(8):
                    p2 = mc % 2
                    def w2l(e, eidx=eidx, mc0=mc, p2=p2, war=g2war[p2]):
                        if war:
                            e.wait_ge(mm.h, war)
                        e.dma_start(out=w2b[p2][:, :],
                                    in_=w2[eidx, :, mc0 * 128:(mc0 + 1) * 128]
                                    .rearrange("(k p) m -> p k m", p=128)).then_inc(g2ld[p2].h, 16)
                    ctx.g(w2l)
                    g2cnt[p2] += 16
                    for half in range(NTW // 512):
                        i2, ps2, wf2 = s.ps.acquire()
                        def w2m(e, mc=mc, p2=p2, half=half, ps2=ps2, wf2=wf2, gc=g2cnt[p2],
                                h_done=h_done):
                            e.wait_ge(g2ld[p2].h, gc)
                            e.wait_ge(sa.h, h_done)
                            pwait(e, wf2)
                            for k in range(32):
                                ins = e.matmul(ps2[:, :], w2b[p2][:, k * 128:(k + 1) * 128],
                                               hu_bf[:, k * NTW + half * 512:k * NTW + (half + 1) * 512],
                                               start=(k == 0), stop=(k == 31))
                            ins.then_inc(mm.h, 1)
                        ctx.t(w2m)
                        g2war[p2] = mm.inc()
                        mmv = g2war[p2]
                        col = lambda mc, half: mc * NTW + half * 512
                        if eidx == 0:
                            def dev(e, mc=mc, half=half, gp=gp, ps2=ps2, mmv=mmv, b_rdy=b_rdy,
                                    gb_done=gb_done):
                                e.wait_ge(mm.h, mmv)
                                e.wait_ge(ev.h, gb_done)
                                e.wait_ge(bld.h, b_rdy)
                                lp = (mc * 2 + half) % 2
                                s.vself.edge(e, e.tensor_scalar_add(s.lnt[lp][:, :], ps2[:, :], b2_sb[:, mc:mc + 1]))
                                e.tensor_mul(yacc[:, mc * NTW + half * 512:mc * NTW + (half + 1) * 512],
                                             s.lnt[lp][:, :],
                                             gb_sb[gp][:, half * 512:(half + 1) * 512]).then_inc(ev.h, 1)
                        else:
                            def dev(e, eidx=eidx, mc=mc, half=half, gp=gp, ps2=ps2, mmv=mmv,
                                    b_rdy=b_rdy, gb_done=gb_done):
                                e.wait_ge(mm.h, mmv)
                                e.wait_ge(ev.h, gb_done)
                                e.wait_ge(bld.h, b_rdy)
                                lp = (mc * 2 + half) % 2
                                s.vself.edge(e, e.tensor_scalar_add(s.lnt[lp][:, :], ps2[:, :], b2_sb[:, mc:mc + 1]))
                                s.vself.edge(e, e.tensor_mul(s.lnt[lp][:, :], s.lnt[lp][:, :],
                                             gb_sb[gp][:, half * 512:(half + 1) * 512]))
                                e.tensor_add(yacc[:, mc * NTW + half * 512:mc * NTW + (half + 1) * 512],
                                             yacc[:, mc * NTW + half * 512:mc * NTW + (half + 1) * 512],
                                             s.lnt[lp][:, :]).then_inc(ev.h, 1)
                        ctx.v(dev)
                        s.ps.release(i2, ev, ev.inc())
                bias_gate[0] = sa.n
                bias_gate[1] = ev.n
                gb_war[gp] = ev.n
            y_done = ev.n
            aux_war[0] = y_done

            for halfq in range(NTW // 512):
                for mc in range(8):
                    def t2f(e, mc=mc, halfq=halfq, y_done=y_done, x_rdy=x_rdy):
                        e.wait_ge(ev.h, y_done)
                        e.wait_ge(xld.h, x_rdy)
                        e.tensor_add(xct[:, mc * NTW + halfq * 512:mc * NTW + halfq * 512 + 512],
                                     xct[:, mc * NTW + halfq * 512:mc * NTW + halfq * 512 + 512],
                                     yacc[:, mc * NTW + halfq * 512:mc * NTW + halfq * 512 + 512]
                                     ).then_inc(ev.h, 1)
                    ctx.v(t2f)
                    ev.inc()
                pre_done = ev.n
                out_ev = plan_ln_tile(ctx, s,
                                      (lambda mc, halfq=halfq: xct[:, mc * NTW + halfq * 512:mc * NTW + halfq * 512 + 512]),
                                      (lambda mc, halfq=halfq: xct[:, mc * NTW + halfq * 512:mc * NTW + halfq * 512 + 512]),
                                      wait_ev=pre_done)
                def stf(e, nt=nt, halfq=halfq, out_ev=out_ev):
                    e.wait_ge(ev.h, out_ev)
                    e.dma_start(out=out_o.rearrange("(c p) n -> p c n", p=128)
                                [:, :, nt * NTW + halfq * 512:nt * NTW + halfq * 512 + 512],
                                in_=xct[:, :].rearrange("p (c n) -> p c n", n=NTW)[:, :, halfq * 512:halfq * 512 + 512]
                                ).then_inc(ast.h, 16)
                ctx.y(stf)
                swar[0] = ast.inc(16)
            xwar[0] = ev.n

        fa = ast.n
        ctx.y(lambda e: e.wait_ge(ast.h, fa))
        ctx.finish()
    return nc



def _fm(a):
    return np.ascontiguousarray(a.T)


_NC_CACHE = {}

def _get(name, fn):
    if name not in _NC_CACHE:
        _NC_CACHE[name] = fn()
    return _NC_CACHE[name]


def run_A1(x_shards, wqkvT, projm):
    nc = _get("A1", build_A1)
    in_maps = [{"x": x_shards[c], "wqkvT": wqkvT, "projm": projm} for c in range(NCORE)]
    res = run_bass_kernel_spmd(nc, in_maps, core_ids=list(range(NCORE)))
    return res.results


def _bias_fm(b, c):
    return np.ascontiguousarray(np.asarray(b, dtype=np.float32).reshape(c, 128).T)


def stage_A(inp, x_shards):
    wqkvT = np.ascontiguousarray(np.asarray(inp["L0_attn_w_qkv"]).T)
    projm = np.ascontiguousarray(np.asarray(inp["L0_attn_proj"]).reshape(D, 256))
    r1 = run_A1(x_shards, wqkvT, projm)
    w_outT = np.ascontiguousarray(np.asarray(inp["L0_attn_w_out"]).T)
    w1T = np.ascontiguousarray(np.asarray(inp["L0_attn_w1"]).T)
    w2T = np.ascontiguousarray(np.asarray(inp["L0_attn_w2"]).T)
    common = {
        "w_outT": w_outT, "w1T": w1T, "w2T": w2T,
        "b_out": _bias_fm(inp["L0_attn_b_out"], 8),
        "gamma": _bias_fm(inp["L0_attn_gamma"], 8),
        "beta": _bias_fm(inp["L0_attn_beta"], 8),
        "b1": _bias_fm(inp["L0_attn_b1"], 32),
        "b2": _bias_fm(inp["L0_attn_b2"], 8),
    }
    nc = _get("A2", build_A2)
    in_maps = []
    for c in range(NCORE):
        m = dict(common)
        m["x"] = x_shards[c]
        m["qf"] = np.asarray(r1[c]["qf"])
        m["kvx_a"] = np.asarray(r1[c]["kvx"])
        m["kvx_b"] = np.asarray(r1[c ^ 1]["kvx"])
        in_maps.append(m)
    res = run_bass_kernel_spmd(nc, in_maps, core_ids=list(range(NCORE)))
    return [np.asarray(res.results[c]["xa"]) for c in range(NCORE)]


def _bf(a):
    import ml_dtypes
    return np.ascontiguousarray(np.asarray(a).astype(ml_dtypes.bfloat16))


def _ct(a):
    return np.ascontiguousarray(np.asarray(a, dtype=np.float32))


def _fm_shard(xf, c):
    return np.ascontiguousarray(xf[c * NL:(c + 1) * NL].T)


def stage_B(inp, xa_shards):
    import ml_dtypes
    rq = np.asarray(inp["L0_ffn_rope_q_w"]).reshape(16, 32, 256)
    rk = np.asarray(inp["L0_ffn_rope_k_w"]).reshape(16, 32, 1024)
    inv_freq = 1.0 / 10000.0 ** (np.arange(0, 32, 2, dtype=np.float64) / 32)
    tpos = np.arange(T, dtype=np.float64)
    fr = tpos[:, None] * inv_freq[None, :]
    cosT = np.ascontiguousarray(np.tile(np.cos(fr).T, (8, 1)).astype(np.float32))
    sinT = np.ascontiguousarray(np.tile(np.sin(fr).T, (8, 1)).astype(np.float32))
    common = {
        "ln1w": _bias_fm(inp["L0_ffn_ln1_w"], 8),
        "kvdT": _ct(np.asarray(inp["L0_ffn_kv_d"]).T),
        "qdT": _ct(np.asarray(inp["L0_ffn_q_d"]).T),
        "kunT": _ct(np.asarray(inp["L0_ffn_k_u"]).reshape(16, 64, 256)[:, 32:].reshape(512, 256).T),
        "qunT": _ct(np.asarray(inp["L0_ffn_q_u"]).reshape(16, 64, 256)[:, 32:].reshape(512, 256).T),
        "vuT": _ct(np.asarray(inp["L0_ffn_v_u"]).T),
        "rqaT": _ct(rq[:, :16].reshape(256, 256).T),
        "rqbT": _ct(rq[:, 16:].reshape(256, 256).T),
        "rkaT": _ct(rk[:, :16].reshape(256, 1024).T),
        "rkbT": _ct(rk[:, 16:].reshape(256, 1024).T),
    }
    nc = _get("B1", build_B1)
    in_maps = []
    for c in range(NCORE):
        m = dict(common)
        m["xa"] = xa_shards[c]
        half = c % 2
        m["cosf"] = np.ascontiguousarray(cosT[:, half * NL:(half + 1) * NL])
        m["sinf"] = np.ascontiguousarray(sinT[:, half * NL:(half + 1) * NL])
        in_maps.append(m)
    r1 = run_bass_kernel_spmd(nc, in_maps, core_ids=list(range(NCORE))).results
    Ks = [np.asarray(r1[c]["K"]) for c in range(NCORE)]
    Qs = [np.asarray(r1[c]["Q"]) for c in range(NCORE)]
    Vs = [np.asarray(r1[c]["V"]) for c in range(NCORE)]

    stair = (np.arange(2176)[None, :] >= (np.arange(128)[:, None] + 1023)).astype(ml_dtypes.bfloat16)
    stair = np.ascontiguousarray(stair)
    owT = _ct(np.asarray(inp["L0_ffn_o_w"]).T)
    t1s = [None] * NCORE
    for odd in (False, True):
        nc2 = _get(f"B2{'o' if odd else 'e'}", lambda odd=odd: build_B2(odd))
        cores = [1, 3, 5, 7] if odd else [0, 2, 4, 6]
        maps = []
        for c in cores:
            b = c // 2
            if odd:
                Kf = np.concatenate([Ks[2 * b], Ks[2 * b + 1]], axis=1)
                Vf = np.concatenate([Vs[2 * b], Vs[2 * b + 1]], axis=0)
            else:
                Kf, Vf = Ks[2 * b], Vs[2 * b]
            maps.append({"xa": xa_shards[c], "K": np.ascontiguousarray(Kf),
                         "Q": Qs[c], "V": np.ascontiguousarray(Vf),
                         "stair": stair, "owT": owT})
        r2 = run_bass_kernel_spmd(nc2, maps, core_ids=cores).results
        for i, c in enumerate(cores):
            t1s[c] = np.asarray(r2[i]["t1"])

    gT = np.concatenate([np.asarray(inp["L0_ffn_shared_gate"]),
                         np.asarray(inp["L0_ffn_routed_gate"])], axis=0).transpose(0, 2, 1)
    uT = np.concatenate([np.asarray(inp["L0_ffn_shared_up"]),
                         np.asarray(inp["L0_ffn_routed_up"])], axis=0).transpose(0, 2, 1)
    dT = np.concatenate([np.asarray(inp["L0_ffn_shared_down"]),
                         np.asarray(inp["L0_ffn_routed_down"])], axis=0).transpose(0, 2, 1)
    common3 = {
        "ln2w": _bias_fm(inp["L0_ffn_ln2_w"], 8),
        "rwT": _ct(np.asarray(inp["L0_ffn_router_w"]).T),
        "rbias": _ct(np.asarray(inp["L0_ffn_routing_bias"]).reshape(1, 7)),
        "gT": _ct(gT), "uT": _ct(uT), "dT": _ct(dT),
    }
    nc3 = _get("B3", build_B3)
    maps = []
    for c in range(NCORE):
        m = dict(common3)
        m["t1"] = t1s[c]
        m["xa"] = xa_shards[c]
        maps.append(m)
    r3 = run_bass_kernel_spmd(nc3, maps, core_ids=list(range(NCORE))).results
    return [np.asarray(r3[c]["xb"]) for c in range(NCORE)]


def stage_C(inp, xb_shards):
    import ml_dtypes
    common1 = {
        "WkT": _ct(np.asarray(inp["L1_attn_Wk"]).T),
        "WvT": _ct(np.asarray(inp["L1_attn_Wv"]).T),
        "bk": _ct(np.asarray(inp["L1_attn_bk"]).reshape(1, D)),
        "bv": _ct(np.asarray(inp["L1_attn_bv"]).reshape(1, D)),
    }
    nc1 = _get("C1", build_C1)
    maps = [dict(common1, xb=xb_shards[c]) for c in range(NCORE)]
    r1 = run_bass_kernel_spmd(nc1, maps, core_ids=list(range(NCORE))).results
    trimask = (np.arange(128)[None, :] >= np.arange(128)[:, None]).astype(ml_dtypes.bfloat16)
    common2 = {
        "WqT": _ct(np.asarray(inp["L1_attn_Wq"]).T),
        "WkT": _ct(np.asarray(inp["L1_attn_Wk"]).T),
        "WoT": _ct(np.asarray(inp["L1_attn_Wo"]).T),
        "bq": _bias_fm(inp["L1_attn_bq"], 8),
        "bkf": _bias_fm(inp["L1_attn_bk"], 8),
        "bo": _bias_fm(inp["L1_attn_bo"], 8),
        "trimask": np.ascontiguousarray(trimask),
    }
    nc2 = _get("C2", build_C2)
    maps = []
    for c in range(NCORE):
        m = dict(common2)
        m["xb"] = xb_shards[c]
        m["Ktm"] = np.asarray(r1[c]["Ktm"])
        m["Vtm"] = np.asarray(r1[c]["Vtm"])
        m["Sin"] = (np.asarray(r1[c - 1]["Sh"]) if c % 2 == 1
                    else np.zeros((128, 512), np.float32))
        maps.append(m)
    r2 = run_bass_kernel_spmd(nc2, maps, core_ids=list(range(NCORE))).results
    return [np.asarray(r2[c]["xc"]) for c in range(NCORE)]


def stage_D(inp, xc_shards):
    import ml_dtypes
    ltri = (np.arange(128)[:, None] < np.arange(128)[None, :]).astype(ml_dtypes.bfloat16)
    common1 = {
        "rwT": _ct(np.asarray(inp["L1_ffn_router_w"]).T),
        "ltri": np.ascontiguousarray(ltri),
    }
    nc1 = _get("D1", build_D1)
    maps = [dict(common1, xc=xc_shards[c]) for c in range(NCORE)]
    r1 = run_bass_kernel_spmd(nc1, maps, core_ids=list(range(NCORE))).results
    cnts_all = np.ascontiguousarray(
        np.concatenate([np.asarray(r1[c]["cnt"]) for c in range(NCORE)], axis=0))
    b1 = np.asarray(inp["L1_ffn_b1"]).reshape(8, 32, 128).transpose(0, 2, 1)
    b2 = np.asarray(inp["L1_ffn_b2"]).reshape(8, 8, 128).transpose(0, 2, 1)
    common2 = {
        "cnts": cnts_all,
        "w1": _ct(np.asarray(inp["L1_ffn_w1"])),
        "b1": _ct(b1),
        "w2": _ct(np.asarray(inp["L1_ffn_w2"])),
        "b2": _ct(b2),
    }
    nc2 = _get("D2", build_D2)
    maps = []
    for c in range(NCORE):
        m = dict(common2)
        m["xc"] = xc_shards[c]
        m["meta"] = np.asarray(r1[c]["meta"])
        m["pmask"] = np.ascontiguousarray((np.arange(8) < c).astype(np.float32).reshape(8, 1))
        maps.append(m)
    r2 = run_bass_kernel_spmd(nc2, maps, core_ids=list(range(NCORE))).results
    return [np.asarray(r2[c]["out"]) for c in range(NCORE)]


def kernel(**inputs):
    x = np.asarray(inputs["L0_attn_x"], dtype=np.float32).reshape(B * T, D)
    x_shards = [_fm_shard(x, c) for c in range(NCORE)]
    xa = stage_A(inputs, x_shards)
    xb = stage_B(inputs, xa)
    xc = stage_C(inputs, xb)
    outs = stage_D(inputs, xc)
    full = np.concatenate([o.T for o in outs], axis=0)
    return np.ascontiguousarray(full.reshape(B, T, D), dtype=np.float32)
